# version 03
# baseline (speedup 1.0000x reference)
import numpy as np
import ml_dtypes
import concourse.bass as bass
import concourse.mybir as mybir
from concourse.bass_utils import run_bass_kernel_spmd

F32 = mybir.dt.float32
BF16 = mybir.dt.bfloat16
I32 = mybir.dt.int32
AF = mybir.ActivationFunctionType
ALU = mybir.AluOpType
AX = mybir.AxisListType
NPBF16 = ml_dtypes.bfloat16

D = 2048
TPC = 1024
HALO = 3
TP = 1032
TG = 344
C = 128
NCH = 8
HV, HQ, DK, DV = 16, 8, 128, 128
NDB = 16
NFQ, NFG = 32, 16
NFB = NFQ + NFG + 1
FFH = 8192
NHB = 64
NEG = -1e30
EPS_LN = 1e-5
EPS_L2 = 1e-6
EPS_RMS = 1e-6
EPS_FFN = 1e-8
LSD = float(np.log(np.sqrt(DK)))
NEWTON_ITERS = 6


class Ctr:
    def __init__(self):
        self.v = {}

    def inc(self, sem, k=1):
        self.v[sem.name] = self.v.get(sem.name, 0) + k
        return self.v[sem.name]

    def get(self, sem):
        return self.v.get(sem.name, 0)


def build(probe=None, upto="all"):
    nc = bass.Bass("TRN2", target_bir_lowering=False, debug=False)

    x_in = nc.dram_tensor("x", [TP, D], F32, kind="ExternalInput").ap()
    wqkv_b = nc.dram_tensor("wqkv_b", [NFQ, D, 128], F32, kind="ExternalInput").ap()
    wgate_b = nc.dram_tensor("wgate_b", [NFG, D, 128], F32, kind="ExternalInput").ap()
    wba_in = nc.dram_tensor("wba", [D, 32], F32, kind="ExternalInput").ap()
    convw_in = nc.dram_tensor("convw", [128, NFQ * 4], F32, kind="ExternalInput").ap()
    la_in = nc.dram_tensor("loga_params", [HV, 2], F32, kind="ExternalInput").ap()
    rmsw_in = nc.dram_tensor("rmsw", [128, 1], F32, kind="ExternalInput").ap()
    wout_b = nc.dram_tensor("wout_b", [4, D, 512], F32, kind="ExternalInput").ap()
    wg_b = nc.dram_tensor("wg_b", [NHB, D, 128], F32, kind="ExternalInput").ap()
    wu_b = nc.dram_tensor("wu_b", [NHB, D, 128], F32, kind="ExternalInput").ap()
    wd_b = nc.dram_tensor("wd_b", [16, FFH, 128], F32, kind="ExternalInput").ap()
    ident_in = nc.dram_tensor("ident_bf", [128, 128], BF16, kind="ExternalInput").ap()
    identf_in = nc.dram_tensor("ident_f", [128, 128], F32, kind="ExternalInput").ap()
    maskLT_in = nc.dram_tensor("maskLT", [128, 128], F32, kind="ExternalInput").ap()
    maskUT_in = nc.dram_tensor("maskUT", [128, 128], F32, kind="ExternalInput").ap()
    maskUD_in = nc.dram_tensor("maskUD", [128, 128], F32, kind="ExternalInput").ap()
    onesc_in = nc.dram_tensor("ones_col_bf", [128, 1], BF16, kind="ExternalInput").ap()
    onesrf_in = nc.dram_tensor("ones_row_f", [1, 128], F32, kind="ExternalInput").ap()
    onesr_in = nc.dram_tensor("ones_row_bf", [1, 128], BF16, kind="ExternalInput").ap()
    zeros_in = nc.dram_tensor("zeros_f", [128, 128], F32, kind="ExternalInput").ap()
    ccols_in = nc.dram_tensor("ccols", [128, 8], F32, kind="ExternalInput").ap()
    hopm_in = nc.dram_tensor("hop_mcol", [128, 7], F32, kind="ExternalInput").ap()
    hopimi_in = nc.dram_tensor("hop_imi", [128, 7 * 128], F32, kind="ExternalInput").ap()

    y_out = nc.dram_tensor("y", [TPC, D], F32, kind="ExternalOutput").ap()

    PROBE_SHAPES = {
        "pre": ([NFB, 128, TP], BF16),
        "qkv": ([4, 1024, TPC], BF16),
        "rows": ([HV, 4 * TPC], F32),
        "spill": ([NCH, HV, 4, 128, 128], BF16),
        "comp": ([2, 128, HV * 128], BF16),
        "state": ([128, HV * 128], BF16),
        "pf": ([NDB, 128, TPC], BF16),
        "x2": ([TPC, D], F32),
        "xq": ([128, NDB * TPC], BF16),
        "p2": ([128, NHB * TPC], BF16),
        "ffn": ([NDB, 128, TPC], BF16),
    }
    probe_out = None
    if probe is not None:
        shp, dt_ = PROBE_SHAPES[probe]
        probe_out = nc.dram_tensor("probe", shp, dt_, kind="ExternalOutput").ap()

    pre_scr = nc.dram_tensor("pre_scr", [NFB, 128, TP], BF16, kind="Internal").ap()
    spill = nc.dram_tensor("spill", [NCH, HV, 4, 128, 128], BF16, kind="Internal").ap()
    p_scr = nc.dram_tensor("p_scr", [NDB, 128, TPC], BF16, kind="Internal").ap()
    x2_scr = nc.dram_tensor("x2_scr", [TPC, D], F32, kind="Internal").ap()
    ffn_scr = nc.dram_tensor("ffn_scr", [NDB, 128, TPC], BF16, kind="Internal").ap()
    cc_in = nc.dram_tensor("cc_in", [128, HV * 2 * 128], BF16, kind="Internal").ap()
    cc_out = nc.dram_tensor("cc_out", [8 * 128, HV * 2 * 128], BF16, kind="Internal").ap()

    ctr = Ctr()
    from contextlib import ExitStack

    with ExitStack() as top:
        E = top.enter_context
        s_ld = E(nc.semaphore("s_ld"))
        s_lda = E(nc.semaphore("s_lda"))
        s_ldb = E(nc.semaphore("s_ldb"))
        s_st = E(nc.semaphore("s_st"))
        s_sta = E(nc.semaphore("s_sta"))
        s_stb = E(nc.semaphore("s_stb"))
        s_ga = E(nc.semaphore("s_ga"))
        s_gb = E(nc.semaphore("s_gb"))
        s_mm = E(nc.semaphore("s_mm"))
        s_v = E(nc.semaphore("s_v"))
        s_s = E(nc.semaphore("s_s"))
        s_g = E(nc.semaphore("s_g"))
        s_cc = E(nc.semaphore("s_cc"))
        sems = dict(ld=s_ld, lda=s_lda, ldb=s_ldb, st=s_st, sta=s_sta, stb=s_stb,
                    ga=s_ga, gb=s_gb, mm=s_mm, v=s_v, s=s_s, g=s_g, cc=s_cc)

        c_ident = E(nc.sbuf_tensor("c_ident", [128, 128], BF16))
        c_identf = E(nc.sbuf_tensor("c_identf", [128, 128], F32))
        c_maskLT = E(nc.sbuf_tensor("c_maskLT", [128, 128], F32))
        c_maskUT = E(nc.sbuf_tensor("c_maskUT", [128, 128], F32))
        c_maskUD = E(nc.sbuf_tensor("c_maskUD", [128, 128], F32))
        c_onec = E(nc.sbuf_tensor("c_onec", [128, 1], BF16))
        c_onerf = E(nc.sbuf_tensor("c_onerf", [1, 128], F32))
        c_oner = E(nc.sbuf_tensor("c_oner", [1, 128], BF16))
        c_zero = E(nc.sbuf_tensor("c_zero", [128, 128], F32))
        c_cols = E(nc.sbuf_tensor("c_cols", [128, 8], F32))
        c_hopm = E(nc.sbuf_tensor("c_hopm", [128, 7], F32))
        c_hopimi = E(nc.sbuf_tensor("c_hopimi", [128, 7 * 128], F32))
        c_rmsw = E(nc.sbuf_tensor("c_rmsw", [128, 1], F32))
        c_convw = E(nc.sbuf_tensor("c_convw", [128, NFQ * 4], F32))
        c_la = E(nc.sbuf_tensor("c_la", [HV, 8], F32))

        blk = E(nc.Block())

        @blk.sync
        def _(q):
            for dst, src in [
                (c_ident[:, :], ident_in[:, :]), (c_identf[:, :], identf_in[:, :]),
                (c_maskLT[:, :], maskLT_in[:, :]), (c_maskUT[:, :], maskUT_in[:, :]),
                (c_maskUD[:, :], maskUD_in[:, :]), (c_onec[:, :], onesc_in[:, :]),
                (c_onerf[:, :], onesrf_in[:, :]), (c_oner[:, :], onesr_in[:, :]),
                (c_zero[:, :], zeros_in[:, :]), (c_cols[:, :], ccols_in[:, :]),
                (c_hopm[:, :], hopm_in[:, :]),
                (c_hopimi[:, :], hopimi_in[:, :]),
                (c_rmsw[:, :], rmsw_in[:, :]),
                (c_convw[:, :], convw_in[:, :]),
                (c_la[:, 0:2], la_in[:, :]),
            ]:
                q.dma_start(out=dst, in_=src).then_inc(s_ld, 16)
                ctr.inc(s_ld, 16)
        CONSTS = ctr.get(s_ld)

        @blk.scalar
        def _(s):
            s.wait_ge(s_ld, CONSTS)
            ctr.inc(s_s, 1)
            s.activation(c_la[:, 4:5], c_la[:, 0:1], AF.Exp).then_inc(s_s, 1)

        @blk.vector
        def _(v):
            v.wait_ge(s_s, ctr.get(s_s))
            v.tensor_scalar(c_la[:, 3:4], c_la[:, 4:5], -1.0, None, ALU.mult)
            ctr.inc(s_v, 1)
            v.tensor_scalar(c_la[:, 5:6], c_la[:, 1:2], -1.0, None, ALU.mult).then_inc(s_v, 1)

        with ExitStack() as sc2:
            E2 = sc2.enter_context
            qk_f = E2(nc.sbuf_tensor("qk_f", [128, 16 * TPC], BF16))
            k_tok = E2(nc.sbuf_tensor("k_tok", [128, HQ * TPC], BF16))
            v_tok = E2(nc.sbuf_tensor("v_tok", [128, HV * TPC], BF16))
            barows = E2(nc.sbuf_tensor("barows", [HV, 4 * TPC], F32))
            state = E2(nc.sbuf_tensor("state", [128, HV * 128], BF16))
            compP = E2(nc.sbuf_tensor("compP", [128, HV * 128], BF16))
            compR = E2(nc.sbuf_tensor("compR", [128, HV * 128], BF16))

            _p1(nc, blk, ctr, sems, locals())
            if probe == "pre":
                _probe_dump(nc, blk, ctr, sems, probe_out[0:NFB - 1, :, :], pre_scr[0:NFB - 1, :, :])
                _probe_dump(nc, blk, ctr, sems, probe_out[NFB - 1, 0:32, :], pre_scr[NFB - 1, 0:32, :])
                _finish_zero(nc, blk, ctr, sems, y_out)
                return nc, probe

            _p15(nc, blk, ctr, sems, locals())
            if probe == "qkv":
                @blk.sync
                def _(q):
                    q.wait_ge(s_v, ctr.get(s_v))
                    q.wait_ge(s_s, ctr.get(s_s))
                    for i, (src, off) in enumerate([(qk_f, 0), (qk_f, 8 * TPC), (k_tok, 0), (v_tok, 0)]):
                        q.dma_start(
                            out=probe_out[i].rearrange("(hb p) t -> p hb t", p=128)[:, 0:8, :],
                            in_=src.ap()[:, off:off + 8 * TPC].rearrange("p (hb t) -> p hb t", t=TPC),
                        ).then_inc(s_ld, 16)
                        ctr.inc(s_ld, 16)
                    q.wait_ge(s_ld, ctr.get(s_ld))
                _finish_zero(nc, blk, ctr, sems, y_out)
                return nc, probe
            if probe == "rows":
                @blk.sync
                def _(q):
                    q.wait_ge(s_v, ctr.get(s_v))
                    q.wait_ge(s_s, ctr.get(s_s))
                    q.dma_start(out=probe_out[:, :], in_=barows[:, :]).then_inc(s_ld, 16)
                    ctr.inc(s_ld, 16)
                    q.wait_ge(s_ld, ctr.get(s_ld))
                _finish_zero(nc, blk, ctr, sems, y_out)
                return nc, probe

            _p2a(nc, blk, ctr, sems, locals())
            if probe == "spill":
                _probe_dump(nc, blk, ctr, sems, probe_out[:, :, :, :, :], spill[:, :, :, :, :])
                _finish_zero(nc, blk, ctr, sems, y_out)
                return nc, probe
            if probe == "comp":
                @blk.sync
                def _(q):
                    q.wait_ge(s_v, ctr.get(s_v))
                    q.wait_ge(s_s, ctr.get(s_s))
                    q.dma_start(out=probe_out[0], in_=compP[:, :]).then_inc(s_ld, 16)
                    q.dma_start(out=probe_out[1], in_=compR[:, :]).then_inc(s_ld, 16)
                    ctr.inc(s_ld, 32)
                    q.wait_ge(s_ld, ctr.get(s_ld))
                _finish_zero(nc, blk, ctr, sems, y_out)
                return nc, probe

            _cc_hop(nc, blk, ctr, sems, locals())
            if probe == "state":
                @blk.sync
                def _(q):
                    q.wait_ge(s_v, ctr.get(s_v))
                    q.dma_start(out=probe_out[:, :], in_=state[:, :]).then_inc(s_ld, 16)
                    ctr.inc(s_ld, 16)
                    q.wait_ge(s_ld, ctr.get(s_ld))
                _finish_zero(nc, blk, ctr, sems, y_out)
                return nc, probe

            _p2b(nc, blk, ctr, sems, locals())
            if probe == "pf":
                _probe_dump(nc, blk, ctr, sems, probe_out[:, :, :], p_scr[:, :, :])
                _finish_zero(nc, blk, ctr, sems, y_out)
                return nc, probe

        with ExitStack() as sc3:
            E3 = sc3.enter_context
            xq_f = E3(nc.sbuf_tensor("xq_f", [128, NDB * TPC], BF16))

            _p3(nc, blk, ctr, sems, locals())
            if probe in ("x2", "xq"):
                if probe == "x2":
                    _probe_dump(nc, blk, ctr, sems, probe_out[:, :], x2_scr[:, :])
                else:
                    @blk.sync
                    def _(q):
                        q.wait_ge(s_s, ctr.get(s_s))
                        q.dma_start(out=probe_out[:, :], in_=xq_f[:, :]).then_inc(s_ld, 16)
                        ctr.inc(s_ld, 16)
                        q.wait_ge(s_ld, ctr.get(s_ld))
                _finish_zero(nc, blk, ctr, sems, y_out)
                return nc, probe

            with ExitStack() as sff:
                EF = sff.enter_context
                p2 = EF(nc.sbuf_tensor("p2", [128, NHB * TPC], BF16))
                _ffn_gu(nc, blk, ctr, sems, locals())
                if probe == "p2":
                    @blk.sync
                    def _(q):
                        q.wait_ge(s_v, ctr.get(s_v))
                        q.dma_start(out=probe_out[:, :], in_=p2[:, :]).then_inc(s_ld, 16)
                        ctr.inc(s_ld, 16)
                        q.wait_ge(s_ld, ctr.get(s_ld))
                    _finish_zero(nc, blk, ctr, sems, y_out)
                    return nc, probe
                _ffn_down(nc, blk, ctr, sems, locals())
            if probe == "ffn":
                _probe_dump(nc, blk, ctr, sems, probe_out[:, :, :], ffn_scr[:, :, :])
                _finish_zero(nc, blk, ctr, sems, y_out)
                return nc, probe

        _final(nc, blk, ctr, sems, locals())
    return nc, probe


def _phase_barrier(nc, blk, ctr, sems):
    names = ("ld", "lda", "ldb", "st", "sta", "stb", "ga", "gb", "mm", "v", "s", "g", "cc")
    vals = {nm: ctr.get(sems[nm]) for nm in names}
    for eng in ("sync", "tensor", "vector", "scalar", "gpsimd"):
        @getattr(blk, eng)
        def _(e, eng=eng):
            for nm in names:
                e.wait_ge(sems[nm], vals[nm])


def _probe_dump(nc, blk, ctr, sems, dst_ap, src_ap):
    @blk.sync
    def _(q):
        for nm in ("st", "sta", "stb", "v", "s", "g", "mm"):
            q.wait_ge(sems[nm], ctr.get(sems[nm]))
        q.dma_start(out=dst_ap, in_=src_ap).then_inc(sems["ld"], 16)
        ctr.inc(sems["ld"], 16)
        q.wait_ge(sems["ld"], ctr.get(sems["ld"]))


def _finish_zero(nc, blk, ctr, sems, y_out):
    s_st, s_g = sems["st"], sems["g"]
    _phase_barrier(nc, blk, ctr, sems)
    with nc.sbuf_tensor("yzero", [128, D], F32) as yz:
        @blk.gpsimd
        def _(g):
            ctr.inc(s_g, 1)
            g.memset(yz[:], 0.0).then_inc(s_g, 1)
        GV = ctr.get(s_g)

        @blk.sync
        def _(q):
            q.wait_ge(s_g, GV)
            for tb in range(TPC // 128):
                q.dma_start(out=y_out[tb * 128:(tb + 1) * 128, :], in_=yz[:]).then_inc(s_st, 16)
                ctr.inc(s_st, 16)
            q.wait_ge(s_st, ctr.get(s_st))


def _p1(nc, blk, ctr, sems, env):
    s_ld, s_st, s_mm, s_v, s_s = sems["ld"], sems["st"], sems["mm"], sems["v"], sems["s"]
    s_lda, s_ldb, s_sta, s_stb = sems["lda"], sems["ldb"], sems["sta"], sems["stb"]
    ldp = [s_lda, s_ldb]
    stp = [s_sta, s_stb]
    x_in, c_ident = env["x_in"], env["c_ident"]
    pre_scr = env["pre_scr"]
    wqkv_b, wgate_b, wba_in = env["wqkv_b"], env["wgate_b"], env["wba_in"]
    CONSTS = env["CONSTS"]
    from contextlib import ExitStack

    with ExitStack() as st1:
        E1 = st1.enter_context
        xtile = E1(nc.sbuf_tensor("xtile", [128, 2 * D], F32))
        xn_t = E1(nc.sbuf_tensor("xn_t", [128, D], BF16))
        xn_f = E1(nc.sbuf_tensor("xn_f", [128, NDB * TP], BF16))
        lnt = E1(nc.sbuf_tensor("lnt", [128, 32], F32))
        ln_stack = ExitStack()
        ps_ln0 = ln_stack.enter_context(nc.psum_tensor("ps_ln0", [128, D], F32))
        ps_ln1 = ln_stack.enter_context(nc.psum_tensor("ps_ln1", [128, D], F32))

        for tb in range(9):
            rows = 128 if tb < 8 else TP - 8 * 128
            xt = xtile[:rows, (tb % 2) * D:(tb % 2) * D + D]
            psl = ps_ln0 if tb % 2 == 0 else ps_ln1

            sl = ldp[tb % 2]

            @blk.sync
            def _(q, tb=tb, rows=rows, xt=xt, sl=sl):
                if tb >= 2:
                    q.wait_ge(s_s, ctr.get(s_s))
                q.wait_ge(sl, ctr.get(sl))
                q.dma_start(out=xt, in_=x_in[tb * 128: tb * 128 + rows, :]).then_inc(sl, 16)
                ctr.inc(sl, 16)
            XW = ctr.get(sl)

            @blk.vector
            def _(v, rows=rows, xt=xt, XW=XW, sl=sl):
                v.wait_ge(sl, XW)
                ins = None
                for i4 in range(4):
                    ins = v.bn_stats(lnt[:rows, i4 * 6:(i4 + 1) * 6], xt[:, i4 * 512:(i4 + 1) * 512])
                ctr.inc(s_v, 1); ins.then_inc(s_v, 1); v.wait_ge(s_v, ctr.get(s_v))
                ins = v.bn_aggr(lnt[:rows, 24:26], lnt[:rows, 0:24])
                ctr.inc(s_v, 1); ins.then_inc(s_v, 1); v.wait_ge(s_v, ctr.get(s_v))
                ctr.inc(s_v, 1)
                v.tensor_scalar(lnt[:rows, 26:27], lnt[:rows, 25:26], EPS_LN, None, ALU.add).then_inc(s_v, 1)
            V1 = ctr.get(s_v)

            @blk.scalar
            def _(s, rows=rows, V1=V1):
                s.wait_ge(s_v, V1)
                ctr.inc(s_s, 1)
                s.activation(lnt[:rows, 27:28], lnt[:rows, 26:27], AF.Sqrt).then_inc(s_s, 1)
            S1 = ctr.get(s_s)

            @blk.vector
            def _(v, rows=rows, S1=S1):
                v.wait_ge(s_s, S1)
                ins = v.reciprocal(lnt[:rows, 28:29], lnt[:rows, 27:28])
                ctr.inc(s_v, 1); ins.then_inc(s_v, 1); v.wait_ge(s_v, ctr.get(s_v))
                ctr.inc(s_v, 1)
                v.scalar_tensor_tensor(out=lnt[:rows, 29:30], in0=lnt[:rows, 24:25],
                                       scalar=-1.0, in1=lnt[:rows, 28:29],
                                       op0=ALU.mult, op1=ALU.mult).then_inc(s_v, 1)
            V2 = ctr.get(s_v)

            @blk.scalar
            def _(s, tb=tb, rows=rows, xt=xt, V2=V2):
                s.wait_ge(s_v, V2)
                if tb >= 1:
                    s.wait_ge(s_mm, ctr.get(s_mm))
                ctr.inc(s_s, 1)
                s.activation(xn_t[:rows, :], xt, AF.Identity,
                             bias=lnt[:rows, 29:30], scale=lnt[:rows, 28:29]).then_inc(s_s, 1)
            S2 = ctr.get(s_s)

            @blk.tensor
            def _(t, tb=tb, rows=rows, S2=S2, psl=psl):
                t.wait_ge(s_s, S2)
                t.wait_ge(s_ld, CONSTS)
                if tb >= 2:
                    t.wait_ge(s_s, ctr.get(s_s))
                mm = None
                for db in range(NDB):
                    mm = t.matmul(psl[:, db * 128:db * 128 + rows],
                                  xn_t[:rows, db * 128:(db + 1) * 128],
                                  c_ident[:rows, :rows], start=True, stop=True)
                ctr.inc(s_mm, 1)
                mm.then_inc(s_mm, 1)
            M1 = ctr.get(s_mm)

            @blk.scalar
            def _(s, tb=tb, rows=rows, M1=M1, psl=psl):
                s.wait_ge(s_mm, M1)
                ctr.inc(s_s, 1)
                s.copy(
                    xn_f.ap().rearrange("p (db c) -> p db c", db=NDB)[:, :, tb * 128:tb * 128 + rows],
                    psl.ap().rearrange("p (db c) -> p db c", db=NDB)[:, :, :rows],
                ).then_inc(s_s, 1)
        XNF = ctr.get(s_s)
        ln_stack.close()

        wf32 = E1(nc.sbuf_tensor("wf32", [128, 2 * NDB * 128], F32))
        wbf = E1(nc.sbuf_tensor("wbf", [128, 2 * NDB * 128], BF16))
        pre_sb = E1(nc.sbuf_tensor("pre_sb", [128, 2 * TP], BF16))
        ps_pa = [E1(nc.psum_tensor(f"ps_pa{i}", [128, TG], F32)) for i in range(6)]

        def wsrc(fb):
            if fb < NFQ:
                return wqkv_b[fb]
            if fb < NFQ + NFG:
                return wgate_b[fb - NFQ]
            return wba_in

        for fb in range(NFB):
            buf = fb % 2
            fcols = 128 if fb < NFB - 1 else 32
            wsl = wf32[:, buf * NDB * 128:][:, :NDB * fcols]
            wbl = wbf[:, buf * NDB * 128:][:, :NDB * fcols]
            psl = ps_pa[buf * 3:(buf + 1) * 3]
            prl = pre_sb[:fcols, buf * TP:buf * TP + TP]

            sl = ldp[fb % 2]

            @blk.sync
            def _(q, fb=fb, wsl=wsl, fcols=fcols, sl=sl):
                if fb == 0:
                    q.wait_ge(s_lda, ctr.get(s_lda))
                    q.wait_ge(s_ldb, ctr.get(s_ldb))
                if fb >= 2:
                    q.wait_ge(s_v, ctr.get(s_v))
                q.wait_ge(sl, ctr.get(sl))
                q.dma_start(out=wsl.rearrange("p (db f) -> p db f", f=fcols),
                            in_=wsrc(fb).rearrange("(db p) f -> p db f", p=128)).then_inc(sl, 16)
                ctr.inc(sl, 16)
            WL = ctr.get(sl)

            @blk.vector
            def _(v, fb=fb, wsl=wsl, wbl=wbl, WL=WL, sl=sl):
                v.wait_ge(sl, WL)
                if fb >= 2:
                    v.wait_ge(s_mm, ctr.get(s_mm))
                ctr.inc(s_v, 1)
                v.tensor_copy(wbl, wsl).then_inc(s_v, 1)
            WC = ctr.get(s_v)

            @blk.tensor
            def _(t, fb=fb, wbl=wbl, psl=psl, fcols=fcols, WC=WC):
                t.wait_ge(s_v, WC)
                t.wait_ge(s_s, XNF)
                if fb >= 2:
                    t.wait_ge(s_s, ctr.get(s_s))
                mm = None
                for tg in range(3):
                    for db in range(NDB):
                        mm = t.matmul(psl[tg][:fcols, :],
                                      wbl.rearrange("p (db f) -> p db f", db=NDB)[:, db, :],
                                      xn_f.ap().rearrange("p (db c) -> p db c", db=NDB)[:, db, tg * TG:(tg + 1) * TG],
                                      start=(db == 0), stop=(db == NDB - 1))
                ctr.inc(s_mm, 1)
                mm.then_inc(s_mm, 1)
            PM = ctr.get(s_mm)

            @blk.scalar
            def _(s, fb=fb, psl=psl, prl=prl, fcols=fcols, PM=PM):
                s.wait_ge(s_mm, PM)
                if fb >= 2:
                    s.wait_ge(stp[fb % 2], ctr.get(stp[fb % 2]))
                cp = None
                for tg in range(3):
                    cp = s.copy(prl[:, tg * TG:(tg + 1) * TG], psl[tg][:fcols, :])
                ctr.inc(s_s, 1)
                cp.then_inc(s_s, 1)
            PE = ctr.get(s_s)

            @blk.gpsimd
            def _(g, fb=fb, prl=prl, fcols=fcols, PE=PE):
                g.wait_ge(s_s, PE)
                g.wait_ge(stp[fb % 2], ctr.get(stp[fb % 2]))
                g.dma_start(out=pre_scr[fb, :fcols, :], in_=prl).then_inc(stp[fb % 2], 16)
                ctr.inc(stp[fb % 2], 16)


def _p15(nc, blk, ctr, sems, env):
    s_ld, s_st, s_mm, s_v, s_s = sems["ld"], sems["st"], sems["mm"], sems["v"], sems["s"]
    s_lda, s_ldb, s_sta, s_stb = sems["lda"], sems["ldb"], sems["sta"], sems["stb"]
    ldp = [s_lda, s_ldb]
    pre_scr, c_convw, c_ident, c_onec, c_onerf = (
        env["pre_scr"], env["c_convw"], env["c_ident"], env["c_onec"], env["c_onerf"])
    qk_f, k_tok, v_tok, barows, c_la, c_zero, c_cols = (
        env["qk_f"], env["k_tok"], env["v_tok"], env["barows"], env["c_la"],
        env["c_zero"], env["c_cols"])
    P1_STA = ctr.get(s_sta)
    P1_STB = ctr.get(s_stb)
    from contextlib import ExitStack

    with ExitStack() as st15:
        E1 = st15.enter_context
        pre_t = E1(nc.sbuf_tensor("pre_t", [128, 2 * TP], BF16))
        cv = E1(nc.sbuf_tensor("cv", [128, TPC], F32))
        sil = E1(nc.sbuf_tensor("sil", [128, TPC], F32))
        sqb = E1(nc.sbuf_tensor("sqb", [128, TPC], BF16))
        rs_row = E1(nc.sbuf_tensor("rs_row", [1, TPC], F32))
        rs_f32 = E1(nc.sbuf_tensor("rs_f32", [1, TPC], F32))
        vtile = E1(nc.sbuf_tensor("vtile", [128, TPC], BF16))
        ps_l2 = E1(nc.psum_tensor("ps_l2", [1, TPC], F32))
        ps_bc = E1(nc.psum_tensor("ps_bc", [128, TPC], F32))
        ps_tr = E1(nc.psum_tensor("ps_tr", [128, TPC], F32))

        for fb in range(NFQ):
            buf = fb % 2
            pt = pre_t[:, buf * TP:buf * TP + TP]
            is_q = fb < 8
            is_k = 8 <= fb < 16
            head = fb if is_q else (fb - 8 if is_k else fb - 16)

            sl = ldp[fb % 2]

            @blk.sync
            def _(q, fb=fb, pt=pt, sl=sl):
                if fb == 0:
                    q.wait_ge(s_sta, P1_STA)
                    q.wait_ge(s_stb, P1_STB)
                    q.wait_ge(s_lda, ctr.get(s_lda))
                    q.wait_ge(s_ldb, ctr.get(s_ldb))
                if fb >= 2:
                    q.wait_ge(s_v, ctr.get(s_v))
                q.wait_ge(sl, ctr.get(sl))
                q.dma_start(out=pt, in_=pre_scr[fb, :, :]).then_inc(sl, 16)
                ctr.inc(sl, 16)
            LD = ctr.get(sl)

            @blk.vector
            def _(v, fb=fb, pt=pt, LD=LD, sl=sl):
                v.wait_ge(sl, LD)
                if fb > 0:
                    v.wait_ge(s_s, ctr.get(s_s))
                ins = v.tensor_scalar(cv[:, :], pt[:, 0:TPC], c_convw[:, fb * 4:fb * 4 + 1], None, ALU.mult)
                for j in range(1, 4):
                    ctr.inc(s_v, 1); ins.then_inc(s_v, 1); v.wait_ge(s_v, ctr.get(s_v))
                    ins = v.scalar_tensor_tensor(
                        out=cv[:, :], in0=pt[:, j:j + TPC],
                        scalar=c_convw[:, fb * 4 + j:fb * 4 + j + 1], in1=cv[:, :],
                        op0=ALU.mult, op1=ALU.add)
                ctr.inc(s_v, 1)
                ins.then_inc(s_v, 1)
            CV = ctr.get(s_v)

            if is_q or is_k:
                @blk.scalar
                def _(s, CV=CV, fb=fb):
                    s.wait_ge(s_v, CV)
                    if fb >= 2:
                        s.wait_ge(s_v, ctr.get(s_v))
                    ins = s.activation(sil[:, :], cv[:, :], AF.Silu)
                    ctr.inc(s_s, 1); ins.then_inc(s_s, 1); s.wait_ge(s_s, ctr.get(s_s))
                    ctr.inc(s_s, 1)
                    s.activation(sqb[:, :], sil[:, :], AF.Square).then_inc(s_s, 1)
                SQ = ctr.get(s_s)

                @blk.tensor
                def _(t, SQ=SQ, fb=fb):
                    t.wait_ge(s_s, SQ)
                    if fb >= 1:
                        t.wait_ge(s_s, ctr.get(s_s))
                    mm = None
                    for half in range(2):
                        mm = t.matmul(ps_l2[0:1, half * 512:(half + 1) * 512], c_onec[:, :],
                                      sqb[:, half * 512:(half + 1) * 512], start=True, stop=True)
                    ctr.inc(s_mm, 1)
                    mm.then_inc(s_mm, 1)
                MMS = ctr.get(s_mm)

                @blk.scalar
                def _(s, MMS=MMS):
                    s.wait_ge(s_mm, MMS)
                    ctr.inc(s_s, 1)
                    s.activation(rs_f32[0:1, :], ps_l2[0:1, :], AF.Sqrt, bias=c_cols[0:1, 0:1]).then_inc(s_s, 1)
                RS = ctr.get(s_s)

                @blk.vector
                def _(v, RS=RS):
                    v.wait_ge(s_s, RS)
                    ctr.inc(s_v, 1)
                    v.reciprocal(rs_row[0:1, :], rs_f32[0:1, :]).then_inc(s_v, 1)
                RR = ctr.get(s_v)

                @blk.tensor
                def _(t, RR=RR, fb=fb):
                    t.wait_ge(s_v, RR)
                    if fb >= 1:
                        t.wait_ge(s_v, ctr.get(s_v))
                    mm = None
                    for half in range(2):
                        mm = t.matmul(ps_bc[:, half * 512:(half + 1) * 512],
                                      c_onerf[0:1, :],
                                      rs_row[0:1, half * 512:(half + 1) * 512], start=True, stop=True)
                    ctr.inc(s_mm, 1)
                    mm.then_inc(s_mm, 1)
                BC = ctr.get(s_mm)

                @blk.vector
                def _(v, fb=fb, head=head, is_q=is_q, BC=BC):
                    v.wait_ge(s_mm, BC)
                    dst = qk_f.ap()[:, (head + (0 if is_q else 8)) * TPC:][:, :TPC]
                    ctr.inc(s_v, 1)
                    v.tensor_tensor(dst, sil[:, :], ps_bc[:, :], ALU.mult).then_inc(s_v, 1)
                QN = ctr.get(s_v)

                if is_k:
                    @blk.tensor
                    def _(t, head=head, QN=QN, fb=fb):
                        t.wait_ge(s_v, QN)
                        if fb >= 9:
                            t.wait_ge(s_s, ctr.get(s_s))
                        src = qk_f.ap()[:, (head + 8) * TPC:][:, :TPC]
                        mm = None
                        for ci in range(NCH):
                            mm = t.matmul(ps_tr[:, ci * 128:(ci + 1) * 128],
                                          src[:, ci * 128:(ci + 1) * 128],
                                          c_ident[:, :], start=True, stop=True)
                        ctr.inc(s_mm, 1)
                        mm.then_inc(s_mm, 1)
                    KT = ctr.get(s_mm)

                    @blk.scalar
                    def _(s, head=head, KT=KT):
                        s.wait_ge(s_mm, KT)
                        ctr.inc(s_s, 1)
                        s.copy(k_tok.ap()[:, head * TPC:(head + 1) * TPC], ps_tr[:, :]).then_inc(s_s, 1)
            else:
                @blk.scalar
                def _(s, CV=CV, fb=fb):
                    s.wait_ge(s_v, CV)
                    if fb >= 17:
                        s.wait_ge(s_mm, ctr.get(s_mm))
                    ctr.inc(s_s, 1)
                    s.activation(vtile[:, :], cv[:, :], AF.Silu).then_inc(s_s, 1)
                VS = ctr.get(s_s)

                @blk.tensor
                def _(t, head=head, VS=VS, fb=fb):
                    t.wait_ge(s_s, VS)
                    if fb >= 17:
                        t.wait_ge(s_s, ctr.get(s_s))
                    mm = None
                    for ci in range(NCH):
                        mm = t.matmul(ps_tr[:, ci * 128:(ci + 1) * 128],
                                      vtile[:, ci * 128:(ci + 1) * 128],
                                      c_ident[:, :], start=True, stop=True)
                    ctr.inc(s_mm, 1)
                    mm.then_inc(s_mm, 1)
                VT = ctr.get(s_mm)

                @blk.scalar
                def _(s, head=head, VT=VT):
                    s.wait_ge(s_mm, VT)
                    ctr.inc(s_s, 1)
                    s.copy(v_tok.ap()[:, head * TPC:(head + 1) * TPC], ps_tr[:, :]).then_inc(s_s, 1)

        ba_t = E1(nc.sbuf_tensor("ba_t", [16, TPC], BF16))
        aa_t = E1(nc.sbuf_tensor("aa_t", [16, TPC], BF16))
        tmp16 = E1(nc.sbuf_tensor("tmp16", [HV, TPC], F32))

        @blk.sync
        def _(q):
            q.wait_ge(s_lda, ctr.get(s_lda))
            q.wait_ge(s_ldb, ctr.get(s_ldb))
            q.dma_start(out=ba_t[:, :], in_=pre_scr[NFB - 1, 0:16, HALO:HALO + TPC]).then_inc(s_lda, 16)
            q.dma_start(out=aa_t[:, :], in_=pre_scr[NFB - 1, 16:32, HALO:HALO + TPC]).then_inc(s_lda, 16)
            ctr.inc(s_lda, 32)
        BA = ctr.get(s_lda)

        @blk.scalar
        def _(s, BA=BA):
            s.wait_ge(s_lda, BA)
            s.wait_ge(s_v, ctr.get(s_v))
            ins = s.activation(barows[:, 0:TPC], ba_t[:, :], AF.Sigmoid)
            ctr.inc(s_s, 1); ins.then_inc(s_s, 1); s.wait_ge(s_s, ctr.get(s_s))
            s.activation(barows[:, TPC:2 * TPC], barows[:, 0:TPC], AF.Ln)
            ins = s.activation(tmp16[:, :], aa_t[:, :], AF.Sigmoid, scale=-1.0, bias=c_la[:, 5:6])
            ctr.inc(s_s, 1); ins.then_inc(s_s, 1); s.wait_ge(s_s, ctr.get(s_s))
            ctr.inc(s_s, 1)
            s.activation(tmp16[:, :], tmp16[:, :], AF.Ln).then_inc(s_s, 1)
        SB = ctr.get(s_s)

        @blk.vector
        def _(v, SB=SB):
            v.wait_ge(s_s, SB)
            ins = v.tensor_scalar(barows[:, 2 * TPC:3 * TPC], tmp16[:, :], c_la[:, 4:5], None, ALU.mult)
            ctr.inc(s_v, 1); ins.then_inc(s_v, 1); v.wait_ge(s_v, ctr.get(s_v))
            ins = None
            for ci in range(NCH):
                ins = v.tensor_tensor_scan(
                    out=barows[:, 3 * TPC + ci * 128:3 * TPC + (ci + 1) * 128],
                    data0=barows[:, 2 * TPC + ci * 128:2 * TPC + (ci + 1) * 128],
                    data1=barows[:, 2 * TPC + ci * 128:2 * TPC + (ci + 1) * 128],
                    initial=c_zero[0:HV, 0:1],
                    op0=ALU.add, op1=ALU.bypass)
            ctr.inc(s_v, 1)
            ins.then_inc(s_v, 1)


def _p2a(nc, blk, ctr, sems, env):
    s_ld, s_st, s_mm, s_v, s_s = sems["ld"], sems["st"], sems["mm"], sems["v"], sems["s"]
    s_sta, s_stb = sems["sta"], sems["stb"]
    stp = [s_sta, s_stb]
    qk_f, k_tok, v_tok, barows = env["qk_f"], env["k_tok"], env["v_tok"], env["barows"]
    compP, compR = env["compP"], env["compR"]
    c_ident, c_identf, c_onerf, c_zero = env["c_ident"], env["c_identf"], env["c_onerf"], env["c_zero"]
    c_maskLT, c_maskUT = env["c_maskLT"], env["c_maskUT"]
    spill = env["spill"]
    from contextlib import ExitStack

    with ExitStack() as sta:
        E1 = sta.enter_context
        mrow_all = E1(nc.sbuf_tensor("mrow_all", [HV, 128], F32))
        cols = E1(nc.sbuf_tensor("cols", [128, 64], F32))
        rowst = E1(nc.sbuf_tensor("rowst", [1, 256], F32))
        G_sb = E1(nc.sbuf_tensor("G_sb", [128, 2 * 128], BF16))
        e_t = E1(nc.sbuf_tensor("e_t", [128, 128], F32))
        e_tt = E1(nc.sbuf_tensor("e_tt", [128, 128], F32))
        p1l = E1(nc.sbuf_tensor("p1l", [128, 128], F32))
        p2m = E1(nc.sbuf_tensor("p2m", [128, 128], F32))
        X = E1(nc.sbuf_tensor("X", [128, 128], BF16))
        Xt = E1(nc.sbuf_tensor("Xt", [128, 128], BF16))
        NT = E1(nc.sbuf_tensor("NT", [128, 128], BF16))
        Hb = E1(nc.sbuf_tensor("Hb", [128, 128], BF16))
        bVt = E1(nc.sbuf_tensor("bVt", [128, 128], BF16))
        bAKt = E1(nc.sbuf_tensor("bAKt", [128, 128], BF16))
        kht = E1(nc.sbuf_tensor("kht", [128, 128], BF16))
        U0sb = E1(nc.sbuf_tensor("U0sb", [128, 128], BF16))
        Ycsb = E1(nc.sbuf_tensor("Ycsb", [128, 128], BF16))
        PTsb = E1(nc.sbuf_tensor("PTsb", [128, 128], BF16))
        Rsb = E1(nc.sbuf_tensor("Rsb", [128, 128], BF16))
        cPn = E1(nc.sbuf_tensor("cPn", [128, 128], BF16))
        ps_a = E1(nc.psum_tensor("ps_a", [128, 128], F32))
        ps_b = E1(nc.psum_tensor("ps_b", [128, 128], F32))
        ps_c = E1(nc.psum_tensor("ps_c", [128, 128], F32))
        ps_row = E1(nc.psum_tensor("ps_row", [1, 128], F32))
        ps_fill = E1(nc.psum_tensor("ps_fill", [128, 2], F32))
        ps_colt = E1(nc.psum_tensor("ps_colt", [128, 48], F32))

        ROWS_V = ctr.get(s_v)

        for ci in range(NCH):
            @blk.vector
            def _(v, ci=ci):
                v.wait_ge(s_v, ROWS_V)
                if ci > 0:
                    v.wait_ge(s_mm, ctr.get(s_mm))
                ctr.inc(s_v, 1)
                v.tensor_tensor(mrow_all[:, :], barows[:, 3 * TPC + ci * 128:3 * TPC + (ci + 1) * 128],
                                barows[:, TPC + ci * 128:TPC + (ci + 1) * 128], ALU.add).then_inc(s_v, 1)
            MR = ctr.get(s_v)

            @blk.tensor
            def _(t, ci=ci, MR=MR):
                t.wait_ge(s_v, MR)
                if ci > 0:
                    t.wait_ge(s_s, ctr.get(s_s))
                t.matmul(ps_colt[:, 0:16], barows[:, 3 * TPC + ci * 128:3 * TPC + (ci + 1) * 128],
                         c_identf[0:HV, 0:HV], start=True, stop=True)
                t.matmul(ps_colt[:, 16:32], mrow_all[:, :], c_identf[0:HV, 0:HV], start=True, stop=True)
                ctr.inc(s_mm, 1)
                t.matmul(ps_colt[:, 32:48], barows[:, ci * 128:(ci + 1) * 128],
                         c_identf[0:HV, 0:HV], start=True, stop=True).then_inc(s_mm, 1)
            CT = ctr.get(s_mm)

            @blk.scalar
            def _(s, CT=CT):
                s.wait_ge(s_mm, CT)
                ctr.inc(s_s, 1)
                s.copy(cols[:, 0:48], ps_colt[:, 0:48]).then_inc(s_s, 1)
            COLS = ctr.get(s_s)

            for h in range(HV):
                hq = h // 2
                ellc = cols[:, h:h + 1]
                mc = cols[:, 16 + h:16 + h + 1]
                betac = cols[:, 32 + h:32 + h + 1]
                kf = qk_f.ap()[:, (8 + hq) * TPC + ci * 128:][:, :128]
                qf = qk_f.ap()[:, hq * TPC + ci * 128:][:, :128]
                ktk = k_tok.ap()[:, hq * TPC + ci * 128:][:, :128]
                vtk = v_tok.ap()[:, h * TPC + ci * 128:][:, :128]

                gbuf = G_sb[:, (hq % 2) * 128:(hq % 2) * 128 + 128]
                if h % 2 == 0:
                    @blk.tensor
                    def _(t, kf=kf, COLS=COLS, ci=ci, h=h):
                        t.wait_ge(s_s, COLS)
                        if not (ci == 0 and h == 0):
                            t.wait_ge(s_s, ctr.get(s_s))
                            t.wait_ge(s_v, ctr.get(s_v))
                        ctr.inc(s_mm, 1)
                        t.matmul(ps_a[:, :], kf, kf, start=True, stop=True).then_inc(s_mm, 1)
                    GK = ctr.get(s_mm)

                    @blk.scalar
                    def _(s, gbuf=gbuf, GK=GK):
                        s.wait_ge(s_mm, GK)
                        ctr.inc(s_s, 1)
                        s.copy(gbuf, ps_a[:, :]).then_inc(s_s, 1)

                @blk.tensor
                def _(t, ellc=ellc, COLS=COLS, ci=ci, h=h):
                    t.wait_ge(s_s, COLS)
                    if not (ci == 0 and h == 0):
                        t.wait_ge(s_s, ctr.get(s_s))
                        t.wait_ge(s_v, ctr.get(s_v))
                    ctr.inc(s_mm, 1)
                    t.matmul(ps_row[0:1, :], ellc, c_identf[:, :], start=True, stop=True).then_inc(s_mm, 1)
                RW1 = ctr.get(s_mm)

                @blk.scalar
                def _(s, RW1=RW1):
                    s.wait_ge(s_mm, RW1)
                    ctr.inc(s_s, 1)
                    s.copy(rowst[0:1, 0:128], ps_row[0:1, :]).then_inc(s_s, 1)
                RW2 = ctr.get(s_s)

                @blk.tensor
                def _(t, mc=mc, RW2=RW2):
                    t.wait_ge(s_s, RW2)
                    t.matmul(ps_fill[:, 0:1], c_onerf[:, :], rowst[0:1, 127:128], start=True, stop=True)
                    t.matmul(ps_b[:, :], c_onerf[:, :], rowst[0:1, 0:128], start=True, stop=True)
                    ctr.inc(s_mm, 1)
                    t.matmul(ps_row[0:1, :], mc, c_identf[:, :], start=True, stop=True).then_inc(s_mm, 1)
                RW3 = ctr.get(s_mm)

                @blk.scalar
                def _(s, RW3=RW3):
                    s.wait_ge(s_mm, RW3)
                    s.copy(p1l[:, :], ps_b[:, :])
                    s.copy(rowst[0:1, 128:256], ps_row[0:1, :])
                    ctr.inc(s_s, 1)
                    s.copy(cols[:, 51:52], ps_fill[:, 0:1]).then_inc(s_s, 1)
                RW4 = ctr.get(s_s)

                @blk.tensor
                def _(t, RW4=RW4):
                    t.wait_ge(s_s, RW4)
                    ctr.inc(s_mm, 1)
                    t.matmul(ps_b[:, :], c_onerf[:, :], rowst[0:1, 128:256], start=True, stop=True).then_inc(s_mm, 1)
                RW5 = ctr.get(s_mm)

                @blk.scalar
                def _(s, RW5=RW5):
                    s.wait_ge(s_mm, RW5)
                    ctr.inc(s_s, 1)
                    s.copy(p2m[:, :], ps_b[:, :]).then_inc(s_s, 1)
                PBC = ctr.get(s_s)

                @blk.scalar
                def _(s, mc=mc, PBC=PBC):
                    s.wait_ge(s_s, PBC)
                    ctr.inc(s_s, 1)
                    s.activation(e_t[:, :], p1l[:, :], AF.Identity, bias=mc, scale=-1.0).then_inc(s_s, 1)
                EA = ctr.get(s_s)

                @blk.vector
                def _(v, ellc=ellc, EA=EA):
                    v.wait_ge(s_s, EA)
                    ins = v.tensor_scalar(e_tt[:, :], p2m[:, :], ellc, None, ALU.subtract)
                    v.tensor_tensor(e_t[:, :], e_t[:, :], c_maskLT[:, :], ALU.add)
                    ctr.inc(s_v, 1); ins.then_inc(s_v, 1); v.wait_ge(s_v, ctr.get(s_v))
                    ctr.inc(s_v, 1)
                    v.tensor_tensor(e_tt[:, :], e_tt[:, :], c_maskUT[:, :], ALU.add).then_inc(s_v, 1)
                EB = ctr.get(s_v)

                @blk.scalar
                def _(s, EB=EB):
                    s.wait_ge(s_v, EB)
                    s.activation(e_t[:, :], e_t[:, :], AF.Exp)
                    ctr.inc(s_s, 1)
                    s.activation(e_tt[:, :], e_tt[:, :], AF.Exp).then_inc(s_s, 1)
                EC = ctr.get(s_s)

                @blk.vector
                def _(v, gbuf=gbuf, EC=EC):
                    v.wait_ge(s_s, EC)
                    v.wait_ge(s_s, ctr.get(s_s))
                    v.tensor_tensor(e_t[:, :], gbuf, e_t[:, :], ALU.mult)
                    ins = v.tensor_tensor(e_tt[:, :], gbuf, e_tt[:, :], ALU.mult)
                    ctr.inc(s_v, 1); ins.then_inc(s_v, 1); v.wait_ge(s_v, ctr.get(s_v))
                    v.tensor_tensor(X[:, :], c_identf[:, :], e_t[:, :], ALU.subtract)
                    v.tensor_tensor(Xt[:, :], c_identf[:, :], e_tt[:, :], ALU.subtract)
                    ctr.inc(s_v, 1)
                    v.tensor_tensor(NT[:, :], c_identf[:, :], e_tt[:, :], ALU.add).then_inc(s_v, 1)
                TX = ctr.get(s_v)

                last = TX
                for it in range(NEWTON_ITERS):
                    @blk.tensor
                    def _(t, last=last, it=it):
                        if it == 0:
                            t.wait_ge(s_v, last)
                        else:
                            t.wait_ge(s_s, last)
                        ctr.inc(s_mm, 1)
                        t.matmul(ps_a[:, :], NT[:, :], X[:, :], start=True, stop=True).then_inc(s_mm, 1)
                    HM = ctr.get(s_mm)

                    @blk.scalar
                    def _(s, HM=HM):
                        s.wait_ge(s_mm, HM)
                        ctr.inc(s_s, 1)
                        s.copy(Hb[:, :], ps_a[:, :]).then_inc(s_s, 1)
                    HE = ctr.get(s_s)

                    @blk.tensor
                    def _(t, HE=HE):
                        t.wait_ge(s_s, HE)
                        ctr.inc(s_mm, 1)
                        t.matmul(ps_b[:, :], Xt[:, :], Hb[:, :], start=True, stop=True).then_inc(s_mm, 1)
                    XHM = ctr.get(s_mm)

                    @blk.vector
                    def _(v, XHM=XHM):
                        v.wait_ge(s_mm, XHM)
                        ctr.inc(s_v, 1)
                        v.scalar_tensor_tensor(out=X[:, :], in0=X[:, :], scalar=2.0,
                                               in1=ps_b[:, :], op0=ALU.mult,
                                               op1=ALU.subtract).then_inc(s_v, 1)
                    XN = ctr.get(s_v)

                    @blk.tensor
                    def _(t, XN=XN):
                        t.wait_ge(s_v, XN)
                        ctr.inc(s_mm, 1)
                        t.matmul(ps_c[:, :], X[:, :], c_ident[:, :], start=True, stop=True).then_inc(s_mm, 1)
                    XTM = ctr.get(s_mm)

                    @blk.scalar
                    def _(s, XTM=XTM):
                        s.wait_ge(s_mm, XTM)
                        ctr.inc(s_s, 1)
                        s.copy(Xt[:, :], ps_c[:, :]).then_inc(s_s, 1)
                    last = ctr.get(s_s)

                @blk.scalar
                def _(s, mc=mc, ellc=ellc, last=last):
                    s.wait_ge(s_s, last)
                    s.activation(cols[:, 48:49], mc, AF.Exp)
                    ctr.inc(s_s, 1)
                    s.activation(cols[:, 49:50], ellc, AF.Exp, scale=-1.0,
                                 bias=cols[:, 51:52]).then_inc(s_s, 1)
                SCC = ctr.get(s_s)

                @blk.vector
                def _(v, vtk=vtk, ktk=ktk, betac=betac, SCC=SCC):
                    v.wait_ge(s_s, SCC)
                    v.tensor_scalar(bVt[:, :], vtk, betac, None, ALU.mult)
                    v.tensor_scalar(bAKt[:, :], ktk, cols[:, 48:49], None, ALU.mult)
                    ctr.inc(s_v, 1)
                    v.tensor_scalar(kht[:, :], ktk, cols[:, 49:50], None, ALU.mult).then_inc(s_v, 1)
                OPS = ctr.get(s_v)

                @blk.tensor
                def _(t, OPS=OPS):
                    t.wait_ge(s_v, OPS)
                    t.matmul(ps_a[:, :], Xt[:, :], bVt[:, :], start=True, stop=True)
                    ctr.inc(s_mm, 1)
                    t.matmul(ps_b[:, :], Xt[:, :], bAKt[:, :], start=True, stop=True).then_inc(s_mm, 1)
                SOL = ctr.get(s_mm)

                pr = (ci * HV + h) % 2

                @blk.scalar
                def _(s, SOL=SOL, pr=pr):
                    s.wait_ge(s_mm, SOL)
                    s.wait_ge(s_sta, ctr.get(s_sta))
                    s.wait_ge(s_stb, ctr.get(s_stb))
                    s.copy(U0sb[:, :], ps_a[:, :])
                    ctr.inc(s_s, 1)
                    s.copy(Ycsb[:, :], ps_b[:, :]).then_inc(s_s, 1)
                UY = ctr.get(s_s)

                @blk.gpsimd
                def _(g, ci=ci, h=h, UY=UY, pr=pr):
                    g.wait_ge(s_s, UY)
                    g.wait_ge(stp[pr], ctr.get(stp[pr]))
                    g.dma_start(out=spill[ci, h, 0], in_=U0sb[:, :]).then_inc(stp[pr], 16)
                    g.dma_start(out=spill[ci, h, 1], in_=Ycsb[:, :]).then_inc(stp[pr], 16)
                    ctr.inc(stp[pr], 32)

                @blk.tensor
                def _(t, UY=UY):
                    t.wait_ge(s_s, UY)
                    t.matmul(ps_a[:, :], kht[:, :], U0sb[:, :], start=True, stop=True)
                    ctr.inc(s_mm, 1)
                    t.matmul(ps_b[:, :], Ycsb[:, :], kht[:, :], start=True, stop=True).then_inc(s_mm, 1)
                RP = ctr.get(s_mm)

                @blk.scalar
                def _(s, RP=RP):
                    s.wait_ge(s_mm, RP)
                    ctr.inc(s_s, 1)
                    s.activation(cols[:, 50:51], cols[:, 51:52], AF.Exp).then_inc(s_s, 1)
                ACF = ctr.get(s_s)

                @blk.vector
                def _(v, ACF=ACF, pr=pr):
                    v.wait_ge(s_s, ACF)
                    v.wait_ge(s_sta, ctr.get(s_sta))
                    v.wait_ge(s_stb, ctr.get(s_stb))
                    v.tensor_copy(Rsb[:, :], ps_a[:, :])
                    ctr.inc(s_v, 1)
                    v.scalar_tensor_tensor(out=PTsb[:, :], in0=c_identf[:, :],
                                           scalar=cols[:, 50:51], in1=ps_b[:, :],
                                           op0=ALU.mult, op1=ALU.subtract).then_inc(s_v, 1)
                PRV = ctr.get(s_v)

                @blk.gpsimd
                def _(g, ci=ci, h=h, PRV=PRV, pr=pr):
                    g.wait_ge(s_v, PRV)
                    g.wait_ge(stp[pr], ctr.get(stp[pr]))
                    g.dma_start(out=spill[ci, h, 2], in_=PTsb[:, :]).then_inc(stp[pr], 16)
                    g.dma_start(out=spill[ci, h, 3], in_=Rsb[:, :]).then_inc(stp[pr], 16)
                    ctr.inc(stp[pr], 32)

                cP = compP.ap()[:, h * 128:(h + 1) * 128]
                cR = compR.ap()[:, h * 128:(h + 1) * 128]
                if ci == 0:
                    @blk.vector
                    def _(v, cP=cP, cR=cR, PRV=PRV):
                        v.wait_ge(s_v, PRV)
                        v.tensor_copy(cP, PTsb[:, :])
                        ctr.inc(s_v, 1)
                        v.tensor_copy(cR, Rsb[:, :]).then_inc(s_v, 1)
                else:
                    @blk.tensor
                    def _(t, cP=cP, cR=cR, PRV=PRV):
                        t.wait_ge(s_v, PRV)
                        t.matmul(ps_a[:, :], PTsb[:, :], cP, start=True, stop=True)
                        ctr.inc(s_mm, 1)
                        t.matmul(ps_b[:, :], PTsb[:, :], cR, start=True, stop=True).then_inc(s_mm, 1)
                    CM = ctr.get(s_mm)

                    @blk.vector
                    def _(v, cP=cP, cR=cR, CM=CM):
                        v.wait_ge(s_mm, CM)
                        v.tensor_copy(cP, ps_a[:, :])
                        ctr.inc(s_v, 1)
                        v.tensor_tensor(cR, ps_b[:, :], Rsb[:, :], ALU.add).then_inc(s_v, 1)


def _cc_hop(nc, blk, ctr, sems, env):
    s_ld, s_st, s_mm, s_v, s_s, s_g, s_cc = (
        sems["ld"], sems["st"], sems["mm"], sems["v"], sems["s"], sems["g"], sems["cc"])
    s_lda, s_ldb = sems["lda"], sems["ldb"]
    ldp = [s_lda, s_ldb]
    compP, compR, state = env["compP"], env["compR"], env["state"]
    c_ident, c_hopm, c_hopimi = env["c_ident"], env["c_hopm"], env["c_hopimi"]
    cc_in, cc_out = env["cc_in"], env["cc_out"]
    from contextlib import ExitStack

    _phase_barrier(nc, blk, ctr, sems)
    with ExitStack() as stc:
        E1 = stc.enter_context
        cPt = E1(nc.sbuf_tensor("cPt", [128, HV * 128], BF16))
        hopbuf = E1(nc.sbuf_tensor("hopbuf", [128, 2 * HV * 2 * 128], BF16))
        Pm = E1(nc.sbuf_tensor("Pm", [128, 128], BF16))
        ps_a = E1(nc.psum_tensor("ps_hop", [128, 128], F32))
        CPD = ctr.get(s_v)

        for h in range(HV):
            @blk.tensor
            def _(t, h=h, CPD=CPD):
                t.wait_ge(s_v, CPD)
                if h > 0:
                    t.wait_ge(s_s, ctr.get(s_s))
                ctr.inc(s_mm, 1)
                t.matmul(ps_a[:, :], compP.ap()[:, h * 128:(h + 1) * 128],
                         c_ident[:, :], start=True, stop=True).then_inc(s_mm, 1)
            TM = ctr.get(s_mm)

            @blk.scalar
            def _(s, h=h, TM=TM):
                s.wait_ge(s_mm, TM)
                ctr.inc(s_s, 1)
                s.copy(cPt[:, h * 128:(h + 1) * 128], ps_a[:, :]).then_inc(s_s, 1)
        CPT = ctr.get(s_s)

        @blk.sync
        def _(q, CPT=CPT):
            q.wait_ge(s_s, CPT)
            q.wait_ge(s_lda, ctr.get(s_lda))
            q.wait_ge(s_ldb, ctr.get(s_ldb))
            q.dma_start(out=cc_in[:, 0:HV * 128], in_=cPt[:, :]).then_inc(s_lda, 16)
            q.dma_start(out=cc_in[:, HV * 128:], in_=compR[:, :]).then_inc(s_lda, 16)
            ctr.inc(s_lda, 32)
        CCI = ctr.get(s_lda)

        @blk.gpsimd
        def _(g, CCI=CCI):
            g.wait_ge(s_lda, CCI)
            ctr.inc(s_cc, 1)
            g.collective_compute(
                "AllGather", ALU.bypass,
                ins=[cc_in[:, :]], outs=[cc_out[:, :]],
                replica_groups=[list(range(8))],
            ).then_inc(s_cc)
            ctr.inc(s_g, 1)
            g.memset(state[:, :], 0.0).then_inc(s_g, 1)
        STZ = ctr.get(s_g)

        for j in range(7):
            sl = ldp[j % 2]

            @blk.sync
            def _(q, j=j, sl=sl):
                q.wait_ge(s_cc, ctr.get(s_cc))
                if j >= 2:
                    q.wait_ge(s_v, ctr.get(s_v))
                q.wait_ge(sl, ctr.get(sl))
                q.dma_start(out=hopbuf[:, (j % 2) * HV * 256:(j % 2 + 1) * HV * 256],
                            in_=cc_out[j * 128:(j + 1) * 128, :]).then_inc(sl, 16)
                ctr.inc(sl, 16)
            HL = ctr.get(sl)

            for h in range(HV):
                pj = hopbuf[:, (j % 2) * HV * 256 + h * 128:][:, :128]
                rj = hopbuf[:, (j % 2) * HV * 256 + HV * 128 + h * 128:][:, :128]
                sh = state.ap()[:, h * 128:(h + 1) * 128]

                @blk.vector
                def _(v, pj=pj, HL=HL, j=j, h=h, STZ=STZ, sl=sl):
                    v.wait_ge(sl, HL)
                    v.wait_ge(s_g, STZ)
                    if j > 0 or h > 0:
                        v.wait_ge(s_mm, ctr.get(s_mm))
                    ctr.inc(s_v, 1)
                    v.scalar_tensor_tensor(out=Pm[:, :], in0=pj,
                                           scalar=c_hopm[:, j:j + 1],
                                           in1=c_hopimi[:, j * 128:(j + 1) * 128],
                                           op0=ALU.mult, op1=ALU.add).then_inc(s_v, 1)
                PMV = ctr.get(s_v)

                @blk.tensor
                def _(t, sh=sh, PMV=PMV):
                    t.wait_ge(s_v, PMV)
                    ctr.inc(s_mm, 1)
                    t.matmul(ps_a[:, :], Pm[:, :], sh, start=True, stop=True).then_inc(s_mm, 1)
                SMM = ctr.get(s_mm)

                @blk.vector
                def _(v, rj=rj, sh=sh, j=j, SMM=SMM):
                    v.wait_ge(s_mm, SMM)
                    ctr.inc(s_v, 1)
                    v.scalar_tensor_tensor(out=sh, in0=rj, scalar=c_hopm[:, j:j + 1],
                                           in1=ps_a[:, :], op0=ALU.mult,
                                           op1=ALU.add).then_inc(s_v, 1)


def _p2b(nc, blk, ctr, sems, env):
    s_mm, s_v, s_s = sems["mm"], sems["v"], sems["s"]
    s_lda, s_ldb, s_sta, s_stb = sems["lda"], sems["ldb"], sems["sta"], sems["stb"]
    ldp = [s_lda, s_ldb]
    stp = [s_sta, s_stb]
    qk_f, barows, state = env["qk_f"], env["barows"], env["state"]
    c_ident, c_identf, c_onerf, c_maskUD, c_rmsw, c_cols = (
        env["c_ident"], env["c_identf"], env["c_onerf"], env["c_maskUD"],
        env["c_rmsw"], env["c_cols"])
    spill, pre_scr, p_scr = env["spill"], env["pre_scr"], env["p_scr"]
    from contextlib import ExitStack

    _phase_barrier(nc, blk, ctr, sems)
    with ExitStack() as stb_:
        E1 = stb_.enter_context
        colsB = E1(nc.sbuf_tensor("colsB", [128, 16], F32))
        rowstB = E1(nc.sbuf_tensor("rowstB", [1, 128], F32))
        p1l = E1(nc.sbuf_tensor("p1lB", [128, 128], F32))
        e_m = E1(nc.sbuf_tensor("e_m", [128, 128], F32))
        Gq = E1(nc.sbuf_tensor("GqB", [128, 128], BF16))
        MT = E1(nc.sbuf_tensor("MTB", [128, 128], BF16))
        qa_t = E1(nc.sbuf_tensor("qa_tB", [128, 128], F32))
        qAf = E1(nc.sbuf_tensor("qAfB", [128, 128], BF16))
        nMYT = E1(nc.sbuf_tensor("nMYTB", [128, 128], BF16))
        U0sb = E1(nc.sbuf_tensor("U0sbB", [128, 2 * 128], BF16))
        Ycsb = E1(nc.sbuf_tensor("YcsbB", [128, 2 * 128], BF16))
        PRsb = E1(nc.sbuf_tensor("PRsbB", [128, 2 * 256], BF16))
        Onsb = E1(nc.sbuf_tensor("OnsbB", [128, 128], BF16))
        Ofp = E1(nc.sbuf_tensor("OfpB", [128, 128], F32))
        sqf = E1(nc.sbuf_tensor("sqfB", [128, 128], F32))
        gate_t = E1(nc.sbuf_tensor("gate_tB", [128, 2 * 128], BF16))
        gsil = E1(nc.sbuf_tensor("gsilB", [128, 128], BF16))
        sqj = E1(nc.sbuf_tensor("sqjB", [128, 128], BF16))
        ptile = E1(nc.sbuf_tensor("ptileB", [128, 2 * 128], BF16))
        rcol = E1(nc.sbuf_tensor("rcolB", [128, 4], F32))
        ps_a = E1(nc.psum_tensor("ps_aB", [128, 128], F32))
        ps_b = E1(nc.psum_tensor("ps_bB", [128, 128], F32))
        ps_o = E1(nc.psum_tensor("ps_oB", [128, 128], F32))
        ps_row = E1(nc.psum_tensor("ps_rowB", [1, 128], F32))
        ps_colt = E1(nc.psum_tensor("ps_coltB", [128, 16], F32))

        for ci in range(NCH):
            @blk.tensor
            def _(t, ci=ci):
                t.wait_ge(s_s, ctr.get(s_s))
                ctr.inc(s_mm, 1)
                t.matmul(ps_colt[:, :], barows[:, 3 * TPC + ci * 128:3 * TPC + (ci + 1) * 128],
                         c_identf[0:HV, 0:HV], start=True, stop=True).then_inc(s_mm, 1)
            CT = ctr.get(s_mm)

            @blk.scalar
            def _(s, CT=CT):
                s.wait_ge(s_mm, CT)
                ctr.inc(s_s, 1)
                s.copy(colsB[:, :], ps_colt[:, :]).then_inc(s_s, 1)
            COLS = ctr.get(s_s)

            for h in range(HV):
                hq = h // 2
                it = ci * HV + h
                buf = it % 2
                ellc = colsB[:, h:h + 1]
                kf = qk_f.ap()[:, (8 + hq) * TPC + ci * 128:][:, :128]
                qf = qk_f.ap()[:, hq * TPC + ci * 128:][:, :128]
                sh = state.ap()[:, h * 128:(h + 1) * 128]
                u0 = U0sb[:, buf * 128:buf * 128 + 128]
                yc = Ycsb[:, buf * 128:buf * 128 + 128]
                ptl = PRsb[:, buf * 256:buf * 256 + 128]
                rtl = PRsb[:, buf * 256 + 128:buf * 256 + 256]
                gt = gate_t[:, buf * 128:buf * 128 + 128]
                pt = ptile[:, buf * 128:buf * 128 + 128]

                @blk.sync
                def _(q, ci=ci, h=h, u0=u0, yc=yc, ptl=ptl, rtl=rtl, gt=gt, buf=buf, it=it):
                    if it < 2:
                        q.wait_ge(s_sta, ctr.get(s_sta))
                        q.wait_ge(s_stb, ctr.get(s_stb))
                    q.wait_ge(s_mm, ctr.get(s_mm))
                    q.wait_ge(s_s, ctr.get(s_s))
                    sl = ldp[buf]
                    q.wait_ge(sl, ctr.get(sl))
                    q.dma_start(out=u0, in_=spill[ci, h, 0]).then_inc(sl, 16)
                    q.dma_start(out=yc, in_=spill[ci, h, 1]).then_inc(sl, 16)
                    q.dma_start(out=ptl, in_=spill[ci, h, 2]).then_inc(sl, 16)
                    q.dma_start(out=rtl, in_=spill[ci, h, 3]).then_inc(sl, 16)
                    q.dma_start(out=gt, in_=pre_scr[NFQ + h, :, HALO + ci * 128:HALO + (ci + 1) * 128]).then_inc(sl, 16)
                    ctr.inc(ldp[buf], 80)
                LDS = ctr.get(ldp[buf])

                if h % 2 == 0:
                    @blk.tensor
                    def _(t, kf=kf, qf=qf, COLS=COLS, it=it):
                        t.wait_ge(s_s, COLS)
                        if it > 0:
                            t.wait_ge(s_v, ctr.get(s_v))
                        ctr.inc(s_mm, 1)
                        t.matmul(ps_a[:, :], kf, qf, start=True, stop=True).then_inc(s_mm, 1)
                    GK = ctr.get(s_mm)

                    @blk.scalar
                    def _(s, GK=GK):
                        s.wait_ge(s_mm, GK)
                        ctr.inc(s_s, 1)
                        s.copy(Gq[:, :], ps_a[:, :]).then_inc(s_s, 1)
                GQR = ctr.get(s_s)

                @blk.tensor
                def _(t, ellc=ellc, COLS=COLS, it=it):
                    t.wait_ge(s_s, ctr.get(s_s))
                    t.wait_ge(s_v, ctr.get(s_v))
                    ctr.inc(s_mm, 1)
                    t.matmul(ps_row[0:1, :], ellc, c_identf[:, :], start=True, stop=True).then_inc(s_mm, 1)
                RW1 = ctr.get(s_mm)

                @blk.scalar
                def _(s, RW1=RW1):
                    s.wait_ge(s_mm, RW1)
                    ctr.inc(s_s, 1)
                    s.copy(rowstB[0:1, :], ps_row[0:1, :]).then_inc(s_s, 1)
                RW2 = ctr.get(s_s)

                @blk.tensor
                def _(t, RW2=RW2):
                    t.wait_ge(s_s, RW2)
                    ctr.inc(s_mm, 1)
                    t.matmul(ps_b[:, :], c_onerf[:, :], rowstB[0:1, :], start=True, stop=True).then_inc(s_mm, 1)
                RW3 = ctr.get(s_mm)

                @blk.scalar
                def _(s, RW3=RW3):
                    s.wait_ge(s_mm, RW3)
                    ctr.inc(s_s, 1)
                    s.copy(p1l[:, :], ps_b[:, :]).then_inc(s_s, 1)
                P1L = ctr.get(s_s)

                @blk.vector
                def _(v, ellc=ellc, P1L=P1L):
                    v.wait_ge(s_s, P1L)
                    ins = v.tensor_scalar(e_m[:, :], p1l[:, :], ellc, None, ALU.subtract)
                    ctr.inc(s_v, 1); ins.then_inc(s_v, 1); v.wait_ge(s_v, ctr.get(s_v))
                    ctr.inc(s_v, 1)
                    v.tensor_tensor(e_m[:, :], e_m[:, :], c_maskUD[:, :], ALU.add).then_inc(s_v, 1)
                EM1 = ctr.get(s_v)

                @blk.scalar
                def _(s, EM1=EM1):
                    s.wait_ge(s_v, EM1)
                    s.activation(e_m[:, :], e_m[:, :], AF.Exp)
                    ctr.inc(s_s, 1)
                    s.activation(qa_t[:, :], p1l[:, :], AF.Exp, bias=c_cols[:, 2:3]).then_inc(s_s, 1)
                EM2 = ctr.get(s_s)

                @blk.vector
                def _(v, qf=qf, EM2=EM2, GQR=GQR):
                    v.wait_ge(s_s, EM2)
                    v.wait_ge(s_s, GQR)
                    v.tensor_tensor(MT[:, :], Gq[:, :], e_m[:, :], ALU.mult)
                    ctr.inc(s_v, 1)
                    v.tensor_tensor(qAf[:, :], qf, qa_t[:, :], ALU.mult).then_inc(s_v, 1)
                MTV = ctr.get(s_v)

                @blk.tensor
                def _(t, yc=yc, MTV=MTV, LDS=LDS, buf=buf):
                    t.wait_ge(s_v, MTV)
                    t.wait_ge(ldp[buf], LDS)
                    ctr.inc(s_mm, 1)
                    t.matmul(ps_a[:, :], yc, MT[:, :], start=True, stop=True).then_inc(s_mm, 1)
                MY = ctr.get(s_mm)

                @blk.scalar
                def _(s, MY=MY):
                    s.wait_ge(s_mm, MY)
                    ctr.inc(s_s, 1)
                    s.activation(nMYT[:, :], ps_a[:, :], AF.Identity, scale=-1.0).then_inc(s_s, 1)
                NM = ctr.get(s_s)

                @blk.tensor
                def _(t, u0=u0, sh=sh, NM=NM, it=it):
                    t.wait_ge(s_s, NM)
                    if it > 0:
                        t.wait_ge(s_v, ctr.get(s_v))
                    t.matmul(ps_o[:, :], MT[:, :], u0, start=True, stop=False)
                    t.matmul(ps_o[:, :], qAf[:, :], sh, start=False, stop=False)
                    ctr.inc(s_mm, 1)
                    t.matmul(ps_o[:, :], nMYT[:, :], sh, start=False, stop=True).then_inc(s_mm, 1)
                OM = ctr.get(s_mm)

                @blk.scalar
                def _(s, OM=OM):
                    s.wait_ge(s_mm, OM)
                    s.wait_ge(s_v, ctr.get(s_v))
                    ins = s.copy(Ofp[:, :], ps_o[:, :])
                    ctr.inc(s_s, 1); ins.then_inc(s_s, 1); s.wait_ge(s_s, ctr.get(s_s))
                    ctr.inc(s_s, 1)
                    s.activation(sqf[:, :], Ofp[:, :], AF.Square).then_inc(s_s, 1)
                OE = ctr.get(s_s)

                @blk.vector
                def _(v, OE=OE):
                    v.wait_ge(s_s, OE)
                    ins = v.tensor_reduce(rcol[:, 0:1], sqf[:, :], AX.X, ALU.add)
                    ctr.inc(s_v, 1); ins.then_inc(s_v, 1); v.wait_ge(s_v, ctr.get(s_v))
                    ctr.inc(s_v, 1)
                    v.tensor_scalar(rcol[:, 1:2], rcol[:, 0:1], 1.0 / DV, EPS_RMS,
                                    ALU.mult, ALU.add).then_inc(s_v, 1)
                RM1 = ctr.get(s_v)

                @blk.scalar
                def _(s, RM1=RM1):
                    s.wait_ge(s_v, RM1)
                    ctr.inc(s_s, 1)
                    s.activation(rcol[:, 1:2], rcol[:, 1:2], AF.Sqrt).then_inc(s_s, 1)
                RM2 = ctr.get(s_s)

                @blk.vector
                def _(v, RM2=RM2):
                    v.wait_ge(s_s, RM2)
                    ins = v.reciprocal(rcol[:, 2:3], rcol[:, 1:2])
                    ctr.inc(s_v, 1); ins.then_inc(s_v, 1); v.wait_ge(s_v, ctr.get(s_v))
                    ctr.inc(s_v, 1)
                    v.tensor_scalar(Onsb[:, :], Ofp[:, :], rcol[:, 2:3], None, ALU.mult).then_inc(s_v, 1)
                ONV = ctr.get(s_v)

                @blk.tensor
                def _(t, ONV=ONV):
                    t.wait_ge(s_v, ONV)
                    t.wait_ge(s_s, ctr.get(s_s))
                    ctr.inc(s_mm, 1)
                    t.matmul(ps_a[:, :], Onsb[:, :], c_ident[:, :], start=True, stop=True).then_inc(s_mm, 1)
                OT = ctr.get(s_mm)

                @blk.scalar
                def _(s, gt=gt, OT=OT, buf=buf, LDS=LDS):
                    s.wait_ge(s_mm, OT)
                    s.wait_ge(ldp[buf], LDS)
                    ins = s.activation(gsil[:, :], gt, AF.Silu)
                    ctr.inc(s_s, 1); ins.then_inc(s_s, 1); s.wait_ge(s_s, ctr.get(s_s))
                    ctr.inc(s_s, 1)
                    s.activation(sqj[:, :], ps_a[:, :], AF.Identity, scale=c_rmsw[:, 0:1]).then_inc(s_s, 1)
                PS1 = ctr.get(s_s)

                @blk.vector
                def _(v, pt=pt, PS1=PS1, buf=buf):
                    v.wait_ge(s_s, PS1)
                    v.wait_ge(stp[buf], ctr.get(stp[buf]))
                    ctr.inc(s_v, 1)
                    v.tensor_tensor(pt, sqj[:, :], gsil[:, :], ALU.mult).then_inc(s_v, 1)
                PV = ctr.get(s_v)

                @blk.gpsimd
                def _(g, ci=ci, h=h, pt=pt, PV=PV, buf=buf):
                    g.wait_ge(s_v, PV)
                    g.wait_ge(stp[buf], ctr.get(stp[buf]))
                    g.dma_start(out=p_scr[h, :, ci * 128:(ci + 1) * 128], in_=pt).then_inc(stp[buf], 16)
                    ctr.inc(stp[buf], 16)

                @blk.tensor
                def _(t, ptl=ptl, sh=sh, OM=OM):
                    ctr.inc(s_mm, 1)
                    t.matmul(ps_b[:, :], ptl, sh, start=True, stop=True).then_inc(s_mm, 1)
                SM = ctr.get(s_mm)

                @blk.vector
                def _(v, rtl=rtl, sh=sh, SM=SM):
                    v.wait_ge(s_mm, SM)
                    ctr.inc(s_v, 1)
                    v.tensor_tensor(sh, ps_b[:, :], rtl, ALU.add).then_inc(s_v, 1)


def _p3(nc, blk, ctr, sems, env):
    s_mm, s_v, s_s = sems["mm"], sems["v"], sems["s"]
    s_lda, s_ldb, s_sta, s_stb = sems["lda"], sems["ldb"], sems["sta"], sems["stb"]
    ldp = [s_lda, s_ldb]
    stp = [s_sta, s_stb]
    c_ident, c_cols = env["c_ident"], env["c_cols"]
    x_in, wout_b, p_scr, x2_scr = env["x_in"], env["wout_b"], env["p_scr"], env["x2_scr"]
    xq_f = env["xq_f"]
    from contextlib import ExitStack

    _phase_barrier(nc, blk, ctr, sems)
    with ExitStack() as st3:
        E1 = st3.enter_context
        pcol = E1(nc.sbuf_tensor("pcol", [128, 2 * 2048], BF16))
        woutb = E1(nc.sbuf_tensor("woutb", [128, 2 * 16 * 512], BF16))
        attn_all = E1(nc.sbuf_tensor("attn_all", [128, 8 * D], F32))
        lnt = E1(nc.sbuf_tensor("lnt3", [128, 40], F32))
        x2t = E1(nc.sbuf_tensor("x2t", [128, 2 * D], F32))
        xnt = E1(nc.sbuf_tensor("xnt3", [128, D], F32))
        xtl = E1(nc.sbuf_tensor("xtl3", [128, 2 * D], F32))
        inti = E1(nc.sbuf_tensor("inti", [128, D], I32))
        intf = E1(nc.sbuf_tensor("intf", [128, D], F32))
        xq_t = E1(nc.sbuf_tensor("xq_t", [128, D], BF16))
        ps_op0 = E1(nc.psum_tensor("ps_op0", [128, 512], F32))
        ps_op1 = E1(nc.psum_tensor("ps_op1", [128, 512], F32))
        ps_big = E1(nc.psum_tensor("ps_big3", [128, D], F32))

        for fg in range(4):
            wsl = woutb[:, (fg % 2) * 16 * 512:][:, :16 * 512]

            @blk.gpsimd
            def _(g, fg=fg, wsl=wsl):
                if fg >= 2:
                    g.wait_ge(s_mm, ctr.get(s_mm))
                g.wait_ge(stp[fg % 2], ctr.get(stp[fg % 2]))
                g.dma_start(out=wsl.rearrange("p (db f) -> p db f", f=512),
                            in_=wout_b[fg].rearrange("(db p) f -> p db f", p=128)).then_inc(stp[fg % 2], 16)
                ctr.inc(stp[fg % 2], 16)
            WL = ctr.get(stp[fg % 2])

            for tb in range(8):
                it = fg * 8 + tb
                pb = it % 2
                psl = ps_op0 if tb % 2 == 0 else ps_op1
                pc = pcol[:, pb * 2048:pb * 2048 + 2048]

                @blk.sync
                def _(q, tb=tb, pc=pc, pb=pb, it=it):
                    if it >= 2:
                        q.wait_ge(s_mm, ctr.get(s_mm))
                    q.wait_ge(ldp[pb], ctr.get(ldp[pb]))
                    q.dma_start(out=pc.rearrange("p (db t) -> p db t", t=128),
                                in_=p_scr[:, :, tb * 128:(tb + 1) * 128].rearrange("db p t -> p db t")).then_inc(ldp[pb], 16)
                    ctr.inc(ldp[pb], 16)
                PL = ctr.get(ldp[pb])

                @blk.tensor
                def _(t, fg=fg, tb=tb, pc=pc, wsl=wsl, psl=psl, PL=PL, WL=WL, pb=pb):
                    t.wait_ge(ldp[pb], PL)
                    t.wait_ge(stp[fg % 2], WL)
                    if fg + tb > 0:
                        t.wait_ge(s_s, ctr.get(s_s))
                    mm = None
                    for db in range(NDB):
                        mm = t.matmul(psl[:, :],
                                      pc[:, db * 128:(db + 1) * 128],
                                      wsl.rearrange("p (db f) -> p db f", f=512)[:, db, :],
                                      start=(db == 0), stop=(db == NDB - 1))
                    ctr.inc(s_mm, 1)
                    mm.then_inc(s_mm, 1)
                OPM = ctr.get(s_mm)

                @blk.scalar
                def _(s, fg=fg, tb=tb, psl=psl, OPM=OPM):
                    s.wait_ge(s_mm, OPM)
                    ctr.inc(s_s, 1)
                    s.copy(attn_all[:, tb * D + fg * 512: tb * D + fg * 512 + 512], psl[:, :]).then_inc(s_s, 1)
        OPDONE = ctr.get(s_s)

        for tb in range(8):
            at = attn_all[:, tb * D:(tb + 1) * D]
            xt = xtl[:, (tb % 2) * D:(tb % 2) * D + D]
            x2 = x2t[:, (tb % 2) * D:(tb % 2) * D + D]

            @blk.sync
            def _(q, tb=tb, xt=xt):
                if tb >= 2:
                    q.wait_ge(s_v, ctr.get(s_v))
                q.wait_ge(ldp[tb % 2], ctr.get(ldp[tb % 2]))
                q.dma_start(out=xt, in_=x_in[HALO + tb * 128: HALO + (tb + 1) * 128, :]).then_inc(ldp[tb % 2], 16)
                ctr.inc(ldp[tb % 2], 16)
            XL = ctr.get(ldp[tb % 2])

            def emit_ln(src_ap, eps_val, dst_col):
                @blk.vector
                def _(v):
                    v.wait_ge(s_v, ctr.get(s_v))
                    ins = None
                    for i4 in range(4):
                        ins = v.bn_stats(lnt[:, i4 * 6:(i4 + 1) * 6], src_ap[:, i4 * 512:(i4 + 1) * 512])
                    ctr.inc(s_v, 1); ins.then_inc(s_v, 1); v.wait_ge(s_v, ctr.get(s_v))
                    ins = v.bn_aggr(lnt[:, 24:26], lnt[:, 0:24])
                    ctr.inc(s_v, 1); ins.then_inc(s_v, 1); v.wait_ge(s_v, ctr.get(s_v))
                    ctr.inc(s_v, 1)
                    v.tensor_scalar(lnt[:, 26:27], lnt[:, 25:26], eps_val, None, ALU.add).then_inc(s_v, 1)
                V1 = ctr.get(s_v)

                @blk.scalar
                def _(s, V1=V1):
                    s.wait_ge(s_v, V1)
                    ctr.inc(s_s, 1)
                    s.activation(lnt[:, 27:28], lnt[:, 26:27], AF.Sqrt).then_inc(s_s, 1)
                S1 = ctr.get(s_s)

                @blk.vector
                def _(v, S1=S1):
                    v.wait_ge(s_s, S1)
                    ins = v.reciprocal(lnt[:, dst_col:dst_col + 1], lnt[:, 27:28])
                    ctr.inc(s_v, 1); ins.then_inc(s_v, 1); v.wait_ge(s_v, ctr.get(s_v))
                    ctr.inc(s_v, 1)
                    v.scalar_tensor_tensor(out=lnt[:, dst_col + 1:dst_col + 2], in0=lnt[:, 24:25],
                                           scalar=-1.0, in1=lnt[:, dst_col:dst_col + 1],
                                           op0=ALU.mult, op1=ALU.mult).then_inc(s_v, 1)
                return ctr.get(s_v)

            @blk.vector
            def _(v, OPDONE=OPDONE):
                v.wait_ge(s_s, OPDONE)
            LN1 = emit_ln(at, EPS_LN, 28)

            @blk.scalar
            def _(s, at=at, LN1=LN1):
                s.wait_ge(s_v, LN1)
                ctr.inc(s_s, 1)
                s.activation(xnt[:, :], at, AF.Identity,
                             bias=lnt[:, 29:30], scale=lnt[:, 28:29]).then_inc(s_s, 1)
            A1 = ctr.get(s_s)

            @blk.vector
            def _(v, x2=x2, xt=xt, A1=A1, XL=XL, tb=tb):
                v.wait_ge(s_s, A1)
                v.wait_ge(ldp[tb % 2], XL)
                v.wait_ge(stp[tb % 2], ctr.get(stp[tb % 2]))
                ctr.inc(s_v, 1)
                v.tensor_tensor(x2, xnt[:, :], xt, ALU.add).then_inc(s_v, 1)
            X2V = ctr.get(s_v)

            @blk.gpsimd
            def _(g, tb=tb, x2=x2, X2V=X2V):
                g.wait_ge(s_v, X2V)
                g.wait_ge(stp[tb % 2], ctr.get(stp[tb % 2]))
                g.dma_start(out=x2_scr[tb * 128:(tb + 1) * 128, :], in_=x2).then_inc(stp[tb % 2], 16)
                ctr.inc(stp[tb % 2], 16)

            LN2 = emit_ln(x2, EPS_FFN, 30)

            @blk.scalar
            def _(s, x2=x2, LN2=LN2):
                s.wait_ge(s_v, LN2)
                ctr.inc(s_s, 1)
                s.activation(xnt[:, :], x2, AF.Identity,
                             bias=lnt[:, 31:32], scale=lnt[:, 30:31]).then_inc(s_s, 1)
            A2 = ctr.get(s_s)

            @blk.vector
            def _(v, A2=A2):
                v.wait_ge(s_s, A2)
                ins = v.tensor_reduce(lnt[:, 32:33], xnt[:, :], AX.X, ALU.max, apply_absolute_value=True)
                ctr.inc(s_v, 1); ins.then_inc(s_v, 1); v.wait_ge(s_v, ctr.get(s_v))
                ins = v.reciprocal(lnt[:, 33:34], lnt[:, 32:33])
                ctr.inc(s_v, 1); ins.then_inc(s_v, 1); v.wait_ge(s_v, ctr.get(s_v))
                v.tensor_scalar(lnt[:, 34:35], lnt[:, 33:34], 127.0, None, ALU.mult)
                ins = v.tensor_scalar(lnt[:, 35:36], lnt[:, 32:33], 1.0 / 127.0, None, ALU.mult)
                ctr.inc(s_v, 1); ins.then_inc(s_v, 1); v.wait_ge(s_v, ctr.get(s_v))
                ins = v.tensor_scalar(intf[:, :], xnt[:, :], lnt[:, 34:35], None, ALU.mult)
                ctr.inc(s_v, 1); ins.then_inc(s_v, 1); v.wait_ge(s_v, ctr.get(s_v))
                ins = v.tensor_copy(inti[:, :], intf[:, :])
                ctr.inc(s_v, 1); ins.then_inc(s_v, 1); v.wait_ge(s_v, ctr.get(s_v))
                ins = v.tensor_copy(intf[:, :], inti[:, :])
                ctr.inc(s_v, 1); ins.then_inc(s_v, 1); v.wait_ge(s_v, ctr.get(s_v))
                ctr.inc(s_v, 1)
                v.tensor_scalar(xq_t[:, :], intf[:, :], lnt[:, 35:36], None, ALU.mult).then_inc(s_v, 1)
            QV = ctr.get(s_v)

            @blk.tensor
            def _(t, tb=tb, QV=QV):
                t.wait_ge(s_v, QV)
                if tb > 0:
                    t.wait_ge(s_s, ctr.get(s_s))
                mm = None
                for db in range(NDB):
                    mm = t.matmul(ps_big[:, db * 128:(db + 1) * 128],
                                  xq_t[:, db * 128:(db + 1) * 128],
                                  c_ident[:, :], start=True, stop=True)
                ctr.inc(s_mm, 1)
                mm.then_inc(s_mm, 1)
            TQ = ctr.get(s_mm)

            @blk.scalar
            def _(s, tb=tb, TQ=TQ):
                s.wait_ge(s_mm, TQ)
                ctr.inc(s_s, 1)
                s.copy(
                    xq_f.ap().rearrange("p (db c) -> p db c", db=NDB)[:, :, tb * 128:(tb + 1) * 128],
                    ps_big.ap().rearrange("p (db c) -> p db c", db=NDB),
                ).then_inc(s_s, 1)


def _ffn_gu(nc, blk, ctr, sems, env):
    s_mm, s_v, s_s = sems["mm"], sems["v"], sems["s"]
    s_lda, s_ldb, s_sta, s_stb = sems["lda"], sems["ldb"], sems["sta"], sems["stb"]
    ldp = [sems["ga"], sems["gb"]]
    xq_f, p2 = env["xq_f"], env["p2"]
    wg_b, wu_b = env["wg_b"], env["wu_b"]
    from contextlib import ExitStack

    _phase_barrier(nc, blk, ctr, sems)
    with ExitStack() as stf:
        E1 = stf.enter_context
        wgb = E1(nc.sbuf_tensor("wgb", [128, 2 * 2048], BF16))
        wub = E1(nc.sbuf_tensor("wub", [128, 2 * 2048], BF16))
        gs = E1(nc.sbuf_tensor("gs", [128, 2 * 512], BF16))
        ps_g0 = E1(nc.psum_tensor("ps_g0", [128, 512], F32))
        ps_g1 = E1(nc.psum_tensor("ps_g1", [128, 512], F32))
        ps_u0 = E1(nc.psum_tensor("ps_u0", [128, 512], F32))
        ps_u1 = E1(nc.psum_tensor("ps_u1", [128, 512], F32))

        for hb in range(NHB):
            buf = hb % 2
            wg = wgb[:, buf * 2048:buf * 2048 + 2048]
            wu = wub[:, buf * 2048:buf * 2048 + 2048]

            @blk.gpsimd
            def _(g, hb=hb, wg=wg, wu=wu, buf=buf):
                if hb >= 2:
                    g.wait_ge(s_mm, ctr.get(s_mm))
                g.wait_ge(ldp[buf], ctr.get(ldp[buf]))
                g.dma_start(out=wg.rearrange("p (db f) -> p db f", f=128),
                            in_=wg_b[hb].rearrange("(db p) f -> p db f", p=128)).then_inc(ldp[buf], 16)
                g.dma_start(out=wu.rearrange("p (db f) -> p db f", f=128),
                            in_=wu_b[hb].rearrange("(db p) f -> p db f", p=128)).then_inc(ldp[buf], 16)
                ctr.inc(ldp[buf], 32)
            WL = ctr.get(ldp[buf])

            @blk.scalar
            def _(s, wg=wg, wu=wu, WL=WL, buf=buf, hb=hb):
                s.wait_ge(ldp[buf], WL)
                if hb >= 2:
                    s.wait_ge(s_mm, ctr.get(s_mm))
                ins = s.sign(wg, wg)
                ins = s.sign(wu, wu)
                ctr.inc(s_s, 1); ins.then_inc(s_s, 1)
            SG = ctr.get(s_s)

            for th in range(2):
                psg = ps_g0 if th == 0 else ps_g1
                psu = ps_u0 if th == 0 else ps_u1

                @blk.tensor
                def _(t, hb=hb, th=th, wg=wg, wu=wu, psg=psg, psu=psu, SG=SG):
                    t.wait_ge(s_s, SG)
                    if hb > 0:
                        t.wait_ge(s_v, ctr.get(s_v))
                    mm = None
                    for db in range(NDB):
                        mm = t.matmul(psg[:, :], wg[:, db * 128:(db + 1) * 128],
                                      xq_f.ap()[:, db * TPC + th * 512:][:, :512],
                                      start=(db == 0), stop=(db == NDB - 1))
                    for db in range(NDB):
                        mm = t.matmul(psu[:, :], wu[:, db * 128:(db + 1) * 128],
                                      xq_f.ap()[:, db * TPC + th * 512:][:, :512],
                                      start=(db == 0), stop=(db == NDB - 1))
                    ctr.inc(s_mm, 1)
                    mm.then_inc(s_mm, 1)
                GM = ctr.get(s_mm)

                @blk.scalar
                def _(s, psg=psg, GM=GM, th=th, hb=hb):
                    s.wait_ge(s_mm, GM)
                    if hb > 0:
                        s.wait_ge(s_v, ctr.get(s_v) - 1)
                    ctr.inc(s_s, 1)
                    s.activation(gs[:, th * 512:(th + 1) * 512], psg[:, :], AF.Silu).then_inc(s_s, 1)
                GS = ctr.get(s_s)

                @blk.vector
                def _(v, hb=hb, th=th, psu=psu, GS=GS):
                    v.wait_ge(s_s, GS)
                    ctr.inc(s_v, 1)
                    v.tensor_tensor(p2.ap()[:, hb * TPC + th * 512:][:, :512],
                                    psu[:, :], gs[:, th * 512:(th + 1) * 512], ALU.mult).then_inc(s_v, 1)


def _ffn_down(nc, blk, ctr, sems, env):
    s_mm, s_v, s_s = sems["mm"], sems["v"], sems["s"]
    s_lda, s_ldb, s_sta, s_stb = sems["lda"], sems["ldb"], sems["sta"], sems["stb"]
    ldp = [sems["ga"], sems["gb"]]
    stp = [s_sta, s_stb]
    p2, wd_b, ffn_scr = env["p2"], env["wd_b"], env["ffn_scr"]
    from contextlib import ExitStack

    _phase_barrier(nc, blk, ctr, sems)
    with ExitStack() as std_:
        E1 = std_.enter_context
        wdb = E1(nc.sbuf_tensor("wdb", [128, 2 * NHB * 128], BF16))
        fout = E1(nc.sbuf_tensor("fout", [128, 2 * TPC], BF16))
        ps_d0 = E1(nc.psum_tensor("ps_d0", [128, TPC], F32))
        ps_d1 = E1(nc.psum_tensor("ps_d1", [128, TPC], F32))

        for fcb in range(16):
            buf = fcb % 2
            wd = wdb[:, buf * NHB * 128:][:, :NHB * 128]
            psd = ps_d0 if fcb % 2 == 0 else ps_d1
            fo = fout[:, buf * TPC:buf * TPC + TPC]

            @blk.gpsimd
            def _(g, fcb=fcb, wd=wd, buf=buf):
                if fcb >= 2:
                    g.wait_ge(s_mm, ctr.get(s_mm))
                g.wait_ge(ldp[buf], ctr.get(ldp[buf]))
                g.dma_start(out=wd.rearrange("p (hb f) -> p hb f", f=128),
                            in_=wd_b[fcb].rearrange("(hb p) f -> p hb f", p=128)).then_inc(ldp[buf], 16)
                ctr.inc(ldp[buf], 16)
            WL = ctr.get(ldp[buf])

            @blk.scalar
            def _(s, wd=wd, WL=WL, buf=buf, fcb=fcb):
                s.wait_ge(ldp[buf], WL)
                if fcb >= 2:
                    s.wait_ge(s_mm, ctr.get(s_mm))
                ctr.inc(s_s, 1)
                s.sign(wd, wd).then_inc(s_s, 1)
            SG = ctr.get(s_s)

            @blk.tensor
            def _(t, fcb=fcb, wd=wd, psd=psd, SG=SG):
                t.wait_ge(s_s, SG)
                if fcb >= 2:
                    t.wait_ge(s_s, ctr.get(s_s))
                mm = None
                for th in range(2):
                    for hb in range(NHB):
                        mm = t.matmul(psd[:, th * 512:(th + 1) * 512],
                                      wd[:, hb * 128:(hb + 1) * 128],
                                      p2.ap()[:, hb * TPC + th * 512:][:, :512],
                                      start=(hb == 0), stop=(hb == NHB - 1))
                ctr.inc(s_mm, 1)
                mm.then_inc(s_mm, 1)
            DM = ctr.get(s_mm)

            @blk.scalar
            def _(s, psd=psd, fo=fo, DM=DM, buf=buf, fcb=fcb):
                s.wait_ge(s_mm, DM)
                if fcb >= 2:
                    s.wait_ge(stp[buf], ctr.get(stp[buf]))
                ctr.inc(s_s, 1)
                s.copy(fo, psd[:, :]).then_inc(s_s, 1)
            FE = ctr.get(s_s)

            @blk.gpsimd
            def _(g, fcb=fcb, fo=fo, FE=FE, buf=buf):
                g.wait_ge(s_s, FE)
                g.wait_ge(stp[buf], ctr.get(stp[buf]))
                g.dma_start(out=ffn_scr[fcb], in_=fo).then_inc(stp[buf], 16)
                ctr.inc(stp[buf], 16)


def _final(nc, blk, ctr, sems, env):
    s_mm, s_v, s_s = sems["mm"], sems["v"], sems["s"]
    s_lda, s_ldb, s_sta, s_stb = sems["lda"], sems["ldb"], sems["sta"], sems["stb"]
    ldp = [s_lda, s_ldb]
    stp = [s_sta, s_stb]
    c_ident = env["c_ident"]
    ffn_scr, x2_scr, y_out = env["ffn_scr"], env["x2_scr"], env["y_out"]
    from contextlib import ExitStack

    _phase_barrier(nc, blk, ctr, sems)
    with ExitStack() as stf:
        E1 = stf.enter_context
        ftile = E1(nc.sbuf_tensor("ftile", [128, 2 * D], BF16))
        x2l = E1(nc.sbuf_tensor("x2l", [128, 2 * D], F32))
        yt = E1(nc.sbuf_tensor("yt", [128, 2 * D], F32))
        lnt = E1(nc.sbuf_tensor("lntF", [128, 32], F32))
        ps_fin0 = E1(nc.psum_tensor("ps_fin0", [128, D], F32))
        ps_fin1 = E1(nc.psum_tensor("ps_fin1", [128, D], F32))

        for tb in range(8):
            buf = tb % 2
            ft = ftile[:, buf * D:buf * D + D]
            xx = x2l[:, buf * D:buf * D + D]
            yy = yt[:, buf * D:buf * D + D]
            psf = ps_fin0 if buf == 0 else ps_fin1

            @blk.sync
            def _(q, tb=tb, ft=ft, xx=xx, buf=buf):
                if tb >= 2:
                    q.wait_ge(s_mm, ctr.get(s_mm))
                    q.wait_ge(s_v, ctr.get(s_v))
                q.wait_ge(ldp[buf], ctr.get(ldp[buf]))
                q.dma_start(out=ft.rearrange("p (db t) -> p db t", t=128),
                            in_=ffn_scr[:, :, tb * 128:(tb + 1) * 128].rearrange("db p t -> p db t")).then_inc(ldp[buf], 16)
                q.dma_start(out=xx, in_=x2_scr[tb * 128:(tb + 1) * 128, :]).then_inc(ldp[buf], 16)
                ctr.inc(ldp[buf], 32)
            FL = ctr.get(ldp[buf])

            @blk.tensor
            def _(t, tb=tb, ft=ft, psf=psf, FL=FL, buf=buf):
                t.wait_ge(ldp[buf], FL)
                if tb >= 2:
                    t.wait_ge(s_s, ctr.get(s_s))
                mm = None
                for db in range(NDB):
                    mm = t.matmul(psf[:, db * 128:(db + 1) * 128],
                                  ft[:, db * 128:(db + 1) * 128],
                                  c_ident[:, :], start=True, stop=True)
                ctr.inc(s_mm, 1)
                mm.then_inc(s_mm, 1)
            FM = ctr.get(s_mm)

            @blk.vector
            def _(v, psf=psf, FM=FM):
                v.wait_ge(s_mm, FM)
                ins = None
                for i4 in range(4):
                    ins = v.bn_stats(lnt[:, i4 * 6:(i4 + 1) * 6], psf[:, i4 * 512:(i4 + 1) * 512])
                ctr.inc(s_v, 1); ins.then_inc(s_v, 1); v.wait_ge(s_v, ctr.get(s_v))
                ins = v.bn_aggr(lnt[:, 24:26], lnt[:, 0:24])
                ctr.inc(s_v, 1); ins.then_inc(s_v, 1); v.wait_ge(s_v, ctr.get(s_v))
                ctr.inc(s_v, 1)
                v.tensor_scalar(lnt[:, 26:27], lnt[:, 25:26], EPS_LN, None, ALU.add).then_inc(s_v, 1)
            V1 = ctr.get(s_v)

            @blk.scalar
            def _(s, V1=V1):
                s.wait_ge(s_v, V1)
                ctr.inc(s_s, 1)
                s.activation(lnt[:, 27:28], lnt[:, 26:27], AF.Sqrt).then_inc(s_s, 1)
            S1 = ctr.get(s_s)

            @blk.vector
            def _(v, S1=S1):
                v.wait_ge(s_s, S1)
                ins = v.reciprocal(lnt[:, 28:29], lnt[:, 27:28])
                ctr.inc(s_v, 1); ins.then_inc(s_v, 1); v.wait_ge(s_v, ctr.get(s_v))
                ctr.inc(s_v, 1)
                v.scalar_tensor_tensor(out=lnt[:, 29:30], in0=lnt[:, 24:25],
                                       scalar=-1.0, in1=lnt[:, 28:29],
                                       op0=ALU.mult, op1=ALU.mult).then_inc(s_v, 1)
            V2 = ctr.get(s_v)

            @blk.scalar
            def _(s, psf=psf, yy=yy, V2=V2, buf=buf, tb=tb):
                s.wait_ge(s_v, V2)
                if tb >= 2:
                    s.wait_ge(stp[buf], ctr.get(stp[buf]))
                ctr.inc(s_s, 1)
                s.activation(yy, psf[:, :], AF.Identity,
                             bias=lnt[:, 29:30], scale=lnt[:, 28:29]).then_inc(s_s, 1)
            S2 = ctr.get(s_s)

            @blk.vector
            def _(v, yy=yy, xx=xx, S2=S2):
                v.wait_ge(s_s, S2)
                ctr.inc(s_v, 1)
                v.tensor_tensor(yy, yy, xx, ALU.add).then_inc(s_v, 1)
            YV = ctr.get(s_v)

            @blk.gpsimd
            def _(g, tb=tb, yy=yy, YV=YV, buf=buf):
                g.wait_ge(s_v, YV)
                g.wait_ge(stp[buf], ctr.get(stp[buf]))
                g.dma_start(out=y_out[tb * 128:(tb + 1) * 128, :], in_=yy).then_inc(stp[buf], 16)
                ctr.inc(stp[buf], 16)

        @blk.sync
        def _(q):
            q.wait_ge(s_sta, ctr.get(s_sta))
            q.wait_ge(s_stb, ctr.get(s_stb))


def host_prep(inputs):
    x = np.ascontiguousarray(np.asarray(inputs["L0_attn_x"], np.float32))
    w_qkv = np.asarray(inputs["L0_attn_w_qkv"], np.float32)
    w_gate = np.asarray(inputs["L0_attn_w_gate"], np.float32)
    w_beta = np.asarray(inputs["L0_attn_w_beta"], np.float32)
    w_alpha = np.asarray(inputs["L0_attn_w_alpha"], np.float32)
    log_A = np.asarray(inputs["L0_attn_log_A"], np.float32)
    dt_bias = np.asarray(inputs["L0_attn_dt_bias"], np.float32)
    conv_w = np.asarray(inputs["L0_attn_conv_w"], np.float32)
    rms_w = np.asarray(inputs["L0_attn_rms_w"], np.float32)
    w_out = np.asarray(inputs["L0_attn_w_out"], np.float32)
    W_g = np.asarray(inputs["L0_ffn_W_g"], np.float32)
    W_u = np.asarray(inputs["L0_ffn_W_u"], np.float32)
    W_d = np.asarray(inputs["L0_ffn_W_d"], np.float32)

    wqkv_bl = np.ascontiguousarray(w_qkv.T.reshape(D, NFQ, 128).transpose(1, 0, 2))
    wgate_bl = np.ascontiguousarray(w_gate.T.reshape(D, NFG, 128).transpose(1, 0, 2))
    wba = np.ascontiguousarray(np.concatenate([w_beta, w_alpha], axis=0).T)
    convw = np.ascontiguousarray(
        conv_w[:, 0, :].reshape(NFQ, 128, 4).transpose(1, 0, 2).reshape(128, NFQ * 4))
    la = np.ascontiguousarray(np.stack([log_A, dt_bias], axis=1))
    rmsw = np.ascontiguousarray(rms_w[:, None])
    wout_bl = np.ascontiguousarray(w_out.T.reshape(D, 4, 512).transpose(1, 0, 2))
    wg_bl = np.ascontiguousarray(W_g.T.reshape(D, NHB, 128).transpose(1, 0, 2))
    wu_bl = np.ascontiguousarray(W_u.T.reshape(D, NHB, 128).transpose(1, 0, 2))
    wd_bl = np.ascontiguousarray(W_d.T.reshape(FFH, 16, 128).transpose(1, 0, 2))

    ident_bf = np.eye(128, dtype=NPBF16)
    ident_f = np.eye(128, dtype=np.float32)
    r_idx, c_idx = np.meshgrid(np.arange(128), np.arange(128), indexing="ij")
    maskLT = np.where(c_idx < r_idx, 0.0, NEG).astype(np.float32)
    maskUT = np.where(r_idx < c_idx, 0.0, NEG).astype(np.float32)
    maskUD = np.where(r_idx <= c_idx, -LSD, NEG).astype(np.float32)
    ones_col = np.ones((128, 1), dtype=NPBF16)
    ones_row_f = np.ones((1, 128), dtype=np.float32)
    zeros_f = np.zeros((128, 128), dtype=np.float32)
    ccols = np.zeros((128, 8), np.float32)
    ccols[:, 0] = EPS_L2
    ccols[:, 1] = EPS_RMS
    ccols[:, 2] = -LSD
    ccols[:, 3] = 2.0
    ccols[:, 4] = EPS_LN
    ccols[:, 5] = EPS_FFN
    ccols[:, 6] = 127.0
    ccols[:, 7] = 1.0

    in_maps = []
    for core in range(8):
        b, g = core // 4, core % 4
        t0 = g * TPC
        xs = np.zeros((TP, D), np.float32)
        lo = max(0, t0 - HALO)
        xs[HALO - (t0 - lo): HALO - (t0 - lo) + (t0 + TPC - lo)] = x[b, lo:t0 + TPC]
        hop_m = np.zeros((128, 7), np.float32)
        hop_imi = np.zeros((128, 7 * 128), np.float32)
        for j in range(7):
            use = 1.0 if (j // 4 == b and j % 4 < g) else 0.0
            hop_m[:, j] = use
            hop_imi[:, j * 128:(j + 1) * 128] = (1.0 - use) * np.eye(128, dtype=np.float32)
        in_maps.append({
            "x": xs, "wqkv_b": wqkv_bl, "wgate_b": wgate_bl, "wba": wba,
            "convw": convw, "loga_params": la, "rmsw": rmsw,
            "wout_b": wout_bl, "wg_b": wg_bl, "wu_b": wu_bl, "wd_b": wd_bl,
            "ident_bf": ident_bf, "ident_f": ident_f, "maskLT": maskLT,
            "maskUT": maskUT, "maskUD": maskUD,
            "ones_col_bf": ones_col, "ones_row_f": ones_row_f, "ccols": ccols,
            "ones_row_bf": np.ones((1, 128), dtype=NPBF16), "zeros_f": zeros_f,
            "hop_mcol": hop_m, "hop_imi": hop_imi,
        })
    return in_maps


def kernel(**inputs):
    in_maps = host_prep(inputs)
    nc, _ = build()
    res = run_bass_kernel_spmd(nc, in_maps, core_ids=list(range(8)))
    y = np.empty((2, 4096, D), np.float32)
    for core in range(8):
        b, g = core // 4, core % 4
        y[b, g * TPC:(g + 1) * TPC] = res.results[core]["y"]
    return y


# version 06
# speedup vs baseline: 1.9717x; 1.9717x over previous
import numpy as np
import ml_dtypes
import concourse.bass as bass
import concourse.mybir as mybir
from concourse.bass_utils import run_bass_kernel_spmd
from concourse import library_config

F32 = mybir.dt.float32
BF16 = mybir.dt.bfloat16
I32 = mybir.dt.int32
AF = mybir.ActivationFunctionType
ALU = mybir.AluOpType
AX = mybir.AxisListType
NPBF16 = ml_dtypes.bfloat16

D = 2048
TPC = 1024
HALO = 3
TP = 1032
TG = 344
C = 128
NCH = 8
HV, HQ, DK, DV = 16, 8, 128, 128
NDB = 16
NFQ, NFG = 32, 16
NFB = NFQ + NFG + 1
FFH = 8192
NHB = 64
NEG = -1e30
EPS_LN = 1e-5
EPS_L2 = 1e-6
EPS_RMS = 1e-6
EPS_FFN = 1e-8
LSD = float(np.log(np.sqrt(DK)))
NEWTON_ITERS = 6


class Ctr:
    def __init__(self):
        self.v = {}

    def inc(self, sem, k=1):
        self.v[sem.name] = self.v.get(sem.name, 0) + k
        return self.v[sem.name]

    def get(self, sem):
        return self.v.get(sem.name, 0)


def build(probe=None, upto="all"):
    nc = bass.Bass("TRN2", target_bir_lowering=False, debug=False)

    x_in = nc.dram_tensor("x", [TP, D], F32, kind="ExternalInput").ap()
    wqkv_b = nc.dram_tensor("wqkv_b", [NFQ, D, 128], F32, kind="ExternalInput").ap()
    wgate_b = nc.dram_tensor("wgate_b", [NFG, D, 128], F32, kind="ExternalInput").ap()
    wba_in = nc.dram_tensor("wba", [D, 32], F32, kind="ExternalInput").ap()
    convw_in = nc.dram_tensor("convw", [128, NFQ * 4], F32, kind="ExternalInput").ap()
    la_in = nc.dram_tensor("loga_params", [HV, 2], F32, kind="ExternalInput").ap()
    rmsw_in = nc.dram_tensor("rmsw", [128, 1], F32, kind="ExternalInput").ap()
    wout_b = nc.dram_tensor("wout_b", [4, D, 512], F32, kind="ExternalInput").ap()
    wg_b = nc.dram_tensor("wg_b", [NHB, D, 128], F32, kind="ExternalInput").ap()
    wu_b = nc.dram_tensor("wu_b", [NHB, D, 128], F32, kind="ExternalInput").ap()
    wd_b = nc.dram_tensor("wd_b", [16, FFH, 128], F32, kind="ExternalInput").ap()
    ident_in = nc.dram_tensor("ident_bf", [128, 128], BF16, kind="ExternalInput").ap()
    identf_in = nc.dram_tensor("ident_f", [128, 128], F32, kind="ExternalInput").ap()
    maskLT_in = nc.dram_tensor("maskLT", [128, 128], F32, kind="ExternalInput").ap()
    maskUT_in = nc.dram_tensor("maskUT", [128, 128], F32, kind="ExternalInput").ap()
    maskUD_in = nc.dram_tensor("maskUD", [128, 128], F32, kind="ExternalInput").ap()
    onesc_in = nc.dram_tensor("ones_col_bf", [128, 1], BF16, kind="ExternalInput").ap()
    onesrf_in = nc.dram_tensor("ones_row_f", [1, 128], F32, kind="ExternalInput").ap()
    onesr_in = nc.dram_tensor("ones_row_bf", [1, 128], BF16, kind="ExternalInput").ap()
    zeros_in = nc.dram_tensor("zeros_f", [128, 128], F32, kind="ExternalInput").ap()
    ccols_in = nc.dram_tensor("ccols", [128, 8], F32, kind="ExternalInput").ap()
    hopm_in = nc.dram_tensor("hop_mcol", [128, 7], F32, kind="ExternalInput").ap()
    hopimi_in = nc.dram_tensor("hop_imi", [128, 7 * 128], F32, kind="ExternalInput").ap()

    y_out = nc.dram_tensor("y", [TPC, D], F32, kind="ExternalOutput").ap()

    PROBE_SHAPES = {
        "pre": ([NFB, 128, TP], BF16),
        "qkv": ([4, 1024, TPC], BF16),
        "rows": ([HV, 4 * TPC], F32),
        "spill": ([NCH, 4, 128, HV * 128], BF16),
        "comp": ([2, 128, HV * 128], BF16),
        "state": ([128, HV * 128], BF16),
        "pf": ([NCH, 128, NDB * 128], BF16),
        "x2": ([TPC, D], F32),
        "xq": ([128, NDB * TPC], BF16),
        "p2": ([128, NHB * TPC], BF16),
        "ffn": ([NDB, 128, TPC], BF16),
    }
    probe_out = None
    if probe is not None:
        shp, dt_ = PROBE_SHAPES[probe]
        probe_out = nc.dram_tensor("probe", shp, dt_, kind="ExternalOutput").ap()

    pre_scr = nc.dram_tensor("pre_scr", [NFB, 128, TP], BF16, kind="Internal").ap()
    spill = nc.dram_tensor("spill", [NCH, 4, 128, HV * 128], BF16, kind="Internal").ap()
    p_scr = nc.dram_tensor("p_scr", [NCH, 128, NDB * 128], BF16, kind="Internal").ap()
    x2_scr = nc.dram_tensor("x2_scr", [TPC, D], F32, kind="Internal").ap()
    ffn_scr = nc.dram_tensor("ffn_scr", [NDB, 128, TPC], BF16, kind="Internal").ap()
    cc_in = nc.dram_tensor("cc_in", [128, HV * 2 * 128], BF16, kind="Internal").ap()
    cc_out = nc.dram_tensor("cc_out", [8 * 128, HV * 2 * 128], BF16, kind="Internal").ap()

    ctr = Ctr()
    from contextlib import ExitStack

    with ExitStack() as top:
        E = top.enter_context
        s_ld = E(nc.semaphore("s_ld"))
        s_lda = E(nc.semaphore("s_lda"))
        s_ldb = E(nc.semaphore("s_ldb"))
        s_st = E(nc.semaphore("s_st"))
        s_sta = E(nc.semaphore("s_sta"))
        s_stb = E(nc.semaphore("s_stb"))
        s_ga = E(nc.semaphore("s_ga"))
        s_gb = E(nc.semaphore("s_gb"))
        s_mm = E(nc.semaphore("s_mm"))
        s_v = E(nc.semaphore("s_v"))
        s_s = E(nc.semaphore("s_s"))
        s_g = E(nc.semaphore("s_g"))
        s_cc = E(nc.semaphore("s_cc"))
        sems = dict(ld=s_ld, lda=s_lda, ldb=s_ldb, st=s_st, sta=s_sta, stb=s_stb,
                    ga=s_ga, gb=s_gb, mm=s_mm, v=s_v, s=s_s, g=s_g, cc=s_cc)

        c_ident = E(nc.sbuf_tensor("c_ident", [128, 128], BF16))
        c_identf = E(nc.sbuf_tensor("c_identf", [128, 128], F32))
        c_maskLT = E(nc.sbuf_tensor("c_maskLT", [128, 128], F32))
        c_maskUT = E(nc.sbuf_tensor("c_maskUT", [128, 128], F32))
        c_maskUD = E(nc.sbuf_tensor("c_maskUD", [128, 128], F32))
        c_onec = E(nc.sbuf_tensor("c_onec", [128, 1], BF16))
        c_onerf = E(nc.sbuf_tensor("c_onerf", [1, 128], F32))
        c_oner = E(nc.sbuf_tensor("c_oner", [1, 128], BF16))
        c_zero = E(nc.sbuf_tensor("c_zero", [128, 128], F32))
        c_cols = E(nc.sbuf_tensor("c_cols", [128, 8], F32))
        c_hopm = E(nc.sbuf_tensor("c_hopm", [128, 7], F32))
        c_hopimi = E(nc.sbuf_tensor("c_hopimi", [128, 7 * 128], F32))
        c_rmsw = E(nc.sbuf_tensor("c_rmsw", [128, 1], F32))
        c_convw = E(nc.sbuf_tensor("c_convw", [128, NFQ * 4], F32))
        c_la = E(nc.sbuf_tensor("c_la", [HV, 8], F32))

        blk = E(nc.Block())

        @blk.sync
        def _(q):
            for dst, src in [
                (c_ident[:, :], ident_in[:, :]), (c_identf[:, :], identf_in[:, :]),
                (c_maskLT[:, :], maskLT_in[:, :]), (c_maskUT[:, :], maskUT_in[:, :]),
                (c_maskUD[:, :], maskUD_in[:, :]), (c_onec[:, :], onesc_in[:, :]),
                (c_onerf[:, :], onesrf_in[:, :]), (c_oner[:, :], onesr_in[:, :]),
                (c_zero[:, :], zeros_in[:, :]), (c_cols[:, :], ccols_in[:, :]),
                (c_hopm[:, :], hopm_in[:, :]),
                (c_hopimi[:, :], hopimi_in[:, :]),
                (c_rmsw[:, :], rmsw_in[:, :]),
                (c_convw[:, :], convw_in[:, :]),
                (c_la[:, 0:2], la_in[:, :]),
            ]:
                q.dma_start(out=dst, in_=src).then_inc(s_ld, 16)
                ctr.inc(s_ld, 16)
        CONSTS = ctr.get(s_ld)

        @blk.scalar
        def _(s):
            s.wait_ge(s_ld, CONSTS)
            ctr.inc(s_s, 1)
            s.activation(c_la[:, 4:5], c_la[:, 0:1], AF.Exp).then_inc(s_s, 1)

        @blk.vector
        def _(v):
            v.wait_ge(s_s, ctr.get(s_s))
            v.tensor_scalar(c_la[:, 3:4], c_la[:, 4:5], -1.0, None, ALU.mult)
            ctr.inc(s_v, 1)
            v.tensor_scalar(c_la[:, 5:6], c_la[:, 1:2], -1.0, None, ALU.mult).then_inc(s_v, 1)

        with ExitStack() as sc2:
            E2 = sc2.enter_context
            qk_f = E2(nc.sbuf_tensor("qk_f", [128, 16 * TPC], BF16))
            k_tok = E2(nc.sbuf_tensor("k_tok", [128, HQ * TPC], BF16))
            v_tok = E2(nc.sbuf_tensor("v_tok", [128, HV * TPC], BF16))
            barows = E2(nc.sbuf_tensor("barows", [HV, 4 * TPC], F32))
            state = E2(nc.sbuf_tensor("state", [128, HV * 128], BF16))
            compP = E2(nc.sbuf_tensor("compP", [128, HV * 128], BF16))
            compR = E2(nc.sbuf_tensor("compR", [128, HV * 128], BF16))

            _p1(nc, blk, ctr, sems, locals())
            if probe == "pre":
                _probe_dump(nc, blk, ctr, sems, probe_out[0:NFB - 1, :, :], pre_scr[0:NFB - 1, :, :])
                _probe_dump(nc, blk, ctr, sems, probe_out[NFB - 1, 0:32, :], pre_scr[NFB - 1, 0:32, :])
                _finish_zero(nc, blk, ctr, sems, y_out)
                return nc, probe

            _p15(nc, blk, ctr, sems, locals())
            if probe == "qkv":
                @blk.sync
                def _(q):
                    q.wait_ge(s_v, ctr.get(s_v))
                    q.wait_ge(s_s, ctr.get(s_s))
                    for i, (src, off) in enumerate([(qk_f, 0), (qk_f, 8 * TPC), (k_tok, 0), (v_tok, 0)]):
                        q.dma_start(
                            out=probe_out[i].rearrange("(hb p) t -> p hb t", p=128)[:, 0:8, :],
                            in_=src.ap()[:, off:off + 8 * TPC].rearrange("p (hb t) -> p hb t", t=TPC),
                        ).then_inc(s_ld, 16)
                        ctr.inc(s_ld, 16)
                    q.wait_ge(s_ld, ctr.get(s_ld))
                _finish_zero(nc, blk, ctr, sems, y_out)
                return nc, probe
            if probe == "rows":
                @blk.sync
                def _(q):
                    q.wait_ge(s_v, ctr.get(s_v))
                    q.wait_ge(s_s, ctr.get(s_s))
                    q.dma_start(out=probe_out[:, :], in_=barows[:, :]).then_inc(s_ld, 16)
                    ctr.inc(s_ld, 16)
                    q.wait_ge(s_ld, ctr.get(s_ld))
                _finish_zero(nc, blk, ctr, sems, y_out)
                return nc, probe

            _p2a(nc, blk, ctr, sems, locals())
            if probe == "spill":
                _probe_dump(nc, blk, ctr, sems, probe_out[:, :, :, :], spill[:, :, :, :])
                _finish_zero(nc, blk, ctr, sems, y_out)
                return nc, probe
            if probe == "comp":
                @blk.sync
                def _(q):
                    q.wait_ge(s_v, ctr.get(s_v))
                    q.wait_ge(s_s, ctr.get(s_s))
                    q.dma_start(out=probe_out[0], in_=compP[:, :]).then_inc(s_ld, 16)
                    q.dma_start(out=probe_out[1], in_=compR[:, :]).then_inc(s_ld, 16)
                    ctr.inc(s_ld, 32)
                    q.wait_ge(s_ld, ctr.get(s_ld))
                _finish_zero(nc, blk, ctr, sems, y_out)
                return nc, probe

            _cc_hop(nc, blk, ctr, sems, locals())
            if probe == "state":
                @blk.sync
                def _(q):
                    q.wait_ge(s_v, ctr.get(s_v))
                    q.dma_start(out=probe_out[:, :], in_=state[:, :]).then_inc(s_ld, 16)
                    ctr.inc(s_ld, 16)
                    q.wait_ge(s_ld, ctr.get(s_ld))
                _finish_zero(nc, blk, ctr, sems, y_out)
                return nc, probe

            _p2b(nc, blk, ctr, sems, locals())
            if probe == "pf":
                _probe_dump(nc, blk, ctr, sems, probe_out[:, :, :], p_scr[:, :, :])
                _finish_zero(nc, blk, ctr, sems, y_out)
                return nc, probe

        with ExitStack() as sc3:
            E3 = sc3.enter_context
            xq_f = E3(nc.sbuf_tensor("xq_f", [128, NDB * TPC], BF16))

            _p3(nc, blk, ctr, sems, locals())
            if probe in ("x2", "xq"):
                if probe == "x2":
                    _probe_dump(nc, blk, ctr, sems, probe_out[:, :], x2_scr[:, :])
                else:
                    @blk.sync
                    def _(q):
                        q.wait_ge(s_s, ctr.get(s_s))
                        q.dma_start(out=probe_out[:, :], in_=xq_f[:, :]).then_inc(s_ld, 16)
                        ctr.inc(s_ld, 16)
                        q.wait_ge(s_ld, ctr.get(s_ld))
                _finish_zero(nc, blk, ctr, sems, y_out)
                return nc, probe

            with ExitStack() as sff:
                EF = sff.enter_context
                p2 = EF(nc.sbuf_tensor("p2", [128, NHB * TPC], BF16))
                _ffn_gu(nc, blk, ctr, sems, locals())
                if probe == "p2":
                    @blk.sync
                    def _(q):
                        q.wait_ge(s_v, ctr.get(s_v))
                        q.dma_start(out=probe_out[:, :], in_=p2[:, :]).then_inc(s_ld, 16)
                        ctr.inc(s_ld, 16)
                        q.wait_ge(s_ld, ctr.get(s_ld))
                    _finish_zero(nc, blk, ctr, sems, y_out)
                    return nc, probe
                _ffn_down(nc, blk, ctr, sems, locals())
            if probe == "ffn":
                _probe_dump(nc, blk, ctr, sems, probe_out[:, :, :], ffn_scr[:, :, :])
                _finish_zero(nc, blk, ctr, sems, y_out)
                return nc, probe

        _final(nc, blk, ctr, sems, locals())
    return nc, probe


def _phase_barrier(nc, blk, ctr, sems):
    names = ("ld", "lda", "ldb", "st", "sta", "stb", "ga", "gb", "mm", "v", "s", "g", "cc")
    vals = {nm: ctr.get(sems[nm]) for nm in names}
    for eng in ("sync", "tensor", "vector", "scalar", "gpsimd"):
        @getattr(blk, eng)
        def _(e, eng=eng):
            for nm in names:
                e.wait_ge(sems[nm], vals[nm])


def _probe_dump(nc, blk, ctr, sems, dst_ap, src_ap):
    @blk.sync
    def _(q):
        for nm in ("st", "sta", "stb", "v", "s", "g", "mm"):
            q.wait_ge(sems[nm], ctr.get(sems[nm]))
        q.dma_start(out=dst_ap, in_=src_ap).then_inc(sems["ld"], 16)
        ctr.inc(sems["ld"], 16)
        q.wait_ge(sems["ld"], ctr.get(sems["ld"]))


def _finish_zero(nc, blk, ctr, sems, y_out):
    s_st, s_g = sems["st"], sems["g"]
    _phase_barrier(nc, blk, ctr, sems)
    with nc.sbuf_tensor("yzero", [128, D], F32) as yz:
        @blk.gpsimd
        def _(g):
            ctr.inc(s_g, 1)
            g.memset(yz[:], 0.0).then_inc(s_g, 1)
        GV = ctr.get(s_g)

        @blk.sync
        def _(q):
            q.wait_ge(s_g, GV)
            for tb in range(TPC // 128):
                q.dma_start(out=y_out[tb * 128:(tb + 1) * 128, :], in_=yz[:]).then_inc(s_st, 16)
                ctr.inc(s_st, 16)
            q.wait_ge(s_st, ctr.get(s_st))


def _p1(nc, blk, ctr, sems, env):
    s_ld, s_st, s_mm, s_v, s_s = sems["ld"], sems["st"], sems["mm"], sems["v"], sems["s"]
    s_lda, s_ldb, s_sta, s_stb = sems["lda"], sems["ldb"], sems["sta"], sems["stb"]
    ldp = [s_lda, s_ldb]
    stp = [s_sta, s_stb]
    x_in, c_ident = env["x_in"], env["c_ident"]
    pre_scr = env["pre_scr"]
    wqkv_b, wgate_b, wba_in = env["wqkv_b"], env["wgate_b"], env["wba_in"]
    CONSTS = env["CONSTS"]
    from contextlib import ExitStack

    with ExitStack() as st1:
        E1 = st1.enter_context
        xtile = E1(nc.sbuf_tensor("xtile", [128, 2 * D], F32))
        xn_t = E1(nc.sbuf_tensor("xn_t", [128, D], BF16))
        xn_f = E1(nc.sbuf_tensor("xn_f", [128, NDB * TP], BF16))
        lnt = E1(nc.sbuf_tensor("lnt", [128, 32], F32))
        ln_stack = ExitStack()
        ps_ln0 = ln_stack.enter_context(nc.psum_tensor("ps_ln0", [128, D], F32))
        ps_ln1 = ln_stack.enter_context(nc.psum_tensor("ps_ln1", [128, D], F32))

        for tb in range(9):
            rows = 128 if tb < 8 else TP - 8 * 128
            xt = xtile[:rows, (tb % 2) * D:(tb % 2) * D + D]
            psl = ps_ln0 if tb % 2 == 0 else ps_ln1

            sl = ldp[tb % 2]

            @blk.sync
            def _(q, tb=tb, rows=rows, xt=xt, sl=sl):
                if tb >= 2:
                    q.wait_ge(s_s, ctr.get(s_s))
                q.wait_ge(sl, ctr.get(sl))
                q.dma_start(out=xt, in_=x_in[tb * 128: tb * 128 + rows, :]).then_inc(sl, 16)
                ctr.inc(sl, 16)
            XW = ctr.get(sl)

            @blk.vector
            def _(v, rows=rows, xt=xt, XW=XW, sl=sl):
                v.wait_ge(sl, XW)
                ins = None
                for i4 in range(4):
                    ins = v.bn_stats(lnt[:rows, i4 * 6:(i4 + 1) * 6], xt[:, i4 * 512:(i4 + 1) * 512])
                ctr.inc(s_v, 1); ins.then_inc(s_v, 1); v.wait_ge(s_v, ctr.get(s_v))
                ins = v.bn_aggr(lnt[:rows, 24:26], lnt[:rows, 0:24])
                ctr.inc(s_v, 1); ins.then_inc(s_v, 1); v.wait_ge(s_v, ctr.get(s_v))
                ctr.inc(s_v, 1)
                v.tensor_scalar(lnt[:rows, 26:27], lnt[:rows, 25:26], EPS_LN, None, ALU.add).then_inc(s_v, 1)
            V1 = ctr.get(s_v)

            @blk.scalar
            def _(s, rows=rows, V1=V1):
                s.wait_ge(s_v, V1)
                ctr.inc(s_s, 1)
                s.activation(lnt[:rows, 27:28], lnt[:rows, 26:27], AF.Sqrt).then_inc(s_s, 1)
            S1 = ctr.get(s_s)

            @blk.vector
            def _(v, rows=rows, S1=S1):
                v.wait_ge(s_s, S1)
                ins = v.reciprocal(lnt[:rows, 28:29], lnt[:rows, 27:28])
                ctr.inc(s_v, 1); ins.then_inc(s_v, 1); v.wait_ge(s_v, ctr.get(s_v))
                ctr.inc(s_v, 1)
                v.scalar_tensor_tensor(out=lnt[:rows, 29:30], in0=lnt[:rows, 24:25],
                                       scalar=-1.0, in1=lnt[:rows, 28:29],
                                       op0=ALU.mult, op1=ALU.mult).then_inc(s_v, 1)
            V2 = ctr.get(s_v)

            @blk.scalar
            def _(s, tb=tb, rows=rows, xt=xt, V2=V2):
                s.wait_ge(s_v, V2)
                if tb >= 1:
                    s.wait_ge(s_mm, ctr.get(s_mm))
                ctr.inc(s_s, 1)
                s.activation(xn_t[:rows, :], xt, AF.Identity,
                             bias=lnt[:rows, 29:30], scale=lnt[:rows, 28:29]).then_inc(s_s, 1)
            S2 = ctr.get(s_s)

            @blk.tensor
            def _(t, tb=tb, rows=rows, S2=S2, psl=psl):
                t.wait_ge(s_s, S2)
                t.wait_ge(s_ld, CONSTS)
                if tb >= 2:
                    t.wait_ge(s_s, ctr.get(s_s))
                mm = None
                for db in range(NDB):
                    mm = t.matmul(psl[:, db * 128:db * 128 + rows],
                                  xn_t[:rows, db * 128:(db + 1) * 128],
                                  c_ident[:rows, :rows], start=True, stop=True)
                ctr.inc(s_mm, 1)
                mm.then_inc(s_mm, 1)
            M1 = ctr.get(s_mm)

            @blk.scalar
            def _(s, tb=tb, rows=rows, M1=M1, psl=psl):
                s.wait_ge(s_mm, M1)
                ctr.inc(s_s, 1)
                s.copy(
                    xn_f.ap().rearrange("p (db c) -> p db c", db=NDB)[:, :, tb * 128:tb * 128 + rows],
                    psl.ap().rearrange("p (db c) -> p db c", db=NDB)[:, :, :rows],
                ).then_inc(s_s, 1)
        XNF = ctr.get(s_s)
        ln_stack.close()

        wf32 = E1(nc.sbuf_tensor("wf32", [128, 2 * NDB * 128], F32))
        wbf = E1(nc.sbuf_tensor("wbf", [128, 2 * NDB * 128], BF16))
        pre_sb = E1(nc.sbuf_tensor("pre_sb", [128, 2 * TP], BF16))
        ps_pa = [E1(nc.psum_tensor(f"ps_pa{i}", [128, TG], F32)) for i in range(6)]

        def wsrc(fb):
            if fb < NFQ:
                return wqkv_b[fb]
            if fb < NFQ + NFG:
                return wgate_b[fb - NFQ]
            return wba_in

        for fb in range(NFB):
            buf = fb % 2
            fcols = 128 if fb < NFB - 1 else 32
            wsl = wf32[:, buf * NDB * 128:][:, :NDB * fcols]
            wbl = wbf[:, buf * NDB * 128:][:, :NDB * fcols]
            psl = ps_pa[buf * 3:(buf + 1) * 3]
            prl = pre_sb[:fcols, buf * TP:buf * TP + TP]

            sl = ldp[fb % 2]

            @blk.sync
            def _(q, fb=fb, wsl=wsl, fcols=fcols, sl=sl):
                if fb == 0:
                    q.wait_ge(s_lda, ctr.get(s_lda))
                    q.wait_ge(s_ldb, ctr.get(s_ldb))
                if fb >= 2:
                    q.wait_ge(s_v, ctr.get(s_v))
                q.wait_ge(sl, ctr.get(sl))
                q.dma_start(out=wsl.rearrange("p (db f) -> p db f", f=fcols),
                            in_=wsrc(fb).rearrange("(db p) f -> p db f", p=128)).then_inc(sl, 16)
                ctr.inc(sl, 16)
            WL = ctr.get(sl)

            @blk.vector
            def _(v, fb=fb, wsl=wsl, wbl=wbl, WL=WL, sl=sl):
                v.wait_ge(sl, WL)
                if fb >= 2:
                    v.wait_ge(s_mm, ctr.get(s_mm))
                ctr.inc(s_v, 1)
                v.tensor_copy(wbl, wsl).then_inc(s_v, 1)
            WC = ctr.get(s_v)

            @blk.tensor
            def _(t, fb=fb, wbl=wbl, psl=psl, fcols=fcols, WC=WC):
                t.wait_ge(s_v, WC)
                t.wait_ge(s_s, XNF)
                if fb >= 2:
                    t.wait_ge(s_s, ctr.get(s_s))
                mm = None
                for tg in range(3):
                    for db in range(NDB):
                        mm = t.matmul(psl[tg][:fcols, :],
                                      wbl.rearrange("p (db f) -> p db f", db=NDB)[:, db, :],
                                      xn_f.ap().rearrange("p (db c) -> p db c", db=NDB)[:, db, tg * TG:(tg + 1) * TG],
                                      start=(db == 0), stop=(db == NDB - 1))
                ctr.inc(s_mm, 1)
                mm.then_inc(s_mm, 1)
            PM = ctr.get(s_mm)

            @blk.scalar
            def _(s, fb=fb, psl=psl, prl=prl, fcols=fcols, PM=PM):
                s.wait_ge(s_mm, PM)
                if fb >= 2:
                    s.wait_ge(stp[fb % 2], ctr.get(stp[fb % 2]))
                cp = None
                for tg in range(3):
                    cp = s.copy(prl[:, tg * TG:(tg + 1) * TG], psl[tg][:fcols, :])
                ctr.inc(s_s, 1)
                cp.then_inc(s_s, 1)
            PE = ctr.get(s_s)

            @blk.gpsimd
            def _(g, fb=fb, prl=prl, fcols=fcols, PE=PE):
                g.wait_ge(s_s, PE)
                g.wait_ge(stp[fb % 2], ctr.get(stp[fb % 2]))
                g.dma_start(out=pre_scr[fb, :fcols, :], in_=prl).then_inc(stp[fb % 2], 16)
                ctr.inc(stp[fb % 2], 16)


def _p15(nc, blk, ctr, sems, env):
    s_ld, s_st, s_mm, s_v, s_s = sems["ld"], sems["st"], sems["mm"], sems["v"], sems["s"]
    s_lda, s_ldb, s_sta, s_stb = sems["lda"], sems["ldb"], sems["sta"], sems["stb"]
    ldp = [s_lda, s_ldb]
    pre_scr, c_convw, c_ident, c_onec, c_onerf = (
        env["pre_scr"], env["c_convw"], env["c_ident"], env["c_onec"], env["c_onerf"])
    qk_f, k_tok, v_tok, barows, c_la, c_zero, c_cols = (
        env["qk_f"], env["k_tok"], env["v_tok"], env["barows"], env["c_la"],
        env["c_zero"], env["c_cols"])
    P1_STA = ctr.get(s_sta)
    P1_STB = ctr.get(s_stb)
    from contextlib import ExitStack

    with ExitStack() as st15:
        E1 = st15.enter_context
        pre_t = E1(nc.sbuf_tensor("pre_t", [128, 2 * TP], BF16))
        cv = E1(nc.sbuf_tensor("cv", [128, TPC], F32))
        sil = E1(nc.sbuf_tensor("sil", [128, TPC], F32))
        sqb = E1(nc.sbuf_tensor("sqb", [128, TPC], BF16))
        rs_row = E1(nc.sbuf_tensor("rs_row", [1, TPC], F32))
        rs_f32 = E1(nc.sbuf_tensor("rs_f32", [1, TPC], F32))
        vtile = E1(nc.sbuf_tensor("vtile", [128, TPC], BF16))
        ps_l2 = E1(nc.psum_tensor("ps_l2", [1, TPC], F32))
        ps_bc = E1(nc.psum_tensor("ps_bc", [128, TPC], F32))
        ps_tr = E1(nc.psum_tensor("ps_tr", [128, TPC], F32))

        for fb in range(NFQ):
            buf = fb % 2
            pt = pre_t[:, buf * TP:buf * TP + TP]
            is_q = fb < 8
            is_k = 8 <= fb < 16
            head = fb if is_q else (fb - 8 if is_k else fb - 16)

            sl = ldp[fb % 2]

            @blk.sync
            def _(q, fb=fb, pt=pt, sl=sl):
                if fb == 0:
                    q.wait_ge(s_sta, P1_STA)
                    q.wait_ge(s_stb, P1_STB)
                    q.wait_ge(s_lda, ctr.get(s_lda))
                    q.wait_ge(s_ldb, ctr.get(s_ldb))
                if fb >= 2:
                    q.wait_ge(s_v, ctr.get(s_v))
                q.wait_ge(sl, ctr.get(sl))
                q.dma_start(out=pt, in_=pre_scr[fb, :, :]).then_inc(sl, 16)
                ctr.inc(sl, 16)
            LD = ctr.get(sl)

            @blk.vector
            def _(v, fb=fb, pt=pt, LD=LD, sl=sl):
                v.wait_ge(sl, LD)
                if fb > 0:
                    v.wait_ge(s_s, ctr.get(s_s))
                ins = v.tensor_scalar(cv[:, :], pt[:, 0:TPC], c_convw[:, fb * 4:fb * 4 + 1], None, ALU.mult)
                for j in range(1, 4):
                    ctr.inc(s_v, 1); ins.then_inc(s_v, 1); v.wait_ge(s_v, ctr.get(s_v))
                    ins = v.scalar_tensor_tensor(
                        out=cv[:, :], in0=pt[:, j:j + TPC],
                        scalar=c_convw[:, fb * 4 + j:fb * 4 + j + 1], in1=cv[:, :],
                        op0=ALU.mult, op1=ALU.add)
                ctr.inc(s_v, 1)
                ins.then_inc(s_v, 1)
            CV = ctr.get(s_v)

            if is_q or is_k:
                @blk.scalar
                def _(s, CV=CV, fb=fb):
                    s.wait_ge(s_v, CV)
                    if fb >= 2:
                        s.wait_ge(s_v, ctr.get(s_v))
                    ins = s.activation(sil[:, :], cv[:, :], AF.Silu)
                    ctr.inc(s_s, 1); ins.then_inc(s_s, 1); s.wait_ge(s_s, ctr.get(s_s))
                    ctr.inc(s_s, 1)
                    s.activation(sqb[:, :], sil[:, :], AF.Square).then_inc(s_s, 1)
                SQ = ctr.get(s_s)

                @blk.tensor
                def _(t, SQ=SQ, fb=fb):
                    t.wait_ge(s_s, SQ)
                    if fb >= 1:
                        t.wait_ge(s_s, ctr.get(s_s))
                    mm = None
                    for half in range(2):
                        mm = t.matmul(ps_l2[0:1, half * 512:(half + 1) * 512], c_onec[:, :],
                                      sqb[:, half * 512:(half + 1) * 512], start=True, stop=True)
                    ctr.inc(s_mm, 1)
                    mm.then_inc(s_mm, 1)
                MMS = ctr.get(s_mm)

                @blk.scalar
                def _(s, MMS=MMS):
                    s.wait_ge(s_mm, MMS)
                    ctr.inc(s_s, 1)
                    s.activation(rs_f32[0:1, :], ps_l2[0:1, :], AF.Sqrt, bias=c_cols[0:1, 0:1]).then_inc(s_s, 1)
                RS = ctr.get(s_s)

                @blk.vector
                def _(v, RS=RS):
                    v.wait_ge(s_s, RS)
                    ctr.inc(s_v, 1)
                    v.reciprocal(rs_row[0:1, :], rs_f32[0:1, :]).then_inc(s_v, 1)
                RR = ctr.get(s_v)

                @blk.tensor
                def _(t, RR=RR, fb=fb):
                    t.wait_ge(s_v, RR)
                    if fb >= 1:
                        t.wait_ge(s_v, ctr.get(s_v))
                    mm = None
                    for half in range(2):
                        mm = t.matmul(ps_bc[:, half * 512:(half + 1) * 512],
                                      c_onerf[0:1, :],
                                      rs_row[0:1, half * 512:(half + 1) * 512], start=True, stop=True)
                    ctr.inc(s_mm, 1)
                    mm.then_inc(s_mm, 1)
                BC = ctr.get(s_mm)

                @blk.vector
                def _(v, fb=fb, head=head, is_q=is_q, BC=BC):
                    v.wait_ge(s_mm, BC)
                    dst = qk_f.ap()[:, (head + (0 if is_q else 8)) * TPC:][:, :TPC]
                    ctr.inc(s_v, 1)
                    v.tensor_tensor(dst, sil[:, :], ps_bc[:, :], ALU.mult).then_inc(s_v, 1)
                QN = ctr.get(s_v)

                if is_k:
                    @blk.tensor
                    def _(t, head=head, QN=QN, fb=fb):
                        t.wait_ge(s_v, QN)
                        if fb >= 9:
                            t.wait_ge(s_s, ctr.get(s_s))
                        src = qk_f.ap()[:, (head + 8) * TPC:][:, :TPC]
                        mm = None
                        for ci in range(NCH):
                            mm = t.matmul(ps_tr[:, ci * 128:(ci + 1) * 128],
                                          src[:, ci * 128:(ci + 1) * 128],
                                          c_ident[:, :], start=True, stop=True)
                        ctr.inc(s_mm, 1)
                        mm.then_inc(s_mm, 1)
                    KT = ctr.get(s_mm)

                    @blk.scalar
                    def _(s, head=head, KT=KT):
                        s.wait_ge(s_mm, KT)
                        ctr.inc(s_s, 1)
                        s.copy(k_tok.ap().rearrange("p (ci q s) -> p ci q s", q=HQ, s=128)[:, :, head, :],
                               ps_tr.ap().rearrange("p (ci s) -> p ci s", s=128)).then_inc(s_s, 1)
            else:
                @blk.scalar
                def _(s, CV=CV, fb=fb):
                    s.wait_ge(s_v, CV)
                    if fb >= 17:
                        s.wait_ge(s_mm, ctr.get(s_mm))
                    ctr.inc(s_s, 1)
                    s.activation(vtile[:, :], cv[:, :], AF.Silu).then_inc(s_s, 1)
                VS = ctr.get(s_s)

                @blk.tensor
                def _(t, head=head, VS=VS, fb=fb):
                    t.wait_ge(s_s, VS)
                    if fb >= 17:
                        t.wait_ge(s_s, ctr.get(s_s))
                    mm = None
                    for ci in range(NCH):
                        mm = t.matmul(ps_tr[:, ci * 128:(ci + 1) * 128],
                                      vtile[:, ci * 128:(ci + 1) * 128],
                                      c_ident[:, :], start=True, stop=True)
                    ctr.inc(s_mm, 1)
                    mm.then_inc(s_mm, 1)
                VT = ctr.get(s_mm)

                @blk.scalar
                def _(s, head=head, VT=VT):
                    s.wait_ge(s_mm, VT)
                    ctr.inc(s_s, 1)
                    s.copy(v_tok.ap().rearrange("p (ci hh s) -> p ci hh s", hh=HV, s=128)[:, :, head, :],
                           ps_tr.ap().rearrange("p (ci s) -> p ci s", s=128)).then_inc(s_s, 1)

        ba_t = E1(nc.sbuf_tensor("ba_t", [16, TPC], BF16))
        aa_t = E1(nc.sbuf_tensor("aa_t", [16, TPC], BF16))
        tmp16 = E1(nc.sbuf_tensor("tmp16", [HV, TPC], F32))

        @blk.sync
        def _(q):
            q.wait_ge(s_lda, ctr.get(s_lda))
            q.wait_ge(s_ldb, ctr.get(s_ldb))
            q.dma_start(out=ba_t[:, :], in_=pre_scr[NFB - 1, 0:16, HALO:HALO + TPC]).then_inc(s_lda, 16)
            q.dma_start(out=aa_t[:, :], in_=pre_scr[NFB - 1, 16:32, HALO:HALO + TPC]).then_inc(s_lda, 16)
            ctr.inc(s_lda, 32)
        BA = ctr.get(s_lda)

        @blk.scalar
        def _(s, BA=BA):
            s.wait_ge(s_lda, BA)
            s.wait_ge(s_v, ctr.get(s_v))
            ins = s.activation(barows[:, 0:TPC], ba_t[:, :], AF.Sigmoid)
            ctr.inc(s_s, 1); ins.then_inc(s_s, 1); s.wait_ge(s_s, ctr.get(s_s))
            s.activation(barows[:, TPC:2 * TPC], barows[:, 0:TPC], AF.Ln)
            ins = s.activation(tmp16[:, :], aa_t[:, :], AF.Sigmoid, scale=-1.0, bias=c_la[:, 5:6])
            ctr.inc(s_s, 1); ins.then_inc(s_s, 1); s.wait_ge(s_s, ctr.get(s_s))
            ctr.inc(s_s, 1)
            s.activation(tmp16[:, :], tmp16[:, :], AF.Ln).then_inc(s_s, 1)
        SB = ctr.get(s_s)

        @blk.vector
        def _(v, SB=SB):
            v.wait_ge(s_s, SB)
            ins = v.tensor_scalar(barows[:, 2 * TPC:3 * TPC], tmp16[:, :], c_la[:, 4:5], None, ALU.mult)
            ctr.inc(s_v, 1); ins.then_inc(s_v, 1); v.wait_ge(s_v, ctr.get(s_v))
            ins = None
            for ci in range(NCH):
                ins = v.tensor_tensor_scan(
                    out=barows[:, 3 * TPC + ci * 128:3 * TPC + (ci + 1) * 128],
                    data0=barows[:, 2 * TPC + ci * 128:2 * TPC + (ci + 1) * 128],
                    data1=barows[:, 2 * TPC + ci * 128:2 * TPC + (ci + 1) * 128],
                    initial=c_zero[0:HV, 0:1],
                    op0=ALU.add, op1=ALU.bypass)
            ctr.inc(s_v, 1)
            ins.then_inc(s_v, 1)


def _p2a(nc, blk, ctr, sems, env):
    s_mm, s_v, s_s, s_g = sems["mm"], sems["v"], sems["s"], sems["g"]
    s_sta, s_stb = sems["sta"], sems["stb"]
    stp = [s_sta, s_stb]
    qk_f, k_tok, v_tok, barows = env["qk_f"], env["k_tok"], env["v_tok"], env["barows"]
    compP, compR = env["compP"], env["compR"]
    c_ident, c_identf, c_onerf = env["c_ident"], env["c_identf"], env["c_onerf"]
    c_maskLT, c_maskUT = env["c_maskLT"], env["c_maskUT"]
    spill = env["spill"]
    from contextlib import ExitStack

    W = HV * 128

    def vx(ap3):
        return ap3

    with ExitStack() as sta_:
        E1 = sta_.enter_context
        mrow_all = E1(nc.sbuf_tensor("mrow_all", [HV, 128], F32))
        cols = E1(nc.sbuf_tensor("colsA", [128, 64], F32))
        rowstall = E1(nc.sbuf_tensor("rowstall", [1, 2 * W], F32))
        e_t = E1(nc.sbuf_tensor("e_t_all", [128, W], F32))
        e_tt = E1(nc.sbuf_tensor("e_tt_all", [128, W], F32))
        X = E1(nc.sbuf_tensor("X_all", [128, W], BF16))
        Xt = E1(nc.sbuf_tensor("Xt_all", [128, W], BF16))
        NT = E1(nc.sbuf_tensor("NT_all", [128, W], BF16))
        Hb = E1(nc.sbuf_tensor("Hb_all", [128, W], BF16))
        G_sb = E1(nc.sbuf_tensor("G_all", [128, HQ * 128], BF16))
        bVt = E1(nc.sbuf_tensor("bVt_all", [128, W], BF16))
        bAKt = E1(nc.sbuf_tensor("bAKt_all", [128, W], BF16))
        kht = E1(nc.sbuf_tensor("kht_all", [128, W], BF16))
        U0 = E1(nc.sbuf_tensor("U0_all", [128, W], BF16))
        Yc = E1(nc.sbuf_tensor("Yc_all", [128, W], BF16))
        Rsb = E1(nc.sbuf_tensor("Rsb_all", [128, W], BF16))
        PT = E1(nc.sbuf_tensor("PT_all", [128, W], BF16))
        acb = E1(nc.sbuf_tensor("acbA", [128, 48], F32))
        ps_A = E1(nc.psum_tensor("ps_A", [128, W], F32))
        ps_B = E1(nc.psum_tensor("ps_B", [128, W], F32))

        ROWS_V = ctr.get(s_v)

        def gview(ap, wslice=128):
            return ap.rearrange("p (q s) -> p q s", s=128).unsqueeze(2).broadcast_to((128, HQ, 2, wslice))

        def colview(ap2):
            return ap2.unsqueeze(2).broadcast_to((128, HV, 128))

        def mview(ap2):
            return ap2.unsqueeze(1).broadcast_to((128, HV, 128))

        def w3(ap):
            return ap.rearrange("p (h s) -> p h s", s=128)

        for ci in range(NCH):
            ell_r = barows[:, 3 * TPC + ci * 128:3 * TPC + (ci + 1) * 128]
            lb_r = barows[:, TPC + ci * 128:TPC + (ci + 1) * 128]
            beta_r = barows[:, ci * 128:(ci + 1) * 128]
            kslab = qk_f
            kf_chunk = lambda hq: qk_f.ap()[:, (8 + hq) * TPC + ci * 128:][:, :128]
            ktok_chunk = k_tok.ap()[:, ci * HQ * 128:(ci + 1) * HQ * 128]
            vtok_chunk = v_tok.ap()[:, ci * W:(ci + 1) * W]

            @blk.vector
            def _(v, ell_r=ell_r, lb_r=lb_r, ci=ci):
                v.wait_ge(s_v, ROWS_V)
                if ci > 0:
                    v.wait_ge(s_mm, ctr.get(s_mm))
                ctr.inc(s_v, 1)
                v.tensor_tensor(mrow_all[:, :], ell_r, lb_r, ALU.add).then_inc(s_v, 1)
            MR = ctr.get(s_v)

            @blk.tensor
            def _(t, ell_r=ell_r, beta_r=beta_r, MR=MR, ci=ci):
                t.wait_ge(s_v, MR)
                if ci > 0:
                    t.wait_ge(s_s, ctr.get(s_s))
                    t.wait_ge(s_g, ctr.get(s_g))
                t.matmul(ps_A[:, 0:16], ell_r, c_identf[0:HV, 0:HV], start=True, stop=True)
                t.matmul(ps_A[:, 16:32], mrow_all[:, :], c_identf[0:HV, 0:HV], start=True, stop=True)
                ctr.inc(s_mm, 1)
                t.matmul(ps_A[:, 32:48], beta_r, c_identf[0:HV, 0:HV], start=True, stop=True).then_inc(s_mm, 1)
            CT = ctr.get(s_mm)

            @blk.scalar
            def _(s, CT=CT):
                s.wait_ge(s_mm, CT)
                ctr.inc(s_s, 1)
                s.copy(cols[:, 0:48], ps_A[:, 0:48]).then_inc(s_s, 1)
            COLS = ctr.get(s_s)

            @blk.tensor
            def _(t, COLS=COLS, ci=ci):
                t.wait_ge(s_s, COLS)
                if ci > 0:
                    t.wait_ge(s_v, ctr.get(s_v))
                mm = None
                for h in range(HV):
                    mm = t.matmul(ps_A[0:1, h * 128:(h + 1) * 128], cols[:, h:h + 1],
                                  c_identf[:, :], start=True, stop=True)
                    mm = t.matmul(ps_B[0:1, h * 128:(h + 1) * 128], cols[:, 16 + h:16 + h + 1],
                                  c_identf[:, :], start=True, stop=True)
                ctr.inc(s_mm, 1)
                mm.then_inc(s_mm, 1)
            RWM = ctr.get(s_mm)

            @blk.scalar
            def _(s, RWM=RWM):
                s.wait_ge(s_mm, RWM)
                s.copy(rowstall[0:1, 0:W], ps_A[0:1, 0:W])
                ctr.inc(s_s, 1)
                s.copy(rowstall[0:1, W:2 * W], ps_B[0:1, 0:W]).then_inc(s_s, 1)
            RWS = ctr.get(s_s)

            @blk.tensor
            def _(t, RWS=RWS):
                t.wait_ge(s_s, RWS)
                mm = None
                for qq in range(4):
                    sl = slice(qq * 512, (qq + 1) * 512)
                    t.matmul(ps_A[:, sl], c_onerf[0:1, :], rowstall[0:1, sl], start=True, stop=True)
                    mm = t.matmul(ps_B[:, sl], c_onerf[0:1, :], rowstall[0:1, W + qq * 512:W + (qq + 1) * 512], start=True, stop=True)
                ctr.inc(s_mm, 1)
                mm.then_inc(s_mm, 1)
            BC = ctr.get(s_mm)

            @blk.vector
            def _(v, BC=BC):
                v.wait_ge(s_mm, BC)
                ins = v.tensor_tensor(w3(e_t.ap()), colview(cols[:, 16:32]), w3(ps_A.ap()[:, 0:W]), ALU.subtract)
                ins = v.tensor_tensor(w3(e_tt.ap()), w3(ps_B.ap()[:, 0:W]), colview(cols[:, 0:16]), ALU.subtract)
                ins = v.tensor_tensor(acb[:, 16:32], ps_A.ap()[:, 127::128], cols[:, 0:16], ALU.subtract)
                ins = v.tensor_copy(acb[:, 0:16], ps_A.ap()[:, 127::128])
                ctr.inc(s_v, 1); ins.then_inc(s_v, 1); v.wait_ge(s_v, ctr.get(s_v))
                v.tensor_tensor(w3(e_t.ap()), w3(e_t.ap()), mview(c_maskLT.ap()), ALU.add)
                ctr.inc(s_v, 1)
                v.tensor_tensor(w3(e_tt.ap()), w3(e_tt.ap()), mview(c_maskUT.ap()), ALU.add).then_inc(s_v, 1)
            EA = ctr.get(s_v)

            @blk.scalar
            def _(s, EA=EA):
                s.wait_ge(s_v, EA)
                s.activation(e_t[:, :], e_t[:, :], AF.Exp)
                s.activation(e_tt[:, :], e_tt[:, :], AF.Exp)
                s.activation(acb[:, 32:48], cols[:, 16:32], AF.Exp)
                ins = s.activation(acb[:, 16:32], acb[:, 16:32], AF.Exp)
                ctr.inc(s_s, 1); ins.then_inc(s_s, 1); s.wait_ge(s_s, ctr.get(s_s))
                ctr.inc(s_s, 1)
                s.activation(acb[:, 0:16], acb[:, 0:16], AF.Exp).then_inc(s_s, 1)
            EX = ctr.get(s_s)

            @blk.tensor
            def _(t, kf_chunk=kf_chunk, EX=EX, ci=ci):
                t.wait_ge(s_s, EX)
                if ci > 0:
                    t.wait_ge(s_v, ctr.get(s_v))
                mm = None
                for hq in range(HQ):
                    mm = t.matmul(ps_A[:, hq * 128:(hq + 1) * 128], kf_chunk(hq), kf_chunk(hq),
                                  start=True, stop=True)
                ctr.inc(s_mm, 1)
                mm.then_inc(s_mm, 1)
            GM = ctr.get(s_mm)

            @blk.scalar
            def _(s, GM=GM):
                s.wait_ge(s_mm, GM)
                ctr.inc(s_s, 1)
                s.copy(G_sb[:, :], ps_A[:, 0:HQ * 128]).then_inc(s_s, 1)
            GE = ctr.get(s_s)

            @blk.vector
            def _(v, GE=GE):
                v.wait_ge(s_s, GE)
                ins = v.tensor_tensor(w3(e_t.ap()).rearrange("p (q r) s -> p q r s", r=2),
                                      gview(G_sb.ap()), w3(e_t.ap()).rearrange("p (q r) s -> p q r s", r=2),
                                      ALU.mult)
                ins = v.tensor_tensor(w3(e_tt.ap()).rearrange("p (q r) s -> p q r s", r=2),
                                      gview(G_sb.ap()), w3(e_tt.ap()).rearrange("p (q r) s -> p q r s", r=2),
                                      ALU.mult)
                ctr.inc(s_v, 1); ins.then_inc(s_v, 1); v.wait_ge(s_v, ctr.get(s_v))
                v.tensor_tensor(w3(X.ap()), mview(c_identf.ap()), w3(e_t.ap()), ALU.subtract)
                v.tensor_tensor(w3(Xt.ap()), mview(c_identf.ap()), w3(e_tt.ap()), ALU.subtract)
                ctr.inc(s_v, 1)
                v.tensor_tensor(w3(NT.ap()), mview(c_identf.ap()), w3(e_tt.ap()), ALU.add).then_inc(s_v, 1)
            TX = ctr.get(s_v)

            last_v = TX
            last_s = ctr.get(s_s)
            for it in range(NEWTON_ITERS):
                @blk.tensor
                def _(t, it=it, last_v=last_v, last_s=last_s):
                    t.wait_ge(s_v, last_v)
                    t.wait_ge(s_s, last_s)
                    mm = None
                    for h in range(HV):
                        mm = t.matmul(ps_A[:, h * 128:(h + 1) * 128], NT[:, h * 128:(h + 1) * 128],
                                      X[:, h * 128:(h + 1) * 128], start=True, stop=True)
                    ctr.inc(s_mm, 1)
                    mm.then_inc(s_mm, 1)
                HM = ctr.get(s_mm)

                @blk.scalar
                def _(s, HM=HM):
                    s.wait_ge(s_mm, HM)
                    ctr.inc(s_s, 1)
                    s.copy(Hb[:, :], ps_A[:, :]).then_inc(s_s, 1)
                HE = ctr.get(s_s)

                @blk.tensor
                def _(t, HE=HE):
                    t.wait_ge(s_s, HE)
                    mm = None
                    for h in range(HV):
                        mm = t.matmul(ps_B[:, h * 128:(h + 1) * 128], Xt[:, h * 128:(h + 1) * 128],
                                      Hb[:, h * 128:(h + 1) * 128], start=True, stop=True)
                    ctr.inc(s_mm, 1)
                    mm.then_inc(s_mm, 1)
                XHM = ctr.get(s_mm)

                @blk.vector
                def _(v, XHM=XHM):
                    v.wait_ge(s_mm, XHM)
                    ctr.inc(s_v, 1)
                    v.scalar_tensor_tensor(out=X[:, :], in0=X[:, :], scalar=2.0,
                                           in1=ps_B[:, :], op0=ALU.mult,
                                           op1=ALU.subtract).then_inc(s_v, 1)
                XN = ctr.get(s_v)

                @blk.tensor
                def _(t, XN=XN):
                    t.wait_ge(s_v, XN)
                    mm = None
                    for h in range(HV):
                        mm = t.matmul(ps_A[:, h * 128:(h + 1) * 128], X[:, h * 128:(h + 1) * 128],
                                      c_ident[:, :], start=True, stop=True)
                    ctr.inc(s_mm, 1)
                    mm.then_inc(s_mm, 1)
                XTM = ctr.get(s_mm)

                @blk.scalar
                def _(s, XTM=XTM):
                    s.wait_ge(s_mm, XTM)
                    ctr.inc(s_s, 1)
                    s.copy(Xt[:, :], ps_A[:, :]).then_inc(s_s, 1)
                last_s = ctr.get(s_s)
                last_v = XN

            @blk.vector
            def _(v, vtok_chunk=vtok_chunk, ktok_chunk=ktok_chunk, last_s=last_s):
                v.wait_ge(s_s, last_s)
                v.tensor_tensor(w3(bVt.ap()), w3(vtok_chunk), colview(cols[:, 32:48]), ALU.mult)
                v.tensor_tensor(w3(bAKt.ap()).rearrange("p (q r) s -> p q r s", r=2),
                                gview(ktok_chunk), colview(acb[:, 32:48]).rearrange("p (q r) s -> p q r s", r=2),
                                ALU.mult)
                ctr.inc(s_v, 1)
                v.tensor_tensor(w3(kht.ap()).rearrange("p (q r) s -> p q r s", r=2),
                                gview(ktok_chunk), colview(acb[:, 16:32]).rearrange("p (q r) s -> p q r s", r=2),
                                ALU.mult).then_inc(s_v, 1)
            OPS = ctr.get(s_v)

            @blk.tensor
            def _(t, OPS=OPS):
                t.wait_ge(s_v, OPS)
                mm = None
                for h in range(HV):
                    sl = slice(h * 128, (h + 1) * 128)
                    t.matmul(ps_A[:, sl], Xt[:, sl], bVt[:, sl], start=True, stop=True)
                    mm = t.matmul(ps_B[:, sl], Xt[:, sl], bAKt[:, sl], start=True, stop=True)
                ctr.inc(s_mm, 1)
                mm.then_inc(s_mm, 1)
            SOL = ctr.get(s_mm)

            @blk.scalar
            def _(s, SOL=SOL):
                s.wait_ge(s_mm, SOL)
                s.wait_ge(s_sta, ctr.get(s_sta))
                s.wait_ge(s_stb, ctr.get(s_stb))
                s.copy(U0[:, :], ps_A[:, :])
                ctr.inc(s_s, 1)
                s.copy(Yc[:, :], ps_B[:, :]).then_inc(s_s, 1)
            UY = ctr.get(s_s)

            @blk.gpsimd
            def _(g, ci=ci, UY=UY):
                g.wait_ge(s_s, UY)
                g.wait_ge(stp[ci % 2], ctr.get(stp[ci % 2]))
                g.dma_start(out=spill[ci, 0], in_=U0[:, :]).then_inc(stp[ci % 2], 16)
                g.dma_start(out=spill[ci, 1], in_=Yc[:, :]).then_inc(stp[ci % 2], 16)
                ctr.inc(stp[ci % 2], 32)

            @blk.tensor
            def _(t, UY=UY):
                t.wait_ge(s_s, UY)
                mm = None
                for h in range(HV):
                    sl = slice(h * 128, (h + 1) * 128)
                    t.matmul(ps_A[:, sl], kht[:, sl], U0[:, sl], start=True, stop=True)
                    mm = t.matmul(ps_B[:, sl], Yc[:, sl], kht[:, sl], start=True, stop=True)
                ctr.inc(s_mm, 1)
                mm.then_inc(s_mm, 1)
            RP = ctr.get(s_mm)

            @blk.vector
            def _(v, RP=RP):
                v.wait_ge(s_mm, RP)
                v.wait_ge(s_sta, ctr.get(s_sta))
                v.wait_ge(s_stb, ctr.get(s_stb))
                v.tensor_copy(Rsb[:, :], ps_A[:, :])
                ins = v.tensor_tensor(w3(e_t.ap()), mview(c_identf.ap()), colview(acb[:, 0:16]), ALU.mult)
                ctr.inc(s_v, 1); ins.then_inc(s_v, 1); v.wait_ge(s_v, ctr.get(s_v))
                ctr.inc(s_v, 1)
                v.tensor_tensor(PT[:, :], e_t[:, :], ps_B[:, :], ALU.subtract).then_inc(s_v, 1)
            PRV = ctr.get(s_v)

            @blk.gpsimd
            def _(g, ci=ci, PRV=PRV):
                g.wait_ge(s_v, PRV)
                g.wait_ge(stp[ci % 2], ctr.get(stp[ci % 2]))
                g.dma_start(out=spill[ci, 2], in_=PT[:, :]).then_inc(stp[ci % 2], 16)
                g.dma_start(out=spill[ci, 3], in_=Rsb[:, :]).then_inc(stp[ci % 2], 16)
                ctr.inc(stp[ci % 2], 32)

            if ci == 0:
                @blk.vector
                def _(v, PRV=PRV):
                    v.wait_ge(s_v, PRV)
                    v.tensor_copy(compP.ap()[:, :], PT[:, :])
                    ctr.inc(s_v, 1)
                    v.tensor_copy(compR.ap()[:, :], Rsb[:, :]).then_inc(s_v, 1)
            else:
                @blk.tensor
                def _(t, PRV=PRV):
                    t.wait_ge(s_v, PRV)
                    mm = None
                    for h in range(HV):
                        sl = slice(h * 128, (h + 1) * 128)
                        t.matmul(ps_A[:, sl], PT[:, sl], compP.ap()[:, sl], start=True, stop=True)
                        mm = t.matmul(ps_B[:, sl], PT[:, sl], compR.ap()[:, sl], start=True, stop=True)
                    ctr.inc(s_mm, 1)
                    mm.then_inc(s_mm, 1)
                CM = ctr.get(s_mm)

                @blk.vector
                def _(v, CM=CM):
                    v.wait_ge(s_mm, CM)
                    v.tensor_copy(compP.ap()[:, :], ps_A[:, :])
                    ctr.inc(s_v, 1)
                    v.tensor_tensor(compR.ap()[:, :], ps_B[:, :], Rsb[:, :], ALU.add).then_inc(s_v, 1)


def _cc_hop(nc, blk, ctr, sems, env):
    s_ld, s_st, s_mm, s_v, s_s, s_g, s_cc = (
        sems["ld"], sems["st"], sems["mm"], sems["v"], sems["s"], sems["g"], sems["cc"])
    s_lda, s_ldb = sems["lda"], sems["ldb"]
    ldp = [s_lda, s_ldb]
    compP, compR, state = env["compP"], env["compR"], env["state"]
    c_ident, c_hopm, c_hopimi = env["c_ident"], env["c_hopm"], env["c_hopimi"]
    cc_in, cc_out = env["cc_in"], env["cc_out"]
    from contextlib import ExitStack

    _phase_barrier(nc, blk, ctr, sems)
    with ExitStack() as stc:
        E1 = stc.enter_context
        cPt = E1(nc.sbuf_tensor("cPt", [128, HV * 128], BF16))
        hopbuf = E1(nc.sbuf_tensor("hopbuf", [128, 2 * HV * 2 * 128], BF16))
        Pm = E1(nc.sbuf_tensor("Pm", [128, 128], BF16))
        ps_a = E1(nc.psum_tensor("ps_hop", [128, 128], F32))
        CPD = ctr.get(s_v)

        for h in range(HV):
            @blk.tensor
            def _(t, h=h, CPD=CPD):
                t.wait_ge(s_v, CPD)
                if h > 0:
                    t.wait_ge(s_s, ctr.get(s_s))
                ctr.inc(s_mm, 1)
                t.matmul(ps_a[:, :], compP.ap()[:, h * 128:(h + 1) * 128],
                         c_ident[:, :], start=True, stop=True).then_inc(s_mm, 1)
            TM = ctr.get(s_mm)

            @blk.scalar
            def _(s, h=h, TM=TM):
                s.wait_ge(s_mm, TM)
                ctr.inc(s_s, 1)
                s.copy(cPt[:, h * 128:(h + 1) * 128], ps_a[:, :]).then_inc(s_s, 1)
        CPT = ctr.get(s_s)

        @blk.sync
        def _(q, CPT=CPT):
            q.wait_ge(s_s, CPT)
            q.wait_ge(s_lda, ctr.get(s_lda))
            q.wait_ge(s_ldb, ctr.get(s_ldb))
            q.dma_start(out=cc_in[:, 0:HV * 128], in_=cPt[:, :]).then_inc(s_lda, 16)
            q.dma_start(out=cc_in[:, HV * 128:], in_=compR[:, :]).then_inc(s_lda, 16)
            ctr.inc(s_lda, 32)
        CCI = ctr.get(s_lda)

        @blk.gpsimd
        def _(g, CCI=CCI):
            g.wait_ge(s_lda, CCI)
            ctr.inc(s_cc, 1)
            g.collective_compute(
                "AllGather", ALU.bypass,
                ins=[cc_in[:, :]], outs=[cc_out[:, :]],
                replica_groups=[list(range(8))],
            ).then_inc(s_cc)
            ctr.inc(s_g, 1)
            g.memset(state[:, :], 0.0).then_inc(s_g, 1)
        STZ = ctr.get(s_g)

        for j in range(7):
            sl = ldp[j % 2]

            @blk.sync
            def _(q, j=j, sl=sl):
                q.wait_ge(s_cc, ctr.get(s_cc))
                if j >= 2:
                    q.wait_ge(s_v, ctr.get(s_v))
                q.wait_ge(sl, ctr.get(sl))
                q.dma_start(out=hopbuf[:, (j % 2) * HV * 256:(j % 2 + 1) * HV * 256],
                            in_=cc_out[j * 128:(j + 1) * 128, :]).then_inc(sl, 16)
                ctr.inc(sl, 16)
            HL = ctr.get(sl)

            for h in range(HV):
                pj = hopbuf[:, (j % 2) * HV * 256 + h * 128:][:, :128]
                rj = hopbuf[:, (j % 2) * HV * 256 + HV * 128 + h * 128:][:, :128]
                sh = state.ap()[:, h * 128:(h + 1) * 128]

                @blk.vector
                def _(v, pj=pj, HL=HL, j=j, h=h, STZ=STZ, sl=sl):
                    v.wait_ge(sl, HL)
                    v.wait_ge(s_g, STZ)
                    if j > 0 or h > 0:
                        v.wait_ge(s_mm, ctr.get(s_mm))
                    ctr.inc(s_v, 1)
                    v.scalar_tensor_tensor(out=Pm[:, :], in0=pj,
                                           scalar=c_hopm[:, j:j + 1],
                                           in1=c_hopimi[:, j * 128:(j + 1) * 128],
                                           op0=ALU.mult, op1=ALU.add).then_inc(s_v, 1)
                PMV = ctr.get(s_v)

                @blk.tensor
                def _(t, sh=sh, PMV=PMV):
                    t.wait_ge(s_v, PMV)
                    ctr.inc(s_mm, 1)
                    t.matmul(ps_a[:, :], Pm[:, :], sh, start=True, stop=True).then_inc(s_mm, 1)
                SMM = ctr.get(s_mm)

                @blk.vector
                def _(v, rj=rj, sh=sh, j=j, SMM=SMM):
                    v.wait_ge(s_mm, SMM)
                    ctr.inc(s_v, 1)
                    v.scalar_tensor_tensor(out=sh, in0=rj, scalar=c_hopm[:, j:j + 1],
                                           in1=ps_a[:, :], op0=ALU.mult,
                                           op1=ALU.add).then_inc(s_v, 1)


def _p2b(nc, blk, ctr, sems, env):
    s_mm, s_v, s_s, s_g = sems["mm"], sems["v"], sems["s"], sems["g"]
    s_lda, s_ldb, s_sta, s_stb = sems["lda"], sems["ldb"], sems["sta"], sems["stb"]
    ldp = [s_lda, s_ldb]
    stp = [s_sta, s_stb]
    qk_f, barows, state = env["qk_f"], env["barows"], env["state"]
    c_ident, c_identf, c_maskUD, c_rmsw, c_onerf = (
        env["c_ident"], env["c_identf"], env["c_maskUD"], env["c_rmsw"], env["c_onerf"])
    spill, pre_scr, p_scr = env["spill"], env["pre_scr"], env["p_scr"]
    from contextlib import ExitStack

    W = HV * 128

    _phase_barrier(nc, blk, ctr, sems)
    with ExitStack() as stb_:
        E1 = stb_.enter_context
        cols = E1(nc.sbuf_tensor("colsB", [128, 16], F32))
        rowstall = E1(nc.sbuf_tensor("rowstallB", [1, W], F32))
        e_m = E1(nc.sbuf_tensor("e_mB", [128, W], F32))
        qa = E1(nc.sbuf_tensor("qaB", [128, W], BF16))
        MT = E1(nc.sbuf_tensor("MTB", [128, W], BF16))
        qAf = E1(nc.sbuf_tensor("qAfB", [128, W], BF16))
        nMYT = E1(nc.sbuf_tensor("nMYTB", [128, W], BF16))
        U0 = E1(nc.sbuf_tensor("U0B", [128, W], BF16))
        Yc = E1(nc.sbuf_tensor("YcB", [128, W], BF16))
        PTl = E1(nc.sbuf_tensor("PTlB", [128, W], BF16))
        Rl = E1(nc.sbuf_tensor("RlB", [128, W], BF16))
        gt = E1(nc.sbuf_tensor("gtB", [128, 2 * W], BF16))
        Ofp = E1(nc.sbuf_tensor("OfpB", [128, W], F32))
        Onsb = E1(nc.sbuf_tensor("OnsbB", [128, W], BF16))
        gsil = E1(nc.sbuf_tensor("gsilB", [128, W], BF16))
        ptile = E1(nc.sbuf_tensor("ptileB", [128, 2 * W], BF16))
        rcol = E1(nc.sbuf_tensor("rcolB", [128, 16], F32))
        ps_A = E1(nc.psum_tensor("ps_AB", [128, W], F32))
        ps_B = E1(nc.psum_tensor("ps_BB", [128, W], F32))

        def colview(ap2):
            return ap2.unsqueeze(2).broadcast_to((128, HV, 128))

        def mview(ap2):
            return ap2.unsqueeze(1).broadcast_to((128, HV, 128))

        def gview(ap):
            return ap.rearrange("p (q s) -> p q s", s=128).unsqueeze(2).broadcast_to((128, HQ, 2, 128))

        def w3(ap):
            return ap.rearrange("p (h s) -> p h s", s=128)

        for ci in range(NCH):
            buf = ci % 2
            ell_r = barows[:, 3 * TPC + ci * 128:3 * TPC + (ci + 1) * 128]
            kf_chunk = lambda hq, ci=ci: qk_f.ap()[:, (8 + hq) * TPC + ci * 128:][:, :128]
            qf_chunk = lambda hq, ci=ci: qk_f.ap()[:, hq * TPC + ci * 128:][:, :128]
            u0 = U0[:, 0:W]
            yc = Yc[:, 0:W]
            ptl = PTl[:, 0:W]
            rtl = Rl[:, 0:W]
            gtl = gt[:, buf * W:buf * W + W]
            pt = ptile[:, buf * W:buf * W + W]

            @blk.sync
            def _(q, ci=ci, buf=buf, u0=u0, yc=yc, ptl=ptl, rtl=rtl, gtl=gtl):
                if ci < 2:
                    q.wait_ge(s_sta, ctr.get(s_sta))
                    q.wait_ge(s_stb, ctr.get(s_stb))
                q.wait_ge(s_mm, ctr.get(s_mm))
                q.wait_ge(s_s, ctr.get(s_s))
                q.wait_ge(s_v, ctr.get(s_v))
                sl = ldp[buf]
                q.wait_ge(sl, ctr.get(sl))
                q.dma_start(out=u0, in_=spill[ci, 0]).then_inc(sl, 16)
                q.dma_start(out=yc, in_=spill[ci, 1]).then_inc(sl, 16)
                q.dma_start(out=ptl, in_=spill[ci, 2]).then_inc(sl, 16)
                q.dma_start(out=rtl, in_=spill[ci, 3]).then_inc(sl, 16)
                q.dma_start(out=gtl.rearrange("p (h s) -> p h s", s=128),
                            in_=pre_scr[NFQ:NFQ + HV, :, HALO + ci * 128:HALO + (ci + 1) * 128].rearrange("h p s -> p h s")).then_inc(sl, 16)
                ctr.inc(ldp[buf], 80)
            LDS = ctr.get(ldp[buf])

            @blk.tensor
            def _(t, ell_r=ell_r, ci=ci):
                t.wait_ge(s_s, ctr.get(s_s))
                t.wait_ge(s_v, ctr.get(s_v))
                t.wait_ge(s_g, ctr.get(s_g))
                ctr.inc(s_mm, 1)
                t.matmul(ps_A[:, 0:16], ell_r, c_identf[0:HV, 0:HV], start=True, stop=True).then_inc(s_mm, 1)
            CT = ctr.get(s_mm)

            @blk.scalar
            def _(s, CT=CT):
                s.wait_ge(s_mm, CT)
                ctr.inc(s_s, 1)
                s.copy(cols[:, 0:16], ps_A[:, 0:16]).then_inc(s_s, 1)
            COLS = ctr.get(s_s)

            @blk.tensor
            def _(t, COLS=COLS):
                t.wait_ge(s_s, COLS)
                mm = None
                for h in range(HV):
                    mm = t.matmul(ps_B[0:1, h * 128:(h + 1) * 128], cols[:, h:h + 1],
                                  c_identf[:, :], start=True, stop=True)
                ctr.inc(s_mm, 1)
                mm.then_inc(s_mm, 1)
            RWM = ctr.get(s_mm)

            @blk.scalar
            def _(s, RWM=RWM):
                s.wait_ge(s_mm, RWM)
                ctr.inc(s_s, 1)
                s.copy(rowstall[0:1, 0:W], ps_B[0:1, 0:W]).then_inc(s_s, 1)
            RWS = ctr.get(s_s)

            @blk.tensor
            def _(t, RWS=RWS):
                t.wait_ge(s_s, RWS)
                mm = None
                for qq in range(4):
                    sl = slice(qq * 512, (qq + 1) * 512)
                    mm = t.matmul(ps_A[:, sl], c_onerf[0:1, :], rowstall[0:1, sl], start=True, stop=True)
                ctr.inc(s_mm, 1)
                mm.then_inc(s_mm, 1)
            BC = ctr.get(s_mm)

            @blk.vector
            def _(v, BC=BC):
                v.wait_ge(s_mm, BC)
                ins = v.tensor_tensor(w3(e_m.ap()), w3(ps_A.ap()[:, 0:W]), colview(cols[:, 0:16]), ALU.subtract)
                ctr.inc(s_v, 1); ins.then_inc(s_v, 1); v.wait_ge(s_v, ctr.get(s_v))
                ctr.inc(s_v, 1)
                v.tensor_tensor(w3(e_m.ap()), w3(e_m.ap()), mview(c_maskUD.ap()), ALU.add).then_inc(s_v, 1)
            EA = ctr.get(s_v)

            @blk.scalar
            def _(s, EA=EA):
                s.wait_ge(s_v, EA)
                ins = s.activation(qa[:, :], ps_A[:, 0:W], AF.Exp, bias=env["c_cols"][:, 2:3])
                ctr.inc(s_s, 1); ins.then_inc(s_s, 1); s.wait_ge(s_s, ctr.get(s_s))
                ctr.inc(s_s, 1)
                s.activation(e_m[:, :], e_m[:, :], AF.Exp).then_inc(s_s, 1)
            EX = ctr.get(s_s)

            @blk.tensor
            def _(t, kf_chunk=kf_chunk, qf_chunk=qf_chunk, EX=EX, ci=ci):
                t.wait_ge(s_s, EX)
                mm = None
                for hq in range(HQ):
                    mm = t.matmul(ps_A[:, hq * 128:(hq + 1) * 128], kf_chunk(hq), qf_chunk(hq),
                                  start=True, stop=True)
                ctr.inc(s_mm, 1)
                mm.then_inc(s_mm, 1)
            GM = ctr.get(s_mm)

            @blk.vector
            def _(v, GM=GM, ci=ci):
                v.wait_ge(s_mm, GM)
                ins = v.tensor_tensor(
                    w3(MT.ap()).rearrange("p (q r) s -> p q r s", r=2),
                    ps_A.ap()[:, 0:HQ * 128].rearrange("p (q s) -> p q s", s=128).unsqueeze(2).broadcast_to((128, HQ, 2, 128)),
                    w3(e_m.ap()).rearrange("p (q r) s -> p q r s", r=2), ALU.mult)
                qsl = qk_f.ap()[:, ci * 128:]
                ctr.inc(s_v, 1)
                v.tensor_tensor(
                    w3(qAf.ap()).rearrange("p (q r) s -> p q r s", r=2),
                    _qview_impl(qk_f, ci),
                    w3(qa.ap()).rearrange("p (q r) s -> p q r s", r=2), ALU.mult).then_inc(s_v, 1)
            MTV = ctr.get(s_v)

            @blk.tensor
            def _(t, MTV=MTV, yc=yc, LDS=LDS, buf=buf):
                t.wait_ge(s_v, MTV)
                t.wait_ge(ldp[buf], LDS)
                mm = None
                for h in range(HV):
                    sl = slice(h * 128, (h + 1) * 128)
                    mm = t.matmul(ps_B[:, sl], yc[:, sl], MT[:, sl], start=True, stop=True)
                ctr.inc(s_mm, 1)
                mm.then_inc(s_mm, 1)
            MY = ctr.get(s_mm)

            @blk.scalar
            def _(s, MY=MY):
                s.wait_ge(s_mm, MY)
                ctr.inc(s_s, 1)
                s.activation(nMYT[:, :], ps_B[:, :], AF.Identity, scale=-1.0).then_inc(s_s, 1)
            NM = ctr.get(s_s)

            @blk.tensor
            def _(t, NM=NM, u0=u0):
                t.wait_ge(s_s, NM)
                mm = None
                for h in range(HV):
                    sl = slice(h * 128, (h + 1) * 128)
                    t.matmul(ps_B[:, sl], MT[:, sl], u0[:, sl], start=True, stop=False)
                    t.matmul(ps_B[:, sl], qAf[:, sl], state.ap()[:, sl], start=False, stop=False)
                    mm = t.matmul(ps_B[:, sl], nMYT[:, sl], state.ap()[:, sl], start=False, stop=True)
                ctr.inc(s_mm, 1)
                mm.then_inc(s_mm, 1)
            OM = ctr.get(s_mm)

            @blk.scalar
            def _(s, OM=OM):
                s.wait_ge(s_mm, OM)
                s.wait_ge(s_v, ctr.get(s_v))
                ins = s.copy(Ofp[:, :], ps_B[:, :])
                ctr.inc(s_s, 1); ins.then_inc(s_s, 1); s.wait_ge(s_s, ctr.get(s_s))
                ctr.inc(s_s, 1)
                s.activation(e_m[:, :], Ofp[:, :], AF.Square).then_inc(s_s, 1)
            OE = ctr.get(s_s)

            @blk.vector
            def _(v, OE=OE):
                v.wait_ge(s_s, OE)
                ins = v.tensor_reduce(rcol[:, 0:HV].unsqueeze(2), w3(e_m.ap()), AX.X, ALU.add)
                ctr.inc(s_v, 1); ins.then_inc(s_v, 1); v.wait_ge(s_v, ctr.get(s_v))
                ctr.inc(s_v, 1)
                v.tensor_scalar(rcol[:, 0:HV], rcol[:, 0:HV], 1.0 / DV, EPS_RMS,
                                ALU.mult, ALU.add).then_inc(s_v, 1)
            RM1 = ctr.get(s_v)

            @blk.scalar
            def _(s, RM1=RM1):
                s.wait_ge(s_v, RM1)
                ctr.inc(s_s, 1)
                s.activation(rcol[:, 0:HV], rcol[:, 0:HV], AF.Sqrt).then_inc(s_s, 1)
            RM2 = ctr.get(s_s)

            @blk.vector
            def _(v, RM2=RM2):
                v.wait_ge(s_s, RM2)
                ins = v.reciprocal(rcol[:, 0:HV], rcol[:, 0:HV])
                ctr.inc(s_v, 1); ins.then_inc(s_v, 1); v.wait_ge(s_v, ctr.get(s_v))
                ctr.inc(s_v, 1)
                v.tensor_tensor(w3(Onsb.ap()), w3(Ofp.ap()), colview(rcol[:, 0:HV]), ALU.mult).then_inc(s_v, 1)
            ONV = ctr.get(s_v)

            @blk.tensor
            def _(t, ONV=ONV):
                t.wait_ge(s_v, ONV)
                t.wait_ge(s_s, ctr.get(s_s))
                mm = None
                for h in range(HV):
                    sl = slice(h * 128, (h + 1) * 128)
                    mm = t.matmul(ps_A[:, sl], Onsb[:, sl], c_ident[:, :], start=True, stop=True)
                ctr.inc(s_mm, 1)
                mm.then_inc(s_mm, 1)
            OT = ctr.get(s_mm)

            @blk.scalar
            def _(s, OT=OT, gtl=gtl):
                s.wait_ge(s_mm, OT)
                ins = s.activation(gsil[:, :], gtl, AF.Silu)
                ctr.inc(s_s, 1); ins.then_inc(s_s, 1); s.wait_ge(s_s, ctr.get(s_s))
                ctr.inc(s_s, 1)
                s.activation(Onsb[:, :], ps_A[:, :], AF.Identity, scale=c_rmsw[:, 0:1]).then_inc(s_s, 1)
            PS1 = ctr.get(s_s)

            @blk.vector
            def _(v, pt=pt, PS1=PS1, buf=buf):
                v.wait_ge(s_s, PS1)
                v.wait_ge(stp[buf], ctr.get(stp[buf]))
                ctr.inc(s_v, 1)
                v.tensor_tensor(pt, Onsb[:, :], gsil[:, :], ALU.mult).then_inc(s_v, 1)
            PV = ctr.get(s_v)

            @blk.gpsimd
            def _(g, ci=ci, pt=pt, PV=PV, buf=buf):
                g.wait_ge(s_v, PV)
                g.wait_ge(stp[buf], ctr.get(stp[buf]))
                g.dma_start(out=p_scr[ci], in_=pt).then_inc(stp[buf], 16)
                ctr.inc(stp[buf], 16)

            @blk.tensor
            def _(t, ptl=ptl, OM=OM):
                t.wait_ge(s_s, ctr.get(s_s))
                mm = None
                for h in range(HV):
                    sl = slice(h * 128, (h + 1) * 128)
                    mm = t.matmul(ps_B[:, sl], ptl[:, sl], state.ap()[:, sl], start=True, stop=True)
                ctr.inc(s_mm, 1)
                mm.then_inc(s_mm, 1)
            SM = ctr.get(s_mm)

            @blk.vector
            def _(v, rtl=rtl, SM=SM):
                v.wait_ge(s_mm, SM)
                ctr.inc(s_v, 1)
                v.tensor_tensor(state.ap()[:, :], ps_B[:, :], rtl, ALU.add).then_inc(s_v, 1)


def _qview(qk_f, ci):
    return (qk_f.ap()[:, ci * 128:ci * 128 + (HQ - 1) * TPC + 128]
            .rearrange("p (q rest) -> p q rest", q=HQ) if False else
            _qview_impl(qk_f, ci))


def _qview_impl(qk_f, ci):
    full = qk_f.ap()
    v = full.rearrange("p (hh t) -> p hh t", t=TPC)[:, 0:HQ, ci * 128:(ci + 1) * 128]
    return v.unsqueeze(2).broadcast_to((128, HQ, 2, 128))


def _p3(nc, blk, ctr, sems, env):
    s_mm, s_v, s_s = sems["mm"], sems["v"], sems["s"]
    s_lda, s_ldb, s_sta, s_stb = sems["lda"], sems["ldb"], sems["sta"], sems["stb"]
    ldp = [s_lda, s_ldb]
    stp = [s_sta, s_stb]
    c_ident, c_cols = env["c_ident"], env["c_cols"]
    x_in, wout_b, p_scr, x2_scr = env["x_in"], env["wout_b"], env["p_scr"], env["x2_scr"]
    xq_f = env["xq_f"]
    from contextlib import ExitStack

    _phase_barrier(nc, blk, ctr, sems)
    with ExitStack() as st3:
        E1 = st3.enter_context
        pcol = E1(nc.sbuf_tensor("pcol", [128, 2 * 2048], BF16))
        woutb = E1(nc.sbuf_tensor("woutb", [128, 2 * 16 * 512], BF16))
        attn_all = E1(nc.sbuf_tensor("attn_all", [128, 8 * D], F32))
        lnt = E1(nc.sbuf_tensor("lnt3", [128, 40], F32))
        x2t = E1(nc.sbuf_tensor("x2t", [128, 2 * D], F32))
        xnt = E1(nc.sbuf_tensor("xnt3", [128, D], F32))
        xtl = E1(nc.sbuf_tensor("xtl3", [128, 2 * D], F32))
        inti = E1(nc.sbuf_tensor("inti", [128, D], I32))
        intf = E1(nc.sbuf_tensor("intf", [128, D], F32))
        xq_t = E1(nc.sbuf_tensor("xq_t", [128, D], BF16))
        ps_op0 = E1(nc.psum_tensor("ps_op0", [128, 512], F32))
        ps_op1 = E1(nc.psum_tensor("ps_op1", [128, 512], F32))
        ps_big = E1(nc.psum_tensor("ps_big3", [128, D], F32))

        for fg in range(4):
            wsl = woutb[:, (fg % 2) * 16 * 512:][:, :16 * 512]

            @blk.gpsimd
            def _(g, fg=fg, wsl=wsl):
                if fg >= 2:
                    g.wait_ge(s_mm, ctr.get(s_mm))
                g.wait_ge(stp[fg % 2], ctr.get(stp[fg % 2]))
                g.dma_start(out=wsl.rearrange("p (db f) -> p db f", f=512),
                            in_=wout_b[fg].rearrange("(db p) f -> p db f", p=128)).then_inc(stp[fg % 2], 16)
                ctr.inc(stp[fg % 2], 16)
            WL = ctr.get(stp[fg % 2])

            for tb in range(8):
                it = fg * 8 + tb
                pb = it % 2
                psl = ps_op0 if tb % 2 == 0 else ps_op1
                pc = pcol[:, pb * 2048:pb * 2048 + 2048]

                @blk.sync
                def _(q, tb=tb, pc=pc, pb=pb, it=it):
                    if it >= 2:
                        q.wait_ge(s_mm, ctr.get(s_mm))
                    q.wait_ge(ldp[pb], ctr.get(ldp[pb]))
                    q.dma_start(out=pc, in_=p_scr[tb]).then_inc(ldp[pb], 16)
                    ctr.inc(ldp[pb], 16)
                PL = ctr.get(ldp[pb])

                @blk.tensor
                def _(t, fg=fg, tb=tb, pc=pc, wsl=wsl, psl=psl, PL=PL, WL=WL, pb=pb):
                    t.wait_ge(ldp[pb], PL)
                    t.wait_ge(stp[fg % 2], WL)
                    if fg + tb > 0:
                        t.wait_ge(s_s, ctr.get(s_s))
                    mm = None
                    for db in range(NDB):
                        mm = t.matmul(psl[:, :],
                                      pc[:, db * 128:(db + 1) * 128],
                                      wsl.rearrange("p (db f) -> p db f", f=512)[:, db, :],
                                      start=(db == 0), stop=(db == NDB - 1))
                    ctr.inc(s_mm, 1)
                    mm.then_inc(s_mm, 1)
                OPM = ctr.get(s_mm)

                @blk.scalar
                def _(s, fg=fg, tb=tb, psl=psl, OPM=OPM):
                    s.wait_ge(s_mm, OPM)
                    ctr.inc(s_s, 1)
                    s.copy(attn_all[:, tb * D + fg * 512: tb * D + fg * 512 + 512], psl[:, :]).then_inc(s_s, 1)
        OPDONE = ctr.get(s_s)

        for tb in range(8):
            at = attn_all[:, tb * D:(tb + 1) * D]
            xt = xtl[:, (tb % 2) * D:(tb % 2) * D + D]
            x2 = x2t[:, (tb % 2) * D:(tb % 2) * D + D]

            @blk.sync
            def _(q, tb=tb, xt=xt):
                if tb >= 2:
                    q.wait_ge(s_v, ctr.get(s_v))
                q.wait_ge(ldp[tb % 2], ctr.get(ldp[tb % 2]))
                q.dma_start(out=xt, in_=x_in[HALO + tb * 128: HALO + (tb + 1) * 128, :]).then_inc(ldp[tb % 2], 16)
                ctr.inc(ldp[tb % 2], 16)
            XL = ctr.get(ldp[tb % 2])

            def emit_ln(src_ap, eps_val, dst_col):
                @blk.vector
                def _(v):
                    v.wait_ge(s_v, ctr.get(s_v))
                    ins = None
                    for i4 in range(4):
                        ins = v.bn_stats(lnt[:, i4 * 6:(i4 + 1) * 6], src_ap[:, i4 * 512:(i4 + 1) * 512])
                    ctr.inc(s_v, 1); ins.then_inc(s_v, 1); v.wait_ge(s_v, ctr.get(s_v))
                    ins = v.bn_aggr(lnt[:, 24:26], lnt[:, 0:24])
                    ctr.inc(s_v, 1); ins.then_inc(s_v, 1); v.wait_ge(s_v, ctr.get(s_v))
                    ctr.inc(s_v, 1)
                    v.tensor_scalar(lnt[:, 26:27], lnt[:, 25:26], eps_val, None, ALU.add).then_inc(s_v, 1)
                V1 = ctr.get(s_v)

                @blk.scalar
                def _(s, V1=V1):
                    s.wait_ge(s_v, V1)
                    ctr.inc(s_s, 1)
                    s.activation(lnt[:, 27:28], lnt[:, 26:27], AF.Sqrt).then_inc(s_s, 1)
                S1 = ctr.get(s_s)

                @blk.vector
                def _(v, S1=S1):
                    v.wait_ge(s_s, S1)
                    ins = v.reciprocal(lnt[:, dst_col:dst_col + 1], lnt[:, 27:28])
                    ctr.inc(s_v, 1); ins.then_inc(s_v, 1); v.wait_ge(s_v, ctr.get(s_v))
                    ctr.inc(s_v, 1)
                    v.scalar_tensor_tensor(out=lnt[:, dst_col + 1:dst_col + 2], in0=lnt[:, 24:25],
                                           scalar=-1.0, in1=lnt[:, dst_col:dst_col + 1],
                                           op0=ALU.mult, op1=ALU.mult).then_inc(s_v, 1)
                return ctr.get(s_v)

            @blk.vector
            def _(v, OPDONE=OPDONE):
                v.wait_ge(s_s, OPDONE)
            LN1 = emit_ln(at, EPS_LN, 28)

            @blk.scalar
            def _(s, at=at, LN1=LN1):
                s.wait_ge(s_v, LN1)
                ctr.inc(s_s, 1)
                s.activation(xnt[:, :], at, AF.Identity,
                             bias=lnt[:, 29:30], scale=lnt[:, 28:29]).then_inc(s_s, 1)
            A1 = ctr.get(s_s)

            @blk.vector
            def _(v, x2=x2, xt=xt, A1=A1, XL=XL, tb=tb):
                v.wait_ge(s_s, A1)
                v.wait_ge(ldp[tb % 2], XL)
                v.wait_ge(stp[tb % 2], ctr.get(stp[tb % 2]))
                ctr.inc(s_v, 1)
                v.tensor_tensor(x2, xnt[:, :], xt, ALU.add).then_inc(s_v, 1)
            X2V = ctr.get(s_v)

            @blk.gpsimd
            def _(g, tb=tb, x2=x2, X2V=X2V):
                g.wait_ge(s_v, X2V)
                g.wait_ge(stp[tb % 2], ctr.get(stp[tb % 2]))
                g.dma_start(out=x2_scr[tb * 128:(tb + 1) * 128, :], in_=x2).then_inc(stp[tb % 2], 16)
                ctr.inc(stp[tb % 2], 16)

            LN2 = emit_ln(x2, EPS_FFN, 30)

            @blk.scalar
            def _(s, x2=x2, LN2=LN2):
                s.wait_ge(s_v, LN2)
                ctr.inc(s_s, 1)
                s.activation(xnt[:, :], x2, AF.Identity,
                             bias=lnt[:, 31:32], scale=lnt[:, 30:31]).then_inc(s_s, 1)
            A2 = ctr.get(s_s)

            @blk.vector
            def _(v, A2=A2):
                v.wait_ge(s_s, A2)
                ins = v.tensor_reduce(lnt[:, 32:33], xnt[:, :], AX.X, ALU.max, apply_absolute_value=True)
                ctr.inc(s_v, 1); ins.then_inc(s_v, 1); v.wait_ge(s_v, ctr.get(s_v))
                ins = v.reciprocal(lnt[:, 33:34], lnt[:, 32:33])
                ctr.inc(s_v, 1); ins.then_inc(s_v, 1); v.wait_ge(s_v, ctr.get(s_v))
                v.tensor_scalar(lnt[:, 34:35], lnt[:, 33:34], 127.0, None, ALU.mult)
                ins = v.tensor_scalar(lnt[:, 35:36], lnt[:, 32:33], 1.0 / 127.0, None, ALU.mult)
                ctr.inc(s_v, 1); ins.then_inc(s_v, 1); v.wait_ge(s_v, ctr.get(s_v))
                ins = v.tensor_scalar(intf[:, :], xnt[:, :], lnt[:, 34:35], None, ALU.mult)
                ctr.inc(s_v, 1); ins.then_inc(s_v, 1); v.wait_ge(s_v, ctr.get(s_v))
                ins = v.tensor_copy(inti[:, :], intf[:, :])
                ctr.inc(s_v, 1); ins.then_inc(s_v, 1); v.wait_ge(s_v, ctr.get(s_v))
                ins = v.tensor_copy(intf[:, :], inti[:, :])
                ctr.inc(s_v, 1); ins.then_inc(s_v, 1); v.wait_ge(s_v, ctr.get(s_v))
                ctr.inc(s_v, 1)
                v.tensor_scalar(xq_t[:, :], intf[:, :], lnt[:, 35:36], None, ALU.mult).then_inc(s_v, 1)
            QV = ctr.get(s_v)

            @blk.tensor
            def _(t, tb=tb, QV=QV):
                t.wait_ge(s_v, QV)
                if tb > 0:
                    t.wait_ge(s_s, ctr.get(s_s))
                mm = None
                for db in range(NDB):
                    mm = t.matmul(ps_big[:, db * 128:(db + 1) * 128],
                                  xq_t[:, db * 128:(db + 1) * 128],
                                  c_ident[:, :], start=True, stop=True)
                ctr.inc(s_mm, 1)
                mm.then_inc(s_mm, 1)
            TQ = ctr.get(s_mm)

            @blk.scalar
            def _(s, tb=tb, TQ=TQ):
                s.wait_ge(s_mm, TQ)
                ctr.inc(s_s, 1)
                s.copy(
                    xq_f.ap().rearrange("p (db c) -> p db c", db=NDB)[:, :, tb * 128:(tb + 1) * 128],
                    ps_big.ap().rearrange("p (db c) -> p db c", db=NDB),
                ).then_inc(s_s, 1)


def _ffn_gu(nc, blk, ctr, sems, env):
    s_mm, s_v, s_s = sems["mm"], sems["v"], sems["s"]
    s_lda, s_ldb, s_sta, s_stb = sems["lda"], sems["ldb"], sems["sta"], sems["stb"]
    ldp = [sems["ga"], sems["gb"]]
    xq_f, p2 = env["xq_f"], env["p2"]
    wg_b, wu_b = env["wg_b"], env["wu_b"]
    from contextlib import ExitStack

    _phase_barrier(nc, blk, ctr, sems)
    with ExitStack() as stf:
        E1 = stf.enter_context
        wgb = E1(nc.sbuf_tensor("wgb", [128, 2 * 2048], BF16))
        wub = E1(nc.sbuf_tensor("wub", [128, 2 * 2048], BF16))
        gs = E1(nc.sbuf_tensor("gs", [128, 2 * 512], BF16))
        ps_g0 = E1(nc.psum_tensor("ps_g0", [128, 512], F32))
        ps_g1 = E1(nc.psum_tensor("ps_g1", [128, 512], F32))
        ps_u0 = E1(nc.psum_tensor("ps_u0", [128, 512], F32))
        ps_u1 = E1(nc.psum_tensor("ps_u1", [128, 512], F32))

        for hb in range(NHB):
            buf = hb % 2
            wg = wgb[:, buf * 2048:buf * 2048 + 2048]
            wu = wub[:, buf * 2048:buf * 2048 + 2048]

            @blk.gpsimd
            def _(g, hb=hb, wg=wg, wu=wu, buf=buf):
                if hb >= 2:
                    g.wait_ge(s_mm, ctr.get(s_mm))
                g.wait_ge(ldp[buf], ctr.get(ldp[buf]))
                g.dma_start(out=wg.rearrange("p (db f) -> p db f", f=128),
                            in_=wg_b[hb].rearrange("(db p) f -> p db f", p=128)).then_inc(ldp[buf], 16)
                g.dma_start(out=wu.rearrange("p (db f) -> p db f", f=128),
                            in_=wu_b[hb].rearrange("(db p) f -> p db f", p=128)).then_inc(ldp[buf], 16)
                ctr.inc(ldp[buf], 32)
            WL = ctr.get(ldp[buf])

            @blk.scalar
            def _(s, wg=wg, wu=wu, WL=WL, buf=buf, hb=hb):
                s.wait_ge(ldp[buf], WL)
                if hb >= 2:
                    s.wait_ge(s_mm, ctr.get(s_mm))
                ins = s.sign(wg, wg)
                ins = s.sign(wu, wu)
                ctr.inc(s_s, 1); ins.then_inc(s_s, 1)
            SG = ctr.get(s_s)

            for th in range(2):
                psg = ps_g0 if th == 0 else ps_g1
                psu = ps_u0 if th == 0 else ps_u1

                @blk.tensor
                def _(t, hb=hb, th=th, wg=wg, wu=wu, psg=psg, psu=psu, SG=SG):
                    t.wait_ge(s_s, SG)
                    if hb > 0:
                        t.wait_ge(s_v, ctr.get(s_v))
                    mm = None
                    for db in range(NDB):
                        mm = t.matmul(psg[:, :], wg[:, db * 128:(db + 1) * 128],
                                      xq_f.ap()[:, db * TPC + th * 512:][:, :512],
                                      start=(db == 0), stop=(db == NDB - 1))
                    for db in range(NDB):
                        mm = t.matmul(psu[:, :], wu[:, db * 128:(db + 1) * 128],
                                      xq_f.ap()[:, db * TPC + th * 512:][:, :512],
                                      start=(db == 0), stop=(db == NDB - 1))
                    ctr.inc(s_mm, 1)
                    mm.then_inc(s_mm, 1)
                GM = ctr.get(s_mm)

                @blk.scalar
                def _(s, psg=psg, GM=GM, th=th, hb=hb):
                    s.wait_ge(s_mm, GM)
                    if hb > 0:
                        s.wait_ge(s_v, ctr.get(s_v) - 1)
                    ctr.inc(s_s, 1)
                    s.activation(gs[:, th * 512:(th + 1) * 512], psg[:, :], AF.Silu).then_inc(s_s, 1)
                GS = ctr.get(s_s)

                @blk.vector
                def _(v, hb=hb, th=th, psu=psu, GS=GS):
                    v.wait_ge(s_s, GS)
                    ctr.inc(s_v, 1)
                    v.tensor_tensor(p2.ap()[:, hb * TPC + th * 512:][:, :512],
                                    psu[:, :], gs[:, th * 512:(th + 1) * 512], ALU.mult).then_inc(s_v, 1)


def _ffn_down(nc, blk, ctr, sems, env):
    s_mm, s_v, s_s = sems["mm"], sems["v"], sems["s"]
    s_lda, s_ldb, s_sta, s_stb = sems["lda"], sems["ldb"], sems["sta"], sems["stb"]
    ldp = [sems["ga"], sems["gb"]]
    stp = [s_sta, s_stb]
    p2, wd_b, ffn_scr = env["p2"], env["wd_b"], env["ffn_scr"]
    from contextlib import ExitStack

    _phase_barrier(nc, blk, ctr, sems)
    with ExitStack() as std_:
        E1 = std_.enter_context
        wdb = E1(nc.sbuf_tensor("wdb", [128, 2 * NHB * 128], BF16))
        fout = E1(nc.sbuf_tensor("fout", [128, 2 * TPC], BF16))
        ps_d0 = E1(nc.psum_tensor("ps_d0", [128, TPC], F32))
        ps_d1 = E1(nc.psum_tensor("ps_d1", [128, TPC], F32))

        for fcb in range(16):
            buf = fcb % 2
            wd = wdb[:, buf * NHB * 128:][:, :NHB * 128]
            psd = ps_d0 if fcb % 2 == 0 else ps_d1
            fo = fout[:, buf * TPC:buf * TPC + TPC]

            @blk.gpsimd
            def _(g, fcb=fcb, wd=wd, buf=buf):
                if fcb >= 2:
                    g.wait_ge(s_mm, ctr.get(s_mm))
                g.wait_ge(ldp[buf], ctr.get(ldp[buf]))
                g.dma_start(out=wd.rearrange("p (hb f) -> p hb f", f=128),
                            in_=wd_b[fcb].rearrange("(hb p) f -> p hb f", p=128)).then_inc(ldp[buf], 16)
                ctr.inc(ldp[buf], 16)
            WL = ctr.get(ldp[buf])

            @blk.scalar
            def _(s, wd=wd, WL=WL, buf=buf, fcb=fcb):
                s.wait_ge(ldp[buf], WL)
                if fcb >= 2:
                    s.wait_ge(s_mm, ctr.get(s_mm))
                ctr.inc(s_s, 1)
                s.sign(wd, wd).then_inc(s_s, 1)
            SG = ctr.get(s_s)

            @blk.tensor
            def _(t, fcb=fcb, wd=wd, psd=psd, SG=SG):
                t.wait_ge(s_s, SG)
                if fcb >= 2:
                    t.wait_ge(s_s, ctr.get(s_s))
                mm = None
                for th in range(2):
                    for hb in range(NHB):
                        mm = t.matmul(psd[:, th * 512:(th + 1) * 512],
                                      wd[:, hb * 128:(hb + 1) * 128],
                                      p2.ap()[:, hb * TPC + th * 512:][:, :512],
                                      start=(hb == 0), stop=(hb == NHB - 1))
                ctr.inc(s_mm, 1)
                mm.then_inc(s_mm, 1)
            DM = ctr.get(s_mm)

            @blk.scalar
            def _(s, psd=psd, fo=fo, DM=DM, buf=buf, fcb=fcb):
                s.wait_ge(s_mm, DM)
                if fcb >= 2:
                    s.wait_ge(stp[buf], ctr.get(stp[buf]))
                ctr.inc(s_s, 1)
                s.copy(fo, psd[:, :]).then_inc(s_s, 1)
            FE = ctr.get(s_s)

            @blk.gpsimd
            def _(g, fcb=fcb, fo=fo, FE=FE, buf=buf):
                g.wait_ge(s_s, FE)
                g.wait_ge(stp[buf], ctr.get(stp[buf]))
                g.dma_start(out=ffn_scr[fcb], in_=fo).then_inc(stp[buf], 16)
                ctr.inc(stp[buf], 16)


def _final(nc, blk, ctr, sems, env):
    s_mm, s_v, s_s = sems["mm"], sems["v"], sems["s"]
    s_lda, s_ldb, s_sta, s_stb = sems["lda"], sems["ldb"], sems["sta"], sems["stb"]
    ldp = [s_lda, s_ldb]
    stp = [s_sta, s_stb]
    c_ident = env["c_ident"]
    ffn_scr, x2_scr, y_out = env["ffn_scr"], env["x2_scr"], env["y_out"]
    from contextlib import ExitStack

    _phase_barrier(nc, blk, ctr, sems)
    with ExitStack() as stf:
        E1 = stf.enter_context
        ftile = E1(nc.sbuf_tensor("ftile", [128, 2 * D], BF16))
        x2l = E1(nc.sbuf_tensor("x2l", [128, 2 * D], F32))
        yt = E1(nc.sbuf_tensor("yt", [128, 2 * D], F32))
        lnt = E1(nc.sbuf_tensor("lntF", [128, 32], F32))
        ps_fin0 = E1(nc.psum_tensor("ps_fin0", [128, D], F32))
        ps_fin1 = E1(nc.psum_tensor("ps_fin1", [128, D], F32))

        for tb in range(8):
            buf = tb % 2
            ft = ftile[:, buf * D:buf * D + D]
            xx = x2l[:, buf * D:buf * D + D]
            yy = yt[:, buf * D:buf * D + D]
            psf = ps_fin0 if buf == 0 else ps_fin1

            @blk.sync
            def _(q, tb=tb, ft=ft, xx=xx, buf=buf):
                if tb >= 2:
                    q.wait_ge(s_mm, ctr.get(s_mm))
                    q.wait_ge(s_v, ctr.get(s_v))
                q.wait_ge(ldp[buf], ctr.get(ldp[buf]))
                q.dma_start(out=ft.rearrange("p (db t) -> p db t", t=128),
                            in_=ffn_scr[:, :, tb * 128:(tb + 1) * 128].rearrange("db p t -> p db t")).then_inc(ldp[buf], 16)
                q.dma_start(out=xx, in_=x2_scr[tb * 128:(tb + 1) * 128, :]).then_inc(ldp[buf], 16)
                ctr.inc(ldp[buf], 32)
            FL = ctr.get(ldp[buf])

            @blk.tensor
            def _(t, tb=tb, ft=ft, psf=psf, FL=FL, buf=buf):
                t.wait_ge(ldp[buf], FL)
                if tb >= 2:
                    t.wait_ge(s_s, ctr.get(s_s))
                mm = None
                for db in range(NDB):
                    mm = t.matmul(psf[:, db * 128:(db + 1) * 128],
                                  ft[:, db * 128:(db + 1) * 128],
                                  c_ident[:, :], start=True, stop=True)
                ctr.inc(s_mm, 1)
                mm.then_inc(s_mm, 1)
            FM = ctr.get(s_mm)

            @blk.vector
            def _(v, psf=psf, FM=FM):
                v.wait_ge(s_mm, FM)
                ins = None
                for i4 in range(4):
                    ins = v.bn_stats(lnt[:, i4 * 6:(i4 + 1) * 6], psf[:, i4 * 512:(i4 + 1) * 512])
                ctr.inc(s_v, 1); ins.then_inc(s_v, 1); v.wait_ge(s_v, ctr.get(s_v))
                ins = v.bn_aggr(lnt[:, 24:26], lnt[:, 0:24])
                ctr.inc(s_v, 1); ins.then_inc(s_v, 1); v.wait_ge(s_v, ctr.get(s_v))
                ctr.inc(s_v, 1)
                v.tensor_scalar(lnt[:, 26:27], lnt[:, 25:26], EPS_LN, None, ALU.add).then_inc(s_v, 1)
            V1 = ctr.get(s_v)

            @blk.scalar
            def _(s, V1=V1):
                s.wait_ge(s_v, V1)
                ctr.inc(s_s, 1)
                s.activation(lnt[:, 27:28], lnt[:, 26:27], AF.Sqrt).then_inc(s_s, 1)
            S1 = ctr.get(s_s)

            @blk.vector
            def _(v, S1=S1):
                v.wait_ge(s_s, S1)
                ins = v.reciprocal(lnt[:, 28:29], lnt[:, 27:28])
                ctr.inc(s_v, 1); ins.then_inc(s_v, 1); v.wait_ge(s_v, ctr.get(s_v))
                ctr.inc(s_v, 1)
                v.scalar_tensor_tensor(out=lnt[:, 29:30], in0=lnt[:, 24:25],
                                       scalar=-1.0, in1=lnt[:, 28:29],
                                       op0=ALU.mult, op1=ALU.mult).then_inc(s_v, 1)
            V2 = ctr.get(s_v)

            @blk.scalar
            def _(s, psf=psf, yy=yy, V2=V2, buf=buf, tb=tb):
                s.wait_ge(s_v, V2)
                if tb >= 2:
                    s.wait_ge(stp[buf], ctr.get(stp[buf]))
                ctr.inc(s_s, 1)
                s.activation(yy, psf[:, :], AF.Identity,
                             bias=lnt[:, 29:30], scale=lnt[:, 28:29]).then_inc(s_s, 1)
            S2 = ctr.get(s_s)

            @blk.vector
            def _(v, yy=yy, xx=xx, S2=S2):
                v.wait_ge(s_s, S2)
                ctr.inc(s_v, 1)
                v.tensor_tensor(yy, yy, xx, ALU.add).then_inc(s_v, 1)
            YV = ctr.get(s_v)

            @blk.gpsimd
            def _(g, tb=tb, yy=yy, YV=YV, buf=buf):
                g.wait_ge(s_v, YV)
                g.wait_ge(stp[buf], ctr.get(stp[buf]))
                g.dma_start(out=y_out[tb * 128:(tb + 1) * 128, :], in_=yy).then_inc(stp[buf], 16)
                ctr.inc(stp[buf], 16)

        @blk.sync
        def _(q):
            q.wait_ge(s_sta, ctr.get(s_sta))
            q.wait_ge(s_stb, ctr.get(s_stb))


def host_prep(inputs):
    x = np.ascontiguousarray(np.asarray(inputs["L0_attn_x"], np.float32))
    w_qkv = np.asarray(inputs["L0_attn_w_qkv"], np.float32)
    w_gate = np.asarray(inputs["L0_attn_w_gate"], np.float32)
    w_beta = np.asarray(inputs["L0_attn_w_beta"], np.float32)
    w_alpha = np.asarray(inputs["L0_attn_w_alpha"], np.float32)
    log_A = np.asarray(inputs["L0_attn_log_A"], np.float32)
    dt_bias = np.asarray(inputs["L0_attn_dt_bias"], np.float32)
    conv_w = np.asarray(inputs["L0_attn_conv_w"], np.float32)
    rms_w = np.asarray(inputs["L0_attn_rms_w"], np.float32)
    w_out = np.asarray(inputs["L0_attn_w_out"], np.float32)
    W_g = np.asarray(inputs["L0_ffn_W_g"], np.float32)
    W_u = np.asarray(inputs["L0_ffn_W_u"], np.float32)
    W_d = np.asarray(inputs["L0_ffn_W_d"], np.float32)

    wqkv_bl = np.ascontiguousarray(w_qkv.T.reshape(D, NFQ, 128).transpose(1, 0, 2))
    wgate_bl = np.ascontiguousarray(w_gate.T.reshape(D, NFG, 128).transpose(1, 0, 2))
    wba = np.ascontiguousarray(np.concatenate([w_beta, w_alpha], axis=0).T)
    convw = np.ascontiguousarray(
        conv_w[:, 0, :].reshape(NFQ, 128, 4).transpose(1, 0, 2).reshape(128, NFQ * 4))
    la = np.ascontiguousarray(np.stack([log_A, dt_bias], axis=1))
    rmsw = np.ascontiguousarray(rms_w[:, None])
    wout_bl = np.ascontiguousarray(w_out.T.reshape(D, 4, 512).transpose(1, 0, 2))
    wg_bl = np.ascontiguousarray(W_g.T.reshape(D, NHB, 128).transpose(1, 0, 2))
    wu_bl = np.ascontiguousarray(W_u.T.reshape(D, NHB, 128).transpose(1, 0, 2))
    wd_bl = np.ascontiguousarray(W_d.T.reshape(FFH, 16, 128).transpose(1, 0, 2))

    ident_bf = np.eye(128, dtype=NPBF16)
    ident_f = np.eye(128, dtype=np.float32)
    r_idx, c_idx = np.meshgrid(np.arange(128), np.arange(128), indexing="ij")
    maskLT = np.where(c_idx < r_idx, 0.0, NEG).astype(np.float32)
    maskUT = np.where(r_idx < c_idx, 0.0, NEG).astype(np.float32)
    maskUD = np.where(r_idx <= c_idx, -LSD, NEG).astype(np.float32)
    ones_col = np.ones((128, 1), dtype=NPBF16)
    ones_row_f = np.ones((1, 128), dtype=np.float32)
    zeros_f = np.zeros((128, 128), dtype=np.float32)
    ccols = np.zeros((128, 8), np.float32)
    ccols[:, 0] = EPS_L2
    ccols[:, 1] = EPS_RMS
    ccols[:, 2] = -LSD
    ccols[:, 3] = 2.0
    ccols[:, 4] = EPS_LN
    ccols[:, 5] = EPS_FFN
    ccols[:, 6] = 127.0
    ccols[:, 7] = 1.0

    in_maps = []
    for core in range(8):
        b, g = core // 4, core % 4
        t0 = g * TPC
        xs = np.zeros((TP, D), np.float32)
        lo = max(0, t0 - HALO)
        xs[HALO - (t0 - lo): HALO - (t0 - lo) + (t0 + TPC - lo)] = x[b, lo:t0 + TPC]
        hop_m = np.zeros((128, 7), np.float32)
        hop_imi = np.zeros((128, 7 * 128), np.float32)
        for j in range(7):
            use = 1.0 if (j // 4 == b and j % 4 < g) else 0.0
            hop_m[:, j] = use
            hop_imi[:, j * 128:(j + 1) * 128] = (1.0 - use) * np.eye(128, dtype=np.float32)
        in_maps.append({
            "x": xs, "wqkv_b": wqkv_bl, "wgate_b": wgate_bl, "wba": wba,
            "convw": convw, "loga_params": la, "rmsw": rmsw,
            "wout_b": wout_bl, "wg_b": wg_bl, "wu_b": wu_bl, "wd_b": wd_bl,
            "ident_bf": ident_bf, "ident_f": ident_f, "maskLT": maskLT,
            "maskUT": maskUT, "maskUD": maskUD,
            "ones_col_bf": ones_col, "ones_row_f": ones_row_f, "ccols": ccols,
            "ones_row_bf": np.ones((1, 128), dtype=NPBF16), "zeros_f": zeros_f,
            "hop_mcol": hop_m, "hop_imi": hop_imi,
        })
    return in_maps


def kernel(**inputs):
    in_maps = host_prep(inputs)
    nc, _ = build()
    res = run_bass_kernel_spmd(nc, in_maps, core_ids=list(range(8)))
    y = np.empty((2, 4096, D), np.float32)
    for core in range(8):
        b, g = core // 4, core % 4
        y[b, g * TPC:(g + 1) * TPC] = res.results[core]["y"]
    return y


# version 07
# speedup vs baseline: 2.3254x; 1.1794x over previous
import numpy as np
import ml_dtypes
import concourse.bass as bass
import concourse.mybir as mybir
from concourse.bass_utils import run_bass_kernel_spmd
from concourse import library_config

F32 = mybir.dt.float32
BF16 = mybir.dt.bfloat16
I32 = mybir.dt.int32
AF = mybir.ActivationFunctionType
ALU = mybir.AluOpType
AX = mybir.AxisListType
NPBF16 = ml_dtypes.bfloat16

D = 2048
TPC = 1024
HALO = 3
TP = 1032
TG = 344
C = 128
NCH = 8
HV, HQ, DK, DV = 16, 8, 128, 128
NDB = 16
NFQ, NFG = 32, 16
NFB = NFQ + NFG + 1
FFH = 8192
NHB = 64
NEG = -1e30
EPS_LN = 1e-5
EPS_L2 = 1e-6
EPS_RMS = 1e-6
EPS_FFN = 1e-8
LSD = float(np.log(np.sqrt(DK)))
NEWTON_ITERS = 6


class Ctr:
    def __init__(self):
        self.v = {}

    def inc(self, sem, k=1):
        self.v[sem.name] = self.v.get(sem.name, 0) + k
        return self.v[sem.name]

    def get(self, sem):
        return self.v.get(sem.name, 0)


def build(probe=None, upto="all"):
    nc = bass.Bass("TRN2", target_bir_lowering=False, debug=False)

    x_in = nc.dram_tensor("x", [TP, D], F32, kind="ExternalInput").ap()
    wqkv_b = nc.dram_tensor("wqkv_b", [NFQ, D, 128], F32, kind="ExternalInput").ap()
    wgate_b = nc.dram_tensor("wgate_b", [NFG, D, 128], F32, kind="ExternalInput").ap()
    wba_in = nc.dram_tensor("wba", [D, 32], F32, kind="ExternalInput").ap()
    convw_in = nc.dram_tensor("convw", [128, NFQ * 4], F32, kind="ExternalInput").ap()
    la_in = nc.dram_tensor("loga_params", [HV, 2], F32, kind="ExternalInput").ap()
    rmsw_in = nc.dram_tensor("rmsw", [128, 1], F32, kind="ExternalInput").ap()
    wout_b = nc.dram_tensor("wout_b", [4, D, 512], F32, kind="ExternalInput").ap()
    wg_b = nc.dram_tensor("wg_b", [NHB, D, 128], F32, kind="ExternalInput").ap()
    wu_b = nc.dram_tensor("wu_b", [NHB, D, 128], F32, kind="ExternalInput").ap()
    wd_b = nc.dram_tensor("wd_b", [16, FFH, 128], F32, kind="ExternalInput").ap()
    ident_in = nc.dram_tensor("ident_bf", [128, 128], BF16, kind="ExternalInput").ap()
    identf_in = nc.dram_tensor("ident_f", [128, 128], F32, kind="ExternalInput").ap()
    maskLT_in = nc.dram_tensor("maskLT", [128, 128], F32, kind="ExternalInput").ap()
    maskUT_in = nc.dram_tensor("maskUT", [128, 128], F32, kind="ExternalInput").ap()
    maskUD_in = nc.dram_tensor("maskUD", [128, 128], F32, kind="ExternalInput").ap()
    onesc_in = nc.dram_tensor("ones_col_bf", [128, 1], BF16, kind="ExternalInput").ap()
    onesrf_in = nc.dram_tensor("ones_row_f", [1, 128], F32, kind="ExternalInput").ap()
    onesr_in = nc.dram_tensor("ones_row_bf", [1, 128], BF16, kind="ExternalInput").ap()
    zeros_in = nc.dram_tensor("zeros_f", [128, 128], F32, kind="ExternalInput").ap()
    ccols_in = nc.dram_tensor("ccols", [128, 8], F32, kind="ExternalInput").ap()
    hopm_in = nc.dram_tensor("hop_mcol", [128, 7], F32, kind="ExternalInput").ap()
    hopimi_in = nc.dram_tensor("hop_imi", [128, 7 * 128], F32, kind="ExternalInput").ap()

    y_out = nc.dram_tensor("y", [TPC, D], F32, kind="ExternalOutput").ap()

    PROBE_SHAPES = {
        "pre": ([NFB, 128, TP], BF16),
        "qkv": ([4, 1024, TPC], BF16),
        "rows": ([HV, 4 * TPC], F32),
        "spill": ([NCH, 4, 128, HV * 128], BF16),
        "comp": ([2, 128, HV * 128], BF16),
        "state": ([128, HV * 128], BF16),
        "pf": ([NCH, 128, NDB * 128], BF16),
        "x2": ([TPC, D], F32),
        "xq": ([128, NDB * TPC], BF16),
        "p2": ([128, NHB * TPC], BF16),
        "ffn": ([NDB, 128, TPC], BF16),
    }
    probe_out = None
    if probe is not None:
        shp, dt_ = PROBE_SHAPES[probe]
        probe_out = nc.dram_tensor("probe", shp, dt_, kind="ExternalOutput").ap()

    pre_scr = nc.dram_tensor("pre_scr", [NFB, 128, TP], BF16, kind="Internal").ap()
    spill = nc.dram_tensor("spill", [NCH, 4, 128, HV * 128], BF16, kind="Internal").ap()
    p_scr = nc.dram_tensor("p_scr", [NCH, 128, NDB * 128], BF16, kind="Internal").ap()
    x2_scr = nc.dram_tensor("x2_scr", [TPC, D], F32, kind="Internal").ap()
    ffn_scr = nc.dram_tensor("ffn_scr", [NDB, 128, TPC], BF16, kind="Internal").ap()
    cc_in = nc.dram_tensor("cc_in", [128, HV * 2 * 128], BF16, kind="Internal").ap()
    cc_out = nc.dram_tensor("cc_out", [8 * 128, HV * 2 * 128], BF16, kind="Internal").ap()

    ctr = Ctr()
    from contextlib import ExitStack

    with ExitStack() as top:
        E = top.enter_context
        s_ld = E(nc.semaphore("s_ld"))
        s_lda = E(nc.semaphore("s_lda"))
        s_ldb = E(nc.semaphore("s_ldb"))
        s_st = E(nc.semaphore("s_st"))
        s_sta = E(nc.semaphore("s_sta"))
        s_stb = E(nc.semaphore("s_stb"))
        s_ga = E(nc.semaphore("s_ga"))
        s_gb = E(nc.semaphore("s_gb"))
        s_mm = E(nc.semaphore("s_mm"))
        s_v = E(nc.semaphore("s_v"))
        s_s = E(nc.semaphore("s_s"))
        s_g = E(nc.semaphore("s_g"))
        s_cc = E(nc.semaphore("s_cc"))
        sems = dict(ld=s_ld, lda=s_lda, ldb=s_ldb, st=s_st, sta=s_sta, stb=s_stb,
                    ga=s_ga, gb=s_gb, mm=s_mm, v=s_v, s=s_s, g=s_g, cc=s_cc)

        c_ident = E(nc.sbuf_tensor("c_ident", [128, 128], BF16))
        c_identf = E(nc.sbuf_tensor("c_identf", [128, 128], F32))
        c_maskLT = E(nc.sbuf_tensor("c_maskLT", [128, 128], F32))
        c_maskUT = E(nc.sbuf_tensor("c_maskUT", [128, 128], F32))
        c_maskUD = E(nc.sbuf_tensor("c_maskUD", [128, 128], F32))
        c_onec = E(nc.sbuf_tensor("c_onec", [128, 1], BF16))
        c_onerf = E(nc.sbuf_tensor("c_onerf", [1, 128], F32))
        c_oner = E(nc.sbuf_tensor("c_oner", [1, 128], BF16))
        c_zero = E(nc.sbuf_tensor("c_zero", [128, 128], F32))
        c_cols = E(nc.sbuf_tensor("c_cols", [128, 8], F32))
        c_hopm = E(nc.sbuf_tensor("c_hopm", [128, 7], F32))
        c_hopimi = E(nc.sbuf_tensor("c_hopimi", [128, 7 * 128], F32))
        c_rmsw = E(nc.sbuf_tensor("c_rmsw", [128, 1], F32))
        c_convw = E(nc.sbuf_tensor("c_convw", [128, NFQ * 4], F32))
        c_la = E(nc.sbuf_tensor("c_la", [HV, 8], F32))

        blk = E(nc.Block())

        @blk.sync
        def _(q):
            for dst, src in [
                (c_ident[:, :], ident_in[:, :]), (c_identf[:, :], identf_in[:, :]),
                (c_maskLT[:, :], maskLT_in[:, :]), (c_maskUT[:, :], maskUT_in[:, :]),
                (c_maskUD[:, :], maskUD_in[:, :]), (c_onec[:, :], onesc_in[:, :]),
                (c_onerf[:, :], onesrf_in[:, :]), (c_oner[:, :], onesr_in[:, :]),
                (c_zero[:, :], zeros_in[:, :]), (c_cols[:, :], ccols_in[:, :]),
                (c_hopm[:, :], hopm_in[:, :]),
                (c_hopimi[:, :], hopimi_in[:, :]),
                (c_rmsw[:, :], rmsw_in[:, :]),
                (c_convw[:, :], convw_in[:, :]),
                (c_la[:, 0:2], la_in[:, :]),
            ]:
                q.dma_start(out=dst, in_=src).then_inc(s_ld, 16)
                ctr.inc(s_ld, 16)
        CONSTS = ctr.get(s_ld)

        @blk.scalar
        def _(s):
            s.wait_ge(s_ld, CONSTS)
            ctr.inc(s_s, 1)
            s.activation(c_la[:, 4:5], c_la[:, 0:1], AF.Exp).then_inc(s_s, 1)

        @blk.vector
        def _(v):
            v.wait_ge(s_s, ctr.get(s_s))
            v.tensor_scalar(c_la[:, 3:4], c_la[:, 4:5], -1.0, None, ALU.mult)
            ctr.inc(s_v, 1)
            v.tensor_scalar(c_la[:, 5:6], c_la[:, 1:2], -1.0, None, ALU.mult).then_inc(s_v, 1)

        with ExitStack() as sc2:
            E2 = sc2.enter_context
            qk_f = E2(nc.sbuf_tensor("qk_f", [128, 16 * TPC], BF16))
            k_tok = E2(nc.sbuf_tensor("k_tok", [128, HQ * TPC], BF16))
            v_tok = E2(nc.sbuf_tensor("v_tok", [128, HV * TPC], BF16))
            barows = E2(nc.sbuf_tensor("barows", [HV, 4 * TPC], F32))
            state = E2(nc.sbuf_tensor("state", [128, HV * 128], BF16))
            compP = E2(nc.sbuf_tensor("compP", [128, HV * 128], BF16))
            compR = E2(nc.sbuf_tensor("compR", [128, HV * 128], BF16))

            _p1(nc, blk, ctr, sems, locals())
            if probe == "pre":
                _probe_dump(nc, blk, ctr, sems, probe_out[0:NFB - 1, :, :], pre_scr[0:NFB - 1, :, :])
                _probe_dump(nc, blk, ctr, sems, probe_out[NFB - 1, 0:32, :], pre_scr[NFB - 1, 0:32, :])
                _finish_zero(nc, blk, ctr, sems, y_out)
                return nc, probe

            _p15(nc, blk, ctr, sems, locals())
            if probe == "qkv":
                @blk.sync
                def _(q):
                    q.wait_ge(s_v, ctr.get(s_v))
                    q.wait_ge(s_s, ctr.get(s_s))
                    for i, (src, off) in enumerate([(qk_f, 0), (qk_f, 8 * TPC), (k_tok, 0), (v_tok, 0)]):
                        q.dma_start(
                            out=probe_out[i].rearrange("(hb p) t -> p hb t", p=128)[:, 0:8, :],
                            in_=src.ap()[:, off:off + 8 * TPC].rearrange("p (hb t) -> p hb t", t=TPC),
                        ).then_inc(s_ld, 16)
                        ctr.inc(s_ld, 16)
                    q.wait_ge(s_ld, ctr.get(s_ld))
                _finish_zero(nc, blk, ctr, sems, y_out)
                return nc, probe
            if probe == "rows":
                @blk.sync
                def _(q):
                    q.wait_ge(s_v, ctr.get(s_v))
                    q.wait_ge(s_s, ctr.get(s_s))
                    q.dma_start(out=probe_out[:, :], in_=barows[:, :]).then_inc(s_ld, 16)
                    ctr.inc(s_ld, 16)
                    q.wait_ge(s_ld, ctr.get(s_ld))
                _finish_zero(nc, blk, ctr, sems, y_out)
                return nc, probe

            _p2a(nc, blk, ctr, sems, locals())
            if probe == "spill":
                _probe_dump(nc, blk, ctr, sems, probe_out[:, :, :, :], spill[:, :, :, :])
                _finish_zero(nc, blk, ctr, sems, y_out)
                return nc, probe
            if probe == "comp":
                @blk.sync
                def _(q):
                    q.wait_ge(s_v, ctr.get(s_v))
                    q.wait_ge(s_s, ctr.get(s_s))
                    q.dma_start(out=probe_out[0], in_=compP[:, :]).then_inc(s_ld, 16)
                    q.dma_start(out=probe_out[1], in_=compR[:, :]).then_inc(s_ld, 16)
                    ctr.inc(s_ld, 32)
                    q.wait_ge(s_ld, ctr.get(s_ld))
                _finish_zero(nc, blk, ctr, sems, y_out)
                return nc, probe

            _cc_hop(nc, blk, ctr, sems, locals())
            if probe == "state":
                @blk.sync
                def _(q):
                    q.wait_ge(s_v, ctr.get(s_v))
                    q.dma_start(out=probe_out[:, :], in_=state[:, :]).then_inc(s_ld, 16)
                    ctr.inc(s_ld, 16)
                    q.wait_ge(s_ld, ctr.get(s_ld))
                _finish_zero(nc, blk, ctr, sems, y_out)
                return nc, probe

            _p2b(nc, blk, ctr, sems, locals())
            if probe == "pf":
                _probe_dump(nc, blk, ctr, sems, probe_out[:, :, :], p_scr[:, :, :])
                _finish_zero(nc, blk, ctr, sems, y_out)
                return nc, probe

        with ExitStack() as sc3:
            E3 = sc3.enter_context
            xq_f = E3(nc.sbuf_tensor("xq_f", [128, NDB * TPC], BF16))

            _p3(nc, blk, ctr, sems, locals())
            if probe in ("x2", "xq"):
                if probe == "x2":
                    _probe_dump(nc, blk, ctr, sems, probe_out[:, :], x2_scr[:, :])
                else:
                    @blk.sync
                    def _(q):
                        q.wait_ge(s_s, ctr.get(s_s))
                        q.dma_start(out=probe_out[:, :], in_=xq_f[:, :]).then_inc(s_ld, 16)
                        ctr.inc(s_ld, 16)
                        q.wait_ge(s_ld, ctr.get(s_ld))
                _finish_zero(nc, blk, ctr, sems, y_out)
                return nc, probe

            with ExitStack() as sff:
                EF = sff.enter_context
                p2 = EF(nc.sbuf_tensor("p2", [128, NHB * TPC], BF16))
                _ffn_gu(nc, blk, ctr, sems, locals())
                if probe == "p2":
                    @blk.sync
                    def _(q):
                        q.wait_ge(s_v, ctr.get(s_v))
                        q.dma_start(out=probe_out[:, :], in_=p2[:, :]).then_inc(s_ld, 16)
                        ctr.inc(s_ld, 16)
                        q.wait_ge(s_ld, ctr.get(s_ld))
                    _finish_zero(nc, blk, ctr, sems, y_out)
                    return nc, probe
                _ffn_down(nc, blk, ctr, sems, locals())
            if probe == "ffn":
                _probe_dump(nc, blk, ctr, sems, probe_out[:, :, :], ffn_scr[:, :, :])
                _finish_zero(nc, blk, ctr, sems, y_out)
                return nc, probe

        _final(nc, blk, ctr, sems, locals())
    return nc, probe


def _phase_barrier(nc, blk, ctr, sems):
    names = ("ld", "lda", "ldb", "st", "sta", "stb", "ga", "gb", "mm", "v", "s", "g", "cc")
    vals = {nm: ctr.get(sems[nm]) for nm in names}
    for eng in ("sync", "tensor", "vector", "scalar", "gpsimd"):
        @getattr(blk, eng)
        def _(e, eng=eng):
            for nm in names:
                e.wait_ge(sems[nm], vals[nm])


def _probe_dump(nc, blk, ctr, sems, dst_ap, src_ap):
    @blk.sync
    def _(q):
        for nm in ("st", "sta", "stb", "v", "s", "g", "mm"):
            q.wait_ge(sems[nm], ctr.get(sems[nm]))
        q.dma_start(out=dst_ap, in_=src_ap).then_inc(sems["ld"], 16)
        ctr.inc(sems["ld"], 16)
        q.wait_ge(sems["ld"], ctr.get(sems["ld"]))


def _finish_zero(nc, blk, ctr, sems, y_out):
    s_st, s_g = sems["st"], sems["g"]
    _phase_barrier(nc, blk, ctr, sems)
    with nc.sbuf_tensor("yzero", [128, D], F32) as yz:
        @blk.gpsimd
        def _(g):
            ctr.inc(s_g, 1)
            g.memset(yz[:], 0.0).then_inc(s_g, 1)
        GV = ctr.get(s_g)

        @blk.sync
        def _(q):
            q.wait_ge(s_g, GV)
            for tb in range(TPC // 128):
                q.dma_start(out=y_out[tb * 128:(tb + 1) * 128, :], in_=yz[:]).then_inc(s_st, 16)
                ctr.inc(s_st, 16)
            q.wait_ge(s_st, ctr.get(s_st))


def _p1(nc, blk, ctr, sems, env):
    s_ld, s_st, s_mm, s_v, s_s = sems["ld"], sems["st"], sems["mm"], sems["v"], sems["s"]
    s_lda, s_ldb, s_sta, s_stb = sems["lda"], sems["ldb"], sems["sta"], sems["stb"]
    ldp = [s_lda, s_ldb]
    stp = [s_sta, s_stb]
    x_in, c_ident = env["x_in"], env["c_ident"]
    pre_scr = env["pre_scr"]
    wqkv_b, wgate_b, wba_in = env["wqkv_b"], env["wgate_b"], env["wba_in"]
    CONSTS = env["CONSTS"]
    from contextlib import ExitStack

    with ExitStack() as st1:
        E1 = st1.enter_context
        xtile = E1(nc.sbuf_tensor("xtile", [128, 2 * D], F32))
        xn_t = E1(nc.sbuf_tensor("xn_t", [128, D], BF16))
        xn_f = E1(nc.sbuf_tensor("xn_f", [128, NDB * TP], BF16))
        lnt = E1(nc.sbuf_tensor("lnt", [128, 32], F32))
        ln_stack = ExitStack()
        ps_ln0 = ln_stack.enter_context(nc.psum_tensor("ps_ln0", [128, D], F32))
        ps_ln1 = ln_stack.enter_context(nc.psum_tensor("ps_ln1", [128, D], F32))

        for tb in range(9):
            rows = 128 if tb < 8 else TP - 8 * 128
            xt = xtile[:rows, (tb % 2) * D:(tb % 2) * D + D]
            psl = ps_ln0 if tb % 2 == 0 else ps_ln1

            sl = ldp[tb % 2]

            @blk.sync
            def _(q, tb=tb, rows=rows, xt=xt, sl=sl):
                if tb >= 2:
                    q.wait_ge(s_s, ctr.get(s_s))
                q.wait_ge(sl, ctr.get(sl))
                q.dma_start(out=xt, in_=x_in[tb * 128: tb * 128 + rows, :]).then_inc(sl, 16)
                ctr.inc(sl, 16)
            XW = ctr.get(sl)

            @blk.vector
            def _(v, rows=rows, xt=xt, XW=XW, sl=sl):
                v.wait_ge(sl, XW)
                ins = None
                for i4 in range(4):
                    ins = v.bn_stats(lnt[:rows, i4 * 6:(i4 + 1) * 6], xt[:, i4 * 512:(i4 + 1) * 512])
                ctr.inc(s_v, 1); ins.then_inc(s_v, 1); v.wait_ge(s_v, ctr.get(s_v))
                ins = v.bn_aggr(lnt[:rows, 24:26], lnt[:rows, 0:24])
                ctr.inc(s_v, 1); ins.then_inc(s_v, 1); v.wait_ge(s_v, ctr.get(s_v))
                ctr.inc(s_v, 1)
                v.tensor_scalar(lnt[:rows, 26:27], lnt[:rows, 25:26], EPS_LN, None, ALU.add).then_inc(s_v, 1)
            V1 = ctr.get(s_v)

            @blk.scalar
            def _(s, rows=rows, V1=V1):
                s.wait_ge(s_v, V1)
                ctr.inc(s_s, 1)
                s.activation(lnt[:rows, 27:28], lnt[:rows, 26:27], AF.Sqrt).then_inc(s_s, 1)
            S1 = ctr.get(s_s)

            @blk.vector
            def _(v, rows=rows, S1=S1):
                v.wait_ge(s_s, S1)
                ins = v.reciprocal(lnt[:rows, 28:29], lnt[:rows, 27:28])
                ctr.inc(s_v, 1); ins.then_inc(s_v, 1); v.wait_ge(s_v, ctr.get(s_v))
                ctr.inc(s_v, 1)
                v.scalar_tensor_tensor(out=lnt[:rows, 29:30], in0=lnt[:rows, 24:25],
                                       scalar=-1.0, in1=lnt[:rows, 28:29],
                                       op0=ALU.mult, op1=ALU.mult).then_inc(s_v, 1)
            V2 = ctr.get(s_v)

            @blk.scalar
            def _(s, tb=tb, rows=rows, xt=xt, V2=V2):
                s.wait_ge(s_v, V2)
                if tb >= 1:
                    s.wait_ge(s_mm, ctr.get(s_mm))
                ctr.inc(s_s, 1)
                s.activation(xn_t[:rows, :], xt, AF.Identity,
                             bias=lnt[:rows, 29:30], scale=lnt[:rows, 28:29]).then_inc(s_s, 1)
            S2 = ctr.get(s_s)

            @blk.tensor
            def _(t, tb=tb, rows=rows, S2=S2, psl=psl):
                t.wait_ge(s_s, S2)
                t.wait_ge(s_ld, CONSTS)
                if tb >= 2:
                    t.wait_ge(s_s, ctr.get(s_s))
                mm = None
                for db in range(NDB):
                    mm = t.matmul(psl[:, db * 128:db * 128 + rows],
                                  xn_t[:rows, db * 128:(db + 1) * 128],
                                  c_ident[:rows, :rows], start=True, stop=True)
                ctr.inc(s_mm, 1)
                mm.then_inc(s_mm, 1)
            M1 = ctr.get(s_mm)

            @blk.scalar
            def _(s, tb=tb, rows=rows, M1=M1, psl=psl):
                s.wait_ge(s_mm, M1)
                ctr.inc(s_s, 1)
                s.copy(
                    xn_f.ap().rearrange("p (db c) -> p db c", db=NDB)[:, :, tb * 128:tb * 128 + rows],
                    psl.ap().rearrange("p (db c) -> p db c", db=NDB)[:, :, :rows],
                ).then_inc(s_s, 1)
        XNF = ctr.get(s_s)
        ln_stack.close()

        wf32 = E1(nc.sbuf_tensor("wf32", [128, 2 * NDB * 128], F32))
        wbf = E1(nc.sbuf_tensor("wbf", [128, 2 * NDB * 128], BF16))
        pre_sb = E1(nc.sbuf_tensor("pre_sb", [128, 2 * TP], BF16))
        ps_pa = [E1(nc.psum_tensor(f"ps_pa{i}", [128, TG], F32)) for i in range(6)]

        def wsrc(fb):
            if fb < NFQ:
                return wqkv_b[fb]
            if fb < NFQ + NFG:
                return wgate_b[fb - NFQ]
            return wba_in

        for fb in range(NFB):
            buf = fb % 2
            fcols = 128 if fb < NFB - 1 else 32
            wsl = wf32[:, buf * NDB * 128:][:, :NDB * fcols]
            wbl = wbf[:, buf * NDB * 128:][:, :NDB * fcols]
            psl = ps_pa[buf * 3:(buf + 1) * 3]
            prl = pre_sb[:fcols, buf * TP:buf * TP + TP]

            sl = ldp[fb % 2]

            @blk.sync
            def _(q, fb=fb, wsl=wsl, fcols=fcols, sl=sl):
                if fb == 0:
                    q.wait_ge(s_lda, ctr.get(s_lda))
                    q.wait_ge(s_ldb, ctr.get(s_ldb))
                if fb >= 2:
                    q.wait_ge(s_v, ctr.get(s_v))
                q.wait_ge(sl, ctr.get(sl))
                q.dma_start(out=wsl.rearrange("p (db f) -> p db f", f=fcols),
                            in_=wsrc(fb).rearrange("(db p) f -> p db f", p=128)).then_inc(sl, 16)
                ctr.inc(sl, 16)
            WL = ctr.get(sl)

            @blk.vector
            def _(v, fb=fb, wsl=wsl, wbl=wbl, WL=WL, sl=sl):
                v.wait_ge(sl, WL)
                if fb >= 2:
                    v.wait_ge(s_mm, ctr.get(s_mm))
                ctr.inc(s_v, 1)
                v.tensor_copy(wbl, wsl).then_inc(s_v, 1)
            WC = ctr.get(s_v)

            @blk.tensor
            def _(t, fb=fb, wbl=wbl, psl=psl, fcols=fcols, WC=WC):
                t.wait_ge(s_v, WC)
                t.wait_ge(s_s, XNF)
                if fb >= 2:
                    t.wait_ge(s_s, ctr.get(s_s))
                mm = None
                for tg in range(3):
                    for db in range(NDB):
                        mm = t.matmul(psl[tg][:fcols, :],
                                      wbl.rearrange("p (db f) -> p db f", db=NDB)[:, db, :],
                                      xn_f.ap().rearrange("p (db c) -> p db c", db=NDB)[:, db, tg * TG:(tg + 1) * TG],
                                      start=(db == 0), stop=(db == NDB - 1))
                ctr.inc(s_mm, 1)
                mm.then_inc(s_mm, 1)
            PM = ctr.get(s_mm)

            @blk.scalar
            def _(s, fb=fb, psl=psl, prl=prl, fcols=fcols, PM=PM):
                s.wait_ge(s_mm, PM)
                if fb >= 2:
                    s.wait_ge(stp[fb % 2], ctr.get(stp[fb % 2]))
                cp = None
                for tg in range(3):
                    cp = s.copy(prl[:, tg * TG:(tg + 1) * TG], psl[tg][:fcols, :])
                ctr.inc(s_s, 1)
                cp.then_inc(s_s, 1)
            PE = ctr.get(s_s)

            @blk.gpsimd
            def _(g, fb=fb, prl=prl, fcols=fcols, PE=PE):
                g.wait_ge(s_s, PE)
                g.wait_ge(stp[fb % 2], ctr.get(stp[fb % 2]))
                g.dma_start(out=pre_scr[fb, :fcols, :], in_=prl).then_inc(stp[fb % 2], 16)
                ctr.inc(stp[fb % 2], 16)


def _p15(nc, blk, ctr, sems, env):
    s_ld, s_st, s_mm, s_v, s_s = sems["ld"], sems["st"], sems["mm"], sems["v"], sems["s"]
    s_lda, s_ldb, s_sta, s_stb = sems["lda"], sems["ldb"], sems["sta"], sems["stb"]
    ldp = [s_lda, s_ldb]
    pre_scr, c_convw, c_ident, c_onec, c_onerf = (
        env["pre_scr"], env["c_convw"], env["c_ident"], env["c_onec"], env["c_onerf"])
    qk_f, k_tok, v_tok, barows, c_la, c_zero, c_cols = (
        env["qk_f"], env["k_tok"], env["v_tok"], env["barows"], env["c_la"],
        env["c_zero"], env["c_cols"])
    P1_STA = ctr.get(s_sta)
    P1_STB = ctr.get(s_stb)
    from contextlib import ExitStack

    with ExitStack() as st15:
        E1 = st15.enter_context
        pre_t = E1(nc.sbuf_tensor("pre_t", [128, 2 * TP], BF16))
        cv = E1(nc.sbuf_tensor("cv", [128, TPC], F32))
        sil = E1(nc.sbuf_tensor("sil", [128, TPC], F32))
        sqb = E1(nc.sbuf_tensor("sqb", [128, TPC], BF16))
        rs_row = E1(nc.sbuf_tensor("rs_row", [1, TPC], F32))
        rs_f32 = E1(nc.sbuf_tensor("rs_f32", [1, TPC], F32))
        vtile = E1(nc.sbuf_tensor("vtile", [128, TPC], BF16))
        ps_l2 = E1(nc.psum_tensor("ps_l2", [1, TPC], F32))
        ps_bc = E1(nc.psum_tensor("ps_bc", [128, TPC], F32))
        ps_tr = E1(nc.psum_tensor("ps_tr", [128, TPC], F32))

        for fb in range(NFQ):
            buf = fb % 2
            pt = pre_t[:, buf * TP:buf * TP + TP]
            is_q = fb < 8
            is_k = 8 <= fb < 16
            head = fb if is_q else (fb - 8 if is_k else fb - 16)

            sl = ldp[fb % 2]

            @blk.sync
            def _(q, fb=fb, pt=pt, sl=sl):
                if fb == 0:
                    q.wait_ge(s_sta, P1_STA)
                    q.wait_ge(s_stb, P1_STB)
                    q.wait_ge(s_lda, ctr.get(s_lda))
                    q.wait_ge(s_ldb, ctr.get(s_ldb))
                if fb >= 2:
                    q.wait_ge(s_v, ctr.get(s_v))
                q.wait_ge(sl, ctr.get(sl))
                q.dma_start(out=pt, in_=pre_scr[fb, :, :]).then_inc(sl, 16)
                ctr.inc(sl, 16)
            LD = ctr.get(sl)

            @blk.vector
            def _(v, fb=fb, pt=pt, LD=LD, sl=sl):
                v.wait_ge(sl, LD)
                if fb > 0:
                    v.wait_ge(s_s, ctr.get(s_s))
                ins = v.tensor_scalar(cv[:, :], pt[:, 0:TPC], c_convw[:, fb * 4:fb * 4 + 1], None, ALU.mult)
                for j in range(1, 4):
                    ctr.inc(s_v, 1); ins.then_inc(s_v, 1); v.wait_ge(s_v, ctr.get(s_v))
                    ins = v.scalar_tensor_tensor(
                        out=cv[:, :], in0=pt[:, j:j + TPC],
                        scalar=c_convw[:, fb * 4 + j:fb * 4 + j + 1], in1=cv[:, :],
                        op0=ALU.mult, op1=ALU.add)
                ctr.inc(s_v, 1)
                ins.then_inc(s_v, 1)
            CV = ctr.get(s_v)

            if is_q or is_k:
                @blk.scalar
                def _(s, CV=CV, fb=fb):
                    s.wait_ge(s_v, CV)
                    if fb >= 2:
                        s.wait_ge(s_v, ctr.get(s_v))
                    ins = s.activation(sil[:, :], cv[:, :], AF.Silu)
                    ctr.inc(s_s, 1); ins.then_inc(s_s, 1); s.wait_ge(s_s, ctr.get(s_s))
                    ctr.inc(s_s, 1)
                    s.activation(sqb[:, :], sil[:, :], AF.Square).then_inc(s_s, 1)
                SQ = ctr.get(s_s)

                @blk.tensor
                def _(t, SQ=SQ, fb=fb):
                    t.wait_ge(s_s, SQ)
                    if fb >= 1:
                        t.wait_ge(s_s, ctr.get(s_s))
                    mm = None
                    for half in range(2):
                        mm = t.matmul(ps_l2[0:1, half * 512:(half + 1) * 512], c_onec[:, :],
                                      sqb[:, half * 512:(half + 1) * 512], start=True, stop=True)
                    ctr.inc(s_mm, 1)
                    mm.then_inc(s_mm, 1)
                MMS = ctr.get(s_mm)

                @blk.scalar
                def _(s, MMS=MMS):
                    s.wait_ge(s_mm, MMS)
                    ctr.inc(s_s, 1)
                    s.activation(rs_f32[0:1, :], ps_l2[0:1, :], AF.Sqrt, bias=c_cols[0:1, 0:1]).then_inc(s_s, 1)
                RS = ctr.get(s_s)

                @blk.vector
                def _(v, RS=RS):
                    v.wait_ge(s_s, RS)
                    ctr.inc(s_v, 1)
                    v.reciprocal(rs_row[0:1, :], rs_f32[0:1, :]).then_inc(s_v, 1)
                RR = ctr.get(s_v)

                @blk.tensor
                def _(t, RR=RR, fb=fb):
                    t.wait_ge(s_v, RR)
                    if fb >= 1:
                        t.wait_ge(s_v, ctr.get(s_v))
                    mm = None
                    for half in range(2):
                        mm = t.matmul(ps_bc[:, half * 512:(half + 1) * 512],
                                      c_onerf[0:1, :],
                                      rs_row[0:1, half * 512:(half + 1) * 512], start=True, stop=True)
                    ctr.inc(s_mm, 1)
                    mm.then_inc(s_mm, 1)
                BC = ctr.get(s_mm)

                @blk.vector
                def _(v, fb=fb, head=head, is_q=is_q, BC=BC):
                    v.wait_ge(s_mm, BC)
                    dst = qk_f.ap()[:, (head + (0 if is_q else 8)) * TPC:][:, :TPC]
                    ctr.inc(s_v, 1)
                    v.tensor_tensor(dst, sil[:, :], ps_bc[:, :], ALU.mult).then_inc(s_v, 1)
                QN = ctr.get(s_v)

                if is_k:
                    @blk.tensor
                    def _(t, head=head, QN=QN, fb=fb):
                        t.wait_ge(s_v, QN)
                        if fb >= 9:
                            t.wait_ge(s_s, ctr.get(s_s))
                        src = qk_f.ap()[:, (head + 8) * TPC:][:, :TPC]
                        mm = None
                        for ci in range(NCH):
                            mm = t.matmul(ps_tr[:, ci * 128:(ci + 1) * 128],
                                          src[:, ci * 128:(ci + 1) * 128],
                                          c_ident[:, :], start=True, stop=True)
                        ctr.inc(s_mm, 1)
                        mm.then_inc(s_mm, 1)
                    KT = ctr.get(s_mm)

                    @blk.scalar
                    def _(s, head=head, KT=KT):
                        s.wait_ge(s_mm, KT)
                        ctr.inc(s_s, 1)
                        s.copy(k_tok.ap().rearrange("p (ci q s) -> p ci q s", q=HQ, s=128)[:, :, head, :],
                               ps_tr.ap().rearrange("p (ci s) -> p ci s", s=128)).then_inc(s_s, 1)
            else:
                @blk.scalar
                def _(s, CV=CV, fb=fb):
                    s.wait_ge(s_v, CV)
                    if fb >= 17:
                        s.wait_ge(s_mm, ctr.get(s_mm))
                    ctr.inc(s_s, 1)
                    s.activation(vtile[:, :], cv[:, :], AF.Silu).then_inc(s_s, 1)
                VS = ctr.get(s_s)

                @blk.tensor
                def _(t, head=head, VS=VS, fb=fb):
                    t.wait_ge(s_s, VS)
                    if fb >= 17:
                        t.wait_ge(s_s, ctr.get(s_s))
                    mm = None
                    for ci in range(NCH):
                        mm = t.matmul(ps_tr[:, ci * 128:(ci + 1) * 128],
                                      vtile[:, ci * 128:(ci + 1) * 128],
                                      c_ident[:, :], start=True, stop=True)
                    ctr.inc(s_mm, 1)
                    mm.then_inc(s_mm, 1)
                VT = ctr.get(s_mm)

                @blk.scalar
                def _(s, head=head, VT=VT):
                    s.wait_ge(s_mm, VT)
                    ctr.inc(s_s, 1)
                    s.copy(v_tok.ap().rearrange("p (ci hh s) -> p ci hh s", hh=HV, s=128)[:, :, head, :],
                           ps_tr.ap().rearrange("p (ci s) -> p ci s", s=128)).then_inc(s_s, 1)

        ba_t = E1(nc.sbuf_tensor("ba_t", [16, TPC], BF16))
        aa_t = E1(nc.sbuf_tensor("aa_t", [16, TPC], BF16))
        tmp16 = E1(nc.sbuf_tensor("tmp16", [HV, TPC], F32))

        @blk.sync
        def _(q):
            q.wait_ge(s_lda, ctr.get(s_lda))
            q.wait_ge(s_ldb, ctr.get(s_ldb))
            q.dma_start(out=ba_t[:, :], in_=pre_scr[NFB - 1, 0:16, HALO:HALO + TPC]).then_inc(s_lda, 16)
            q.dma_start(out=aa_t[:, :], in_=pre_scr[NFB - 1, 16:32, HALO:HALO + TPC]).then_inc(s_lda, 16)
            ctr.inc(s_lda, 32)
        BA = ctr.get(s_lda)

        @blk.scalar
        def _(s, BA=BA):
            s.wait_ge(s_lda, BA)
            s.wait_ge(s_v, ctr.get(s_v))
            ins = s.activation(barows[:, 0:TPC], ba_t[:, :], AF.Sigmoid)
            ctr.inc(s_s, 1); ins.then_inc(s_s, 1); s.wait_ge(s_s, ctr.get(s_s))
            s.activation(barows[:, TPC:2 * TPC], barows[:, 0:TPC], AF.Ln)
            ins = s.activation(tmp16[:, :], aa_t[:, :], AF.Sigmoid, scale=-1.0, bias=c_la[:, 5:6])
            ctr.inc(s_s, 1); ins.then_inc(s_s, 1); s.wait_ge(s_s, ctr.get(s_s))
            ctr.inc(s_s, 1)
            s.activation(tmp16[:, :], tmp16[:, :], AF.Ln).then_inc(s_s, 1)
        SB = ctr.get(s_s)

        @blk.vector
        def _(v, SB=SB):
            v.wait_ge(s_s, SB)
            ins = v.tensor_scalar(barows[:, 2 * TPC:3 * TPC], tmp16[:, :], c_la[:, 4:5], None, ALU.mult)
            ctr.inc(s_v, 1); ins.then_inc(s_v, 1); v.wait_ge(s_v, ctr.get(s_v))
            ins = None
            for ci in range(NCH):
                ins = v.tensor_tensor_scan(
                    out=barows[:, 3 * TPC + ci * 128:3 * TPC + (ci + 1) * 128],
                    data0=barows[:, 2 * TPC + ci * 128:2 * TPC + (ci + 1) * 128],
                    data1=barows[:, 2 * TPC + ci * 128:2 * TPC + (ci + 1) * 128],
                    initial=c_zero[0:HV, 0:1],
                    op0=ALU.add, op1=ALU.bypass)
            ctr.inc(s_v, 1)
            ins.then_inc(s_v, 1)


def _p2a(nc, blk, ctr, sems, env):
    s_mm, s_v, s_s, s_g = sems["mm"], sems["v"], sems["s"], sems["g"]
    s_sta, s_stb = sems["sta"], sems["stb"]
    stp = [s_sta, s_stb]
    qk_f, k_tok, v_tok, barows = env["qk_f"], env["k_tok"], env["v_tok"], env["barows"]
    compP, compR = env["compP"], env["compR"]
    c_ident, c_identf, c_onerf = env["c_ident"], env["c_identf"], env["c_onerf"]
    c_maskLT, c_maskUT = env["c_maskLT"], env["c_maskUT"]
    spill = env["spill"]
    from contextlib import ExitStack

    W = HV * 128

    def vx(ap3):
        return ap3

    with ExitStack() as sta_:
        E1 = sta_.enter_context
        mrow_all = E1(nc.sbuf_tensor("mrow_all", [HV, 128], F32))
        cols = E1(nc.sbuf_tensor("colsA", [128, 64], F32))
        rowstall = E1(nc.sbuf_tensor("rowstall", [1, 2 * W], F32))
        e_t = E1(nc.sbuf_tensor("e_t_all", [128, W], F32))
        e_tt = E1(nc.sbuf_tensor("e_tt_all", [128, W], F32))
        X = E1(nc.sbuf_tensor("X_all", [128, W], BF16))
        Xt = E1(nc.sbuf_tensor("Xt_all", [128, W], BF16))
        NT = E1(nc.sbuf_tensor("NT_all", [128, W], BF16))
        Hb = E1(nc.sbuf_tensor("Hb_all", [128, W], BF16))
        G_sb = E1(nc.sbuf_tensor("G_all", [128, HQ * 128], BF16))
        bVt = E1(nc.sbuf_tensor("bVt_all", [128, W], BF16))
        bAKt = E1(nc.sbuf_tensor("bAKt_all", [128, W], BF16))
        kht = E1(nc.sbuf_tensor("kht_all", [128, W], BF16))
        U0 = E1(nc.sbuf_tensor("U0_all", [128, W], BF16))
        Yc = E1(nc.sbuf_tensor("Yc_all", [128, W], BF16))
        Rsb = E1(nc.sbuf_tensor("Rsb_all", [128, W], BF16))
        PT = E1(nc.sbuf_tensor("PT_all", [128, W], BF16))
        acb = E1(nc.sbuf_tensor("acbA", [128, 48], F32))
        ps_A = E1(nc.psum_tensor("ps_A", [128, W], F32))
        ps_B = E1(nc.psum_tensor("ps_B", [128, W], F32))

        ROWS_V = ctr.get(s_v)

        def gview(ap, wslice=128):
            return ap.rearrange("p (q s) -> p q s", s=128).unsqueeze(2).broadcast_to((128, HQ, 2, wslice))

        def colview(ap2):
            return ap2.unsqueeze(2).broadcast_to((128, HV, 128))

        def mview(ap2):
            return ap2.unsqueeze(1).broadcast_to((128, HV, 128))

        def w3(ap):
            return ap.rearrange("p (h s) -> p h s", s=128)

        for ci in range(NCH):
            ell_r = barows[:, 3 * TPC + ci * 128:3 * TPC + (ci + 1) * 128]
            lb_r = barows[:, TPC + ci * 128:TPC + (ci + 1) * 128]
            beta_r = barows[:, ci * 128:(ci + 1) * 128]
            kslab = qk_f
            kf_chunk = lambda hq: qk_f.ap()[:, (8 + hq) * TPC + ci * 128:][:, :128]
            ktok_chunk = k_tok.ap()[:, ci * HQ * 128:(ci + 1) * HQ * 128]
            vtok_chunk = v_tok.ap()[:, ci * W:(ci + 1) * W]

            @blk.vector
            def _(v, ell_r=ell_r, lb_r=lb_r, ci=ci):
                v.wait_ge(s_v, ROWS_V)
                if ci > 0:
                    v.wait_ge(s_mm, ctr.get(s_mm))
                ctr.inc(s_v, 1)
                v.tensor_tensor(mrow_all[:, :], ell_r, lb_r, ALU.add).then_inc(s_v, 1)
            MR = ctr.get(s_v)

            @blk.tensor
            def _(t, ell_r=ell_r, beta_r=beta_r, MR=MR, ci=ci):
                t.wait_ge(s_v, MR)
                if ci > 0:
                    t.wait_ge(s_s, ctr.get(s_s))
                    t.wait_ge(s_g, ctr.get(s_g))
                t.matmul(ps_A[:, 0:16], ell_r, c_identf[0:HV, 0:HV], start=True, stop=True)
                t.matmul(ps_A[:, 16:32], mrow_all[:, :], c_identf[0:HV, 0:HV], start=True, stop=True)
                ctr.inc(s_mm, 1)
                t.matmul(ps_A[:, 32:48], beta_r, c_identf[0:HV, 0:HV], start=True, stop=True).then_inc(s_mm, 1)
            CT = ctr.get(s_mm)

            @blk.scalar
            def _(s, CT=CT):
                s.wait_ge(s_mm, CT)
                ctr.inc(s_s, 1)
                s.copy(cols[:, 0:48], ps_A[:, 0:48]).then_inc(s_s, 1)
            COLS = ctr.get(s_s)

            @blk.tensor
            def _(t, COLS=COLS, ci=ci):
                t.wait_ge(s_s, COLS)
                if ci > 0:
                    t.wait_ge(s_v, ctr.get(s_v))
                mm = None
                for h in range(HV):
                    mm = t.matmul(ps_A[0:1, h * 128:(h + 1) * 128], cols[:, h:h + 1],
                                  c_identf[:, :], start=True, stop=True)
                    mm = t.matmul(ps_B[0:1, h * 128:(h + 1) * 128], cols[:, 16 + h:16 + h + 1],
                                  c_identf[:, :], start=True, stop=True)
                ctr.inc(s_mm, 1)
                mm.then_inc(s_mm, 1)
            RWM = ctr.get(s_mm)

            @blk.scalar
            def _(s, RWM=RWM):
                s.wait_ge(s_mm, RWM)
                s.copy(rowstall[0:1, 0:W], ps_A[0:1, 0:W])
                ctr.inc(s_s, 1)
                s.copy(rowstall[0:1, W:2 * W], ps_B[0:1, 0:W]).then_inc(s_s, 1)
            RWS = ctr.get(s_s)

            @blk.tensor
            def _(t, RWS=RWS):
                t.wait_ge(s_s, RWS)
                mm = None
                for qq in range(4):
                    sl = slice(qq * 512, (qq + 1) * 512)
                    t.matmul(ps_A[:, sl], c_onerf[0:1, :], rowstall[0:1, sl], start=True, stop=True)
                    mm = t.matmul(ps_B[:, sl], c_onerf[0:1, :], rowstall[0:1, W + qq * 512:W + (qq + 1) * 512], start=True, stop=True)
                ctr.inc(s_mm, 1)
                mm.then_inc(s_mm, 1)
            BC = ctr.get(s_mm)

            @blk.vector
            def _(v, BC=BC):
                v.wait_ge(s_mm, BC)
                ins = v.tensor_tensor(w3(e_t.ap()), colview(cols[:, 16:32]), w3(ps_A.ap()[:, 0:W]), ALU.subtract)
                ins = v.tensor_tensor(w3(e_tt.ap()), w3(ps_B.ap()[:, 0:W]), colview(cols[:, 0:16]), ALU.subtract)
                ins = v.tensor_tensor(acb[:, 16:32], ps_A.ap()[:, 127::128], cols[:, 0:16], ALU.subtract)
                ins = v.tensor_copy(acb[:, 0:16], ps_A.ap()[:, 127::128])
                ctr.inc(s_v, 1); ins.then_inc(s_v, 1); v.wait_ge(s_v, ctr.get(s_v))
                v.tensor_tensor(w3(e_t.ap()), w3(e_t.ap()), mview(c_maskLT.ap()), ALU.add)
                ctr.inc(s_v, 1)
                v.tensor_tensor(w3(e_tt.ap()), w3(e_tt.ap()), mview(c_maskUT.ap()), ALU.add).then_inc(s_v, 1)
            EA = ctr.get(s_v)

            @blk.scalar
            def _(s, EA=EA):
                s.wait_ge(s_v, EA)
                s.activation(e_t[:, :], e_t[:, :], AF.Exp)
                s.activation(e_tt[:, :], e_tt[:, :], AF.Exp)
                s.activation(acb[:, 32:48], cols[:, 16:32], AF.Exp)
                ins = s.activation(acb[:, 16:32], acb[:, 16:32], AF.Exp)
                ctr.inc(s_s, 1); ins.then_inc(s_s, 1); s.wait_ge(s_s, ctr.get(s_s))
                ctr.inc(s_s, 1)
                s.activation(acb[:, 0:16], acb[:, 0:16], AF.Exp).then_inc(s_s, 1)
            EX = ctr.get(s_s)

            @blk.tensor
            def _(t, kf_chunk=kf_chunk, EX=EX, ci=ci):
                t.wait_ge(s_s, EX)
                if ci > 0:
                    t.wait_ge(s_v, ctr.get(s_v))
                mm = None
                for hq in range(HQ):
                    mm = t.matmul(ps_A[:, hq * 128:(hq + 1) * 128], kf_chunk(hq), kf_chunk(hq),
                                  start=True, stop=True)
                ctr.inc(s_mm, 1)
                mm.then_inc(s_mm, 1)
            GM = ctr.get(s_mm)

            @blk.scalar
            def _(s, GM=GM):
                s.wait_ge(s_mm, GM)
                ctr.inc(s_s, 1)
                s.copy(G_sb[:, :], ps_A[:, 0:HQ * 128]).then_inc(s_s, 1)
            GE = ctr.get(s_s)

            @blk.vector
            def _(v, GE=GE):
                v.wait_ge(s_s, GE)
                ins = v.tensor_tensor(w3(e_t.ap()).rearrange("p (q r) s -> p q r s", r=2),
                                      gview(G_sb.ap()), w3(e_t.ap()).rearrange("p (q r) s -> p q r s", r=2),
                                      ALU.mult)
                ins = v.tensor_tensor(w3(e_tt.ap()).rearrange("p (q r) s -> p q r s", r=2),
                                      gview(G_sb.ap()), w3(e_tt.ap()).rearrange("p (q r) s -> p q r s", r=2),
                                      ALU.mult)
                ctr.inc(s_v, 1); ins.then_inc(s_v, 1); v.wait_ge(s_v, ctr.get(s_v))
                v.tensor_tensor(w3(X.ap()), mview(c_identf.ap()), w3(e_t.ap()), ALU.subtract)
                v.tensor_tensor(w3(Xt.ap()), mview(c_identf.ap()), w3(e_tt.ap()), ALU.subtract)
                ctr.inc(s_v, 1)
                v.tensor_tensor(w3(NT.ap()), mview(c_identf.ap()), w3(e_tt.ap()), ALU.add).then_inc(s_v, 1)
            TX = ctr.get(s_v)

            last_v = TX
            last_s = ctr.get(s_s)
            for it in range(NEWTON_ITERS):
                @blk.tensor
                def _(t, it=it, last_v=last_v, last_s=last_s):
                    t.wait_ge(s_v, last_v)
                    t.wait_ge(s_s, last_s)
                    mm = None
                    for h in range(HV):
                        mm = t.matmul(ps_A[:, h * 128:(h + 1) * 128], NT[:, h * 128:(h + 1) * 128],
                                      X[:, h * 128:(h + 1) * 128], start=True, stop=True)
                    ctr.inc(s_mm, 1)
                    mm.then_inc(s_mm, 1)
                HM = ctr.get(s_mm)

                @blk.scalar
                def _(s, HM=HM):
                    s.wait_ge(s_mm, HM)
                    ctr.inc(s_s, 1)
                    s.copy(Hb[:, :], ps_A[:, :]).then_inc(s_s, 1)
                HE = ctr.get(s_s)

                @blk.tensor
                def _(t, HE=HE):
                    t.wait_ge(s_s, HE)
                    mm = None
                    for h in range(HV):
                        mm = t.matmul(ps_B[:, h * 128:(h + 1) * 128], Xt[:, h * 128:(h + 1) * 128],
                                      Hb[:, h * 128:(h + 1) * 128], start=True, stop=True)
                    ctr.inc(s_mm, 1)
                    mm.then_inc(s_mm, 1)
                XHM = ctr.get(s_mm)

                @blk.vector
                def _(v, XHM=XHM):
                    v.wait_ge(s_mm, XHM)
                    ctr.inc(s_v, 1)
                    v.scalar_tensor_tensor(out=X[:, :], in0=X[:, :], scalar=2.0,
                                           in1=ps_B[:, :], op0=ALU.mult,
                                           op1=ALU.subtract).then_inc(s_v, 1)
                XN = ctr.get(s_v)

                @blk.tensor
                def _(t, XN=XN):
                    t.wait_ge(s_v, XN)
                    mm = None
                    for h in range(HV):
                        mm = t.matmul(ps_A[:, h * 128:(h + 1) * 128], X[:, h * 128:(h + 1) * 128],
                                      c_ident[:, :], start=True, stop=True)
                    ctr.inc(s_mm, 1)
                    mm.then_inc(s_mm, 1)
                XTM = ctr.get(s_mm)

                @blk.scalar
                def _(s, XTM=XTM):
                    s.wait_ge(s_mm, XTM)
                    ctr.inc(s_s, 1)
                    s.copy(Xt[:, :], ps_A[:, :]).then_inc(s_s, 1)
                last_s = ctr.get(s_s)
                last_v = XN

            @blk.vector
            def _(v, vtok_chunk=vtok_chunk, ktok_chunk=ktok_chunk, last_s=last_s):
                v.wait_ge(s_s, last_s)
                v.tensor_tensor(w3(bVt.ap()), w3(vtok_chunk), colview(cols[:, 32:48]), ALU.mult)
                v.tensor_tensor(w3(bAKt.ap()).rearrange("p (q r) s -> p q r s", r=2),
                                gview(ktok_chunk), colview(acb[:, 32:48]).rearrange("p (q r) s -> p q r s", r=2),
                                ALU.mult)
                ctr.inc(s_v, 1)
                v.tensor_tensor(w3(kht.ap()).rearrange("p (q r) s -> p q r s", r=2),
                                gview(ktok_chunk), colview(acb[:, 16:32]).rearrange("p (q r) s -> p q r s", r=2),
                                ALU.mult).then_inc(s_v, 1)
            OPS = ctr.get(s_v)

            @blk.tensor
            def _(t, OPS=OPS):
                t.wait_ge(s_v, OPS)
                mm = None
                for h in range(HV):
                    sl = slice(h * 128, (h + 1) * 128)
                    t.matmul(ps_A[:, sl], Xt[:, sl], bVt[:, sl], start=True, stop=True)
                    mm = t.matmul(ps_B[:, sl], Xt[:, sl], bAKt[:, sl], start=True, stop=True)
                ctr.inc(s_mm, 1)
                mm.then_inc(s_mm, 1)
            SOL = ctr.get(s_mm)

            @blk.scalar
            def _(s, SOL=SOL):
                s.wait_ge(s_mm, SOL)
                s.wait_ge(s_sta, ctr.get(s_sta))
                s.wait_ge(s_stb, ctr.get(s_stb))
                s.copy(U0[:, :], ps_A[:, :])
                ctr.inc(s_s, 1)
                s.copy(Yc[:, :], ps_B[:, :]).then_inc(s_s, 1)
            UY = ctr.get(s_s)

            @blk.gpsimd
            def _(g, ci=ci, UY=UY):
                g.wait_ge(s_s, UY)
                g.wait_ge(stp[ci % 2], ctr.get(stp[ci % 2]))
                g.dma_start(out=spill[ci, 0], in_=U0[:, :]).then_inc(stp[ci % 2], 16)
                g.dma_start(out=spill[ci, 1], in_=Yc[:, :]).then_inc(stp[ci % 2], 16)
                ctr.inc(stp[ci % 2], 32)

            @blk.tensor
            def _(t, UY=UY):
                t.wait_ge(s_s, UY)
                mm = None
                for h in range(HV):
                    sl = slice(h * 128, (h + 1) * 128)
                    t.matmul(ps_A[:, sl], kht[:, sl], U0[:, sl], start=True, stop=True)
                    mm = t.matmul(ps_B[:, sl], Yc[:, sl], kht[:, sl], start=True, stop=True)
                ctr.inc(s_mm, 1)
                mm.then_inc(s_mm, 1)
            RP = ctr.get(s_mm)

            @blk.vector
            def _(v, RP=RP):
                v.wait_ge(s_mm, RP)
                v.wait_ge(s_sta, ctr.get(s_sta))
                v.wait_ge(s_stb, ctr.get(s_stb))
                v.tensor_copy(Rsb[:, :], ps_A[:, :])
                ins = v.tensor_tensor(w3(e_t.ap()), mview(c_identf.ap()), colview(acb[:, 0:16]), ALU.mult)
                ctr.inc(s_v, 1); ins.then_inc(s_v, 1); v.wait_ge(s_v, ctr.get(s_v))
                ctr.inc(s_v, 1)
                v.tensor_tensor(PT[:, :], e_t[:, :], ps_B[:, :], ALU.subtract).then_inc(s_v, 1)
            PRV = ctr.get(s_v)

            @blk.gpsimd
            def _(g, ci=ci, PRV=PRV):
                g.wait_ge(s_v, PRV)
                g.wait_ge(stp[ci % 2], ctr.get(stp[ci % 2]))
                g.dma_start(out=spill[ci, 2], in_=PT[:, :]).then_inc(stp[ci % 2], 16)
                g.dma_start(out=spill[ci, 3], in_=Rsb[:, :]).then_inc(stp[ci % 2], 16)
                ctr.inc(stp[ci % 2], 32)

            if ci == 0:
                @blk.vector
                def _(v, PRV=PRV):
                    v.wait_ge(s_v, PRV)
                    v.tensor_copy(compP.ap()[:, :], PT[:, :])
                    ctr.inc(s_v, 1)
                    v.tensor_copy(compR.ap()[:, :], Rsb[:, :]).then_inc(s_v, 1)
            else:
                @blk.tensor
                def _(t, PRV=PRV):
                    t.wait_ge(s_v, PRV)
                    mm = None
                    for h in range(HV):
                        sl = slice(h * 128, (h + 1) * 128)
                        t.matmul(ps_A[:, sl], PT[:, sl], compP.ap()[:, sl], start=True, stop=True)
                        mm = t.matmul(ps_B[:, sl], PT[:, sl], compR.ap()[:, sl], start=True, stop=True)
                    ctr.inc(s_mm, 1)
                    mm.then_inc(s_mm, 1)
                CM = ctr.get(s_mm)

                @blk.vector
                def _(v, CM=CM):
                    v.wait_ge(s_mm, CM)
                    v.tensor_copy(compP.ap()[:, :], ps_A[:, :])
                    ctr.inc(s_v, 1)
                    v.tensor_tensor(compR.ap()[:, :], ps_B[:, :], Rsb[:, :], ALU.add).then_inc(s_v, 1)


def _cc_hop(nc, blk, ctr, sems, env):
    s_ld, s_st, s_mm, s_v, s_s, s_g, s_cc = (
        sems["ld"], sems["st"], sems["mm"], sems["v"], sems["s"], sems["g"], sems["cc"])
    s_lda, s_ldb = sems["lda"], sems["ldb"]
    ldp = [s_lda, s_ldb]
    compP, compR, state = env["compP"], env["compR"], env["state"]
    c_ident, c_hopm, c_hopimi = env["c_ident"], env["c_hopm"], env["c_hopimi"]
    cc_in, cc_out = env["cc_in"], env["cc_out"]
    from contextlib import ExitStack

    _phase_barrier(nc, blk, ctr, sems)
    with ExitStack() as stc:
        E1 = stc.enter_context
        cPt = E1(nc.sbuf_tensor("cPt", [128, HV * 128], BF16))
        hopbuf = E1(nc.sbuf_tensor("hopbuf", [128, 2 * HV * 2 * 128], BF16))
        Pm = E1(nc.sbuf_tensor("Pm", [128, 128], BF16))
        ps_a = E1(nc.psum_tensor("ps_hop", [128, 128], F32))
        CPD = ctr.get(s_v)

        for h in range(HV):
            @blk.tensor
            def _(t, h=h, CPD=CPD):
                t.wait_ge(s_v, CPD)
                if h > 0:
                    t.wait_ge(s_s, ctr.get(s_s))
                ctr.inc(s_mm, 1)
                t.matmul(ps_a[:, :], compP.ap()[:, h * 128:(h + 1) * 128],
                         c_ident[:, :], start=True, stop=True).then_inc(s_mm, 1)
            TM = ctr.get(s_mm)

            @blk.scalar
            def _(s, h=h, TM=TM):
                s.wait_ge(s_mm, TM)
                ctr.inc(s_s, 1)
                s.copy(cPt[:, h * 128:(h + 1) * 128], ps_a[:, :]).then_inc(s_s, 1)
        CPT = ctr.get(s_s)

        @blk.sync
        def _(q, CPT=CPT):
            q.wait_ge(s_s, CPT)
            q.wait_ge(s_lda, ctr.get(s_lda))
            q.wait_ge(s_ldb, ctr.get(s_ldb))
            q.dma_start(out=cc_in[:, 0:HV * 128], in_=cPt[:, :]).then_inc(s_lda, 16)
            q.dma_start(out=cc_in[:, HV * 128:], in_=compR[:, :]).then_inc(s_lda, 16)
            ctr.inc(s_lda, 32)
        CCI = ctr.get(s_lda)

        @blk.gpsimd
        def _(g, CCI=CCI):
            g.wait_ge(s_lda, CCI)
            ctr.inc(s_cc, 1)
            g.collective_compute(
                "AllGather", ALU.bypass,
                ins=[cc_in[:, :]], outs=[cc_out[:, :]],
                replica_groups=[list(range(8))],
            ).then_inc(s_cc)
            ctr.inc(s_g, 1)
            g.memset(state[:, :], 0.0).then_inc(s_g, 1)
        STZ = ctr.get(s_g)

        for j in range(7):
            sl = ldp[j % 2]

            @blk.sync
            def _(q, j=j, sl=sl):
                q.wait_ge(s_cc, ctr.get(s_cc))
                if j >= 2:
                    q.wait_ge(s_v, ctr.get(s_v))
                q.wait_ge(sl, ctr.get(sl))
                q.dma_start(out=hopbuf[:, (j % 2) * HV * 256:(j % 2 + 1) * HV * 256],
                            in_=cc_out[j * 128:(j + 1) * 128, :]).then_inc(sl, 16)
                ctr.inc(sl, 16)
            HL = ctr.get(sl)

            for h in range(HV):
                pj = hopbuf[:, (j % 2) * HV * 256 + h * 128:][:, :128]
                rj = hopbuf[:, (j % 2) * HV * 256 + HV * 128 + h * 128:][:, :128]
                sh = state.ap()[:, h * 128:(h + 1) * 128]

                @blk.vector
                def _(v, pj=pj, HL=HL, j=j, h=h, STZ=STZ, sl=sl):
                    v.wait_ge(sl, HL)
                    v.wait_ge(s_g, STZ)
                    if j > 0 or h > 0:
                        v.wait_ge(s_mm, ctr.get(s_mm))
                    ctr.inc(s_v, 1)
                    v.scalar_tensor_tensor(out=Pm[:, :], in0=pj,
                                           scalar=c_hopm[:, j:j + 1],
                                           in1=c_hopimi[:, j * 128:(j + 1) * 128],
                                           op0=ALU.mult, op1=ALU.add).then_inc(s_v, 1)
                PMV = ctr.get(s_v)

                @blk.tensor
                def _(t, sh=sh, PMV=PMV):
                    t.wait_ge(s_v, PMV)
                    ctr.inc(s_mm, 1)
                    t.matmul(ps_a[:, :], Pm[:, :], sh, start=True, stop=True).then_inc(s_mm, 1)
                SMM = ctr.get(s_mm)

                @blk.vector
                def _(v, rj=rj, sh=sh, j=j, SMM=SMM):
                    v.wait_ge(s_mm, SMM)
                    ctr.inc(s_v, 1)
                    v.scalar_tensor_tensor(out=sh, in0=rj, scalar=c_hopm[:, j:j + 1],
                                           in1=ps_a[:, :], op0=ALU.mult,
                                           op1=ALU.add).then_inc(s_v, 1)


def _p2b(nc, blk, ctr, sems, env):
    s_mm, s_v, s_s, s_g = sems["mm"], sems["v"], sems["s"], sems["g"]
    s_lda, s_ldb, s_sta, s_stb = sems["lda"], sems["ldb"], sems["sta"], sems["stb"]
    ldp = [s_lda, s_ldb]
    stp = [s_sta, s_stb]
    qk_f, barows, state = env["qk_f"], env["barows"], env["state"]
    c_ident, c_identf, c_maskUD, c_rmsw, c_onerf = (
        env["c_ident"], env["c_identf"], env["c_maskUD"], env["c_rmsw"], env["c_onerf"])
    spill, pre_scr, p_scr = env["spill"], env["pre_scr"], env["p_scr"]
    from contextlib import ExitStack

    W = HV * 128

    _phase_barrier(nc, blk, ctr, sems)
    with ExitStack() as stb_:
        E1 = stb_.enter_context
        cols = E1(nc.sbuf_tensor("colsB", [128, 16], F32))
        rowstall = E1(nc.sbuf_tensor("rowstallB", [1, W], F32))
        e_m = E1(nc.sbuf_tensor("e_mB", [128, W], F32))
        qa = E1(nc.sbuf_tensor("qaB", [128, W], BF16))
        MT = E1(nc.sbuf_tensor("MTB", [128, W], BF16))
        qAf = E1(nc.sbuf_tensor("qAfB", [128, W], BF16))
        nMYT = E1(nc.sbuf_tensor("nMYTB", [128, W], BF16))
        U0 = E1(nc.sbuf_tensor("U0B", [128, W], BF16))
        Yc = E1(nc.sbuf_tensor("YcB", [128, W], BF16))
        PTl = E1(nc.sbuf_tensor("PTlB", [128, W], BF16))
        Rl = E1(nc.sbuf_tensor("RlB", [128, W], BF16))
        gt = E1(nc.sbuf_tensor("gtB", [128, 2 * W], BF16))
        Ofp = E1(nc.sbuf_tensor("OfpB", [128, W], F32))
        Onsb = E1(nc.sbuf_tensor("OnsbB", [128, W], BF16))
        gsil = E1(nc.sbuf_tensor("gsilB", [128, W], BF16))
        ptile = E1(nc.sbuf_tensor("ptileB", [128, 2 * W], BF16))
        rcol = E1(nc.sbuf_tensor("rcolB", [128, 16], F32))
        ps_A = E1(nc.psum_tensor("ps_AB", [128, W], F32))
        ps_B = E1(nc.psum_tensor("ps_BB", [128, W], F32))

        def colview(ap2):
            return ap2.unsqueeze(2).broadcast_to((128, HV, 128))

        def mview(ap2):
            return ap2.unsqueeze(1).broadcast_to((128, HV, 128))

        def gview(ap):
            return ap.rearrange("p (q s) -> p q s", s=128).unsqueeze(2).broadcast_to((128, HQ, 2, 128))

        def w3(ap):
            return ap.rearrange("p (h s) -> p h s", s=128)

        for ci in range(NCH):
            buf = ci % 2
            ell_r = barows[:, 3 * TPC + ci * 128:3 * TPC + (ci + 1) * 128]
            kf_chunk = lambda hq, ci=ci: qk_f.ap()[:, (8 + hq) * TPC + ci * 128:][:, :128]
            qf_chunk = lambda hq, ci=ci: qk_f.ap()[:, hq * TPC + ci * 128:][:, :128]
            u0 = U0[:, 0:W]
            yc = Yc[:, 0:W]
            ptl = PTl[:, 0:W]
            rtl = Rl[:, 0:W]
            gtl = gt[:, buf * W:buf * W + W]
            pt = ptile[:, buf * W:buf * W + W]

            @blk.sync
            def _(q, ci=ci, buf=buf, u0=u0, yc=yc, ptl=ptl, rtl=rtl, gtl=gtl):
                if ci < 2:
                    q.wait_ge(s_sta, ctr.get(s_sta))
                    q.wait_ge(s_stb, ctr.get(s_stb))
                q.wait_ge(s_mm, ctr.get(s_mm))
                q.wait_ge(s_s, ctr.get(s_s))
                q.wait_ge(s_v, ctr.get(s_v))
                sl = ldp[buf]
                q.wait_ge(sl, ctr.get(sl))
                q.dma_start(out=u0, in_=spill[ci, 0]).then_inc(sl, 16)
                q.dma_start(out=yc, in_=spill[ci, 1]).then_inc(sl, 16)
                q.dma_start(out=ptl, in_=spill[ci, 2]).then_inc(sl, 16)
                q.dma_start(out=rtl, in_=spill[ci, 3]).then_inc(sl, 16)
                q.dma_start(out=gtl.rearrange("p (h s) -> p h s", s=128),
                            in_=pre_scr[NFQ:NFQ + HV, :, HALO + ci * 128:HALO + (ci + 1) * 128].rearrange("h p s -> p h s")).then_inc(sl, 16)
                ctr.inc(ldp[buf], 80)
            LDS = ctr.get(ldp[buf])

            @blk.tensor
            def _(t, ell_r=ell_r, ci=ci):
                t.wait_ge(s_s, ctr.get(s_s))
                t.wait_ge(s_v, ctr.get(s_v))
                t.wait_ge(s_g, ctr.get(s_g))
                ctr.inc(s_mm, 1)
                t.matmul(ps_A[:, 0:16], ell_r, c_identf[0:HV, 0:HV], start=True, stop=True).then_inc(s_mm, 1)
            CT = ctr.get(s_mm)

            @blk.scalar
            def _(s, CT=CT):
                s.wait_ge(s_mm, CT)
                ctr.inc(s_s, 1)
                s.copy(cols[:, 0:16], ps_A[:, 0:16]).then_inc(s_s, 1)
            COLS = ctr.get(s_s)

            @blk.tensor
            def _(t, COLS=COLS):
                t.wait_ge(s_s, COLS)
                mm = None
                for h in range(HV):
                    mm = t.matmul(ps_B[0:1, h * 128:(h + 1) * 128], cols[:, h:h + 1],
                                  c_identf[:, :], start=True, stop=True)
                ctr.inc(s_mm, 1)
                mm.then_inc(s_mm, 1)
            RWM = ctr.get(s_mm)

            @blk.scalar
            def _(s, RWM=RWM):
                s.wait_ge(s_mm, RWM)
                ctr.inc(s_s, 1)
                s.copy(rowstall[0:1, 0:W], ps_B[0:1, 0:W]).then_inc(s_s, 1)
            RWS = ctr.get(s_s)

            @blk.tensor
            def _(t, RWS=RWS):
                t.wait_ge(s_s, RWS)
                mm = None
                for qq in range(4):
                    sl = slice(qq * 512, (qq + 1) * 512)
                    mm = t.matmul(ps_A[:, sl], c_onerf[0:1, :], rowstall[0:1, sl], start=True, stop=True)
                ctr.inc(s_mm, 1)
                mm.then_inc(s_mm, 1)
            BC = ctr.get(s_mm)

            @blk.vector
            def _(v, BC=BC):
                v.wait_ge(s_mm, BC)
                ins = v.tensor_tensor(w3(e_m.ap()), w3(ps_A.ap()[:, 0:W]), colview(cols[:, 0:16]), ALU.subtract)
                ctr.inc(s_v, 1); ins.then_inc(s_v, 1); v.wait_ge(s_v, ctr.get(s_v))
                ctr.inc(s_v, 1)
                v.tensor_tensor(w3(e_m.ap()), w3(e_m.ap()), mview(c_maskUD.ap()), ALU.add).then_inc(s_v, 1)
            EA = ctr.get(s_v)

            @blk.scalar
            def _(s, EA=EA):
                s.wait_ge(s_v, EA)
                ins = s.activation(qa[:, :], ps_A[:, 0:W], AF.Exp, bias=env["c_cols"][:, 2:3])
                ctr.inc(s_s, 1); ins.then_inc(s_s, 1); s.wait_ge(s_s, ctr.get(s_s))
                ctr.inc(s_s, 1)
                s.activation(e_m[:, :], e_m[:, :], AF.Exp).then_inc(s_s, 1)
            EX = ctr.get(s_s)

            @blk.tensor
            def _(t, kf_chunk=kf_chunk, qf_chunk=qf_chunk, EX=EX, ci=ci):
                t.wait_ge(s_s, EX)
                mm = None
                for hq in range(HQ):
                    mm = t.matmul(ps_A[:, hq * 128:(hq + 1) * 128], kf_chunk(hq), qf_chunk(hq),
                                  start=True, stop=True)
                ctr.inc(s_mm, 1)
                mm.then_inc(s_mm, 1)
            GM = ctr.get(s_mm)

            @blk.vector
            def _(v, GM=GM, ci=ci):
                v.wait_ge(s_mm, GM)
                ins = v.tensor_tensor(
                    w3(MT.ap()).rearrange("p (q r) s -> p q r s", r=2),
                    ps_A.ap()[:, 0:HQ * 128].rearrange("p (q s) -> p q s", s=128).unsqueeze(2).broadcast_to((128, HQ, 2, 128)),
                    w3(e_m.ap()).rearrange("p (q r) s -> p q r s", r=2), ALU.mult)
                qsl = qk_f.ap()[:, ci * 128:]
                ctr.inc(s_v, 1)
                v.tensor_tensor(
                    w3(qAf.ap()).rearrange("p (q r) s -> p q r s", r=2),
                    _qview_impl(qk_f, ci),
                    w3(qa.ap()).rearrange("p (q r) s -> p q r s", r=2), ALU.mult).then_inc(s_v, 1)
            MTV = ctr.get(s_v)

            @blk.tensor
            def _(t, MTV=MTV, yc=yc, LDS=LDS, buf=buf):
                t.wait_ge(s_v, MTV)
                t.wait_ge(ldp[buf], LDS)
                mm = None
                for h in range(HV):
                    sl = slice(h * 128, (h + 1) * 128)
                    mm = t.matmul(ps_B[:, sl], yc[:, sl], MT[:, sl], start=True, stop=True)
                ctr.inc(s_mm, 1)
                mm.then_inc(s_mm, 1)
            MY = ctr.get(s_mm)

            @blk.scalar
            def _(s, MY=MY):
                s.wait_ge(s_mm, MY)
                ctr.inc(s_s, 1)
                s.activation(nMYT[:, :], ps_B[:, :], AF.Identity, scale=-1.0).then_inc(s_s, 1)
            NM = ctr.get(s_s)

            @blk.tensor
            def _(t, NM=NM, u0=u0):
                t.wait_ge(s_s, NM)
                mm = None
                for h in range(HV):
                    sl = slice(h * 128, (h + 1) * 128)
                    t.matmul(ps_B[:, sl], MT[:, sl], u0[:, sl], start=True, stop=False)
                    t.matmul(ps_B[:, sl], qAf[:, sl], state.ap()[:, sl], start=False, stop=False)
                    mm = t.matmul(ps_B[:, sl], nMYT[:, sl], state.ap()[:, sl], start=False, stop=True)
                ctr.inc(s_mm, 1)
                mm.then_inc(s_mm, 1)
            OM = ctr.get(s_mm)

            @blk.scalar
            def _(s, OM=OM):
                s.wait_ge(s_mm, OM)
                s.wait_ge(s_v, ctr.get(s_v))
                ins = s.copy(Ofp[:, :], ps_B[:, :])
                ctr.inc(s_s, 1); ins.then_inc(s_s, 1); s.wait_ge(s_s, ctr.get(s_s))
                ctr.inc(s_s, 1)
                s.activation(e_m[:, :], Ofp[:, :], AF.Square).then_inc(s_s, 1)
            OE = ctr.get(s_s)

            @blk.vector
            def _(v, OE=OE):
                v.wait_ge(s_s, OE)
                ins = v.tensor_reduce(rcol[:, 0:HV].unsqueeze(2), w3(e_m.ap()), AX.X, ALU.add)
                ctr.inc(s_v, 1); ins.then_inc(s_v, 1); v.wait_ge(s_v, ctr.get(s_v))
                ctr.inc(s_v, 1)
                v.tensor_scalar(rcol[:, 0:HV], rcol[:, 0:HV], 1.0 / DV, EPS_RMS,
                                ALU.mult, ALU.add).then_inc(s_v, 1)
            RM1 = ctr.get(s_v)

            @blk.scalar
            def _(s, RM1=RM1):
                s.wait_ge(s_v, RM1)
                ctr.inc(s_s, 1)
                s.activation(rcol[:, 0:HV], rcol[:, 0:HV], AF.Sqrt).then_inc(s_s, 1)
            RM2 = ctr.get(s_s)

            @blk.vector
            def _(v, RM2=RM2):
                v.wait_ge(s_s, RM2)
                ins = v.reciprocal(rcol[:, 0:HV], rcol[:, 0:HV])
                ctr.inc(s_v, 1); ins.then_inc(s_v, 1); v.wait_ge(s_v, ctr.get(s_v))
                ctr.inc(s_v, 1)
                v.tensor_tensor(w3(Onsb.ap()), w3(Ofp.ap()), colview(rcol[:, 0:HV]), ALU.mult).then_inc(s_v, 1)
            ONV = ctr.get(s_v)

            @blk.tensor
            def _(t, ONV=ONV):
                t.wait_ge(s_v, ONV)
                t.wait_ge(s_s, ctr.get(s_s))
                mm = None
                for h in range(HV):
                    sl = slice(h * 128, (h + 1) * 128)
                    mm = t.matmul(ps_A[:, sl], Onsb[:, sl], c_ident[:, :], start=True, stop=True)
                ctr.inc(s_mm, 1)
                mm.then_inc(s_mm, 1)
            OT = ctr.get(s_mm)

            @blk.scalar
            def _(s, OT=OT, gtl=gtl):
                s.wait_ge(s_mm, OT)
                ins = s.activation(gsil[:, :], gtl, AF.Silu)
                ctr.inc(s_s, 1); ins.then_inc(s_s, 1); s.wait_ge(s_s, ctr.get(s_s))
                ctr.inc(s_s, 1)
                s.activation(Onsb[:, :], ps_A[:, :], AF.Identity, scale=c_rmsw[:, 0:1]).then_inc(s_s, 1)
            PS1 = ctr.get(s_s)

            @blk.vector
            def _(v, pt=pt, PS1=PS1, buf=buf):
                v.wait_ge(s_s, PS1)
                v.wait_ge(stp[buf], ctr.get(stp[buf]))
                ctr.inc(s_v, 1)
                v.tensor_tensor(pt, Onsb[:, :], gsil[:, :], ALU.mult).then_inc(s_v, 1)
            PV = ctr.get(s_v)

            @blk.gpsimd
            def _(g, ci=ci, pt=pt, PV=PV, buf=buf):
                g.wait_ge(s_v, PV)
                g.wait_ge(stp[buf], ctr.get(stp[buf]))
                g.dma_start(out=p_scr[ci], in_=pt).then_inc(stp[buf], 16)
                ctr.inc(stp[buf], 16)

            @blk.tensor
            def _(t, ptl=ptl, OM=OM):
                t.wait_ge(s_s, ctr.get(s_s))
                mm = None
                for h in range(HV):
                    sl = slice(h * 128, (h + 1) * 128)
                    mm = t.matmul(ps_B[:, sl], ptl[:, sl], state.ap()[:, sl], start=True, stop=True)
                ctr.inc(s_mm, 1)
                mm.then_inc(s_mm, 1)
            SM = ctr.get(s_mm)

            @blk.vector
            def _(v, rtl=rtl, SM=SM):
                v.wait_ge(s_mm, SM)
                ctr.inc(s_v, 1)
                v.tensor_tensor(state.ap()[:, :], ps_B[:, :], rtl, ALU.add).then_inc(s_v, 1)


def _qview(qk_f, ci):
    return (qk_f.ap()[:, ci * 128:ci * 128 + (HQ - 1) * TPC + 128]
            .rearrange("p (q rest) -> p q rest", q=HQ) if False else
            _qview_impl(qk_f, ci))


def _qview_impl(qk_f, ci):
    full = qk_f.ap()
    v = full.rearrange("p (hh t) -> p hh t", t=TPC)[:, 0:HQ, ci * 128:(ci + 1) * 128]
    return v.unsqueeze(2).broadcast_to((128, HQ, 2, 128))


def _p3(nc, blk, ctr, sems, env):
    s_mm, s_v, s_s = sems["mm"], sems["v"], sems["s"]
    s_lda, s_ldb, s_sta, s_stb = sems["lda"], sems["ldb"], sems["sta"], sems["stb"]
    ldp = [s_lda, s_ldb]
    stp = [s_sta, s_stb]
    c_ident, c_cols = env["c_ident"], env["c_cols"]
    x_in, wout_b, p_scr, x2_scr = env["x_in"], env["wout_b"], env["p_scr"], env["x2_scr"]
    xq_f = env["xq_f"]
    from contextlib import ExitStack

    _phase_barrier(nc, blk, ctr, sems)
    with ExitStack() as st3:
        E1 = st3.enter_context
        pcol = E1(nc.sbuf_tensor("pcol", [128, 2 * 2048], BF16))
        woutb = E1(nc.sbuf_tensor("woutb", [128, 2 * 16 * 512], BF16))
        attn_all = E1(nc.sbuf_tensor("attn_all", [128, 8 * D], F32))
        lnt = E1(nc.sbuf_tensor("lnt3", [128, 40], F32))
        x2t = E1(nc.sbuf_tensor("x2t", [128, 2 * D], F32))
        xnt = E1(nc.sbuf_tensor("xnt3", [128, D], F32))
        xtl = E1(nc.sbuf_tensor("xtl3", [128, 2 * D], F32))
        inti = E1(nc.sbuf_tensor("inti", [128, D], I32))
        intf = E1(nc.sbuf_tensor("intf", [128, D], F32))
        xq_t = E1(nc.sbuf_tensor("xq_t", [128, D], BF16))
        ps_op0 = E1(nc.psum_tensor("ps_op0", [128, 512], F32))
        ps_op1 = E1(nc.psum_tensor("ps_op1", [128, 512], F32))
        ps_big = E1(nc.psum_tensor("ps_big3", [128, D], F32))

        for fg in range(4):
            wsl = woutb[:, (fg % 2) * 16 * 512:][:, :16 * 512]

            @blk.gpsimd
            def _(g, fg=fg, wsl=wsl):
                if fg >= 2:
                    g.wait_ge(s_mm, ctr.get(s_mm))
                g.wait_ge(stp[fg % 2], ctr.get(stp[fg % 2]))
                g.dma_start(out=wsl.rearrange("p (db f) -> p db f", f=512),
                            in_=wout_b[fg].rearrange("(db p) f -> p db f", p=128)).then_inc(stp[fg % 2], 16)
                ctr.inc(stp[fg % 2], 16)
            WL = ctr.get(stp[fg % 2])

            for tb in range(8):
                it = fg * 8 + tb
                pb = it % 2
                psl = ps_op0 if tb % 2 == 0 else ps_op1
                pc = pcol[:, pb * 2048:pb * 2048 + 2048]

                @blk.sync
                def _(q, tb=tb, pc=pc, pb=pb, it=it):
                    if it >= 2:
                        q.wait_ge(s_mm, ctr.get(s_mm))
                    q.wait_ge(ldp[pb], ctr.get(ldp[pb]))
                    q.dma_start(out=pc, in_=p_scr[tb]).then_inc(ldp[pb], 16)
                    ctr.inc(ldp[pb], 16)
                PL = ctr.get(ldp[pb])

                @blk.tensor
                def _(t, fg=fg, tb=tb, pc=pc, wsl=wsl, psl=psl, PL=PL, WL=WL, pb=pb):
                    t.wait_ge(ldp[pb], PL)
                    t.wait_ge(stp[fg % 2], WL)
                    if fg + tb > 0:
                        t.wait_ge(s_s, ctr.get(s_s))
                    mm = None
                    for db in range(NDB):
                        mm = t.matmul(psl[:, :],
                                      pc[:, db * 128:(db + 1) * 128],
                                      wsl.rearrange("p (db f) -> p db f", f=512)[:, db, :],
                                      start=(db == 0), stop=(db == NDB - 1))
                    ctr.inc(s_mm, 1)
                    mm.then_inc(s_mm, 1)
                OPM = ctr.get(s_mm)

                @blk.scalar
                def _(s, fg=fg, tb=tb, psl=psl, OPM=OPM):
                    s.wait_ge(s_mm, OPM)
                    ctr.inc(s_s, 1)
                    s.copy(attn_all[:, tb * D + fg * 512: tb * D + fg * 512 + 512], psl[:, :]).then_inc(s_s, 1)
        OPDONE = ctr.get(s_s)

        for tb in range(8):
            at = attn_all[:, tb * D:(tb + 1) * D]
            xt = xtl[:, (tb % 2) * D:(tb % 2) * D + D]
            x2 = x2t[:, (tb % 2) * D:(tb % 2) * D + D]

            @blk.sync
            def _(q, tb=tb, xt=xt):
                if tb >= 2:
                    q.wait_ge(s_v, ctr.get(s_v))
                q.wait_ge(ldp[tb % 2], ctr.get(ldp[tb % 2]))
                q.dma_start(out=xt, in_=x_in[HALO + tb * 128: HALO + (tb + 1) * 128, :]).then_inc(ldp[tb % 2], 16)
                ctr.inc(ldp[tb % 2], 16)
            XL = ctr.get(ldp[tb % 2])

            def emit_ln(src_ap, eps_val, dst_col):
                @blk.vector
                def _(v):
                    v.wait_ge(s_v, ctr.get(s_v))
                    ins = None
                    for i4 in range(4):
                        ins = v.bn_stats(lnt[:, i4 * 6:(i4 + 1) * 6], src_ap[:, i4 * 512:(i4 + 1) * 512])
                    ctr.inc(s_v, 1); ins.then_inc(s_v, 1); v.wait_ge(s_v, ctr.get(s_v))
                    ins = v.bn_aggr(lnt[:, 24:26], lnt[:, 0:24])
                    ctr.inc(s_v, 1); ins.then_inc(s_v, 1); v.wait_ge(s_v, ctr.get(s_v))
                    ctr.inc(s_v, 1)
                    v.tensor_scalar(lnt[:, 26:27], lnt[:, 25:26], eps_val, None, ALU.add).then_inc(s_v, 1)
                V1 = ctr.get(s_v)

                @blk.scalar
                def _(s, V1=V1):
                    s.wait_ge(s_v, V1)
                    ctr.inc(s_s, 1)
                    s.activation(lnt[:, 27:28], lnt[:, 26:27], AF.Sqrt).then_inc(s_s, 1)
                S1 = ctr.get(s_s)

                @blk.vector
                def _(v, S1=S1):
                    v.wait_ge(s_s, S1)
                    ins = v.reciprocal(lnt[:, dst_col:dst_col + 1], lnt[:, 27:28])
                    ctr.inc(s_v, 1); ins.then_inc(s_v, 1); v.wait_ge(s_v, ctr.get(s_v))
                    ctr.inc(s_v, 1)
                    v.scalar_tensor_tensor(out=lnt[:, dst_col + 1:dst_col + 2], in0=lnt[:, 24:25],
                                           scalar=-1.0, in1=lnt[:, dst_col:dst_col + 1],
                                           op0=ALU.mult, op1=ALU.mult).then_inc(s_v, 1)
                return ctr.get(s_v)

            @blk.vector
            def _(v, OPDONE=OPDONE):
                v.wait_ge(s_s, OPDONE)
            LN1 = emit_ln(at, EPS_LN, 28)

            @blk.scalar
            def _(s, at=at, LN1=LN1):
                s.wait_ge(s_v, LN1)
                ctr.inc(s_s, 1)
                s.activation(xnt[:, :], at, AF.Identity,
                             bias=lnt[:, 29:30], scale=lnt[:, 28:29]).then_inc(s_s, 1)
            A1 = ctr.get(s_s)

            @blk.vector
            def _(v, x2=x2, xt=xt, A1=A1, XL=XL, tb=tb):
                v.wait_ge(s_s, A1)
                v.wait_ge(ldp[tb % 2], XL)
                v.wait_ge(stp[tb % 2], ctr.get(stp[tb % 2]))
                ctr.inc(s_v, 1)
                v.tensor_tensor(x2, xnt[:, :], xt, ALU.add).then_inc(s_v, 1)
            X2V = ctr.get(s_v)

            @blk.gpsimd
            def _(g, tb=tb, x2=x2, X2V=X2V):
                g.wait_ge(s_v, X2V)
                g.wait_ge(stp[tb % 2], ctr.get(stp[tb % 2]))
                g.dma_start(out=x2_scr[tb * 128:(tb + 1) * 128, :], in_=x2).then_inc(stp[tb % 2], 16)
                ctr.inc(stp[tb % 2], 16)

            LN2 = emit_ln(x2, EPS_FFN, 30)

            @blk.scalar
            def _(s, x2=x2, LN2=LN2):
                s.wait_ge(s_v, LN2)
                ctr.inc(s_s, 1)
                s.activation(xnt[:, :], x2, AF.Identity,
                             bias=lnt[:, 31:32], scale=lnt[:, 30:31]).then_inc(s_s, 1)
            A2 = ctr.get(s_s)

            @blk.vector
            def _(v, A2=A2):
                v.wait_ge(s_s, A2)
                ins = v.tensor_reduce(lnt[:, 32:33], xnt[:, :], AX.X, ALU.max, apply_absolute_value=True)
                ctr.inc(s_v, 1); ins.then_inc(s_v, 1); v.wait_ge(s_v, ctr.get(s_v))
                ins = v.reciprocal(lnt[:, 33:34], lnt[:, 32:33])
                ctr.inc(s_v, 1); ins.then_inc(s_v, 1); v.wait_ge(s_v, ctr.get(s_v))
                v.tensor_scalar(lnt[:, 34:35], lnt[:, 33:34], 127.0, None, ALU.mult)
                ins = v.tensor_scalar(lnt[:, 35:36], lnt[:, 32:33], 1.0 / 127.0, None, ALU.mult)
                ctr.inc(s_v, 1); ins.then_inc(s_v, 1); v.wait_ge(s_v, ctr.get(s_v))
                ins = v.tensor_scalar(intf[:, :], xnt[:, :], lnt[:, 34:35], None, ALU.mult)
                ctr.inc(s_v, 1); ins.then_inc(s_v, 1); v.wait_ge(s_v, ctr.get(s_v))
                ins = v.tensor_copy(inti[:, :], intf[:, :])
                ctr.inc(s_v, 1); ins.then_inc(s_v, 1); v.wait_ge(s_v, ctr.get(s_v))
                ins = v.tensor_copy(intf[:, :], inti[:, :])
                ctr.inc(s_v, 1); ins.then_inc(s_v, 1); v.wait_ge(s_v, ctr.get(s_v))
                ctr.inc(s_v, 1)
                v.tensor_scalar(xq_t[:, :], intf[:, :], lnt[:, 35:36], None, ALU.mult).then_inc(s_v, 1)
            QV = ctr.get(s_v)

            @blk.tensor
            def _(t, tb=tb, QV=QV):
                t.wait_ge(s_v, QV)
                if tb > 0:
                    t.wait_ge(s_s, ctr.get(s_s))
                mm = None
                for db in range(NDB):
                    mm = t.matmul(ps_big[:, db * 128:(db + 1) * 128],
                                  xq_t[:, db * 128:(db + 1) * 128],
                                  c_ident[:, :], start=True, stop=True)
                ctr.inc(s_mm, 1)
                mm.then_inc(s_mm, 1)
            TQ = ctr.get(s_mm)

            @blk.scalar
            def _(s, tb=tb, TQ=TQ):
                s.wait_ge(s_mm, TQ)
                ctr.inc(s_s, 1)
                s.copy(
                    xq_f.ap().rearrange("p (db c) -> p db c", db=NDB)[:, :, tb * 128:(tb + 1) * 128],
                    ps_big.ap().rearrange("p (db c) -> p db c", db=NDB),
                ).then_inc(s_s, 1)


def _ffn_gu(nc, blk, ctr, sems, env):
    s_mm, s_v, s_s = sems["mm"], sems["v"], sems["s"]
    s_lda, s_ldb, s_sta, s_stb = sems["lda"], sems["ldb"], sems["sta"], sems["stb"]
    ldp = [sems["ga"], sems["gb"]]
    xq_f, p2 = env["xq_f"], env["p2"]
    wg_b, wu_b = env["wg_b"], env["wu_b"]
    from contextlib import ExitStack

    _phase_barrier(nc, blk, ctr, sems)
    with ExitStack() as stf:
        E1 = stf.enter_context
        wgb = E1(nc.sbuf_tensor("wgb", [128, 2 * 2048], BF16))
        wub = E1(nc.sbuf_tensor("wub", [128, 2 * 2048], BF16))
        gs = E1(nc.sbuf_tensor("gs", [128, 2 * 512], BF16))
        ps_g0 = E1(nc.psum_tensor("ps_g0", [128, 512], F32))
        ps_g1 = E1(nc.psum_tensor("ps_g1", [128, 512], F32))
        ps_u0 = E1(nc.psum_tensor("ps_u0", [128, 512], F32))
        ps_u1 = E1(nc.psum_tensor("ps_u1", [128, 512], F32))

        mm_hist = {}
        load_done = {}
        sign_done = {}

        def emit_load(hb):
            buf = hb % 2
            wg = wgb[:, buf * 2048:buf * 2048 + 2048]
            wu = wub[:, buf * 2048:buf * 2048 + 2048]

            @blk.gpsimd
            def _(g, hb=hb, wg=wg, wu=wu, buf=buf):
                if hb >= 2:
                    g.wait_ge(s_mm, mm_hist[hb - 2])
                g.wait_ge(ldp[buf], ctr.get(ldp[buf]))
                g.dma_start(out=wg.rearrange("p (db f) -> p db f", f=128),
                            in_=wg_b[hb].rearrange("(db p) f -> p db f", p=128)).then_inc(ldp[buf], 16)
                g.dma_start(out=wu.rearrange("p (db f) -> p db f", f=128),
                            in_=wu_b[hb].rearrange("(db p) f -> p db f", p=128)).then_inc(ldp[buf], 16)
                ctr.inc(ldp[buf], 32)
            load_done[hb] = ctr.get(ldp[buf])

        def emit_sign(hb):
            buf = hb % 2
            wg = wgb[:, buf * 2048:buf * 2048 + 2048]
            wu = wub[:, buf * 2048:buf * 2048 + 2048]

            @blk.scalar
            def _(s, hb=hb, wg=wg, wu=wu, buf=buf):
                s.wait_ge(ldp[buf], load_done[hb])
                s.sign(wg, wg)
                ctr.inc(s_s, 1)
                s.sign(wu, wu).then_inc(s_s, 1)
            sign_done[hb] = ctr.get(s_s)

        emit_load(0)
        emit_load(1)
        emit_sign(0)

        for hb in range(NHB):
            buf = hb % 2
            wg = wgb[:, buf * 2048:buf * 2048 + 2048]
            wu = wub[:, buf * 2048:buf * 2048 + 2048]
            if hb + 1 < NHB:
                emit_sign(hb + 1)
            for th in range(2):
                psg = ps_g0 if th == 0 else ps_g1
                psu = ps_u0 if th == 0 else ps_u1

                @blk.tensor
                def _(t, hb=hb, th=th, wg=wg, wu=wu, psg=psg, psu=psu):
                    t.wait_ge(s_s, sign_done[hb])
                    if hb > 0:
                        t.wait_ge(s_v, ctr.get(s_v) - 1)
                    mm = None
                    for db in range(NDB):
                        mm = t.matmul(psg[:, :], wg[:, db * 128:(db + 1) * 128],
                                      xq_f.ap()[:, db * TPC + th * 512:][:, :512],
                                      start=(db == 0), stop=(db == NDB - 1))
                    for db in range(NDB):
                        mm = t.matmul(psu[:, :], wu[:, db * 128:(db + 1) * 128],
                                      xq_f.ap()[:, db * TPC + th * 512:][:, :512],
                                      start=(db == 0), stop=(db == NDB - 1))
                    ctr.inc(s_mm, 1)
                    mm.then_inc(s_mm, 1)
                GM = ctr.get(s_mm)

                @blk.scalar
                def _(s, psg=psg, GM=GM, th=th):
                    s.wait_ge(s_mm, GM)
                    ctr.inc(s_s, 1)
                    s.activation(gs[:, th * 512:(th + 1) * 512], psg[:, :], AF.Silu).then_inc(s_s, 1)
                GS = ctr.get(s_s)

                @blk.vector
                def _(v, hb=hb, th=th, psu=psu, GS=GS):
                    v.wait_ge(s_s, GS)
                    ctr.inc(s_v, 1)
                    v.tensor_tensor(p2.ap()[:, hb * TPC + th * 512:][:, :512],
                                    psu[:, :], gs[:, th * 512:(th + 1) * 512], ALU.mult).then_inc(s_v, 1)
            mm_hist[hb] = ctr.get(s_mm)
            if hb + 2 < NHB:
                emit_load(hb + 2)


def _ffn_down(nc, blk, ctr, sems, env):
    s_mm, s_v, s_s = sems["mm"], sems["v"], sems["s"]
    s_lda, s_ldb, s_sta, s_stb = sems["lda"], sems["ldb"], sems["sta"], sems["stb"]
    ldp = [sems["ga"], sems["gb"]]
    stp = [s_sta, s_stb]
    p2, wd_b, ffn_scr = env["p2"], env["wd_b"], env["ffn_scr"]
    from contextlib import ExitStack

    _phase_barrier(nc, blk, ctr, sems)
    with ExitStack() as std_:
        E1 = std_.enter_context
        wdb = E1(nc.sbuf_tensor("wdb", [128, 2 * NHB * 128], BF16))
        fout = E1(nc.sbuf_tensor("fout", [128, 2 * TPC], BF16))
        ps_d0 = E1(nc.psum_tensor("ps_d0", [128, TPC], F32))
        ps_d1 = E1(nc.psum_tensor("ps_d1", [128, TPC], F32))

        mm_hist = {}
        load_done = {}
        sign_done = {}

        def emit_load(fcb):
            buf = fcb % 2
            wd = wdb[:, buf * NHB * 128:][:, :NHB * 128]

            @blk.gpsimd
            def _(g, fcb=fcb, wd=wd, buf=buf):
                if fcb >= 2:
                    g.wait_ge(s_mm, mm_hist[fcb - 2])
                g.wait_ge(ldp[buf], ctr.get(ldp[buf]))
                g.dma_start(out=wd.rearrange("p (hb f) -> p hb f", f=128),
                            in_=wd_b[fcb].rearrange("(hb p) f -> p hb f", p=128)).then_inc(ldp[buf], 16)
                ctr.inc(ldp[buf], 16)
            load_done[fcb] = ctr.get(ldp[buf])

        def emit_sign(fcb):
            buf = fcb % 2
            wd = wdb[:, buf * NHB * 128:][:, :NHB * 128]

            @blk.scalar
            def _(s, fcb=fcb, wd=wd, buf=buf):
                s.wait_ge(ldp[buf], load_done[fcb])
                ctr.inc(s_s, 1)
                s.sign(wd, wd).then_inc(s_s, 1)
            sign_done[fcb] = ctr.get(s_s)

        emit_load(0)
        emit_load(1)
        emit_sign(0)

        for fcb in range(16):
            buf = fcb % 2
            wd = wdb[:, buf * NHB * 128:][:, :NHB * 128]
            psd = ps_d0 if fcb % 2 == 0 else ps_d1
            fo = fout[:, buf * TPC:buf * TPC + TPC]
            if fcb + 1 < 16:
                emit_sign(fcb + 1)

            @blk.tensor
            def _(t, fcb=fcb, wd=wd, psd=psd):
                t.wait_ge(s_s, sign_done[fcb])
                if fcb >= 2:
                    t.wait_ge(s_s, ctr.get(s_s))
                mm = None
                for th in range(2):
                    for hb in range(NHB):
                        mm = t.matmul(psd[:, th * 512:(th + 1) * 512],
                                      wd[:, hb * 128:(hb + 1) * 128],
                                      p2.ap()[:, hb * TPC + th * 512:][:, :512],
                                      start=(hb == 0), stop=(hb == NHB - 1))
                ctr.inc(s_mm, 1)
                mm.then_inc(s_mm, 1)
            DM = ctr.get(s_mm)

            @blk.scalar
            def _(s, psd=psd, fo=fo, DM=DM, buf=buf, fcb=fcb):
                s.wait_ge(s_mm, DM)
                if fcb >= 2:
                    s.wait_ge(stp[buf], ctr.get(stp[buf]))
                ctr.inc(s_s, 1)
                s.copy(fo, psd[:, :]).then_inc(s_s, 1)
            FE = ctr.get(s_s)

            @blk.gpsimd
            def _(g, fcb=fcb, fo=fo, FE=FE, buf=buf):
                g.wait_ge(s_s, FE)
                g.wait_ge(stp[buf], ctr.get(stp[buf]))
                g.dma_start(out=ffn_scr[fcb], in_=fo).then_inc(stp[buf], 16)
                ctr.inc(stp[buf], 16)
            mm_hist[fcb] = ctr.get(s_mm)
            if fcb + 2 < 16:
                emit_load(fcb + 2)


def _final(nc, blk, ctr, sems, env):
    s_mm, s_v, s_s = sems["mm"], sems["v"], sems["s"]
    s_lda, s_ldb, s_sta, s_stb = sems["lda"], sems["ldb"], sems["sta"], sems["stb"]
    ldp = [s_lda, s_ldb]
    stp = [s_sta, s_stb]
    c_ident = env["c_ident"]
    ffn_scr, x2_scr, y_out = env["ffn_scr"], env["x2_scr"], env["y_out"]
    from contextlib import ExitStack

    _phase_barrier(nc, blk, ctr, sems)
    with ExitStack() as stf:
        E1 = stf.enter_context
        ftile = E1(nc.sbuf_tensor("ftile", [128, 2 * D], BF16))
        x2l = E1(nc.sbuf_tensor("x2l", [128, 2 * D], F32))
        yt = E1(nc.sbuf_tensor("yt", [128, 2 * D], F32))
        lnt = E1(nc.sbuf_tensor("lntF", [128, 32], F32))
        ps_fin0 = E1(nc.psum_tensor("ps_fin0", [128, D], F32))
        ps_fin1 = E1(nc.psum_tensor("ps_fin1", [128, D], F32))

        for tb in range(8):
            buf = tb % 2
            ft = ftile[:, buf * D:buf * D + D]
            xx = x2l[:, buf * D:buf * D + D]
            yy = yt[:, buf * D:buf * D + D]
            psf = ps_fin0 if buf == 0 else ps_fin1

            @blk.sync
            def _(q, tb=tb, ft=ft, xx=xx, buf=buf):
                if tb >= 2:
                    q.wait_ge(s_mm, ctr.get(s_mm))
                    q.wait_ge(s_v, ctr.get(s_v))
                q.wait_ge(ldp[buf], ctr.get(ldp[buf]))
                q.dma_start(out=ft.rearrange("p (db t) -> p db t", t=128),
                            in_=ffn_scr[:, :, tb * 128:(tb + 1) * 128].rearrange("db p t -> p db t")).then_inc(ldp[buf], 16)
                q.dma_start(out=xx, in_=x2_scr[tb * 128:(tb + 1) * 128, :]).then_inc(ldp[buf], 16)
                ctr.inc(ldp[buf], 32)
            FL = ctr.get(ldp[buf])

            @blk.tensor
            def _(t, tb=tb, ft=ft, psf=psf, FL=FL, buf=buf):
                t.wait_ge(ldp[buf], FL)
                if tb >= 2:
                    t.wait_ge(s_s, ctr.get(s_s))
                mm = None
                for db in range(NDB):
                    mm = t.matmul(psf[:, db * 128:(db + 1) * 128],
                                  ft[:, db * 128:(db + 1) * 128],
                                  c_ident[:, :], start=True, stop=True)
                ctr.inc(s_mm, 1)
                mm.then_inc(s_mm, 1)
            FM = ctr.get(s_mm)

            @blk.vector
            def _(v, psf=psf, FM=FM):
                v.wait_ge(s_mm, FM)
                ins = None
                for i4 in range(4):
                    ins = v.bn_stats(lnt[:, i4 * 6:(i4 + 1) * 6], psf[:, i4 * 512:(i4 + 1) * 512])
                ctr.inc(s_v, 1); ins.then_inc(s_v, 1); v.wait_ge(s_v, ctr.get(s_v))
                ins = v.bn_aggr(lnt[:, 24:26], lnt[:, 0:24])
                ctr.inc(s_v, 1); ins.then_inc(s_v, 1); v.wait_ge(s_v, ctr.get(s_v))
                ctr.inc(s_v, 1)
                v.tensor_scalar(lnt[:, 26:27], lnt[:, 25:26], EPS_LN, None, ALU.add).then_inc(s_v, 1)
            V1 = ctr.get(s_v)

            @blk.scalar
            def _(s, V1=V1):
                s.wait_ge(s_v, V1)
                ctr.inc(s_s, 1)
                s.activation(lnt[:, 27:28], lnt[:, 26:27], AF.Sqrt).then_inc(s_s, 1)
            S1 = ctr.get(s_s)

            @blk.vector
            def _(v, S1=S1):
                v.wait_ge(s_s, S1)
                ins = v.reciprocal(lnt[:, 28:29], lnt[:, 27:28])
                ctr.inc(s_v, 1); ins.then_inc(s_v, 1); v.wait_ge(s_v, ctr.get(s_v))
                ctr.inc(s_v, 1)
                v.scalar_tensor_tensor(out=lnt[:, 29:30], in0=lnt[:, 24:25],
                                       scalar=-1.0, in1=lnt[:, 28:29],
                                       op0=ALU.mult, op1=ALU.mult).then_inc(s_v, 1)
            V2 = ctr.get(s_v)

            @blk.scalar
            def _(s, psf=psf, yy=yy, V2=V2, buf=buf, tb=tb):
                s.wait_ge(s_v, V2)
                if tb >= 2:
                    s.wait_ge(stp[buf], ctr.get(stp[buf]))
                ctr.inc(s_s, 1)
                s.activation(yy, psf[:, :], AF.Identity,
                             bias=lnt[:, 29:30], scale=lnt[:, 28:29]).then_inc(s_s, 1)
            S2 = ctr.get(s_s)

            @blk.vector
            def _(v, yy=yy, xx=xx, S2=S2):
                v.wait_ge(s_s, S2)
                ctr.inc(s_v, 1)
                v.tensor_tensor(yy, yy, xx, ALU.add).then_inc(s_v, 1)
            YV = ctr.get(s_v)

            @blk.gpsimd
            def _(g, tb=tb, yy=yy, YV=YV, buf=buf):
                g.wait_ge(s_v, YV)
                g.wait_ge(stp[buf], ctr.get(stp[buf]))
                g.dma_start(out=y_out[tb * 128:(tb + 1) * 128, :], in_=yy).then_inc(stp[buf], 16)
                ctr.inc(stp[buf], 16)

        @blk.sync
        def _(q):
            q.wait_ge(s_sta, ctr.get(s_sta))
            q.wait_ge(s_stb, ctr.get(s_stb))


def host_prep(inputs):
    x = np.ascontiguousarray(np.asarray(inputs["L0_attn_x"], np.float32))
    w_qkv = np.asarray(inputs["L0_attn_w_qkv"], np.float32)
    w_gate = np.asarray(inputs["L0_attn_w_gate"], np.float32)
    w_beta = np.asarray(inputs["L0_attn_w_beta"], np.float32)
    w_alpha = np.asarray(inputs["L0_attn_w_alpha"], np.float32)
    log_A = np.asarray(inputs["L0_attn_log_A"], np.float32)
    dt_bias = np.asarray(inputs["L0_attn_dt_bias"], np.float32)
    conv_w = np.asarray(inputs["L0_attn_conv_w"], np.float32)
    rms_w = np.asarray(inputs["L0_attn_rms_w"], np.float32)
    w_out = np.asarray(inputs["L0_attn_w_out"], np.float32)
    W_g = np.asarray(inputs["L0_ffn_W_g"], np.float32)
    W_u = np.asarray(inputs["L0_ffn_W_u"], np.float32)
    W_d = np.asarray(inputs["L0_ffn_W_d"], np.float32)

    wqkv_bl = np.ascontiguousarray(w_qkv.T.reshape(D, NFQ, 128).transpose(1, 0, 2))
    wgate_bl = np.ascontiguousarray(w_gate.T.reshape(D, NFG, 128).transpose(1, 0, 2))
    wba = np.ascontiguousarray(np.concatenate([w_beta, w_alpha], axis=0).T)
    convw = np.ascontiguousarray(
        conv_w[:, 0, :].reshape(NFQ, 128, 4).transpose(1, 0, 2).reshape(128, NFQ * 4))
    la = np.ascontiguousarray(np.stack([log_A, dt_bias], axis=1))
    rmsw = np.ascontiguousarray(rms_w[:, None])
    wout_bl = np.ascontiguousarray(w_out.T.reshape(D, 4, 512).transpose(1, 0, 2))
    wg_bl = np.ascontiguousarray(W_g.T.reshape(D, NHB, 128).transpose(1, 0, 2))
    wu_bl = np.ascontiguousarray(W_u.T.reshape(D, NHB, 128).transpose(1, 0, 2))
    wd_bl = np.ascontiguousarray(W_d.T.reshape(FFH, 16, 128).transpose(1, 0, 2))

    ident_bf = np.eye(128, dtype=NPBF16)
    ident_f = np.eye(128, dtype=np.float32)
    r_idx, c_idx = np.meshgrid(np.arange(128), np.arange(128), indexing="ij")
    maskLT = np.where(c_idx < r_idx, 0.0, NEG).astype(np.float32)
    maskUT = np.where(r_idx < c_idx, 0.0, NEG).astype(np.float32)
    maskUD = np.where(r_idx <= c_idx, -LSD, NEG).astype(np.float32)
    ones_col = np.ones((128, 1), dtype=NPBF16)
    ones_row_f = np.ones((1, 128), dtype=np.float32)
    zeros_f = np.zeros((128, 128), dtype=np.float32)
    ccols = np.zeros((128, 8), np.float32)
    ccols[:, 0] = EPS_L2
    ccols[:, 1] = EPS_RMS
    ccols[:, 2] = -LSD
    ccols[:, 3] = 2.0
    ccols[:, 4] = EPS_LN
    ccols[:, 5] = EPS_FFN
    ccols[:, 6] = 127.0
    ccols[:, 7] = 1.0

    in_maps = []
    for core in range(8):
        b, g = core // 4, core % 4
        t0 = g * TPC
        xs = np.zeros((TP, D), np.float32)
        lo = max(0, t0 - HALO)
        xs[HALO - (t0 - lo): HALO - (t0 - lo) + (t0 + TPC - lo)] = x[b, lo:t0 + TPC]
        hop_m = np.zeros((128, 7), np.float32)
        hop_imi = np.zeros((128, 7 * 128), np.float32)
        for j in range(7):
            use = 1.0 if (j // 4 == b and j % 4 < g) else 0.0
            hop_m[:, j] = use
            hop_imi[:, j * 128:(j + 1) * 128] = (1.0 - use) * np.eye(128, dtype=np.float32)
        in_maps.append({
            "x": xs, "wqkv_b": wqkv_bl, "wgate_b": wgate_bl, "wba": wba,
            "convw": convw, "loga_params": la, "rmsw": rmsw,
            "wout_b": wout_bl, "wg_b": wg_bl, "wu_b": wu_bl, "wd_b": wd_bl,
            "ident_bf": ident_bf, "ident_f": ident_f, "maskLT": maskLT,
            "maskUT": maskUT, "maskUD": maskUD,
            "ones_col_bf": ones_col, "ones_row_f": ones_row_f, "ccols": ccols,
            "ones_row_bf": np.ones((1, 128), dtype=NPBF16), "zeros_f": zeros_f,
            "hop_mcol": hop_m, "hop_imi": hop_imi,
        })
    return in_maps


def kernel(**inputs):
    in_maps = host_prep(inputs)
    nc, _ = build()
    res = run_bass_kernel_spmd(nc, in_maps, core_ids=list(range(8)))
    y = np.empty((2, 4096, D), np.float32)
    for core in range(8):
        b, g = core // 4, core % 4
        y[b, g * TPC:(g + 1) * TPC] = res.results[core]["y"]
    return y


# version 08
# speedup vs baseline: 2.4722x; 1.0631x over previous
import numpy as np
import ml_dtypes
import concourse.bass as bass
import concourse.mybir as mybir
from concourse.bass_utils import run_bass_kernel_spmd
from concourse import library_config

F32 = mybir.dt.float32
BF16 = mybir.dt.bfloat16
I32 = mybir.dt.int32
AF = mybir.ActivationFunctionType
ALU = mybir.AluOpType
AX = mybir.AxisListType
NPBF16 = ml_dtypes.bfloat16

D = 2048
TPC = 1024
HALO = 3
TP = 1032
TG = 344
C = 128
NCH = 8
HV, HQ, DK, DV = 16, 8, 128, 128
NDB = 16
NFQ, NFG = 32, 16
NFB = NFQ + NFG + 1
FFH = 8192
NHB = 64
NEG = -1e30
EPS_LN = 1e-5
EPS_L2 = 1e-6
EPS_RMS = 1e-6
EPS_FFN = 1e-8
LSD = float(np.log(np.sqrt(DK)))
NEWTON_ITERS = 6


class Ctr:
    def __init__(self):
        self.v = {}

    def inc(self, sem, k=1):
        self.v[sem.name] = self.v.get(sem.name, 0) + k
        return self.v[sem.name]

    def get(self, sem):
        return self.v.get(sem.name, 0)


def build(probe=None, upto="all"):
    nc = bass.Bass("TRN2", target_bir_lowering=False, debug=False)

    x_in = nc.dram_tensor("x", [TP, D], F32, kind="ExternalInput").ap()
    wqkv_b = nc.dram_tensor("wqkv_b", [NFQ, D, 128], F32, kind="ExternalInput").ap()
    wgate_b = nc.dram_tensor("wgate_b", [NFG, D, 128], F32, kind="ExternalInput").ap()
    wba_in = nc.dram_tensor("wba", [D, 32], F32, kind="ExternalInput").ap()
    convw_in = nc.dram_tensor("convw", [128, NFQ * 4], F32, kind="ExternalInput").ap()
    la_in = nc.dram_tensor("loga_params", [HV, 2], F32, kind="ExternalInput").ap()
    rmsw_in = nc.dram_tensor("rmsw", [128, 1], F32, kind="ExternalInput").ap()
    wout_b = nc.dram_tensor("wout_b", [4, D, 512], F32, kind="ExternalInput").ap()
    wg_b = nc.dram_tensor("wg_b", [NHB, D, 128], F32, kind="ExternalInput").ap()
    wu_b = nc.dram_tensor("wu_b", [NHB, D, 128], F32, kind="ExternalInput").ap()
    wd_b = nc.dram_tensor("wd_b", [16, FFH, 128], F32, kind="ExternalInput").ap()
    ident_in = nc.dram_tensor("ident_bf", [128, 128], BF16, kind="ExternalInput").ap()
    identf_in = nc.dram_tensor("ident_f", [128, 128], F32, kind="ExternalInput").ap()
    maskLT_in = nc.dram_tensor("maskLT", [128, 128], F32, kind="ExternalInput").ap()
    maskUT_in = nc.dram_tensor("maskUT", [128, 128], F32, kind="ExternalInput").ap()
    maskUD_in = nc.dram_tensor("maskUD", [128, 128], F32, kind="ExternalInput").ap()
    onesc_in = nc.dram_tensor("ones_col_bf", [128, 1], BF16, kind="ExternalInput").ap()
    onesrf_in = nc.dram_tensor("ones_row_f", [1, 128], F32, kind="ExternalInput").ap()
    onesr_in = nc.dram_tensor("ones_row_bf", [1, 128], BF16, kind="ExternalInput").ap()
    zeros_in = nc.dram_tensor("zeros_f", [128, 128], F32, kind="ExternalInput").ap()
    ccols_in = nc.dram_tensor("ccols", [128, 8], F32, kind="ExternalInput").ap()
    hopm_in = nc.dram_tensor("hop_mcol", [128, 7], F32, kind="ExternalInput").ap()
    hopimi_in = nc.dram_tensor("hop_imi", [128, 7 * 128], F32, kind="ExternalInput").ap()

    y_out = nc.dram_tensor("y", [TPC, D], F32, kind="ExternalOutput").ap()

    PROBE_SHAPES = {
        "pre": ([NFB, 128, TP], BF16),
        "qkv": ([4, 1024, TPC], BF16),
        "rows": ([HV, 4 * TPC], F32),
        "spill": ([NCH, 4, 128, HV * 128], BF16),
        "comp": ([2, 128, HV * 128], BF16),
        "state": ([128, HV * 128], BF16),
        "pf": ([NCH, 128, NDB * 128], BF16),
        "x2": ([TPC, D], F32),
        "xq": ([128, NDB * TPC], BF16),
        "p2": ([128, NHB * TPC], BF16),
        "ffn": ([NDB, 128, TPC], BF16),
    }
    probe_out = None
    if probe is not None:
        shp, dt_ = PROBE_SHAPES[probe]
        probe_out = nc.dram_tensor("probe", shp, dt_, kind="ExternalOutput").ap()

    pre_scr = nc.dram_tensor("pre_scr", [NFB, 128, TP], BF16, kind="Internal").ap()
    spill = nc.dram_tensor("spill", [NCH, 4, 128, HV * 128], BF16, kind="Internal").ap()
    p_scr = nc.dram_tensor("p_scr", [NCH, 128, NDB * 128], BF16, kind="Internal").ap()
    x2_scr = nc.dram_tensor("x2_scr", [TPC, D], F32, kind="Internal").ap()
    ffn_scr = nc.dram_tensor("ffn_scr", [NDB, 128, TPC], BF16, kind="Internal").ap()
    cc_in = nc.dram_tensor("cc_in", [128, HV * 2 * 128], BF16, kind="Internal").ap()
    cc_out = nc.dram_tensor("cc_out", [8 * 128, HV * 2 * 128], BF16, kind="Internal").ap()

    ctr = Ctr()
    from contextlib import ExitStack

    with ExitStack() as top:
        E = top.enter_context
        s_ld = E(nc.semaphore("s_ld"))
        s_lda = E(nc.semaphore("s_lda"))
        s_ldb = E(nc.semaphore("s_ldb"))
        s_st = E(nc.semaphore("s_st"))
        s_sta = E(nc.semaphore("s_sta"))
        s_stb = E(nc.semaphore("s_stb"))
        s_ga = E(nc.semaphore("s_ga"))
        s_gb = E(nc.semaphore("s_gb"))
        s_mm = E(nc.semaphore("s_mm"))
        s_v = E(nc.semaphore("s_v"))
        s_s = E(nc.semaphore("s_s"))
        s_g = E(nc.semaphore("s_g"))
        s_cc = E(nc.semaphore("s_cc"))
        sems = dict(ld=s_ld, lda=s_lda, ldb=s_ldb, st=s_st, sta=s_sta, stb=s_stb,
                    ga=s_ga, gb=s_gb, mm=s_mm, v=s_v, s=s_s, g=s_g, cc=s_cc)

        c_ident = E(nc.sbuf_tensor("c_ident", [128, 128], BF16))
        c_identf = E(nc.sbuf_tensor("c_identf", [128, 128], F32))
        c_maskLT = E(nc.sbuf_tensor("c_maskLT", [128, 128], F32))
        c_maskUT = E(nc.sbuf_tensor("c_maskUT", [128, 128], F32))
        c_maskUD = E(nc.sbuf_tensor("c_maskUD", [128, 128], F32))
        c_onec = E(nc.sbuf_tensor("c_onec", [128, 1], BF16))
        c_onerf = E(nc.sbuf_tensor("c_onerf", [1, 128], F32))
        c_oner = E(nc.sbuf_tensor("c_oner", [1, 128], BF16))
        c_zero = E(nc.sbuf_tensor("c_zero", [128, 128], F32))
        c_cols = E(nc.sbuf_tensor("c_cols", [128, 8], F32))
        c_hopm = E(nc.sbuf_tensor("c_hopm", [128, 7], F32))
        c_hopimi = E(nc.sbuf_tensor("c_hopimi", [128, 7 * 128], F32))
        c_rmsw = E(nc.sbuf_tensor("c_rmsw", [128, 1], F32))
        c_convw = E(nc.sbuf_tensor("c_convw", [128, NFQ * 4], F32))
        c_la = E(nc.sbuf_tensor("c_la", [HV, 8], F32))

        blk = E(nc.Block())

        @blk.sync
        def _(q):
            for dst, src in [
                (c_ident[:, :], ident_in[:, :]), (c_identf[:, :], identf_in[:, :]),
                (c_maskLT[:, :], maskLT_in[:, :]), (c_maskUT[:, :], maskUT_in[:, :]),
                (c_maskUD[:, :], maskUD_in[:, :]), (c_onec[:, :], onesc_in[:, :]),
                (c_onerf[:, :], onesrf_in[:, :]), (c_oner[:, :], onesr_in[:, :]),
                (c_zero[:, :], zeros_in[:, :]), (c_cols[:, :], ccols_in[:, :]),
                (c_hopm[:, :], hopm_in[:, :]),
                (c_hopimi[:, :], hopimi_in[:, :]),
                (c_rmsw[:, :], rmsw_in[:, :]),
                (c_convw[:, :], convw_in[:, :]),
                (c_la[:, 0:2], la_in[:, :]),
            ]:
                q.dma_start(out=dst, in_=src).then_inc(s_ld, 16)
                ctr.inc(s_ld, 16)
        CONSTS = ctr.get(s_ld)

        @blk.scalar
        def _(s):
            s.wait_ge(s_ld, CONSTS)
            ctr.inc(s_s, 1)
            s.activation(c_la[:, 4:5], c_la[:, 0:1], AF.Exp).then_inc(s_s, 1)

        @blk.vector
        def _(v):
            v.wait_ge(s_s, ctr.get(s_s))
            v.tensor_scalar(c_la[:, 3:4], c_la[:, 4:5], -1.0, None, ALU.mult)
            ctr.inc(s_v, 1)
            v.tensor_scalar(c_la[:, 5:6], c_la[:, 1:2], -1.0, None, ALU.mult).then_inc(s_v, 1)

        with ExitStack() as sc2:
            E2 = sc2.enter_context
            qk_f = E2(nc.sbuf_tensor("qk_f", [128, 16 * TPC], BF16))
            k_tok = E2(nc.sbuf_tensor("k_tok", [128, HQ * TPC], BF16))
            v_tok = E2(nc.sbuf_tensor("v_tok", [128, HV * TPC], BF16))
            barows = E2(nc.sbuf_tensor("barows", [HV, 4 * TPC], F32))
            state = E2(nc.sbuf_tensor("state", [128, HV * 128], BF16))
            compP = E2(nc.sbuf_tensor("compP", [128, HV * 128], BF16))
            compR = E2(nc.sbuf_tensor("compR", [128, HV * 128], BF16))

            _p1(nc, blk, ctr, sems, locals())
            if probe == "pre":
                _probe_dump(nc, blk, ctr, sems, probe_out[0:NFB - 1, :, :], pre_scr[0:NFB - 1, :, :])
                _probe_dump(nc, blk, ctr, sems, probe_out[NFB - 1, 0:32, :], pre_scr[NFB - 1, 0:32, :])
                _finish_zero(nc, blk, ctr, sems, y_out)
                return nc, probe

            _p15(nc, blk, ctr, sems, locals())
            if probe == "qkv":
                @blk.sync
                def _(q):
                    q.wait_ge(s_v, ctr.get(s_v))
                    q.wait_ge(s_s, ctr.get(s_s))
                    for i, (src, off) in enumerate([(qk_f, 0), (qk_f, 8 * TPC), (k_tok, 0), (v_tok, 0)]):
                        q.dma_start(
                            out=probe_out[i].rearrange("(hb p) t -> p hb t", p=128)[:, 0:8, :],
                            in_=src.ap()[:, off:off + 8 * TPC].rearrange("p (hb t) -> p hb t", t=TPC),
                        ).then_inc(s_ld, 16)
                        ctr.inc(s_ld, 16)
                    q.wait_ge(s_ld, ctr.get(s_ld))
                _finish_zero(nc, blk, ctr, sems, y_out)
                return nc, probe
            if probe == "rows":
                @blk.sync
                def _(q):
                    q.wait_ge(s_v, ctr.get(s_v))
                    q.wait_ge(s_s, ctr.get(s_s))
                    q.dma_start(out=probe_out[:, :], in_=barows[:, :]).then_inc(s_ld, 16)
                    ctr.inc(s_ld, 16)
                    q.wait_ge(s_ld, ctr.get(s_ld))
                _finish_zero(nc, blk, ctr, sems, y_out)
                return nc, probe

            _p2a(nc, blk, ctr, sems, locals())
            if probe == "spill":
                _probe_dump(nc, blk, ctr, sems, probe_out[:, :, :, :], spill[:, :, :, :])
                _finish_zero(nc, blk, ctr, sems, y_out)
                return nc, probe
            if probe == "comp":
                @blk.sync
                def _(q):
                    q.wait_ge(s_v, ctr.get(s_v))
                    q.wait_ge(s_s, ctr.get(s_s))
                    q.dma_start(out=probe_out[0], in_=compP[:, :]).then_inc(s_ld, 16)
                    q.dma_start(out=probe_out[1], in_=compR[:, :]).then_inc(s_ld, 16)
                    ctr.inc(s_ld, 32)
                    q.wait_ge(s_ld, ctr.get(s_ld))
                _finish_zero(nc, blk, ctr, sems, y_out)
                return nc, probe

            _cc_hop(nc, blk, ctr, sems, locals())
            if probe == "state":
                @blk.sync
                def _(q):
                    q.wait_ge(s_v, ctr.get(s_v))
                    q.dma_start(out=probe_out[:, :], in_=state[:, :]).then_inc(s_ld, 16)
                    ctr.inc(s_ld, 16)
                    q.wait_ge(s_ld, ctr.get(s_ld))
                _finish_zero(nc, blk, ctr, sems, y_out)
                return nc, probe

            _p2b(nc, blk, ctr, sems, locals())
            if probe == "pf":
                _probe_dump(nc, blk, ctr, sems, probe_out[:, :, :], p_scr[:, :, :])
                _finish_zero(nc, blk, ctr, sems, y_out)
                return nc, probe

        with ExitStack() as sc3:
            E3 = sc3.enter_context
            xq_f = E3(nc.sbuf_tensor("xq_f", [128, NDB * TPC], BF16))

            _p3(nc, blk, ctr, sems, locals())
            if probe in ("x2", "xq"):
                if probe == "x2":
                    _probe_dump(nc, blk, ctr, sems, probe_out[:, :], x2_scr[:, :])
                else:
                    @blk.sync
                    def _(q):
                        q.wait_ge(s_s, ctr.get(s_s))
                        q.dma_start(out=probe_out[:, :], in_=xq_f[:, :]).then_inc(s_ld, 16)
                        ctr.inc(s_ld, 16)
                        q.wait_ge(s_ld, ctr.get(s_ld))
                _finish_zero(nc, blk, ctr, sems, y_out)
                return nc, probe

            with ExitStack() as sff:
                EF = sff.enter_context
                p2 = EF(nc.sbuf_tensor("p2", [128, NHB * TPC], BF16))
                _ffn_gu(nc, blk, ctr, sems, locals())
                if probe == "p2":
                    @blk.sync
                    def _(q):
                        q.wait_ge(s_v, ctr.get(s_v))
                        q.dma_start(out=probe_out[:, :], in_=p2[:, :]).then_inc(s_ld, 16)
                        ctr.inc(s_ld, 16)
                        q.wait_ge(s_ld, ctr.get(s_ld))
                    _finish_zero(nc, blk, ctr, sems, y_out)
                    return nc, probe
                _ffn_down(nc, blk, ctr, sems, locals())
            if probe == "ffn":
                _probe_dump(nc, blk, ctr, sems, probe_out[:, :, :], ffn_scr[:, :, :])
                _finish_zero(nc, blk, ctr, sems, y_out)
                return nc, probe

        _final(nc, blk, ctr, sems, locals())
    return nc, probe


def _phase_barrier(nc, blk, ctr, sems):
    names = ("ld", "lda", "ldb", "st", "sta", "stb", "ga", "gb", "mm", "v", "s", "g", "cc")
    vals = {nm: ctr.get(sems[nm]) for nm in names}
    for eng in ("sync", "tensor", "vector", "scalar", "gpsimd"):
        @getattr(blk, eng)
        def _(e, eng=eng):
            for nm in names:
                e.wait_ge(sems[nm], vals[nm])


def _probe_dump(nc, blk, ctr, sems, dst_ap, src_ap):
    @blk.sync
    def _(q):
        for nm in ("st", "sta", "stb", "v", "s", "g", "mm"):
            q.wait_ge(sems[nm], ctr.get(sems[nm]))
        q.dma_start(out=dst_ap, in_=src_ap).then_inc(sems["ld"], 16)
        ctr.inc(sems["ld"], 16)
        q.wait_ge(sems["ld"], ctr.get(sems["ld"]))


def _finish_zero(nc, blk, ctr, sems, y_out):
    s_st, s_g = sems["st"], sems["g"]
    _phase_barrier(nc, blk, ctr, sems)
    with nc.sbuf_tensor("yzero", [128, D], F32) as yz:
        @blk.gpsimd
        def _(g):
            ctr.inc(s_g, 1)
            g.memset(yz[:], 0.0).then_inc(s_g, 1)
        GV = ctr.get(s_g)

        @blk.sync
        def _(q):
            q.wait_ge(s_g, GV)
            for tb in range(TPC // 128):
                q.dma_start(out=y_out[tb * 128:(tb + 1) * 128, :], in_=yz[:]).then_inc(s_st, 16)
                ctr.inc(s_st, 16)
            q.wait_ge(s_st, ctr.get(s_st))


def _p1(nc, blk, ctr, sems, env):
    s_ld, s_st, s_mm, s_v, s_s = sems["ld"], sems["st"], sems["mm"], sems["v"], sems["s"]
    s_lda, s_ldb, s_sta, s_stb = sems["lda"], sems["ldb"], sems["sta"], sems["stb"]
    ldp = [s_lda, s_ldb]
    stp = [s_sta, s_stb]
    x_in, c_ident = env["x_in"], env["c_ident"]
    pre_scr = env["pre_scr"]
    wqkv_b, wgate_b, wba_in = env["wqkv_b"], env["wgate_b"], env["wba_in"]
    CONSTS = env["CONSTS"]
    from contextlib import ExitStack

    with ExitStack() as st1:
        E1 = st1.enter_context
        xtile = E1(nc.sbuf_tensor("xtile", [128, 2 * D], F32))
        xn_t = E1(nc.sbuf_tensor("xn_t", [128, D], BF16))
        xn_f = E1(nc.sbuf_tensor("xn_f", [128, NDB * TP], BF16))
        lnt = E1(nc.sbuf_tensor("lnt", [128, 32], F32))
        ln_stack = ExitStack()
        ps_ln0 = ln_stack.enter_context(nc.psum_tensor("ps_ln0", [128, D], F32))
        ps_ln1 = ln_stack.enter_context(nc.psum_tensor("ps_ln1", [128, D], F32))

        for tb in range(9):
            rows = 128 if tb < 8 else TP - 8 * 128
            xt = xtile[:rows, (tb % 2) * D:(tb % 2) * D + D]
            psl = ps_ln0 if tb % 2 == 0 else ps_ln1

            sl = ldp[tb % 2]

            @blk.sync
            def _(q, tb=tb, rows=rows, xt=xt, sl=sl):
                if tb >= 2:
                    q.wait_ge(s_s, ctr.get(s_s))
                q.wait_ge(sl, ctr.get(sl))
                q.dma_start(out=xt, in_=x_in[tb * 128: tb * 128 + rows, :]).then_inc(sl, 16)
                ctr.inc(sl, 16)
            XW = ctr.get(sl)

            @blk.vector
            def _(v, rows=rows, xt=xt, XW=XW, sl=sl):
                v.wait_ge(sl, XW)
                ins = None
                for i4 in range(4):
                    ins = v.bn_stats(lnt[:rows, i4 * 6:(i4 + 1) * 6], xt[:, i4 * 512:(i4 + 1) * 512])
                ctr.inc(s_v, 1); ins.then_inc(s_v, 1); v.wait_ge(s_v, ctr.get(s_v))
                ins = v.bn_aggr(lnt[:rows, 24:26], lnt[:rows, 0:24])
                ctr.inc(s_v, 1); ins.then_inc(s_v, 1); v.wait_ge(s_v, ctr.get(s_v))
                ctr.inc(s_v, 1)
                v.tensor_scalar(lnt[:rows, 26:27], lnt[:rows, 25:26], EPS_LN, None, ALU.add).then_inc(s_v, 1)
            V1 = ctr.get(s_v)

            @blk.scalar
            def _(s, rows=rows, V1=V1):
                s.wait_ge(s_v, V1)
                ctr.inc(s_s, 1)
                s.activation(lnt[:rows, 27:28], lnt[:rows, 26:27], AF.Sqrt).then_inc(s_s, 1)
            S1 = ctr.get(s_s)

            @blk.vector
            def _(v, rows=rows, S1=S1):
                v.wait_ge(s_s, S1)
                ins = v.reciprocal(lnt[:rows, 28:29], lnt[:rows, 27:28])
                ctr.inc(s_v, 1); ins.then_inc(s_v, 1); v.wait_ge(s_v, ctr.get(s_v))
                ctr.inc(s_v, 1)
                v.scalar_tensor_tensor(out=lnt[:rows, 29:30], in0=lnt[:rows, 24:25],
                                       scalar=-1.0, in1=lnt[:rows, 28:29],
                                       op0=ALU.mult, op1=ALU.mult).then_inc(s_v, 1)
            V2 = ctr.get(s_v)

            @blk.scalar
            def _(s, tb=tb, rows=rows, xt=xt, V2=V2):
                s.wait_ge(s_v, V2)
                if tb >= 1:
                    s.wait_ge(s_mm, ctr.get(s_mm))
                ctr.inc(s_s, 1)
                s.activation(xn_t[:rows, :], xt, AF.Identity,
                             bias=lnt[:rows, 29:30], scale=lnt[:rows, 28:29]).then_inc(s_s, 1)
            S2 = ctr.get(s_s)

            @blk.tensor
            def _(t, tb=tb, rows=rows, S2=S2, psl=psl):
                t.wait_ge(s_s, S2)
                t.wait_ge(s_ld, CONSTS)
                if tb >= 2:
                    t.wait_ge(s_s, ctr.get(s_s))
                mm = None
                for db in range(NDB):
                    mm = t.matmul(psl[:, db * 128:db * 128 + rows],
                                  xn_t[:rows, db * 128:(db + 1) * 128],
                                  c_ident[:rows, :rows], start=True, stop=True)
                ctr.inc(s_mm, 1)
                mm.then_inc(s_mm, 1)
            M1 = ctr.get(s_mm)

            @blk.scalar
            def _(s, tb=tb, rows=rows, M1=M1, psl=psl):
                s.wait_ge(s_mm, M1)
                ctr.inc(s_s, 1)
                s.copy(
                    xn_f.ap().rearrange("p (db c) -> p db c", db=NDB)[:, :, tb * 128:tb * 128 + rows],
                    psl.ap().rearrange("p (db c) -> p db c", db=NDB)[:, :, :rows],
                ).then_inc(s_s, 1)
        XNF = ctr.get(s_s)
        ln_stack.close()

        wf32 = E1(nc.sbuf_tensor("wf32", [128, 2 * NDB * 128], F32))
        wbf = E1(nc.sbuf_tensor("wbf", [128, 2 * NDB * 128], BF16))
        pre_sb = E1(nc.sbuf_tensor("pre_sb", [128, 2 * TP], BF16))
        ps_pa = [E1(nc.psum_tensor(f"ps_pa{i}", [128, TG], F32)) for i in range(6)]

        def wsrc(fb):
            if fb < NFQ:
                return wqkv_b[fb]
            if fb < NFQ + NFG:
                return wgate_b[fb - NFQ]
            return wba_in

        for fb in range(NFB):
            buf = fb % 2
            fcols = 128 if fb < NFB - 1 else 32
            wsl = wf32[:, buf * NDB * 128:][:, :NDB * fcols]
            wbl = wbf[:, buf * NDB * 128:][:, :NDB * fcols]
            psl = ps_pa[buf * 3:(buf + 1) * 3]
            prl = pre_sb[:fcols, buf * TP:buf * TP + TP]

            sl = ldp[fb % 2]

            @blk.sync
            def _(q, fb=fb, wsl=wsl, fcols=fcols, sl=sl):
                if fb == 0:
                    q.wait_ge(s_lda, ctr.get(s_lda))
                    q.wait_ge(s_ldb, ctr.get(s_ldb))
                if fb >= 2:
                    q.wait_ge(s_v, ctr.get(s_v))
                q.wait_ge(sl, ctr.get(sl))
                q.dma_start(out=wsl.rearrange("p (db f) -> p db f", f=fcols),
                            in_=wsrc(fb).rearrange("(db p) f -> p db f", p=128)).then_inc(sl, 16)
                ctr.inc(sl, 16)
            WL = ctr.get(sl)

            @blk.vector
            def _(v, fb=fb, wsl=wsl, wbl=wbl, WL=WL, sl=sl):
                v.wait_ge(sl, WL)
                if fb >= 2:
                    v.wait_ge(s_mm, ctr.get(s_mm))
                ctr.inc(s_v, 1)
                v.tensor_copy(wbl, wsl).then_inc(s_v, 1)
            WC = ctr.get(s_v)

            @blk.tensor
            def _(t, fb=fb, wbl=wbl, psl=psl, fcols=fcols, WC=WC):
                t.wait_ge(s_v, WC)
                t.wait_ge(s_s, XNF)
                if fb >= 2:
                    t.wait_ge(s_s, ctr.get(s_s))
                mm = None
                for tg in range(3):
                    for db in range(NDB):
                        mm = t.matmul(psl[tg][:fcols, :],
                                      wbl.rearrange("p (db f) -> p db f", db=NDB)[:, db, :],
                                      xn_f.ap().rearrange("p (db c) -> p db c", db=NDB)[:, db, tg * TG:(tg + 1) * TG],
                                      start=(db == 0), stop=(db == NDB - 1))
                ctr.inc(s_mm, 1)
                mm.then_inc(s_mm, 1)
            PM = ctr.get(s_mm)

            @blk.scalar
            def _(s, fb=fb, psl=psl, prl=prl, fcols=fcols, PM=PM):
                s.wait_ge(s_mm, PM)
                if fb >= 2:
                    s.wait_ge(stp[fb % 2], ctr.get(stp[fb % 2]))
                cp = None
                for tg in range(3):
                    cp = s.copy(prl[:, tg * TG:(tg + 1) * TG], psl[tg][:fcols, :])
                ctr.inc(s_s, 1)
                cp.then_inc(s_s, 1)
            PE = ctr.get(s_s)

            @blk.gpsimd
            def _(g, fb=fb, prl=prl, fcols=fcols, PE=PE):
                g.wait_ge(s_s, PE)
                g.wait_ge(stp[fb % 2], ctr.get(stp[fb % 2]))
                g.dma_start(out=pre_scr[fb, :fcols, :], in_=prl).then_inc(stp[fb % 2], 16)
                ctr.inc(stp[fb % 2], 16)


def _p15(nc, blk, ctr, sems, env):
    s_ld, s_st, s_mm, s_v, s_s = sems["ld"], sems["st"], sems["mm"], sems["v"], sems["s"]
    s_lda, s_ldb, s_sta, s_stb = sems["lda"], sems["ldb"], sems["sta"], sems["stb"]
    ldp = [s_lda, s_ldb]
    pre_scr, c_convw, c_ident, c_onec, c_onerf = (
        env["pre_scr"], env["c_convw"], env["c_ident"], env["c_onec"], env["c_onerf"])
    qk_f, k_tok, v_tok, barows, c_la, c_zero, c_cols = (
        env["qk_f"], env["k_tok"], env["v_tok"], env["barows"], env["c_la"],
        env["c_zero"], env["c_cols"])
    P1_STA = ctr.get(s_sta)
    P1_STB = ctr.get(s_stb)
    from contextlib import ExitStack

    with ExitStack() as st15:
        E1 = st15.enter_context
        pre_t = E1(nc.sbuf_tensor("pre_t", [128, 2 * TP], BF16))
        cv = E1(nc.sbuf_tensor("cv", [128, TPC], F32))
        sil = E1(nc.sbuf_tensor("sil", [128, TPC], F32))
        sqb = E1(nc.sbuf_tensor("sqb", [128, TPC], BF16))
        rs_row = E1(nc.sbuf_tensor("rs_row", [1, TPC], F32))
        rs_f32 = E1(nc.sbuf_tensor("rs_f32", [1, TPC], F32))
        vtile = E1(nc.sbuf_tensor("vtile", [128, TPC], BF16))
        ps_l2 = E1(nc.psum_tensor("ps_l2", [1, TPC], F32))
        ps_bc = E1(nc.psum_tensor("ps_bc", [128, TPC], F32))
        ps_tr = E1(nc.psum_tensor("ps_tr", [128, TPC], F32))

        for fb in range(NFQ):
            buf = fb % 2
            pt = pre_t[:, buf * TP:buf * TP + TP]
            is_q = fb < 8
            is_k = 8 <= fb < 16
            head = fb if is_q else (fb - 8 if is_k else fb - 16)

            sl = ldp[fb % 2]

            @blk.sync
            def _(q, fb=fb, pt=pt, sl=sl):
                if fb == 0:
                    q.wait_ge(s_sta, P1_STA)
                    q.wait_ge(s_stb, P1_STB)
                    q.wait_ge(s_lda, ctr.get(s_lda))
                    q.wait_ge(s_ldb, ctr.get(s_ldb))
                if fb >= 2:
                    q.wait_ge(s_v, ctr.get(s_v))
                q.wait_ge(sl, ctr.get(sl))
                q.dma_start(out=pt, in_=pre_scr[fb, :, :]).then_inc(sl, 16)
                ctr.inc(sl, 16)
            LD = ctr.get(sl)

            @blk.vector
            def _(v, fb=fb, pt=pt, LD=LD, sl=sl):
                v.wait_ge(sl, LD)
                if fb > 0:
                    v.wait_ge(s_s, ctr.get(s_s))
                ins = v.tensor_scalar(cv[:, :], pt[:, 0:TPC], c_convw[:, fb * 4:fb * 4 + 1], None, ALU.mult)
                for j in range(1, 4):
                    ctr.inc(s_v, 1); ins.then_inc(s_v, 1); v.wait_ge(s_v, ctr.get(s_v))
                    ins = v.scalar_tensor_tensor(
                        out=cv[:, :], in0=pt[:, j:j + TPC],
                        scalar=c_convw[:, fb * 4 + j:fb * 4 + j + 1], in1=cv[:, :],
                        op0=ALU.mult, op1=ALU.add)
                ctr.inc(s_v, 1)
                ins.then_inc(s_v, 1)
            CV = ctr.get(s_v)

            if is_q or is_k:
                @blk.scalar
                def _(s, CV=CV, fb=fb):
                    s.wait_ge(s_v, CV)
                    if fb >= 2:
                        s.wait_ge(s_v, ctr.get(s_v))
                    ins = s.activation(sil[:, :], cv[:, :], AF.Silu)
                    ctr.inc(s_s, 1); ins.then_inc(s_s, 1); s.wait_ge(s_s, ctr.get(s_s))
                    ctr.inc(s_s, 1)
                    s.activation(sqb[:, :], sil[:, :], AF.Square).then_inc(s_s, 1)
                SQ = ctr.get(s_s)

                @blk.tensor
                def _(t, SQ=SQ, fb=fb):
                    t.wait_ge(s_s, SQ)
                    if fb >= 1:
                        t.wait_ge(s_s, ctr.get(s_s))
                    mm = None
                    for half in range(2):
                        mm = t.matmul(ps_l2[0:1, half * 512:(half + 1) * 512], c_onec[:, :],
                                      sqb[:, half * 512:(half + 1) * 512], start=True, stop=True)
                    ctr.inc(s_mm, 1)
                    mm.then_inc(s_mm, 1)
                MMS = ctr.get(s_mm)

                @blk.scalar
                def _(s, MMS=MMS):
                    s.wait_ge(s_mm, MMS)
                    ctr.inc(s_s, 1)
                    s.activation(rs_f32[0:1, :], ps_l2[0:1, :], AF.Sqrt, bias=c_cols[0:1, 0:1]).then_inc(s_s, 1)
                RS = ctr.get(s_s)

                @blk.vector
                def _(v, RS=RS):
                    v.wait_ge(s_s, RS)
                    ctr.inc(s_v, 1)
                    v.reciprocal(rs_row[0:1, :], rs_f32[0:1, :]).then_inc(s_v, 1)
                RR = ctr.get(s_v)

                @blk.tensor
                def _(t, RR=RR, fb=fb):
                    t.wait_ge(s_v, RR)
                    if fb >= 1:
                        t.wait_ge(s_v, ctr.get(s_v))
                    mm = None
                    for half in range(2):
                        mm = t.matmul(ps_bc[:, half * 512:(half + 1) * 512],
                                      c_onerf[0:1, :],
                                      rs_row[0:1, half * 512:(half + 1) * 512], start=True, stop=True)
                    ctr.inc(s_mm, 1)
                    mm.then_inc(s_mm, 1)
                BC = ctr.get(s_mm)

                @blk.vector
                def _(v, fb=fb, head=head, is_q=is_q, BC=BC):
                    v.wait_ge(s_mm, BC)
                    dst = qk_f.ap()[:, (head + (0 if is_q else 8)) * TPC:][:, :TPC]
                    ctr.inc(s_v, 1)
                    v.tensor_tensor(dst, sil[:, :], ps_bc[:, :], ALU.mult).then_inc(s_v, 1)
                QN = ctr.get(s_v)

                if is_k:
                    @blk.tensor
                    def _(t, head=head, QN=QN, fb=fb):
                        t.wait_ge(s_v, QN)
                        if fb >= 9:
                            t.wait_ge(s_s, ctr.get(s_s))
                        src = qk_f.ap()[:, (head + 8) * TPC:][:, :TPC]
                        mm = None
                        for ci in range(NCH):
                            mm = t.matmul(ps_tr[:, ci * 128:(ci + 1) * 128],
                                          src[:, ci * 128:(ci + 1) * 128],
                                          c_ident[:, :], start=True, stop=True)
                        ctr.inc(s_mm, 1)
                        mm.then_inc(s_mm, 1)
                    KT = ctr.get(s_mm)

                    @blk.scalar
                    def _(s, head=head, KT=KT):
                        s.wait_ge(s_mm, KT)
                        ctr.inc(s_s, 1)
                        s.copy(k_tok.ap().rearrange("p (ci q s) -> p ci q s", q=HQ, s=128)[:, :, head, :],
                               ps_tr.ap().rearrange("p (ci s) -> p ci s", s=128)).then_inc(s_s, 1)
            else:
                @blk.scalar
                def _(s, CV=CV, fb=fb):
                    s.wait_ge(s_v, CV)
                    if fb >= 17:
                        s.wait_ge(s_mm, ctr.get(s_mm))
                    ctr.inc(s_s, 1)
                    s.activation(vtile[:, :], cv[:, :], AF.Silu).then_inc(s_s, 1)
                VS = ctr.get(s_s)

                @blk.tensor
                def _(t, head=head, VS=VS, fb=fb):
                    t.wait_ge(s_s, VS)
                    if fb >= 17:
                        t.wait_ge(s_s, ctr.get(s_s))
                    mm = None
                    for ci in range(NCH):
                        mm = t.matmul(ps_tr[:, ci * 128:(ci + 1) * 128],
                                      vtile[:, ci * 128:(ci + 1) * 128],
                                      c_ident[:, :], start=True, stop=True)
                    ctr.inc(s_mm, 1)
                    mm.then_inc(s_mm, 1)
                VT = ctr.get(s_mm)

                @blk.scalar
                def _(s, head=head, VT=VT):
                    s.wait_ge(s_mm, VT)
                    ctr.inc(s_s, 1)
                    s.copy(v_tok.ap().rearrange("p (ci hh s) -> p ci hh s", hh=HV, s=128)[:, :, head, :],
                           ps_tr.ap().rearrange("p (ci s) -> p ci s", s=128)).then_inc(s_s, 1)

        ba_t = E1(nc.sbuf_tensor("ba_t", [16, TPC], BF16))
        aa_t = E1(nc.sbuf_tensor("aa_t", [16, TPC], BF16))
        tmp16 = E1(nc.sbuf_tensor("tmp16", [HV, TPC], F32))

        @blk.sync
        def _(q):
            q.wait_ge(s_lda, ctr.get(s_lda))
            q.wait_ge(s_ldb, ctr.get(s_ldb))
            q.dma_start(out=ba_t[:, :], in_=pre_scr[NFB - 1, 0:16, HALO:HALO + TPC]).then_inc(s_lda, 16)
            q.dma_start(out=aa_t[:, :], in_=pre_scr[NFB - 1, 16:32, HALO:HALO + TPC]).then_inc(s_lda, 16)
            ctr.inc(s_lda, 32)
        BA = ctr.get(s_lda)

        @blk.scalar
        def _(s, BA=BA):
            s.wait_ge(s_lda, BA)
            s.wait_ge(s_v, ctr.get(s_v))
            ins = s.activation(barows[:, 0:TPC], ba_t[:, :], AF.Sigmoid)
            ctr.inc(s_s, 1); ins.then_inc(s_s, 1); s.wait_ge(s_s, ctr.get(s_s))
            s.activation(barows[:, TPC:2 * TPC], barows[:, 0:TPC], AF.Ln)
            ins = s.activation(tmp16[:, :], aa_t[:, :], AF.Sigmoid, scale=-1.0, bias=c_la[:, 5:6])
            ctr.inc(s_s, 1); ins.then_inc(s_s, 1); s.wait_ge(s_s, ctr.get(s_s))
            ctr.inc(s_s, 1)
            s.activation(tmp16[:, :], tmp16[:, :], AF.Ln).then_inc(s_s, 1)
        SB = ctr.get(s_s)

        @blk.vector
        def _(v, SB=SB):
            v.wait_ge(s_s, SB)
            ins = v.tensor_scalar(barows[:, 2 * TPC:3 * TPC], tmp16[:, :], c_la[:, 4:5], None, ALU.mult)
            ctr.inc(s_v, 1); ins.then_inc(s_v, 1); v.wait_ge(s_v, ctr.get(s_v))
            ins = None
            for ci in range(NCH):
                ins = v.tensor_tensor_scan(
                    out=barows[:, 3 * TPC + ci * 128:3 * TPC + (ci + 1) * 128],
                    data0=barows[:, 2 * TPC + ci * 128:2 * TPC + (ci + 1) * 128],
                    data1=barows[:, 2 * TPC + ci * 128:2 * TPC + (ci + 1) * 128],
                    initial=c_zero[0:HV, 0:1],
                    op0=ALU.add, op1=ALU.bypass)
            ctr.inc(s_v, 1)
            ins.then_inc(s_v, 1)


def _p2a(nc, blk, ctr, sems, env):
    s_mm, s_v, s_s, s_g = sems["mm"], sems["v"], sems["s"], sems["g"]
    s_sta, s_stb = sems["sta"], sems["stb"]
    stp = [s_sta, s_stb]
    qk_f, k_tok, v_tok, barows = env["qk_f"], env["k_tok"], env["v_tok"], env["barows"]
    compP, compR = env["compP"], env["compR"]
    c_ident, c_identf, c_onerf = env["c_ident"], env["c_identf"], env["c_onerf"]
    c_maskLT, c_maskUT = env["c_maskLT"], env["c_maskUT"]
    spill = env["spill"]
    from contextlib import ExitStack

    W = HV * 128

    def vx(ap3):
        return ap3

    with ExitStack() as sta_:
        E1 = sta_.enter_context
        mrow_all = E1(nc.sbuf_tensor("mrow_all", [HV, 128], F32))
        cols = E1(nc.sbuf_tensor("colsA", [128, 64], F32))
        rowstall = E1(nc.sbuf_tensor("rowstall", [1, 2 * W], F32))
        e_t = E1(nc.sbuf_tensor("e_t_all", [128, W], F32))
        e_tt = E1(nc.sbuf_tensor("e_tt_all", [128, W], F32))
        X = E1(nc.sbuf_tensor("X_all", [128, W], BF16))
        Xt = E1(nc.sbuf_tensor("Xt_all", [128, W], BF16))
        NT = E1(nc.sbuf_tensor("NT_all", [128, W], BF16))
        Hb = E1(nc.sbuf_tensor("Hb_all", [128, W], BF16))
        G_sb = E1(nc.sbuf_tensor("G_all", [128, HQ * 128], BF16))
        bVt = E1(nc.sbuf_tensor("bVt_all", [128, W], BF16))
        bAKt = E1(nc.sbuf_tensor("bAKt_all", [128, W], BF16))
        kht = E1(nc.sbuf_tensor("kht_all", [128, W], BF16))
        U0 = E1(nc.sbuf_tensor("U0_all", [128, W], BF16))
        Yc = E1(nc.sbuf_tensor("Yc_all", [128, W], BF16))
        Rsb = E1(nc.sbuf_tensor("Rsb_all", [128, W], BF16))
        PT = E1(nc.sbuf_tensor("PT_all", [128, W], BF16))
        acb = E1(nc.sbuf_tensor("acbA", [128, 48], F32))
        ps_A = E1(nc.psum_tensor("ps_A", [128, W], F32))
        ps_B = E1(nc.psum_tensor("ps_B", [128, W], F32))

        ROWS_V = ctr.get(s_v)

        def gview(ap, wslice=128):
            return ap.rearrange("p (q s) -> p q s", s=128).unsqueeze(2).broadcast_to((128, HQ, 2, wslice))

        def colview(ap2):
            return ap2.unsqueeze(2).broadcast_to((128, HV, 128))

        def mview(ap2):
            return ap2.unsqueeze(1).broadcast_to((128, HV, 128))

        def w3(ap):
            return ap.rearrange("p (h s) -> p h s", s=128)

        for ci in range(NCH):
            ell_r = barows[:, 3 * TPC + ci * 128:3 * TPC + (ci + 1) * 128]
            lb_r = barows[:, TPC + ci * 128:TPC + (ci + 1) * 128]
            beta_r = barows[:, ci * 128:(ci + 1) * 128]
            kslab = qk_f
            kf_chunk = lambda hq: qk_f.ap()[:, (8 + hq) * TPC + ci * 128:][:, :128]
            ktok_chunk = k_tok.ap()[:, ci * HQ * 128:(ci + 1) * HQ * 128]
            vtok_chunk = v_tok.ap()[:, ci * W:(ci + 1) * W]

            @blk.vector
            def _(v, ell_r=ell_r, lb_r=lb_r, ci=ci):
                v.wait_ge(s_v, ROWS_V)
                if ci > 0:
                    v.wait_ge(s_mm, ctr.get(s_mm))
                ctr.inc(s_v, 1)
                v.tensor_tensor(mrow_all[:, :], ell_r, lb_r, ALU.add).then_inc(s_v, 1)
            MR = ctr.get(s_v)

            @blk.tensor
            def _(t, ell_r=ell_r, beta_r=beta_r, MR=MR, ci=ci):
                t.wait_ge(s_v, MR)
                if ci > 0:
                    t.wait_ge(s_s, ctr.get(s_s))
                    t.wait_ge(s_g, ctr.get(s_g))
                t.matmul(ps_A[:, 0:16], ell_r, c_identf[0:HV, 0:HV], start=True, stop=True)
                t.matmul(ps_A[:, 16:32], mrow_all[:, :], c_identf[0:HV, 0:HV], start=True, stop=True)
                ctr.inc(s_mm, 1)
                t.matmul(ps_A[:, 32:48], beta_r, c_identf[0:HV, 0:HV], start=True, stop=True).then_inc(s_mm, 1)
            CT = ctr.get(s_mm)

            @blk.scalar
            def _(s, CT=CT):
                s.wait_ge(s_mm, CT)
                ctr.inc(s_s, 1)
                s.copy(cols[:, 0:48], ps_A[:, 0:48]).then_inc(s_s, 1)
            COLS = ctr.get(s_s)

            @blk.tensor
            def _(t, COLS=COLS, ci=ci):
                t.wait_ge(s_s, COLS)
                if ci > 0:
                    t.wait_ge(s_v, ctr.get(s_v))
                mm = None
                for h in range(HV):
                    mm = t.matmul(ps_A[0:1, h * 128:(h + 1) * 128], cols[:, h:h + 1],
                                  c_identf[:, :], start=True, stop=True)
                    mm = t.matmul(ps_B[0:1, h * 128:(h + 1) * 128], cols[:, 16 + h:16 + h + 1],
                                  c_identf[:, :], start=True, stop=True)
                ctr.inc(s_mm, 1)
                mm.then_inc(s_mm, 1)
            RWM = ctr.get(s_mm)

            @blk.scalar
            def _(s, RWM=RWM):
                s.wait_ge(s_mm, RWM)
                s.copy(rowstall[0:1, 0:W], ps_A[0:1, 0:W])
                ctr.inc(s_s, 1)
                s.copy(rowstall[0:1, W:2 * W], ps_B[0:1, 0:W]).then_inc(s_s, 1)
            RWS = ctr.get(s_s)

            @blk.tensor
            def _(t, RWS=RWS):
                t.wait_ge(s_s, RWS)
                mm = None
                for qq in range(4):
                    sl = slice(qq * 512, (qq + 1) * 512)
                    t.matmul(ps_A[:, sl], c_onerf[0:1, :], rowstall[0:1, sl], start=True, stop=True)
                    mm = t.matmul(ps_B[:, sl], c_onerf[0:1, :], rowstall[0:1, W + qq * 512:W + (qq + 1) * 512], start=True, stop=True)
                ctr.inc(s_mm, 1)
                mm.then_inc(s_mm, 1)
            BC = ctr.get(s_mm)

            @blk.vector
            def _(v, BC=BC):
                v.wait_ge(s_mm, BC)
                ins = v.tensor_tensor(w3(e_t.ap()), colview(cols[:, 16:32]), w3(ps_A.ap()[:, 0:W]), ALU.subtract)
                ins = v.tensor_tensor(w3(e_tt.ap()), w3(ps_B.ap()[:, 0:W]), colview(cols[:, 0:16]), ALU.subtract)
                ins = v.tensor_tensor(acb[:, 16:32], ps_A.ap()[:, 127::128], cols[:, 0:16], ALU.subtract)
                ins = v.tensor_copy(acb[:, 0:16], ps_A.ap()[:, 127::128])
                ctr.inc(s_v, 1); ins.then_inc(s_v, 1); v.wait_ge(s_v, ctr.get(s_v))
                v.tensor_tensor(w3(e_t.ap()), w3(e_t.ap()), mview(c_maskLT.ap()), ALU.add)
                ctr.inc(s_v, 1)
                v.tensor_tensor(w3(e_tt.ap()), w3(e_tt.ap()), mview(c_maskUT.ap()), ALU.add).then_inc(s_v, 1)
            EA = ctr.get(s_v)

            @blk.scalar
            def _(s, EA=EA):
                s.wait_ge(s_v, EA)
                s.activation(e_t[:, :], e_t[:, :], AF.Exp)
                s.activation(e_tt[:, :], e_tt[:, :], AF.Exp)
                s.activation(acb[:, 32:48], cols[:, 16:32], AF.Exp)
                ins = s.activation(acb[:, 16:32], acb[:, 16:32], AF.Exp)
                ctr.inc(s_s, 1); ins.then_inc(s_s, 1); s.wait_ge(s_s, ctr.get(s_s))
                ctr.inc(s_s, 1)
                s.activation(acb[:, 0:16], acb[:, 0:16], AF.Exp).then_inc(s_s, 1)
            EX = ctr.get(s_s)

            @blk.tensor
            def _(t, kf_chunk=kf_chunk, EX=EX, ci=ci):
                t.wait_ge(s_s, EX)
                if ci > 0:
                    t.wait_ge(s_v, ctr.get(s_v))
                mm = None
                for hq in range(HQ):
                    mm = t.matmul(ps_A[:, hq * 128:(hq + 1) * 128], kf_chunk(hq), kf_chunk(hq),
                                  start=True, stop=True)
                ctr.inc(s_mm, 1)
                mm.then_inc(s_mm, 1)
            GM = ctr.get(s_mm)

            @blk.scalar
            def _(s, GM=GM):
                s.wait_ge(s_mm, GM)
                ctr.inc(s_s, 1)
                s.copy(G_sb[:, :], ps_A[:, 0:HQ * 128]).then_inc(s_s, 1)
            GE = ctr.get(s_s)

            @blk.vector
            def _(v, GE=GE):
                v.wait_ge(s_s, GE)
                ins = v.tensor_tensor(w3(e_t.ap()).rearrange("p (q r) s -> p q r s", r=2),
                                      gview(G_sb.ap()), w3(e_t.ap()).rearrange("p (q r) s -> p q r s", r=2),
                                      ALU.mult)
                ins = v.tensor_tensor(w3(e_tt.ap()).rearrange("p (q r) s -> p q r s", r=2),
                                      gview(G_sb.ap()), w3(e_tt.ap()).rearrange("p (q r) s -> p q r s", r=2),
                                      ALU.mult)
                ctr.inc(s_v, 1); ins.then_inc(s_v, 1); v.wait_ge(s_v, ctr.get(s_v))
                v.tensor_tensor(w3(X.ap()), mview(c_identf.ap()), w3(e_t.ap()), ALU.subtract)
                v.tensor_tensor(w3(Xt.ap()), mview(c_identf.ap()), w3(e_tt.ap()), ALU.subtract)
                ctr.inc(s_v, 1)
                v.tensor_tensor(w3(NT.ap()), mview(c_identf.ap()), w3(e_tt.ap()), ALU.add).then_inc(s_v, 1)
            TX = ctr.get(s_v)

            last_v = TX
            last_s = ctr.get(s_s)
            for it in range(NEWTON_ITERS):
                @blk.tensor
                def _(t, it=it, last_v=last_v, last_s=last_s):
                    t.wait_ge(s_v, last_v)
                    t.wait_ge(s_s, last_s)
                    mm = None
                    for h in range(HV):
                        mm = t.matmul(ps_A[:, h * 128:(h + 1) * 128], NT[:, h * 128:(h + 1) * 128],
                                      X[:, h * 128:(h + 1) * 128], start=True, stop=True)
                    ctr.inc(s_mm, 1)
                    mm.then_inc(s_mm, 1)
                HM = ctr.get(s_mm)

                @blk.scalar
                def _(s, HM=HM):
                    s.wait_ge(s_mm, HM)
                    ctr.inc(s_s, 1)
                    s.copy(Hb[:, :], ps_A[:, :]).then_inc(s_s, 1)
                HE = ctr.get(s_s)

                @blk.tensor
                def _(t, HE=HE):
                    t.wait_ge(s_s, HE)
                    mm = None
                    for h in range(HV):
                        mm = t.matmul(ps_B[:, h * 128:(h + 1) * 128], Xt[:, h * 128:(h + 1) * 128],
                                      Hb[:, h * 128:(h + 1) * 128], start=True, stop=True)
                    ctr.inc(s_mm, 1)
                    mm.then_inc(s_mm, 1)
                XHM = ctr.get(s_mm)

                @blk.vector
                def _(v, XHM=XHM):
                    v.wait_ge(s_mm, XHM)
                    ctr.inc(s_v, 1)
                    v.scalar_tensor_tensor(out=X[:, :], in0=X[:, :], scalar=2.0,
                                           in1=ps_B[:, :], op0=ALU.mult,
                                           op1=ALU.subtract).then_inc(s_v, 1)
                XN = ctr.get(s_v)

                @blk.tensor
                def _(t, XN=XN):
                    t.wait_ge(s_v, XN)
                    mm = None
                    for h in range(HV):
                        mm = t.matmul(ps_A[:, h * 128:(h + 1) * 128], X[:, h * 128:(h + 1) * 128],
                                      c_ident[:, :], start=True, stop=True)
                    ctr.inc(s_mm, 1)
                    mm.then_inc(s_mm, 1)
                XTM = ctr.get(s_mm)

                @blk.scalar
                def _(s, XTM=XTM):
                    s.wait_ge(s_mm, XTM)
                    ctr.inc(s_s, 1)
                    s.copy(Xt[:, :], ps_A[:, :]).then_inc(s_s, 1)
                last_s = ctr.get(s_s)
                last_v = XN

            @blk.vector
            def _(v, vtok_chunk=vtok_chunk, ktok_chunk=ktok_chunk, last_s=last_s):
                v.wait_ge(s_s, last_s)
                v.tensor_tensor(w3(bVt.ap()), w3(vtok_chunk), colview(cols[:, 32:48]), ALU.mult)
                v.tensor_tensor(w3(bAKt.ap()).rearrange("p (q r) s -> p q r s", r=2),
                                gview(ktok_chunk), colview(acb[:, 32:48]).rearrange("p (q r) s -> p q r s", r=2),
                                ALU.mult)
                ctr.inc(s_v, 1)
                v.tensor_tensor(w3(kht.ap()).rearrange("p (q r) s -> p q r s", r=2),
                                gview(ktok_chunk), colview(acb[:, 16:32]).rearrange("p (q r) s -> p q r s", r=2),
                                ALU.mult).then_inc(s_v, 1)
            OPS = ctr.get(s_v)

            @blk.tensor
            def _(t, OPS=OPS):
                t.wait_ge(s_v, OPS)
                mm = None
                for h in range(HV):
                    sl = slice(h * 128, (h + 1) * 128)
                    t.matmul(ps_A[:, sl], Xt[:, sl], bVt[:, sl], start=True, stop=True)
                    mm = t.matmul(ps_B[:, sl], Xt[:, sl], bAKt[:, sl], start=True, stop=True)
                ctr.inc(s_mm, 1)
                mm.then_inc(s_mm, 1)
            SOL = ctr.get(s_mm)

            @blk.scalar
            def _(s, SOL=SOL):
                s.wait_ge(s_mm, SOL)
                s.wait_ge(s_sta, ctr.get(s_sta))
                s.wait_ge(s_stb, ctr.get(s_stb))
                s.copy(U0[:, :], ps_A[:, :])
                ctr.inc(s_s, 1)
                s.copy(Yc[:, :], ps_B[:, :]).then_inc(s_s, 1)
            UY = ctr.get(s_s)

            @blk.gpsimd
            def _(g, ci=ci, UY=UY):
                g.wait_ge(s_s, UY)
                g.wait_ge(stp[ci % 2], ctr.get(stp[ci % 2]))
                g.dma_start(out=spill[ci, 0], in_=U0[:, :]).then_inc(stp[ci % 2], 16)
                g.dma_start(out=spill[ci, 1], in_=Yc[:, :]).then_inc(stp[ci % 2], 16)
                ctr.inc(stp[ci % 2], 32)

            @blk.tensor
            def _(t, UY=UY):
                t.wait_ge(s_s, UY)
                mm = None
                for h in range(HV):
                    sl = slice(h * 128, (h + 1) * 128)
                    t.matmul(ps_A[:, sl], kht[:, sl], U0[:, sl], start=True, stop=True)
                    mm = t.matmul(ps_B[:, sl], Yc[:, sl], kht[:, sl], start=True, stop=True)
                ctr.inc(s_mm, 1)
                mm.then_inc(s_mm, 1)
            RP = ctr.get(s_mm)

            @blk.vector
            def _(v, RP=RP):
                v.wait_ge(s_mm, RP)
                v.wait_ge(s_sta, ctr.get(s_sta))
                v.wait_ge(s_stb, ctr.get(s_stb))
                v.tensor_copy(Rsb[:, :], ps_A[:, :])
                ins = v.tensor_tensor(w3(e_t.ap()), mview(c_identf.ap()), colview(acb[:, 0:16]), ALU.mult)
                ctr.inc(s_v, 1); ins.then_inc(s_v, 1); v.wait_ge(s_v, ctr.get(s_v))
                ctr.inc(s_v, 1)
                v.tensor_tensor(PT[:, :], e_t[:, :], ps_B[:, :], ALU.subtract).then_inc(s_v, 1)
            PRV = ctr.get(s_v)

            @blk.gpsimd
            def _(g, ci=ci, PRV=PRV):
                g.wait_ge(s_v, PRV)
                g.wait_ge(stp[ci % 2], ctr.get(stp[ci % 2]))
                g.dma_start(out=spill[ci, 2], in_=PT[:, :]).then_inc(stp[ci % 2], 16)
                g.dma_start(out=spill[ci, 3], in_=Rsb[:, :]).then_inc(stp[ci % 2], 16)
                ctr.inc(stp[ci % 2], 32)

            if ci == 0:
                @blk.vector
                def _(v, PRV=PRV):
                    v.wait_ge(s_v, PRV)
                    v.tensor_copy(compP.ap()[:, :], PT[:, :])
                    ctr.inc(s_v, 1)
                    v.tensor_copy(compR.ap()[:, :], Rsb[:, :]).then_inc(s_v, 1)
            else:
                @blk.tensor
                def _(t, PRV=PRV):
                    t.wait_ge(s_v, PRV)
                    mm = None
                    for h in range(HV):
                        sl = slice(h * 128, (h + 1) * 128)
                        t.matmul(ps_A[:, sl], PT[:, sl], compP.ap()[:, sl], start=True, stop=True)
                        mm = t.matmul(ps_B[:, sl], PT[:, sl], compR.ap()[:, sl], start=True, stop=True)
                    ctr.inc(s_mm, 1)
                    mm.then_inc(s_mm, 1)
                CM = ctr.get(s_mm)

                @blk.vector
                def _(v, CM=CM):
                    v.wait_ge(s_mm, CM)
                    v.tensor_copy(compP.ap()[:, :], ps_A[:, :])
                    ctr.inc(s_v, 1)
                    v.tensor_tensor(compR.ap()[:, :], ps_B[:, :], Rsb[:, :], ALU.add).then_inc(s_v, 1)


def _cc_hop(nc, blk, ctr, sems, env):
    s_ld, s_st, s_mm, s_v, s_s, s_g, s_cc = (
        sems["ld"], sems["st"], sems["mm"], sems["v"], sems["s"], sems["g"], sems["cc"])
    s_lda, s_ldb = sems["lda"], sems["ldb"]
    ldp = [s_lda, s_ldb]
    compP, compR, state = env["compP"], env["compR"], env["state"]
    c_ident, c_hopm, c_hopimi = env["c_ident"], env["c_hopm"], env["c_hopimi"]
    cc_in, cc_out = env["cc_in"], env["cc_out"]
    from contextlib import ExitStack

    _phase_barrier(nc, blk, ctr, sems)
    with ExitStack() as stc:
        E1 = stc.enter_context
        cPt = E1(nc.sbuf_tensor("cPt", [128, HV * 128], BF16))
        hopbuf = E1(nc.sbuf_tensor("hopbuf", [128, 2 * HV * 2 * 128], BF16))
        Pm = E1(nc.sbuf_tensor("Pm", [128, 128], BF16))
        ps_a = E1(nc.psum_tensor("ps_hop", [128, 128], F32))
        CPD = ctr.get(s_v)

        for h in range(HV):
            @blk.tensor
            def _(t, h=h, CPD=CPD):
                t.wait_ge(s_v, CPD)
                if h > 0:
                    t.wait_ge(s_s, ctr.get(s_s))
                ctr.inc(s_mm, 1)
                t.matmul(ps_a[:, :], compP.ap()[:, h * 128:(h + 1) * 128],
                         c_ident[:, :], start=True, stop=True).then_inc(s_mm, 1)
            TM = ctr.get(s_mm)

            @blk.scalar
            def _(s, h=h, TM=TM):
                s.wait_ge(s_mm, TM)
                ctr.inc(s_s, 1)
                s.copy(cPt[:, h * 128:(h + 1) * 128], ps_a[:, :]).then_inc(s_s, 1)
        CPT = ctr.get(s_s)

        @blk.sync
        def _(q, CPT=CPT):
            q.wait_ge(s_s, CPT)
            q.wait_ge(s_lda, ctr.get(s_lda))
            q.wait_ge(s_ldb, ctr.get(s_ldb))
            q.dma_start(out=cc_in[:, 0:HV * 128], in_=cPt[:, :]).then_inc(s_lda, 16)
            q.dma_start(out=cc_in[:, HV * 128:], in_=compR[:, :]).then_inc(s_lda, 16)
            ctr.inc(s_lda, 32)
        CCI = ctr.get(s_lda)

        @blk.gpsimd
        def _(g, CCI=CCI):
            g.wait_ge(s_lda, CCI)
            ctr.inc(s_cc, 1)
            g.collective_compute(
                "AllGather", ALU.bypass,
                ins=[cc_in[:, :]], outs=[cc_out[:, :]],
                replica_groups=[list(range(8))],
            ).then_inc(s_cc)
            ctr.inc(s_g, 1)
            g.memset(state[:, :], 0.0).then_inc(s_g, 1)
        STZ = ctr.get(s_g)

        for j in range(7):
            sl = ldp[j % 2]

            @blk.sync
            def _(q, j=j, sl=sl):
                q.wait_ge(s_cc, ctr.get(s_cc))
                if j >= 2:
                    q.wait_ge(s_v, ctr.get(s_v))
                q.wait_ge(sl, ctr.get(sl))
                q.dma_start(out=hopbuf[:, (j % 2) * HV * 256:(j % 2 + 1) * HV * 256],
                            in_=cc_out[j * 128:(j + 1) * 128, :]).then_inc(sl, 16)
                ctr.inc(sl, 16)
            HL = ctr.get(sl)

            for h in range(HV):
                pj = hopbuf[:, (j % 2) * HV * 256 + h * 128:][:, :128]
                rj = hopbuf[:, (j % 2) * HV * 256 + HV * 128 + h * 128:][:, :128]
                sh = state.ap()[:, h * 128:(h + 1) * 128]

                @blk.vector
                def _(v, pj=pj, HL=HL, j=j, h=h, STZ=STZ, sl=sl):
                    v.wait_ge(sl, HL)
                    v.wait_ge(s_g, STZ)
                    if j > 0 or h > 0:
                        v.wait_ge(s_mm, ctr.get(s_mm))
                    ctr.inc(s_v, 1)
                    v.scalar_tensor_tensor(out=Pm[:, :], in0=pj,
                                           scalar=c_hopm[:, j:j + 1],
                                           in1=c_hopimi[:, j * 128:(j + 1) * 128],
                                           op0=ALU.mult, op1=ALU.add).then_inc(s_v, 1)
                PMV = ctr.get(s_v)

                @blk.tensor
                def _(t, sh=sh, PMV=PMV):
                    t.wait_ge(s_v, PMV)
                    ctr.inc(s_mm, 1)
                    t.matmul(ps_a[:, :], Pm[:, :], sh, start=True, stop=True).then_inc(s_mm, 1)
                SMM = ctr.get(s_mm)

                @blk.vector
                def _(v, rj=rj, sh=sh, j=j, SMM=SMM):
                    v.wait_ge(s_mm, SMM)
                    ctr.inc(s_v, 1)
                    v.scalar_tensor_tensor(out=sh, in0=rj, scalar=c_hopm[:, j:j + 1],
                                           in1=ps_a[:, :], op0=ALU.mult,
                                           op1=ALU.add).then_inc(s_v, 1)


def _p2b(nc, blk, ctr, sems, env):
    s_mm, s_v, s_s, s_g = sems["mm"], sems["v"], sems["s"], sems["g"]
    s_lda, s_ldb, s_sta, s_stb = sems["lda"], sems["ldb"], sems["sta"], sems["stb"]
    ldp = [s_lda, s_ldb]
    stp = [s_sta, s_stb]
    qk_f, barows, state = env["qk_f"], env["barows"], env["state"]
    c_ident, c_identf, c_maskUD, c_rmsw, c_onerf = (
        env["c_ident"], env["c_identf"], env["c_maskUD"], env["c_rmsw"], env["c_onerf"])
    spill, pre_scr, p_scr = env["spill"], env["pre_scr"], env["p_scr"]
    from contextlib import ExitStack

    W = HV * 128

    _phase_barrier(nc, blk, ctr, sems)
    with ExitStack() as stb_:
        E1 = stb_.enter_context
        cols = E1(nc.sbuf_tensor("colsB", [128, 16], F32))
        rowstall = E1(nc.sbuf_tensor("rowstallB", [1, W], F32))
        e_m = E1(nc.sbuf_tensor("e_mB", [128, W], F32))
        qa = E1(nc.sbuf_tensor("qaB", [128, W], BF16))
        MT = E1(nc.sbuf_tensor("MTB", [128, W], BF16))
        qAf = E1(nc.sbuf_tensor("qAfB", [128, W], BF16))
        nMYT = E1(nc.sbuf_tensor("nMYTB", [128, W], BF16))
        U0 = E1(nc.sbuf_tensor("U0B", [128, W], BF16))
        Yc = E1(nc.sbuf_tensor("YcB", [128, W], BF16))
        PTl = E1(nc.sbuf_tensor("PTlB", [128, W], BF16))
        Rl = E1(nc.sbuf_tensor("RlB", [128, W], BF16))
        gt = E1(nc.sbuf_tensor("gtB", [128, 2 * W], BF16))
        Ofp = E1(nc.sbuf_tensor("OfpB", [128, W], F32))
        Onsb = E1(nc.sbuf_tensor("OnsbB", [128, W], BF16))
        gsil = E1(nc.sbuf_tensor("gsilB", [128, W], BF16))
        ptile = E1(nc.sbuf_tensor("ptileB", [128, 2 * W], BF16))
        rcol = E1(nc.sbuf_tensor("rcolB", [128, 16], F32))
        ps_A = E1(nc.psum_tensor("ps_AB", [128, W], F32))
        ps_B = E1(nc.psum_tensor("ps_BB", [128, W], F32))

        def colview(ap2):
            return ap2.unsqueeze(2).broadcast_to((128, HV, 128))

        def mview(ap2):
            return ap2.unsqueeze(1).broadcast_to((128, HV, 128))

        def gview(ap):
            return ap.rearrange("p (q s) -> p q s", s=128).unsqueeze(2).broadcast_to((128, HQ, 2, 128))

        def w3(ap):
            return ap.rearrange("p (h s) -> p h s", s=128)

        for ci in range(NCH):
            buf = ci % 2
            ell_r = barows[:, 3 * TPC + ci * 128:3 * TPC + (ci + 1) * 128]
            kf_chunk = lambda hq, ci=ci: qk_f.ap()[:, (8 + hq) * TPC + ci * 128:][:, :128]
            qf_chunk = lambda hq, ci=ci: qk_f.ap()[:, hq * TPC + ci * 128:][:, :128]
            u0 = U0[:, 0:W]
            yc = Yc[:, 0:W]
            ptl = PTl[:, 0:W]
            rtl = Rl[:, 0:W]
            gtl = gt[:, buf * W:buf * W + W]
            pt = ptile[:, buf * W:buf * W + W]

            @blk.sync
            def _(q, ci=ci, buf=buf, u0=u0, yc=yc, ptl=ptl, rtl=rtl, gtl=gtl):
                if ci < 2:
                    q.wait_ge(s_sta, ctr.get(s_sta))
                    q.wait_ge(s_stb, ctr.get(s_stb))
                q.wait_ge(s_mm, ctr.get(s_mm))
                q.wait_ge(s_s, ctr.get(s_s))
                q.wait_ge(s_v, ctr.get(s_v))
                sl = ldp[buf]
                q.wait_ge(sl, ctr.get(sl))
                q.dma_start(out=u0, in_=spill[ci, 0]).then_inc(sl, 16)
                q.dma_start(out=yc, in_=spill[ci, 1]).then_inc(sl, 16)
                q.dma_start(out=ptl, in_=spill[ci, 2]).then_inc(sl, 16)
                q.dma_start(out=rtl, in_=spill[ci, 3]).then_inc(sl, 16)
                q.dma_start(out=gtl.rearrange("p (h s) -> p h s", s=128),
                            in_=pre_scr[NFQ:NFQ + HV, :, HALO + ci * 128:HALO + (ci + 1) * 128].rearrange("h p s -> p h s")).then_inc(sl, 16)
                ctr.inc(ldp[buf], 80)
            LDS = ctr.get(ldp[buf])

            @blk.tensor
            def _(t, ell_r=ell_r, ci=ci):
                t.wait_ge(s_s, ctr.get(s_s))
                t.wait_ge(s_v, ctr.get(s_v))
                t.wait_ge(s_g, ctr.get(s_g))
                ctr.inc(s_mm, 1)
                t.matmul(ps_A[:, 0:16], ell_r, c_identf[0:HV, 0:HV], start=True, stop=True).then_inc(s_mm, 1)
            CT = ctr.get(s_mm)

            @blk.scalar
            def _(s, CT=CT):
                s.wait_ge(s_mm, CT)
                ctr.inc(s_s, 1)
                s.copy(cols[:, 0:16], ps_A[:, 0:16]).then_inc(s_s, 1)
            COLS = ctr.get(s_s)

            @blk.tensor
            def _(t, COLS=COLS):
                t.wait_ge(s_s, COLS)
                mm = None
                for h in range(HV):
                    mm = t.matmul(ps_B[0:1, h * 128:(h + 1) * 128], cols[:, h:h + 1],
                                  c_identf[:, :], start=True, stop=True)
                ctr.inc(s_mm, 1)
                mm.then_inc(s_mm, 1)
            RWM = ctr.get(s_mm)

            @blk.scalar
            def _(s, RWM=RWM):
                s.wait_ge(s_mm, RWM)
                ctr.inc(s_s, 1)
                s.copy(rowstall[0:1, 0:W], ps_B[0:1, 0:W]).then_inc(s_s, 1)
            RWS = ctr.get(s_s)

            @blk.tensor
            def _(t, RWS=RWS):
                t.wait_ge(s_s, RWS)
                mm = None
                for qq in range(4):
                    sl = slice(qq * 512, (qq + 1) * 512)
                    mm = t.matmul(ps_A[:, sl], c_onerf[0:1, :], rowstall[0:1, sl], start=True, stop=True)
                ctr.inc(s_mm, 1)
                mm.then_inc(s_mm, 1)
            BC = ctr.get(s_mm)

            @blk.vector
            def _(v, BC=BC):
                v.wait_ge(s_mm, BC)
                ins = v.tensor_tensor(w3(e_m.ap()), w3(ps_A.ap()[:, 0:W]), colview(cols[:, 0:16]), ALU.subtract)
                ctr.inc(s_v, 1); ins.then_inc(s_v, 1); v.wait_ge(s_v, ctr.get(s_v))
                ctr.inc(s_v, 1)
                v.tensor_tensor(w3(e_m.ap()), w3(e_m.ap()), mview(c_maskUD.ap()), ALU.add).then_inc(s_v, 1)
            EA = ctr.get(s_v)

            @blk.scalar
            def _(s, EA=EA):
                s.wait_ge(s_v, EA)
                ins = s.activation(qa[:, :], ps_A[:, 0:W], AF.Exp, bias=env["c_cols"][:, 2:3])
                ctr.inc(s_s, 1); ins.then_inc(s_s, 1); s.wait_ge(s_s, ctr.get(s_s))
                ctr.inc(s_s, 1)
                s.activation(e_m[:, :], e_m[:, :], AF.Exp).then_inc(s_s, 1)
            EX = ctr.get(s_s)

            @blk.tensor
            def _(t, kf_chunk=kf_chunk, qf_chunk=qf_chunk, EX=EX, ci=ci):
                t.wait_ge(s_s, EX)
                mm = None
                for hq in range(HQ):
                    mm = t.matmul(ps_A[:, hq * 128:(hq + 1) * 128], kf_chunk(hq), qf_chunk(hq),
                                  start=True, stop=True)
                ctr.inc(s_mm, 1)
                mm.then_inc(s_mm, 1)
            GM = ctr.get(s_mm)

            @blk.vector
            def _(v, GM=GM, ci=ci):
                v.wait_ge(s_mm, GM)
                ins = v.tensor_tensor(
                    w3(MT.ap()).rearrange("p (q r) s -> p q r s", r=2),
                    ps_A.ap()[:, 0:HQ * 128].rearrange("p (q s) -> p q s", s=128).unsqueeze(2).broadcast_to((128, HQ, 2, 128)),
                    w3(e_m.ap()).rearrange("p (q r) s -> p q r s", r=2), ALU.mult)
                qsl = qk_f.ap()[:, ci * 128:]
                ctr.inc(s_v, 1)
                v.tensor_tensor(
                    w3(qAf.ap()).rearrange("p (q r) s -> p q r s", r=2),
                    _qview_impl(qk_f, ci),
                    w3(qa.ap()).rearrange("p (q r) s -> p q r s", r=2), ALU.mult).then_inc(s_v, 1)
            MTV = ctr.get(s_v)

            @blk.tensor
            def _(t, MTV=MTV, yc=yc, LDS=LDS, buf=buf):
                t.wait_ge(s_v, MTV)
                t.wait_ge(ldp[buf], LDS)
                mm = None
                for h in range(HV):
                    sl = slice(h * 128, (h + 1) * 128)
                    mm = t.matmul(ps_B[:, sl], yc[:, sl], MT[:, sl], start=True, stop=True)
                ctr.inc(s_mm, 1)
                mm.then_inc(s_mm, 1)
            MY = ctr.get(s_mm)

            @blk.scalar
            def _(s, MY=MY):
                s.wait_ge(s_mm, MY)
                ctr.inc(s_s, 1)
                s.activation(nMYT[:, :], ps_B[:, :], AF.Identity, scale=-1.0).then_inc(s_s, 1)
            NM = ctr.get(s_s)

            @blk.tensor
            def _(t, NM=NM, u0=u0):
                t.wait_ge(s_s, NM)
                mm = None
                for h in range(HV):
                    sl = slice(h * 128, (h + 1) * 128)
                    t.matmul(ps_B[:, sl], MT[:, sl], u0[:, sl], start=True, stop=False)
                    t.matmul(ps_B[:, sl], qAf[:, sl], state.ap()[:, sl], start=False, stop=False)
                    mm = t.matmul(ps_B[:, sl], nMYT[:, sl], state.ap()[:, sl], start=False, stop=True)
                ctr.inc(s_mm, 1)
                mm.then_inc(s_mm, 1)
            OM = ctr.get(s_mm)

            @blk.scalar
            def _(s, OM=OM):
                s.wait_ge(s_mm, OM)
                s.wait_ge(s_v, ctr.get(s_v))
                ins = s.copy(Ofp[:, :], ps_B[:, :])
                ctr.inc(s_s, 1); ins.then_inc(s_s, 1); s.wait_ge(s_s, ctr.get(s_s))
                ctr.inc(s_s, 1)
                s.activation(e_m[:, :], Ofp[:, :], AF.Square).then_inc(s_s, 1)
            OE = ctr.get(s_s)

            @blk.vector
            def _(v, OE=OE):
                v.wait_ge(s_s, OE)
                ins = v.tensor_reduce(rcol[:, 0:HV].unsqueeze(2), w3(e_m.ap()), AX.X, ALU.add)
                ctr.inc(s_v, 1); ins.then_inc(s_v, 1); v.wait_ge(s_v, ctr.get(s_v))
                ctr.inc(s_v, 1)
                v.tensor_scalar(rcol[:, 0:HV], rcol[:, 0:HV], 1.0 / DV, EPS_RMS,
                                ALU.mult, ALU.add).then_inc(s_v, 1)
            RM1 = ctr.get(s_v)

            @blk.scalar
            def _(s, RM1=RM1):
                s.wait_ge(s_v, RM1)
                ctr.inc(s_s, 1)
                s.activation(rcol[:, 0:HV], rcol[:, 0:HV], AF.Sqrt).then_inc(s_s, 1)
            RM2 = ctr.get(s_s)

            @blk.vector
            def _(v, RM2=RM2):
                v.wait_ge(s_s, RM2)
                ins = v.reciprocal(rcol[:, 0:HV], rcol[:, 0:HV])
                ctr.inc(s_v, 1); ins.then_inc(s_v, 1); v.wait_ge(s_v, ctr.get(s_v))
                ctr.inc(s_v, 1)
                v.tensor_tensor(w3(Onsb.ap()), w3(Ofp.ap()), colview(rcol[:, 0:HV]), ALU.mult).then_inc(s_v, 1)
            ONV = ctr.get(s_v)

            @blk.tensor
            def _(t, ONV=ONV):
                t.wait_ge(s_v, ONV)
                t.wait_ge(s_s, ctr.get(s_s))
                mm = None
                for h in range(HV):
                    sl = slice(h * 128, (h + 1) * 128)
                    mm = t.matmul(ps_A[:, sl], Onsb[:, sl], c_ident[:, :], start=True, stop=True)
                ctr.inc(s_mm, 1)
                mm.then_inc(s_mm, 1)
            OT = ctr.get(s_mm)

            @blk.scalar
            def _(s, OT=OT, gtl=gtl):
                s.wait_ge(s_mm, OT)
                ins = s.activation(gsil[:, :], gtl, AF.Silu)
                ctr.inc(s_s, 1); ins.then_inc(s_s, 1); s.wait_ge(s_s, ctr.get(s_s))
                ctr.inc(s_s, 1)
                s.activation(Onsb[:, :], ps_A[:, :], AF.Identity, scale=c_rmsw[:, 0:1]).then_inc(s_s, 1)
            PS1 = ctr.get(s_s)

            @blk.vector
            def _(v, pt=pt, PS1=PS1, buf=buf):
                v.wait_ge(s_s, PS1)
                v.wait_ge(stp[buf], ctr.get(stp[buf]))
                ctr.inc(s_v, 1)
                v.tensor_tensor(pt, Onsb[:, :], gsil[:, :], ALU.mult).then_inc(s_v, 1)
            PV = ctr.get(s_v)

            @blk.gpsimd
            def _(g, ci=ci, pt=pt, PV=PV, buf=buf):
                g.wait_ge(s_v, PV)
                g.wait_ge(stp[buf], ctr.get(stp[buf]))
                g.dma_start(out=p_scr[ci], in_=pt).then_inc(stp[buf], 16)
                ctr.inc(stp[buf], 16)

            @blk.tensor
            def _(t, ptl=ptl, OM=OM):
                t.wait_ge(s_s, ctr.get(s_s))
                mm = None
                for h in range(HV):
                    sl = slice(h * 128, (h + 1) * 128)
                    mm = t.matmul(ps_B[:, sl], ptl[:, sl], state.ap()[:, sl], start=True, stop=True)
                ctr.inc(s_mm, 1)
                mm.then_inc(s_mm, 1)
            SM = ctr.get(s_mm)

            @blk.vector
            def _(v, rtl=rtl, SM=SM):
                v.wait_ge(s_mm, SM)
                ctr.inc(s_v, 1)
                v.tensor_tensor(state.ap()[:, :], ps_B[:, :], rtl, ALU.add).then_inc(s_v, 1)


def _qview(qk_f, ci):
    return (qk_f.ap()[:, ci * 128:ci * 128 + (HQ - 1) * TPC + 128]
            .rearrange("p (q rest) -> p q rest", q=HQ) if False else
            _qview_impl(qk_f, ci))


def _qview_impl(qk_f, ci):
    full = qk_f.ap()
    v = full.rearrange("p (hh t) -> p hh t", t=TPC)[:, 0:HQ, ci * 128:(ci + 1) * 128]
    return v.unsqueeze(2).broadcast_to((128, HQ, 2, 128))


def _p3(nc, blk, ctr, sems, env):
    s_mm, s_v, s_s = sems["mm"], sems["v"], sems["s"]
    s_lda, s_ldb, s_sta, s_stb = sems["lda"], sems["ldb"], sems["sta"], sems["stb"]
    ldp = [s_lda, s_ldb]
    stp = [s_sta, s_stb]
    c_ident, c_cols = env["c_ident"], env["c_cols"]
    x_in, wout_b, p_scr, x2_scr = env["x_in"], env["wout_b"], env["p_scr"], env["x2_scr"]
    xq_f = env["xq_f"]
    from contextlib import ExitStack

    _phase_barrier(nc, blk, ctr, sems)
    with ExitStack() as st3:
        E1 = st3.enter_context
        pcol = E1(nc.sbuf_tensor("pcol", [128, 2 * 2048], BF16))
        woutb = E1(nc.sbuf_tensor("woutb", [128, 2 * 16 * 512], BF16))
        attn_all = E1(nc.sbuf_tensor("attn_all", [128, 8 * D], F32))
        lnt = E1(nc.sbuf_tensor("lnt3", [128, 40], F32))
        x2t = E1(nc.sbuf_tensor("x2t", [128, 2 * D], F32))
        xnt = E1(nc.sbuf_tensor("xnt3", [128, D], F32))
        xtl = E1(nc.sbuf_tensor("xtl3", [128, 2 * D], F32))
        inti = E1(nc.sbuf_tensor("inti", [128, D], I32))
        intf = E1(nc.sbuf_tensor("intf", [128, D], F32))
        xq_t = E1(nc.sbuf_tensor("xq_t", [128, D], BF16))
        ps_op0 = E1(nc.psum_tensor("ps_op0", [128, 512], F32))
        ps_op1 = E1(nc.psum_tensor("ps_op1", [128, 512], F32))
        ps_big = E1(nc.psum_tensor("ps_big3", [128, D], F32))

        def emit_lnq(tb, ready):
            at = attn_all[:, tb * D:(tb + 1) * D]
            xt = xtl[:, (tb % 2) * D:(tb % 2) * D + D]
            x2 = x2t[:, (tb % 2) * D:(tb % 2) * D + D]

            @blk.sync
            def _(q, tb=tb, xt=xt):
                if tb >= 2:
                    q.wait_ge(s_v, ctr.get(s_v))
                q.wait_ge(ldp[tb % 2], ctr.get(ldp[tb % 2]))
                q.dma_start(out=xt, in_=x_in[HALO + tb * 128: HALO + (tb + 1) * 128, :]).then_inc(ldp[tb % 2], 16)
                ctr.inc(ldp[tb % 2], 16)
            XL = ctr.get(ldp[tb % 2])

            def emit_ln(src_ap, eps_val, dst_col):
                @blk.vector
                def _(v):
                    v.wait_ge(s_v, ctr.get(s_v))
                    ins = None
                    for i4 in range(4):
                        ins = v.bn_stats(lnt[:, i4 * 6:(i4 + 1) * 6], src_ap[:, i4 * 512:(i4 + 1) * 512])
                    ctr.inc(s_v, 1); ins.then_inc(s_v, 1); v.wait_ge(s_v, ctr.get(s_v))
                    ins = v.bn_aggr(lnt[:, 24:26], lnt[:, 0:24])
                    ctr.inc(s_v, 1); ins.then_inc(s_v, 1); v.wait_ge(s_v, ctr.get(s_v))
                    ctr.inc(s_v, 1)
                    v.tensor_scalar(lnt[:, 26:27], lnt[:, 25:26], eps_val, None, ALU.add).then_inc(s_v, 1)
                V1 = ctr.get(s_v)

                @blk.scalar
                def _(s, V1=V1):
                    s.wait_ge(s_v, V1)
                    ctr.inc(s_s, 1)
                    s.activation(lnt[:, 27:28], lnt[:, 26:27], AF.Sqrt).then_inc(s_s, 1)
                S1 = ctr.get(s_s)

                @blk.vector
                def _(v, S1=S1):
                    v.wait_ge(s_s, S1)
                    ins = v.reciprocal(lnt[:, dst_col:dst_col + 1], lnt[:, 27:28])
                    ctr.inc(s_v, 1); ins.then_inc(s_v, 1); v.wait_ge(s_v, ctr.get(s_v))
                    ctr.inc(s_v, 1)
                    v.scalar_tensor_tensor(out=lnt[:, dst_col + 1:dst_col + 2], in0=lnt[:, 24:25],
                                           scalar=-1.0, in1=lnt[:, dst_col:dst_col + 1],
                                           op0=ALU.mult, op1=ALU.mult).then_inc(s_v, 1)
                return ctr.get(s_v)

            @blk.vector
            def _(v, ready=ready):
                v.wait_ge(s_s, ready)
            LN1 = emit_ln(at, EPS_LN, 28)

            @blk.scalar
            def _(s, at=at, LN1=LN1):
                s.wait_ge(s_v, LN1)
                ctr.inc(s_s, 1)
                s.activation(xnt[:, :], at, AF.Identity,
                             bias=lnt[:, 29:30], scale=lnt[:, 28:29]).then_inc(s_s, 1)
            A1 = ctr.get(s_s)

            @blk.vector
            def _(v, x2=x2, xt=xt, A1=A1, XL=XL, tb=tb):
                v.wait_ge(s_s, A1)
                v.wait_ge(ldp[tb % 2], XL)
                v.wait_ge(stp[tb % 2], ctr.get(stp[tb % 2]))
                ctr.inc(s_v, 1)
                v.tensor_tensor(x2, xnt[:, :], xt, ALU.add).then_inc(s_v, 1)
            X2V = ctr.get(s_v)

            @blk.gpsimd
            def _(g, tb=tb, x2=x2, X2V=X2V):
                g.wait_ge(s_v, X2V)
                g.wait_ge(stp[tb % 2], ctr.get(stp[tb % 2]))
                g.dma_start(out=x2_scr[tb * 128:(tb + 1) * 128, :], in_=x2).then_inc(stp[tb % 2], 16)
                ctr.inc(stp[tb % 2], 16)

            LN2 = emit_ln(x2, EPS_FFN, 30)

            @blk.scalar
            def _(s, x2=x2, LN2=LN2):
                s.wait_ge(s_v, LN2)
                ctr.inc(s_s, 1)
                s.activation(xnt[:, :], x2, AF.Identity,
                             bias=lnt[:, 31:32], scale=lnt[:, 30:31]).then_inc(s_s, 1)
            A2 = ctr.get(s_s)

            @blk.vector
            def _(v, A2=A2):
                v.wait_ge(s_s, A2)
                ins = v.tensor_reduce(lnt[:, 32:33], xnt[:, :], AX.X, ALU.max, apply_absolute_value=True)
                ctr.inc(s_v, 1); ins.then_inc(s_v, 1); v.wait_ge(s_v, ctr.get(s_v))
                ins = v.reciprocal(lnt[:, 33:34], lnt[:, 32:33])
                ctr.inc(s_v, 1); ins.then_inc(s_v, 1); v.wait_ge(s_v, ctr.get(s_v))
                v.tensor_scalar(lnt[:, 34:35], lnt[:, 33:34], 127.0, None, ALU.mult)
                ins = v.tensor_scalar(lnt[:, 35:36], lnt[:, 32:33], 1.0 / 127.0, None, ALU.mult)
                ctr.inc(s_v, 1); ins.then_inc(s_v, 1); v.wait_ge(s_v, ctr.get(s_v))
                ins = v.tensor_scalar(intf[:, :], xnt[:, :], lnt[:, 34:35], None, ALU.mult)
                ctr.inc(s_v, 1); ins.then_inc(s_v, 1); v.wait_ge(s_v, ctr.get(s_v))
                ins = v.tensor_copy(inti[:, :], intf[:, :])
                ctr.inc(s_v, 1); ins.then_inc(s_v, 1); v.wait_ge(s_v, ctr.get(s_v))
                ins = v.tensor_copy(intf[:, :], inti[:, :])
                ctr.inc(s_v, 1); ins.then_inc(s_v, 1); v.wait_ge(s_v, ctr.get(s_v))
                ctr.inc(s_v, 1)
                v.tensor_scalar(xq_t[:, :], intf[:, :], lnt[:, 35:36], None, ALU.mult).then_inc(s_v, 1)
            QV = ctr.get(s_v)

            @blk.tensor
            def _(t, tb=tb, QV=QV):
                t.wait_ge(s_v, QV)
                if tb > 0:
                    t.wait_ge(s_s, ctr.get(s_s))
                mm = None
                for db in range(NDB):
                    mm = t.matmul(ps_big[:, db * 128:(db + 1) * 128],
                                  xq_t[:, db * 128:(db + 1) * 128],
                                  c_ident[:, :], start=True, stop=True)
                ctr.inc(s_mm, 1)
                mm.then_inc(s_mm, 1)
            TQ = ctr.get(s_mm)

            @blk.scalar
            def _(s, tb=tb, TQ=TQ):
                s.wait_ge(s_mm, TQ)
                ctr.inc(s_s, 1)
                s.copy(
                    xq_f.ap().rearrange("p (db c) -> p db c", db=NDB)[:, :, tb * 128:(tb + 1) * 128],
                    ps_big.ap().rearrange("p (db c) -> p db c", db=NDB),
                ).then_inc(s_s, 1)



        for fg in range(4):
            wsl = woutb[:, (fg % 2) * 16 * 512:][:, :16 * 512]

            @blk.gpsimd
            def _(g, fg=fg, wsl=wsl):
                if fg >= 2:
                    g.wait_ge(s_mm, ctr.get(s_mm))
                g.wait_ge(stp[fg % 2], ctr.get(stp[fg % 2]))
                g.dma_start(out=wsl.rearrange("p (db f) -> p db f", f=512),
                            in_=wout_b[fg].rearrange("(db p) f -> p db f", p=128)).then_inc(stp[fg % 2], 16)
                ctr.inc(stp[fg % 2], 16)
            WL = ctr.get(stp[fg % 2])

            for tb in range(8):
                it = fg * 8 + tb
                pb = it % 2
                psl = ps_op0 if tb % 2 == 0 else ps_op1
                pc = pcol[:, pb * 2048:pb * 2048 + 2048]

                @blk.sync
                def _(q, tb=tb, pc=pc, pb=pb, it=it):
                    if it >= 2:
                        q.wait_ge(s_mm, ctr.get(s_mm))
                    q.wait_ge(ldp[pb], ctr.get(ldp[pb]))
                    q.dma_start(out=pc, in_=p_scr[tb]).then_inc(ldp[pb], 16)
                    ctr.inc(ldp[pb], 16)
                PL = ctr.get(ldp[pb])

                @blk.tensor
                def _(t, fg=fg, tb=tb, pc=pc, wsl=wsl, psl=psl, PL=PL, WL=WL, pb=pb):
                    t.wait_ge(ldp[pb], PL)
                    t.wait_ge(stp[fg % 2], WL)
                    if fg + tb > 0:
                        t.wait_ge(s_s, ctr.get(s_s))
                    mm = None
                    for db in range(NDB):
                        mm = t.matmul(psl[:, :],
                                      pc[:, db * 128:(db + 1) * 128],
                                      wsl.rearrange("p (db f) -> p db f", f=512)[:, db, :],
                                      start=(db == 0), stop=(db == NDB - 1))
                    ctr.inc(s_mm, 1)
                    mm.then_inc(s_mm, 1)
                OPM = ctr.get(s_mm)

                @blk.scalar
                def _(s, fg=fg, tb=tb, psl=psl, OPM=OPM):
                    s.wait_ge(s_mm, OPM)
                    ctr.inc(s_s, 1)
                    s.copy(attn_all[:, tb * D + fg * 512: tb * D + fg * 512 + 512], psl[:, :]).then_inc(s_s, 1)
                if fg == 3:
                    emit_lnq(tb, ctr.get(s_s))
        OPDONE = ctr.get(s_s)

def _ffn_gu(nc, blk, ctr, sems, env):
    s_mm, s_v, s_s = sems["mm"], sems["v"], sems["s"]
    s_lda, s_ldb, s_sta, s_stb = sems["lda"], sems["ldb"], sems["sta"], sems["stb"]
    ldp = [sems["ga"], sems["gb"]]
    xq_f, p2 = env["xq_f"], env["p2"]
    wg_b, wu_b = env["wg_b"], env["wu_b"]
    from contextlib import ExitStack

    _phase_barrier(nc, blk, ctr, sems)
    with ExitStack() as stf:
        E1 = stf.enter_context
        wgb = E1(nc.sbuf_tensor("wgb", [128, 2 * 2048], BF16))
        wub = E1(nc.sbuf_tensor("wub", [128, 2 * 2048], BF16))
        gs = E1(nc.sbuf_tensor("gs", [128, 2 * 512], BF16))
        ps_g0 = E1(nc.psum_tensor("ps_g0", [128, 512], F32))
        ps_g1 = E1(nc.psum_tensor("ps_g1", [128, 512], F32))
        ps_u0 = E1(nc.psum_tensor("ps_u0", [128, 512], F32))
        ps_u1 = E1(nc.psum_tensor("ps_u1", [128, 512], F32))

        mm_hist = {}
        load_done = {}
        sign_done = {}

        def emit_load(hb):
            buf = hb % 2
            wg = wgb[:, buf * 2048:buf * 2048 + 2048]
            wu = wub[:, buf * 2048:buf * 2048 + 2048]

            @blk.gpsimd
            def _(g, hb=hb, wg=wg, wu=wu, buf=buf):
                if hb >= 2:
                    g.wait_ge(s_mm, mm_hist[hb - 2])
                g.wait_ge(ldp[buf], ctr.get(ldp[buf]))
                g.dma_start(out=wg.rearrange("p (db f) -> p db f", f=128),
                            in_=wg_b[hb].rearrange("(db p) f -> p db f", p=128)).then_inc(ldp[buf], 16)
                g.dma_start(out=wu.rearrange("p (db f) -> p db f", f=128),
                            in_=wu_b[hb].rearrange("(db p) f -> p db f", p=128)).then_inc(ldp[buf], 16)
                ctr.inc(ldp[buf], 32)
            load_done[hb] = ctr.get(ldp[buf])

        def emit_sign(hb):
            buf = hb % 2
            wg = wgb[:, buf * 2048:buf * 2048 + 2048]
            wu = wub[:, buf * 2048:buf * 2048 + 2048]

            @blk.scalar
            def _(s, hb=hb, wg=wg, wu=wu, buf=buf):
                s.wait_ge(ldp[buf], load_done[hb])
                s.sign(wg, wg)
                ctr.inc(s_s, 1)
                s.sign(wu, wu).then_inc(s_s, 1)
            sign_done[hb] = ctr.get(s_s)

        emit_load(0)
        emit_load(1)
        emit_sign(0)

        for hb in range(NHB):
            buf = hb % 2
            wg = wgb[:, buf * 2048:buf * 2048 + 2048]
            wu = wub[:, buf * 2048:buf * 2048 + 2048]
            if hb + 1 < NHB:
                emit_sign(hb + 1)
            for th in range(2):
                psg = ps_g0 if th == 0 else ps_g1
                psu = ps_u0 if th == 0 else ps_u1

                @blk.tensor
                def _(t, hb=hb, th=th, wg=wg, wu=wu, psg=psg, psu=psu):
                    t.wait_ge(s_s, sign_done[hb])
                    if hb > 0:
                        t.wait_ge(s_v, ctr.get(s_v) - 1)
                    mm = None
                    for db in range(NDB):
                        mm = t.matmul(psg[:, :], wg[:, db * 128:(db + 1) * 128],
                                      xq_f.ap()[:, db * TPC + th * 512:][:, :512],
                                      start=(db == 0), stop=(db == NDB - 1))
                    for db in range(NDB):
                        mm = t.matmul(psu[:, :], wu[:, db * 128:(db + 1) * 128],
                                      xq_f.ap()[:, db * TPC + th * 512:][:, :512],
                                      start=(db == 0), stop=(db == NDB - 1))
                    ctr.inc(s_mm, 1)
                    mm.then_inc(s_mm, 1)
                GM = ctr.get(s_mm)

                @blk.scalar
                def _(s, psg=psg, GM=GM, th=th):
                    s.wait_ge(s_mm, GM)
                    ctr.inc(s_s, 1)
                    s.activation(gs[:, th * 512:(th + 1) * 512], psg[:, :], AF.Silu).then_inc(s_s, 1)
                GS = ctr.get(s_s)

                @blk.vector
                def _(v, hb=hb, th=th, psu=psu, GS=GS):
                    v.wait_ge(s_s, GS)
                    ctr.inc(s_v, 1)
                    v.tensor_tensor(p2.ap()[:, hb * TPC + th * 512:][:, :512],
                                    psu[:, :], gs[:, th * 512:(th + 1) * 512], ALU.mult).then_inc(s_v, 1)
            mm_hist[hb] = ctr.get(s_mm)
            if hb + 2 < NHB:
                emit_load(hb + 2)


def _ffn_down(nc, blk, ctr, sems, env):
    s_mm, s_v, s_s = sems["mm"], sems["v"], sems["s"]
    s_lda, s_ldb, s_sta, s_stb = sems["lda"], sems["ldb"], sems["sta"], sems["stb"]
    ldp = [sems["ga"], sems["gb"]]
    stp = [s_sta, s_stb]
    p2, wd_b, ffn_scr = env["p2"], env["wd_b"], env["ffn_scr"]
    from contextlib import ExitStack

    _phase_barrier(nc, blk, ctr, sems)
    with ExitStack() as std_:
        E1 = std_.enter_context
        wdb = E1(nc.sbuf_tensor("wdb", [128, 2 * NHB * 128], BF16))
        fout = E1(nc.sbuf_tensor("fout", [128, 2 * TPC], BF16))
        ps_d0 = E1(nc.psum_tensor("ps_d0", [128, TPC], F32))
        ps_d1 = E1(nc.psum_tensor("ps_d1", [128, TPC], F32))

        mm_hist = {}
        load_done = {}
        sign_done = {}
        fo_hist = {}

        def emit_load(fcb):
            buf = fcb % 2
            wd = wdb[:, buf * NHB * 128:][:, :NHB * 128]

            @blk.gpsimd
            def _(g, fcb=fcb, wd=wd, buf=buf):
                if fcb >= 2:
                    g.wait_ge(s_mm, mm_hist[fcb - 2])
                g.wait_ge(ldp[buf], ctr.get(ldp[buf]))
                g.dma_start(out=wd.rearrange("p (hb f) -> p hb f", f=128),
                            in_=wd_b[fcb].rearrange("(hb p) f -> p hb f", p=128)).then_inc(ldp[buf], 16)
                ctr.inc(ldp[buf], 16)
            load_done[fcb] = ctr.get(ldp[buf])

        def emit_sign(fcb):
            buf = fcb % 2
            wd = wdb[:, buf * NHB * 128:][:, :NHB * 128]

            @blk.scalar
            def _(s, fcb=fcb, wd=wd, buf=buf):
                s.wait_ge(ldp[buf], load_done[fcb])
                ctr.inc(s_s, 1)
                s.sign(wd, wd).then_inc(s_s, 1)
            sign_done[fcb] = ctr.get(s_s)

        emit_load(0)
        emit_load(1)
        emit_sign(0)

        for fcb in range(16):
            buf = fcb % 2
            wd = wdb[:, buf * NHB * 128:][:, :NHB * 128]
            psd = ps_d0 if fcb % 2 == 0 else ps_d1
            fo = fout[:, buf * TPC:buf * TPC + TPC]
            if fcb + 1 < 16:
                emit_sign(fcb + 1)

            @blk.tensor
            def _(t, fcb=fcb, wd=wd, psd=psd):
                t.wait_ge(s_s, sign_done[fcb])
                if fcb >= 2:
                    t.wait_ge(s_s, fo_hist[fcb - 2])
                mm = None
                for th in range(2):
                    for hb in range(NHB):
                        mm = t.matmul(psd[:, th * 512:(th + 1) * 512],
                                      wd[:, hb * 128:(hb + 1) * 128],
                                      p2.ap()[:, hb * TPC + th * 512:][:, :512],
                                      start=(hb == 0), stop=(hb == NHB - 1))
                ctr.inc(s_mm, 1)
                mm.then_inc(s_mm, 1)
            DM = ctr.get(s_mm)

            @blk.scalar
            def _(s, psd=psd, fo=fo, DM=DM, buf=buf, fcb=fcb):
                s.wait_ge(s_mm, DM)
                if fcb >= 2:
                    s.wait_ge(stp[buf], ctr.get(stp[buf]))
                ctr.inc(s_s, 1)
                s.copy(fo, psd[:, :]).then_inc(s_s, 1)
            FE = ctr.get(s_s)
            fo_hist[fcb] = FE

            @blk.gpsimd
            def _(g, fcb=fcb, fo=fo, FE=FE, buf=buf):
                g.wait_ge(s_s, FE)
                g.wait_ge(stp[buf], ctr.get(stp[buf]))
                g.dma_start(out=ffn_scr[fcb], in_=fo).then_inc(stp[buf], 16)
                ctr.inc(stp[buf], 16)
            mm_hist[fcb] = ctr.get(s_mm)
            if fcb + 2 < 16:
                emit_load(fcb + 2)


def _final(nc, blk, ctr, sems, env):
    s_mm, s_v, s_s = sems["mm"], sems["v"], sems["s"]
    s_lda, s_ldb, s_sta, s_stb = sems["lda"], sems["ldb"], sems["sta"], sems["stb"]
    ldp = [s_lda, s_ldb]
    stp = [s_sta, s_stb]
    c_ident = env["c_ident"]
    ffn_scr, x2_scr, y_out = env["ffn_scr"], env["x2_scr"], env["y_out"]
    from contextlib import ExitStack

    _phase_barrier(nc, blk, ctr, sems)
    with ExitStack() as stf:
        E1 = stf.enter_context
        ftile = E1(nc.sbuf_tensor("ftile", [128, 2 * D], BF16))
        x2l = E1(nc.sbuf_tensor("x2l", [128, 2 * D], F32))
        yt = E1(nc.sbuf_tensor("yt", [128, 2 * D], F32))
        lnt = E1(nc.sbuf_tensor("lntF", [128, 32], F32))
        ps_fin0 = E1(nc.psum_tensor("ps_fin0", [128, D], F32))
        ps_fin1 = E1(nc.psum_tensor("ps_fin1", [128, D], F32))

        for tb in range(8):
            buf = tb % 2
            ft = ftile[:, buf * D:buf * D + D]
            xx = x2l[:, buf * D:buf * D + D]
            yy = yt[:, buf * D:buf * D + D]
            psf = ps_fin0 if buf == 0 else ps_fin1

            @blk.sync
            def _(q, tb=tb, ft=ft, xx=xx, buf=buf):
                if tb >= 2:
                    q.wait_ge(s_mm, ctr.get(s_mm))
                    q.wait_ge(s_v, ctr.get(s_v))
                q.wait_ge(ldp[buf], ctr.get(ldp[buf]))
                q.dma_start(out=ft.rearrange("p (db t) -> p db t", t=128),
                            in_=ffn_scr[:, :, tb * 128:(tb + 1) * 128].rearrange("db p t -> p db t")).then_inc(ldp[buf], 16)
                q.dma_start(out=xx, in_=x2_scr[tb * 128:(tb + 1) * 128, :]).then_inc(ldp[buf], 16)
                ctr.inc(ldp[buf], 32)
            FL = ctr.get(ldp[buf])

            @blk.tensor
            def _(t, tb=tb, ft=ft, psf=psf, FL=FL, buf=buf):
                t.wait_ge(ldp[buf], FL)
                if tb >= 2:
                    t.wait_ge(s_s, ctr.get(s_s))
                mm = None
                for db in range(NDB):
                    mm = t.matmul(psf[:, db * 128:(db + 1) * 128],
                                  ft[:, db * 128:(db + 1) * 128],
                                  c_ident[:, :], start=True, stop=True)
                ctr.inc(s_mm, 1)
                mm.then_inc(s_mm, 1)
            FM = ctr.get(s_mm)

            @blk.vector
            def _(v, psf=psf, FM=FM):
                v.wait_ge(s_mm, FM)
                ins = None
                for i4 in range(4):
                    ins = v.bn_stats(lnt[:, i4 * 6:(i4 + 1) * 6], psf[:, i4 * 512:(i4 + 1) * 512])
                ctr.inc(s_v, 1); ins.then_inc(s_v, 1); v.wait_ge(s_v, ctr.get(s_v))
                ins = v.bn_aggr(lnt[:, 24:26], lnt[:, 0:24])
                ctr.inc(s_v, 1); ins.then_inc(s_v, 1); v.wait_ge(s_v, ctr.get(s_v))
                ctr.inc(s_v, 1)
                v.tensor_scalar(lnt[:, 26:27], lnt[:, 25:26], EPS_LN, None, ALU.add).then_inc(s_v, 1)
            V1 = ctr.get(s_v)

            @blk.scalar
            def _(s, V1=V1):
                s.wait_ge(s_v, V1)
                ctr.inc(s_s, 1)
                s.activation(lnt[:, 27:28], lnt[:, 26:27], AF.Sqrt).then_inc(s_s, 1)
            S1 = ctr.get(s_s)

            @blk.vector
            def _(v, S1=S1):
                v.wait_ge(s_s, S1)
                ins = v.reciprocal(lnt[:, 28:29], lnt[:, 27:28])
                ctr.inc(s_v, 1); ins.then_inc(s_v, 1); v.wait_ge(s_v, ctr.get(s_v))
                ctr.inc(s_v, 1)
                v.scalar_tensor_tensor(out=lnt[:, 29:30], in0=lnt[:, 24:25],
                                       scalar=-1.0, in1=lnt[:, 28:29],
                                       op0=ALU.mult, op1=ALU.mult).then_inc(s_v, 1)
            V2 = ctr.get(s_v)

            @blk.scalar
            def _(s, psf=psf, yy=yy, V2=V2, buf=buf, tb=tb):
                s.wait_ge(s_v, V2)
                if tb >= 2:
                    s.wait_ge(stp[buf], ctr.get(stp[buf]))
                ctr.inc(s_s, 1)
                s.activation(yy, psf[:, :], AF.Identity,
                             bias=lnt[:, 29:30], scale=lnt[:, 28:29]).then_inc(s_s, 1)
            S2 = ctr.get(s_s)

            @blk.vector
            def _(v, yy=yy, xx=xx, S2=S2):
                v.wait_ge(s_s, S2)
                ctr.inc(s_v, 1)
                v.tensor_tensor(yy, yy, xx, ALU.add).then_inc(s_v, 1)
            YV = ctr.get(s_v)

            @blk.gpsimd
            def _(g, tb=tb, yy=yy, YV=YV, buf=buf):
                g.wait_ge(s_v, YV)
                g.wait_ge(stp[buf], ctr.get(stp[buf]))
                g.dma_start(out=y_out[tb * 128:(tb + 1) * 128, :], in_=yy).then_inc(stp[buf], 16)
                ctr.inc(stp[buf], 16)

        @blk.sync
        def _(q):
            q.wait_ge(s_sta, ctr.get(s_sta))
            q.wait_ge(s_stb, ctr.get(s_stb))


def host_prep(inputs):
    x = np.ascontiguousarray(np.asarray(inputs["L0_attn_x"], np.float32))
    w_qkv = np.asarray(inputs["L0_attn_w_qkv"], np.float32)
    w_gate = np.asarray(inputs["L0_attn_w_gate"], np.float32)
    w_beta = np.asarray(inputs["L0_attn_w_beta"], np.float32)
    w_alpha = np.asarray(inputs["L0_attn_w_alpha"], np.float32)
    log_A = np.asarray(inputs["L0_attn_log_A"], np.float32)
    dt_bias = np.asarray(inputs["L0_attn_dt_bias"], np.float32)
    conv_w = np.asarray(inputs["L0_attn_conv_w"], np.float32)
    rms_w = np.asarray(inputs["L0_attn_rms_w"], np.float32)
    w_out = np.asarray(inputs["L0_attn_w_out"], np.float32)
    W_g = np.asarray(inputs["L0_ffn_W_g"], np.float32)
    W_u = np.asarray(inputs["L0_ffn_W_u"], np.float32)
    W_d = np.asarray(inputs["L0_ffn_W_d"], np.float32)

    wqkv_bl = np.ascontiguousarray(w_qkv.T.reshape(D, NFQ, 128).transpose(1, 0, 2))
    wgate_bl = np.ascontiguousarray(w_gate.T.reshape(D, NFG, 128).transpose(1, 0, 2))
    wba = np.ascontiguousarray(np.concatenate([w_beta, w_alpha], axis=0).T)
    convw = np.ascontiguousarray(
        conv_w[:, 0, :].reshape(NFQ, 128, 4).transpose(1, 0, 2).reshape(128, NFQ * 4))
    la = np.ascontiguousarray(np.stack([log_A, dt_bias], axis=1))
    rmsw = np.ascontiguousarray(rms_w[:, None])
    wout_bl = np.ascontiguousarray(w_out.T.reshape(D, 4, 512).transpose(1, 0, 2))
    wg_bl = np.ascontiguousarray(W_g.T.reshape(D, NHB, 128).transpose(1, 0, 2))
    wu_bl = np.ascontiguousarray(W_u.T.reshape(D, NHB, 128).transpose(1, 0, 2))
    wd_bl = np.ascontiguousarray(W_d.T.reshape(FFH, 16, 128).transpose(1, 0, 2))

    ident_bf = np.eye(128, dtype=NPBF16)
    ident_f = np.eye(128, dtype=np.float32)
    r_idx, c_idx = np.meshgrid(np.arange(128), np.arange(128), indexing="ij")
    maskLT = np.where(c_idx < r_idx, 0.0, NEG).astype(np.float32)
    maskUT = np.where(r_idx < c_idx, 0.0, NEG).astype(np.float32)
    maskUD = np.where(r_idx <= c_idx, -LSD, NEG).astype(np.float32)
    ones_col = np.ones((128, 1), dtype=NPBF16)
    ones_row_f = np.ones((1, 128), dtype=np.float32)
    zeros_f = np.zeros((128, 128), dtype=np.float32)
    ccols = np.zeros((128, 8), np.float32)
    ccols[:, 0] = EPS_L2
    ccols[:, 1] = EPS_RMS
    ccols[:, 2] = -LSD
    ccols[:, 3] = 2.0
    ccols[:, 4] = EPS_LN
    ccols[:, 5] = EPS_FFN
    ccols[:, 6] = 127.0
    ccols[:, 7] = 1.0

    in_maps = []
    for core in range(8):
        b, g = core // 4, core % 4
        t0 = g * TPC
        xs = np.zeros((TP, D), np.float32)
        lo = max(0, t0 - HALO)
        xs[HALO - (t0 - lo): HALO - (t0 - lo) + (t0 + TPC - lo)] = x[b, lo:t0 + TPC]
        hop_m = np.zeros((128, 7), np.float32)
        hop_imi = np.zeros((128, 7 * 128), np.float32)
        for j in range(7):
            use = 1.0 if (j // 4 == b and j % 4 < g) else 0.0
            hop_m[:, j] = use
            hop_imi[:, j * 128:(j + 1) * 128] = (1.0 - use) * np.eye(128, dtype=np.float32)
        in_maps.append({
            "x": xs, "wqkv_b": wqkv_bl, "wgate_b": wgate_bl, "wba": wba,
            "convw": convw, "loga_params": la, "rmsw": rmsw,
            "wout_b": wout_bl, "wg_b": wg_bl, "wu_b": wu_bl, "wd_b": wd_bl,
            "ident_bf": ident_bf, "ident_f": ident_f, "maskLT": maskLT,
            "maskUT": maskUT, "maskUD": maskUD,
            "ones_col_bf": ones_col, "ones_row_f": ones_row_f, "ccols": ccols,
            "ones_row_bf": np.ones((1, 128), dtype=NPBF16), "zeros_f": zeros_f,
            "hop_mcol": hop_m, "hop_imi": hop_imi,
        })
    return in_maps


def kernel(**inputs):
    in_maps = host_prep(inputs)
    nc, _ = build()
    res = run_bass_kernel_spmd(nc, in_maps, core_ids=list(range(8)))
    y = np.empty((2, 4096, D), np.float32)
    for core in range(8):
        b, g = core // 4, core % 4
        y[b, g * TPC:(g + 1) * TPC] = res.results[core]["y"]
    return y
